# version 05
# baseline (speedup 1.0000x reference)
import numpy as np
import concourse.bass as bass
import concourse.mybir as mybir
from concourse.bass import IndirectOffsetOnAxis
from concourse.bass_utils import run_bass_kernel_spmd

F32 = mybir.dt.float32
BF16 = mybir.dt.bfloat16
U32 = mybir.dt.uint32
U8 = mybir.dt.uint8
AF = mybir.ActivationFunctionType
ALU = mybir.AluOpType
AX = mybir.AxisListType


COMPUTE = ('T', 'V', 'S', 'P')
QUEUES = ('SQ', 'AQ', 'GQ')
ALL_ENG = COMPUTE + QUEUES
STREAM = {'T': 'tensor', 'V': 'vector', 'S': 'scalar', 'P': 'gpsimd',
          'SQ': 'sync', 'AQ': 'scalar', 'GQ': 'gpsimd'}
INC = {'T': 1, 'V': 1, 'S': 1, 'P': 1, 'SQ': 16, 'AQ': 16, 'GQ': 16}
AUTO_SERIAL = ('V', 'S', 'P')


class _Op:
    __slots__ = ('eng', 'idx', 'fn', 'deps', 'needs_inc', 'label', 'slot')

    def __init__(self, eng, idx, fn, deps, label, slot=None):
        self.eng, self.idx, self.fn, self.deps = eng, idx, fn, deps
        self.needs_inc = False
        self.label = label
        self.slot = slot


class Prog:
    def __init__(self, nc):
        self.nc = nc
        self.seq = []
        self.n = {e: 0 for e in ALL_ENG}
        self.bounds_vals = set()
        self._bc_regs = {}

    def bc_reg(self, val):
        return self._bc_regs[val]

    def add(self, eng, fn, deps=(), label='', slot=None, barrier=False):
        deps = [dd for dd in deps if dd is not None]
        if (eng in AUTO_SERIAL or (barrier and eng in COMPUTE)) \
                and self.n[eng] > 0:
            deps.append((eng, self.n[eng] - 1))
        if eng in QUEUES:
            assert slot is not None, f'queue op {label} needs a slot'
        op = _Op(eng, self.n[eng], fn, deps, label, slot)
        self.n[eng] += 1
        self.seq.append(op)
        return (eng, op.idx)

    def barrier(self, eng, deps, label='barrier'):
        return self.add(eng, lambda s: s.nop(), deps, label, barrier=True)

    def emit(self):
        self.label_map = {}
        nc = self.nc
        by_eng = {e: [] for e in ALL_ENG}
        for op in self.seq:
            by_eng[op.eng].append(op)
        for op in self.seq:
            if op.eng in QUEUES:
                op.needs_inc = True
            for (de, di) in op.deps:
                by_eng[de][di].needs_inc = True
        slot_of = {}
        slot_counts = {}
        for e in ALL_ENG:
            c = 0
            for op in by_eng[e]:
                if op.needs_inc:
                    if e in QUEUES:
                        key = (e, op.slot)
                        slot_counts[key] = slot_counts.get(key, 0) + 1
                        slot_of[(e, op.idx)] = (key, slot_counts[key] * INC[e])
                    else:
                        slot_of[(e, op.idx)] = ((e, 0), (c + 1) * INC[e])
                    c += 1

        self._cms = []
        sems = {}
        for e in COMPUTE:
            cm = nc.semaphore(f'sem_{e}_0')
            sems[(e, 0)] = cm.__enter__()
            self._cms.append(cm)
        for key in slot_counts:
            cm = nc.semaphore(f'sem_{key[0]}_{key[1]}')
            sems[key] = cm.__enter__()
            self._cms.append(cm)
        blk_cm = nc.Block()
        blk = blk_cm.__enter__()
        self._cms.append(blk_cm)

        streams = {}
        for op in self.seq:
            streams.setdefault(STREAM[op.eng], []).append(op)

        def make_body(ops_list, is_gpsimd=False):
            def body(section):
                if is_gpsimd:
                    for val in sorted(self.bounds_vals):
                        r = section.alloc_register(f'bc{val}')
                        section.reg_mov(r, val)
                        self._bc_regs[val] = r
                for op in ops_list:
                    need = {}
                    for (de, di) in op.deps:
                        key, v = slot_of[(de, di)]
                        need[key] = max(need.get(key, 0), v)
                    for key, v in need.items():
                        section.wait_ge(sems[key], v)
                    inst = op.fn(section)
                    if inst is not None:
                        try:
                            self.label_map[inst.ins.name] = (op.eng, op.idx,
                                                             op.label)
                        except Exception:
                            pass
                    if op.needs_inc:
                        assert inst is not None, f'{op.label}: needs inc, no inst'
                        key, v = slot_of[(op.eng, op.idx)]
                        inst.then_inc(sems[key], INC[op.eng])
            return body

        for stream_name, ops_list in streams.items():
            getattr(blk, stream_name)(
                make_body(ops_list, is_gpsimd=(stream_name == 'gpsimd')))
        return blk

    def close(self, exc=(None, None, None)):
        for cm in reversed(self._cms):
            cm.__exit__(*exc)


ARENA_NB = 82688


class Ctx:
    def __init__(self, T, NQ, NKV, n_iters=50, bis_iters=30, detect_races=True):
        self.T = T
        self.F = T // 128
        self.NG = T // 512
        self.NQ, self.NKV = NQ, NKV
        self.n_iters = n_iters
        self.bis_iters = bis_iters
        self.nc = bass.Bass('TRN2', target_bir_lowering=False,
                            detect_race_conditions=detect_races)
        self.p = Prog(self.nc)
        self._sb_bytes = 0
        self._cms = []
        self._wstg_war = [None] * 2
        self._arena = None
        self._aL, self._aR = 0, ARENA_NB
        self._apeak = 0
        self._groups = {}
        self._gstack = {'left': [], 'right': []}

    def sb(self, name, shape, dtype, group=None, side=None):
        if group is None:
            cm = self.nc.sbuf_tensor(name, list(shape), dtype)
            t = cm.__enter__()
            self._cms.append(cm)
            self._sb_bytes += int(np.prod(shape[1:])) * mybir.dt.size(dtype) * 128
            return t
        if self._arena is None:
            cm = self.nc.sbuf_tensor('arena', [128, ARENA_NB], U8)
            self._arena = cm.__enter__()
            self._cms.append(cm)
            self._sb_bytes += ARENA_NB * 128
        nb = int(np.prod(shape[1:])) * mybir.dt.size(dtype)
        nb_al = (nb + 31) & ~31
        sd = side or 'left'
        if group not in self._groups:
            self._groups[group] = (sd, self._aL if sd == 'left' else self._aR)
            self._gstack[sd].append(group)
        assert self._groups[group][0] == sd, f'{group}: side mismatch'
        assert self._gstack[sd][-1] == group, \
            f'{group}: allocations into non-top group'
        if sd == 'left':
            off = self._aL
            self._aL += nb_al
        else:
            self._aR -= nb_al
            off = self._aR
        assert self._aL <= self._aR, f'arena overflow at {name}'
        self._apeak = max(self._apeak, self._aL + (ARENA_NB - self._aR))
        return self._arena[0:shape[0], off:off + nb].bitcast(dtype)

    def free_group(self, group):
        sd, saved = self._groups.pop(group)
        assert self._gstack[sd][-1] == group, f'free {group}: not stack top'
        self._gstack[sd].pop()
        if sd == 'left':
            self._aL = saved
        else:
            self._aR = saved

    def ps(self, name, shape, dtype=F32):
        cm = self.nc.psum_tensor(name, list(shape), dtype)
        t = cm.__enter__()
        self._cms.append(cm)
        return t

    def dram(self, name, shape, dtype, kind):
        return self.nc.dram_tensor(name, list(shape), dtype, kind=kind).ap()

    def close(self):
        self.p.close()
        for cm in reversed(self._cms):
            cm.__exit__(None, None, None)



W_QKV = 0
W_OUTL = 12288
W_QH = 16384
W_KVH = 20480
W_OUTH = 28672
W_GQ, W_GK, W_GV, W_GG, W_GO = 0, 8192, 16384, 24576, 32768


def declare(ctx):
    T, DM = ctx.T, 1024
    d = {}
    d['x'] = ctx.dram('x', [T, DM], F32, 'ExternalInput')
    d['out'] = ctx.dram('out', [T, DM], F32, 'ExternalOutput')
    for nm, shp in [('Wq', [DM, DM]), ('Wk', [DM, DM]), ('Wv', [DM, DM]),
                    ('Wg', [DM, DM]), ('Wo', [DM, DM]), ('Wg1', [DM, 16]),
                    ('Wg2a', [17, DM]), ('g_norm_w', [1, 64]),
                    ('qkv_l', [DM, 1536]), ('w_out_l', [512, DM]),
                    ('null_q', [1, DM]), ('rt_q', [1, DM]), ('rt_kv', [1, DM]),
                    ('rms_g', [1, DM]), ('w_q_h', [DM, 512]),
                    ('w_kv_h', [DM, DM]), ('null_kv', [2, 512]),
                    ('w_out_h', [512, DM]), ('ln_g', [1, DM]), ('ln_b', [1, DM])]:
        d[nm] = ctx.dram(nm, shp, F32, 'ExternalInput')
    d['XLN'] = ctx.dram('XLN', [T, DM], F32, 'Internal')
    d['WSTASH'] = ctx.dram('WSTASH', [128, 32768], BF16, 'Internal')
    d['X2'] = ctx.dram('X2', [T, DM], F32, 'Internal')
    d['X3'] = ctx.dram('X3', [T, DM], F32, 'Internal')
    d['XLN3'] = ctx.dram('XLN3', [T, DM], F32, 'Internal')
    d['LIGHT'] = ctx.dram('LIGHT', [T, DM], F32, 'Internal')
    d['LISTQ'] = ctx.dram('LISTQ', [max(ctx.NQ, 128)], U32, 'Internal')
    d['LISTKV'] = ctx.dram('LISTKV', [max(ctx.NKV, 128)], U32, 'Internal')
    ctx.d = d

    F = ctx.F
    ctx.wreg = ctx.sb('wreg', [128, 40960], BF16)
    ctx.wstg = [ctx.sb(f'wstg{i}', [128, 512], F32) for i in range(2)]
    ctx.ones_col = ctx.sb('ones_col', [128, 1], F32)
    ctx.ones_colb = ctx.sb('ones_colb', [128, 1], BF16)
    ctx.ones_rowb = ctx.sb('ones_rowb', [1, 512], BF16)
    ctx.ones_rowf = ctx.sb('ones_rowf', [1, 128], F32)
    ctx.half_rowf = ctx.sb('half_rowf', [1, 128], F32)
    ctx.consts = ctx.sb('consts', [128, 8], F32)
    ctx.ident_bf = ctx.sb('ident_bf', [128, 128], BF16)
    ctx.ident_f = ctx.sb('ident_f', [128, 128], F32)
    ctx.trimask = ctx.sb('trimask', [128, 128], BF16)
    ctx.rtq_b = ctx.sb('rtq_b', [128, DM], F32)
    ctx.rtkv_b = ctx.sb('rtkv_b', [128, DM], F32)
    ctx.nullh_b = ctx.sb('nullh_b', [128, DM], F32)
    ctx.smallw = ctx.sb('smallw', [128, 40], F32)
    ctx.c0row = ctx.sb('c0row', [1, 1536], BF16)
    ctx.iota_tok = ctx.sb('iota_tok', [128, F], F32)
    ctx.gnw_col = ctx.sb('gnw_col', [128, 1], F32)
    ctx.wg1_bf = ctx.sb('wg1_bf', [128, 128], BF16)
    ctx.wg2a_bf = ctx.sb('wg2a_bf', [17, 1024], BF16)
    ctx.nullk_bf = ctx.sb('nullk_bf', [128, 8], BF16)
    ctx.nullv_bf = ctx.sb('nullv_bf', [1, 520], BF16)
    ctx.s2 = ctx.sb('s2', [128, 2 * F], F32)
    ctx.relu2 = ctx.sb('relu2', [128, 2 * F], F32)
    ctx.e2 = ctx.sb('e2', [128, 2 * F], F32)
    ctx.racc = ctx.sb('racc', [128, 8], F32)
    ctx.rsc = ctx.sb('rsc', [128, 32], F32)
    ctx.aval = ctx.sb('aval', [128, 2], F32)
    ctx.logk = ctx.sb('logk', [128, 2], F32)
    ctx.selscr = ctx.sb('selscr', [128, 4 * F], F32)
    ctx.dest_f = ctx.sb('dest_f', [128, F], F32)
    ctx.dest_u = ctx.sb('dest_u', [128, F], U32)
    ctx.iota_u = ctx.sb('iota_u', [128, F], U32)
    ctx.idxq = ctx.sb('idxq', [128, 16], U32)
    ctx.idxkv = ctx.sb('idxkv', [128, 16], U32)
    ctx.xb = [ctx.sb(f'xb{i}', [128, DM], F32) for i in range(3)]
    ctx.stat = ctx.sb('stat', [128, 16], F32)
    ctx.colv = [ctx.sb(f'colv{i}', [128, 1], F32) for i in range(8)]
    ctx.zeroF = ctx.sb('zeroF', [128, max(F, 32)], F32)
    ctx.zero128 = ctx.sb('zero128', [128, 128], F32)
    ctx.skipc = ctx.sb('skipc', [128, 16], U32)
    ctx.junk = ctx.sb('junk', [128, 128], F32)
    ctx.junkw = ctx.sb('junkw', [128, DM], F32)

    ctx.ps_a = ctx.ps('ps_a', [128, 512])
    ctx.ps_b = ctx.ps('ps_b', [128, 512])
    ctx.ps_c = ctx.ps('ps_c', [128, 512])
    ctx.ps_d = ctx.ps('ps_d', [128, 512])
    ctx.ps_e = ctx.ps('ps_e', [128, 512])
    ctx.ps_f = ctx.ps('ps_f', [128, 512])
    ctx.ps_tr = ctx.ps('ps_tr', [128, 1024], BF16)
    ctx.ps_u = ctx.ps('ps_u', [128, 512])


def prep_consts(ctx, marks):
    p = ctx.p
    F = ctx.F
    p.add('P', lambda s: s.memset(ctx.ones_col[:], 1.0), label='ones')
    m_onesb = p.add('P', lambda s: s.memset(ctx.ones_colb[:], 1.0))
    m_onesr = p.add('P', lambda s: s.memset(ctx.ones_rowb[:], 1.0))
    m_onesrf = p.add('P', lambda s: s.memset(ctx.ones_rowf[:], 1.0))
    m_half = p.add('P', lambda s: s.memset(ctx.half_rowf[:], 0.5))
    p.add('P', lambda s: s.memset(ctx.consts[:, 0:1], 1e-5))
    p.add('P', lambda s: s.memset(ctx.consts[:, 1:2], 1e-12))
    p.add('P', lambda s: s.memset(ctx.consts[:, 2:3], -1.0))
    p.add('P', lambda s: s.memset(ctx.consts[:, 3:4], 0.0625))
    p.add('P', lambda s: s.memset(ctx.consts[:, 4:5], -0.0625))
    p.add('P', lambda s: s.memset(ctx.consts[:, 5:6], 1.0 / 64.0))
    m_iotaf = p.add('P', lambda s: s.iota(ctx.iota_tok[:], [[1, F]], base=0,
                                          channel_multiplier=F,
                                          allow_small_or_imprecise_dtypes=True))
    m_iotau = p.add('P', lambda s: s.iota(ctx.iota_u[:], [[1, F]], base=0,
                                          channel_multiplier=F))
    m_ci = p.add('P', lambda s: s.iota(ctx.junk[:], [[1, 128]], base=0,
                                       channel_multiplier=0,
                                       allow_small_or_imprecise_dtypes=True))
    m_pi = p.add('P', lambda s: s.iota(ctx.colv[0][:], [[0, 1]], base=0,
                                       channel_multiplier=1,
                                       allow_small_or_imprecise_dtypes=True))
    m_id = p.add('V', lambda s: s.tensor_scalar(ctx.ident_bf[:], ctx.junk[:],
                                                ctx.colv[0][:], None,
                                                op0=ALU.is_equal),
                 deps=[m_ci, m_pi], label='ident')
    m_idf = p.add('V', lambda s: s.tensor_scalar(ctx.ident_f[:], ctx.junk[:],
                                                 ctx.colv[0][:], None,
                                                 op0=ALU.is_equal),
                  deps=[m_ci, m_pi], label='identf')
    marks['identf'] = m_idf
    m_tri = p.add('V', lambda s: s.tensor_scalar(ctx.trimask[:], ctx.junk[:],
                                                 ctx.colv[0][:], None,
                                                 op0=ALU.is_ge),
                  deps=[m_ci, m_pi], label='trimask')
    import math
    lkq = float(np.log(np.float32(min(ctx.NQ * 9.0 / 8.0, float(ctx.T)))
                       ).astype(np.float32))
    lkkv = float(np.log(np.float32(min(ctx.NKV * 9.0 / 8.0, float(ctx.T)))
                        ).astype(np.float32))
    p.add('P', lambda s: s.memset(ctx.zeroF[:], 0.0))
    p.add('P', lambda s: s.memset(ctx.zero128[:], 0.0))
    m_skip = p.add('P', lambda s: s.iota(ctx.skipc[:], [[0, 16]],
                                         base=(1 << 22) + 7,
                                         channel_multiplier=0))
    marks['skipc'] = m_skip
    p.add('P', lambda s: s.memset(ctx.logk[:, 0:1], lkq))
    m_lk = p.add('P', lambda s: s.memset(ctx.logk[:, 1:2], lkkv))
    marks['ident'] = m_id
    marks['tri'] = m_tri
    marks['onesb'] = m_onesb
    marks['onesr'] = m_onesr
    marks['onesrf'] = m_onesrf
    marks['half'] = m_half
    marks['iota_u'] = m_iotau
    marks['iota_f'] = m_iotaf
    marks['consts'] = p.barrier('P', [m_lk], label='consts_done')

    d = ctx.d
    rowbuf = ctx.junkw
    war = None
    psu_war = None
    for nm, dst, scale_row, smark in [
            ('rt_q', ctx.rtq_b, ctx.ones_rowf, m_onesrf),
            ('rt_kv', ctx.rtkv_b, ctx.ones_rowf, m_onesrf),
            ('null_q', ctx.nullh_b, ctx.half_rowf, m_half)]:
        m_ld = p.add('SQ', (lambda s, nm=nm: s.dma_start(out=rowbuf[0:1, :],
                                                         in_=d[nm])),
                     deps=[war] if war else [], label=f'ld_{nm}', slot='junkw')
        last = None
        for off in range(0, 1024, 512):
            def mm(s, off=off, scale_row=scale_row):
                return s.matmul(ctx.ps_u[:, 0:512], scale_row[:, 0:128],
                                rowbuf[0:1, off:off + 512], start=True, stop=True)
            m1 = p.add('T', mm, deps=[m_ld, smark, psu_war],
                       label=f'bc_{nm}_{off}')
            def cp(s, off=off, dst=dst):
                return s.tensor_copy(dst[:, off:off + 512], ctx.ps_u[:, 0:512])
            m2 = p.add('V', cp, deps=[m1], label=f'bccp_{nm}_{off}')
            psu_war = m2
            last = m2
        war = last
    marks['rt_b'] = war
    marks['rowbuf_free'] = war
    marks['junkw_free'] = war
    marks['psu_war'] = psu_war



def load_fold_weight(ctx, dram_ap, n_cols, wreg_off, fold, tag,
                     c0_dst=None, kchunks=8, colscale=None, extra_dep=None):
    p = ctx.p
    PIECE = 512
    npieces = (n_cols + PIECE - 1) // PIECE
    war = ctx._wstg_war
    if extra_dep is not None:
        m_x = p.barrier('V', [war[0], war[1], extra_dep],
                        label=f'w{tag}_xdep')
        war[0] = m_x
        war[1] = m_x
    cast_last = None
    c0_last = None
    si = 0
    for c in range(kchunks):
        for j in range(npieces):
            w = min(PIECE, n_cols - j * PIECE)
            sbuf = ctx.wstg[si % 2]
            def ld(s, c=c, j=j, w=w, sbuf=sbuf):
                return s.dma_start(out=sbuf[:, 0:w],
                                   in_=dram_ap[c * 128:(c + 1) * 128,
                                               j * PIECE:j * PIECE + w])
            m_ld = p.add('SQ', ld, deps=[war[si % 2]], label=f'w{tag}_ld{c}_{j}',
                         slot=f'wstg{si % 2}')
            consumers = []
            if c0_dst is not None:
                psum_t, po, lnb_tile, m_lnb = c0_dst[j]
                def mmc0(s, c=c, w=w, sbuf=sbuf, psum_t=psum_t, po=po,
                         lnb_tile=lnb_tile):
                    return s.matmul(psum_t[0:1, po:po + w], lnb_tile[:, c:c + 1],
                                    sbuf[:, 0:w], start=(c == 0),
                                    stop=(c == kchunks - 1))
                c0_last = p.add('T', mmc0, deps=[m_ld, m_lnb],
                                label=f'w{tag}_c0_{c}_{j}')
                consumers.append(c0_last)
            dst = ctx.wreg[:, wreg_off + c * n_cols + j * PIECE:
                           wreg_off + c * n_cols + j * PIECE + w]
            if fold is not None:
                ft, ft_mark = fold
                cs = colscale(j) if colscale is not None else 1.0
                def cast(s, c=c, w=w, sbuf=sbuf, dst=dst, ft=ft, cs=cs):
                    return s.tensor_scalar(dst, sbuf[:, 0:w], ft[:, c:c + 1],
                                           cs, op0=ALU.mult, op1=ALU.mult)
                m_cast = p.add('V', cast, deps=[m_ld, ft_mark],
                               label=f'w{tag}_cast{c}_{j}')
            else:
                def cast(s, w=w, sbuf=sbuf, dst=dst):
                    return s.tensor_copy(dst, sbuf[:, 0:w])
                m_cast = p.add('V', cast, deps=[m_ld], label=f'w{tag}_cast{c}_{j}')
            consumers.append(m_cast)
            war[si % 2] = (p.barrier('V', consumers) if len(consumers) > 1
                           else m_cast)
            cast_last = m_cast
            si += 1
    return cast_last, c0_last


def load_vec_tile(ctx, dram_row, dst_cols, n, tag, dep=None, marks=None,
                  nrows=128):
    p = ctx.p
    nch = n // nrows
    m_ld = p.add('SQ', (lambda s: s.dma_start(out=ctx.junkw[0:1, 0:n],
                                              in_=dram_row)),
                 deps=[dep, marks.get('junkw_free') if marks else None],
                 label=f'vec_{tag}_ld', slot='junkw')
    tr_ms = []
    for c in range(nch):
        def tr(s, c=c):
            return s.transpose(ctx.ps_u[0:nrows, c:c + 1],
                               ctx.junkw[0:1, c * nrows:(c + 1) * nrows],
                               ctx.ident_f[0:1, 0:1])
        tr_ms.append(p.add('T', tr,
                           deps=[m_ld, marks['identf'] if marks else None,
                                 marks.get('_psu_war') if marks else None],
                           label=f'vec_{tag}_t{c}'))
    def ev(s):
        return s.tensor_copy(dst_cols[0:nrows, :], ctx.ps_u[0:nrows, 0:nch])
    m_ev = p.add('V', ev, deps=tr_ms, label=f'vec_{tag}_ev')
    if marks is not None:
        marks['_psu_war'] = m_ev
        marks['junkw_free'] = m_ev
    return m_ev


def prep_weights_ffn(ctx, marks, tag, dep=None, reload=False):
    p, d = ctx.p, ctx.d
    if dep is not None:
        ctx._wstg_war = [dep] * 2
    if not reload:
        m_lng = load_vec_tile(ctx, d['ln_g'], ctx.smallw[:, 0:8], 1024,
                              f'lng{tag}', dep=dep, marks=marks)
        m_lnb = load_vec_tile(ctx, d['ln_b'], ctx.smallw[:, 8:16], 1024,
                              f'lnb{tag}', marks=marks)
        m_rms = load_vec_tile(ctx, d['rms_g'], ctx.smallw[:, 16:24], 1024,
                              f'rms{tag}', marks=marks)
        lng = (ctx.smallw[:, 0:8], m_lng)
        rmsg = (ctx.smallw[:, 16:24], m_rms)
        c0_dst = [(ctx.ps_a, 0, ctx.smallw[:, 8:16], m_lnb),
                  (ctx.ps_b, 0, ctx.smallw[:, 8:16], m_lnb),
                  (ctx.ps_c, 0, ctx.smallw[:, 8:16], m_lnb)]
        m1, m1c0 = load_fold_weight(ctx, d['qkv_l'], 1536, W_QKV, lng,
                                    f'qkv{tag}', c0_dst=c0_dst,
                                    colscale=lambda j: 0.125 if j == 0 else 1.0)
        c0e = None
        for j in range(3):
            pt, po, _, _ = c0_dst[j]
            sc = 0.125 if j == 0 else 1.0
            def ev(s, pt=pt, po=po, j=j, sc=sc):
                return s.tensor_scalar(ctx.c0row[:, j * 512:(j + 1) * 512],
                                       pt[0:1, po:po + 512], sc, None,
                                       op0=ALU.mult)
            c0e = p.add('V', ev, deps=[m1c0], label=f'c0ev{tag}{j}')
        xq = xo = xv = xkv = None
    else:
        ws = d['WSTASH']
        ms = []
        for nm, a, bnd, xdep in [
                ('qo', 0, 16384, marks['gla_qkfree']),
                ('qh', 16384, 20480, marks['gla_vfree']),
                ('kvo', 20480, 32768, marks['gla_done'])]:
            m_l = p.add('SQ', (lambda s, a=a, bnd=bnd: s.dma_start(
                out=ctx.wreg[:, a:bnd], in_=ws[:, a:bnd])),
                deps=[marks['wstash'], xdep], label=f'wrld_{nm}',
                slot=f'wrld{nm}')
            ms.append(m_l)
        mb = p.barrier('V', ms, label=f'wffn_done_{tag}')
        marks[f'wffn_{tag}'] = mb
        return mb
    m2, _ = load_fold_weight(ctx, d['w_out_l'], 1024, W_OUTL, None,
                             f'wol{tag}', kchunks=4, extra_dep=xo)
    m3, _ = load_fold_weight(ctx, d['w_q_h'], 512, W_QH, rmsg, f'wqh{tag}',
                             extra_dep=xv)
    m4, _ = load_fold_weight(ctx, d['w_kv_h'], 1024, W_KVH, rmsg, f'wkvh{tag}',
                             extra_dep=xkv)
    m5, _ = load_fold_weight(ctx, d['w_out_h'], 1024, W_OUTH, None,
                             f'woh{tag}', kchunks=4)
    mb = p.barrier('V', [m1, c0e, m2, m3, m4, m5], label=f'wffn_done_{tag}')
    marks[f'wffn_{tag}'] = mb
    marks['wstash'] = p.add('AQ', lambda s: s.dma_start(
        out=d['WSTASH'], in_=ctx.wreg[:, 0:32768]), deps=[mb],
        label='wstash', slot='wstash')
    return mb


def prep_weights_gla(ctx, marks, dep):
    p, d = ctx.p, ctx.d
    ms = []
    m_ws = marks.get('wstash')
    for nm, off, xdep in [('Wq', W_GQ, marks.get('x2_done_1')),
                          ('Wk', W_GK, marks.get('x2_done_1')),
                          ('Wv', W_GV, marks.get('heavy_1')),
                          ('Wg', W_GG, marks.get('heavy_1'))]:
        xd = p.barrier('V', [xdep, m_ws], label=f'g{nm}_ws')
        m, _ = load_fold_weight(ctx, d[nm], 1024, off, None, f'g{nm}',
                                extra_dep=xd)
        ms.append(m)
    m_gt = load_vec_tile(ctx, d['g_norm_w'], ctx.gnw_col[:, 0:1], 64, 'gnw',
                         dep=marks.get('heavy_1'), marks=marks, nrows=64)
    p.add('V', lambda s: s.stream_shuffle(ctx.gnw_col[64:96, 0:1],
                                          ctx.gnw_col[0:32, 0:1],
                                          list(range(32))), deps=[m_gt])
    p.add('V', lambda s: s.stream_shuffle(ctx.gnw_col[96:128, 0:1],
                                          ctx.gnw_col[32:64, 0:1],
                                          list(range(32))))
    m_rep = p.add('V', lambda s: s.tensor_copy(
        ctx.smallw[:, 24:32], ctx.gnw_col[:, 0:1].to_broadcast([128, 8])),
        label='gnwrep')
    m, _ = load_fold_weight(ctx, d['Wo'], 1024, W_GO,
                            (ctx.smallw[:, 24:32], m_rep), 'gWo')
    ms.append(m)
    for c in range(8):
        def ld(s, c=c):
            return s.dma_start(out=ctx.wstg[c % 2][:, 0:16],
                               in_=d['Wg1'][c * 128:(c + 1) * 128, :])
        m_ld = p.add('SQ', ld, deps=[ctx._wstg_war[c % 2]], label=f'wg1ld{c}',
                     slot=f'wstg{c % 2}')
        def cast(s, c=c):
            return s.tensor_copy(ctx.wg1_bf[:, 16 * c:16 * c + 16],
                                 ctx.wstg[c % 2][:, 0:16])
        m_c = p.add('V', cast, deps=[m_ld], label=f'wg1c{c}')
        ctx._wstg_war[c % 2] = m_c
        ms.append(m_c)
    m_ld = p.add('SQ', lambda s: s.dma_start(out=ctx.junkw[0:17, :],
                                             in_=d['Wg2a']),
                 deps=[marks.get('junkw_free')], label='wg2ld', slot='junkw')
    m_w2 = p.add('V', lambda s: s.tensor_copy(ctx.wg2a_bf[:], ctx.junkw[0:17, :]),
                 deps=[m_ld], label='wg2c')
    ms.append(m_w2)
    marks['junkw_free'] = m_w2
    mb = p.barrier('V', ms, label='wgla_done')
    marks['wgla'] = mb
    return mb


def prep_null_kv(ctx, marks, dep=None):
    p, d = ctx.p, ctx.d
    m_ld = p.add('SQ', lambda s: s.dma_start(out=ctx.junkw[0:1, 0:512],
                                             in_=d['null_kv'][0:1, :]),
                 deps=[dep, marks.get('junkw_free')], label='nkv_ld', slot='junkw')
    tr_ms = []
    for h in range(8):
        def tr(s, h=h):
            return s.transpose(ctx.ps_u[0:64, h:h + 1],
                               ctx.junkw[0:1, h * 64:h * 64 + 64],
                               ctx.ident_f[0:1, 0:1])
        tr_ms.append(p.add('T', tr, deps=[m_ld, marks['identf'],
                                          marks.get('_psu_war')],
                           label=f'nkv_t{h}'))
    m_ev = p.add('V', lambda s: s.tensor_copy(ctx.nullk_bf[0:64, :],
                                              ctx.ps_u[0:64, 0:8]),
                 deps=tr_ms, label='nkv_ev')
    marks['_psu_war'] = m_ev
    p.add('V', lambda s: s.stream_shuffle(ctx.nullk_bf[64:96, :],
                                          ctx.nullk_bf[0:32, :],
                                          list(range(32))))
    p.add('V', lambda s: s.stream_shuffle(ctx.nullk_bf[96:128, :],
                                          ctx.nullk_bf[32:64, :],
                                          list(range(32))))
    m1c = p.add('V', lambda s: s.nop(), label='nkv_kdone')
    m2 = p.add('SQ', lambda s: s.dma_start(out=ctx.junkw[0:1, 512:1024],
                                           in_=d['null_kv'][1:2, :]),
                deps=[m_ld], label='nkv_v', slot='junkw')
    vview = ctx.nullv_bf[0:1, :].rearrange('o (h e) -> o h e', e=65)
    m_vc = p.add('V', (lambda s: s.tensor_copy(
        vview[:, :, 0:64],
        ctx.junkw[0:1, 512:1024].rearrange('o (h e) -> o h e', e=64))),
        deps=[m2], label='nkv_vc')
    m2c = p.add('V', lambda s: s.memset(vview[:, :, 64:65], 1.0),
                label='nkv_ones')
    marks['junkw_free'] = m2c
    marks['nullkv'] = p.barrier('V', [m1c, m2c], label='nullkv_done')
    return marks['nullkv']



def phase_ln_scores(ctx, marks, src_ap, xln_ap, tag, dep=None):
    p = ctx.p
    F = ctx.F
    nbuf = len(ctx.xb)
    war = [marks.get(f'xb{i}_war') for i in range(nbuf)]
    last_marks = []
    src = src_ap if src_ap is not None else xln_ap
    src_r = src.rearrange('(q f) d -> f q d', f=F)
    xln_r = xln_ap.rearrange('(q f) d -> f q d', f=F) if src_ap is not None else None
    junk_dep = marks.get('junkw_free')
    for f in range(F):
        xb = ctx.xb[f % nbuf]
        m_ld = p.add('SQ', (lambda s, f=f, xb=xb: s.dma_start(out=xb[:],
                                                              in_=src_r[f])),
                     deps=[war[f % nbuf], dep], label=f'{tag}_ld{f}',
                     slot=f'xb{f % nbuf}')
        if src_ap is not None:
            p.add('V', (lambda s, xb=xb: s.bn_stats(ctx.stat[:, 0:6],
                                                    xb[:, 0:512])),
                  deps=[m_ld], label=f'{tag}_bn1_{f}')
            p.add('V', (lambda s, xb=xb: s.bn_stats(ctx.stat[:, 6:12],
                                                    xb[:, 512:1024])),
                  deps=[m_ld], label=f'{tag}_bn2_{f}')
            m_ag = p.add('V', lambda s: s.bn_aggr(ctx.stat[:, 12:14],
                                                  ctx.stat[:, 0:12]),
                         label=f'{tag}_ag{f}')
            m_sq = p.add('S', lambda s: s.activation(ctx.colv[1][:],
                                                     ctx.stat[:, 13:14], AF.Sqrt,
                                                     bias=ctx.consts[:, 0:1]),
                         deps=[m_ag, marks['consts']], label=f'{tag}_sq{f}')
            m_rc = p.add('V', lambda s: s.reciprocal(ctx.colv[2][:],
                                                     ctx.colv[1][:]),
                         deps=[m_sq], label=f'{tag}_rc{f}')
            m_nb = p.add('V', lambda s: s.scalar_tensor_tensor(
                ctx.colv[3][:], ctx.stat[:, 12:13], -1.0, ctx.colv[2][:],
                op0=ALU.mult, op1=ALU.mult), label=f'{tag}_nb{f}')
            m_ln = p.add('S', (lambda s, xb=xb: s.activation(
                xb[:], xb[:], AF.Identity, bias=ctx.colv[3][:],
                scale=ctx.colv[2][:])), deps=[m_rc, m_nb], label=f'{tag}_ln{f}')
            m_st = p.add('AQ', (lambda s, f=f, xb=xb: s.dma_start(
                out=xln_r[f], in_=xb[:])), deps=[m_ln], label=f'{tag}_st{f}',
                         slot=f'xlnst{f % nbuf}')
            marks[f'xlnt_{tag}_{f}'] = m_st
        else:
            m_ln = m_ld
            m_st = None
        m_sc1 = p.add('V', (lambda s, f=f, xb=xb: s.scalar_tensor_tensor(
            ctx.junkw[:], xb[:], 1.0, ctx.rtq_b[:], op0=ALU.mult, op1=ALU.mult,
            accum_out=ctx.s2[:, f:f + 1])),
            deps=[m_ln, marks['rt_b'], junk_dep], label=f'{tag}_scq{f}')
        junk_dep = None
        m_sc2 = p.add('V', (lambda s, f=f, xb=xb: s.scalar_tensor_tensor(
            ctx.junkw[:], xb[:], 1.0, ctx.rtkv_b[:], op0=ALU.mult, op1=ALU.mult,
            accum_out=ctx.s2[:, F + f:F + f + 1])),
            deps=[m_ln], label=f'{tag}_sckv{f}')
        frees = [m_sc2] + ([m_st] if m_st else [])
        war[f % nbuf] = p.barrier('V', frees, label=f'{tag}_war{f}')
        last_marks.append(war[f % nbuf])
    for i in range(nbuf):
        marks[f'xb{i}_war'] = war[i]
    marks['junkw_free'] = last_marks[-1]
    marks[f'scores_{tag}'] = last_marks[-1]
    marks[f'xln_done_{tag}'] = p.barrier('V', war, label=f'{tag}_alldone')
    return marks[f'scores_{tag}']



BIG_DEST = float(1 << 22)


def xpart_sum_rep(ctx, src_cols, n, dst_cols, dep=None, tag=''):
    p = ctx.p
    rsc = ctx.rsc
    p.add('V', lambda s: s.tensor_copy(rsc[:, 0:n], src_cols),
          deps=[dep], label=f'xps_cp_{tag}')
    if n < 32:
        p.add('V', lambda s: s.memset(rsc[:, n:32], 0.0))
    p.add('V', lambda s: s.transpose(rsc[:, 0:32], rsc[:, 0:32]),
          label=f'xps_t_{tag}')
    p.add('V', lambda s: s.tensor_reduce(rsc[:, 0:1], rsc[:, 0:32],
                                         axis=AX.X, op=ALU.add),
          label=f'xps_r_{tag}')
    p.add('V', lambda s: s.stream_shuffle(rsc[0:32, 1:2], rsc[32:64, 0:1],
                                          list(range(32))))
    p.add('V', lambda s: s.stream_shuffle(rsc[0:32, 2:3], rsc[64:96, 0:1],
                                          list(range(32))))
    p.add('V', lambda s: s.stream_shuffle(rsc[0:32, 3:4], rsc[96:128, 0:1],
                                          list(range(32))))
    p.add('V', lambda s: s.tensor_tensor(rsc[0:32, 0:1], rsc[0:32, 0:1],
                                         rsc[0:32, 1:2], op=ALU.add))
    p.add('V', lambda s: s.tensor_tensor(rsc[0:32, 2:3], rsc[0:32, 2:3],
                                         rsc[0:32, 3:4], op=ALU.add))
    p.add('V', lambda s: s.tensor_tensor(rsc[0:32, 0:1], rsc[0:32, 0:1],
                                         rsc[0:32, 2:3], op=ALU.add))
    p.add('V', lambda s: s.transpose(rsc[:, 0:32], rsc[:, 0:32]),
          label=f'xps_t2_{tag}')
    p.add('V', lambda s: s.stream_shuffle(rsc[32:64, 0:n], rsc[0:32, 0:n],
                                          [0] * 32))
    p.add('V', lambda s: s.stream_shuffle(rsc[64:96, 0:n], rsc[0:32, 0:n],
                                          [0] * 32))
    p.add('V', lambda s: s.stream_shuffle(rsc[96:128, 0:n], rsc[0:32, 0:n],
                                          [0] * 32))
    p.add('V', lambda s: s.stream_shuffle(rsc[:, 0:n], rsc[:, 0:n], [0] * 32),
          label=f'xps_b_{tag}')
    m3 = p.add('V', lambda s: s.tensor_copy(dst_cols, rsc[:, 0:n]),
               label=f'xps_o_{tag}')
    return m3


def xpart_prefix(ctx, mask_ap, pref_ap, rowcnt_col, tot_col, tag):
    p = ctx.p
    rsc = ctx.rsc
    p.add('V', lambda s: s.tensor_tensor_scan(pref_ap, mask_ap,
                                              ctx.zeroF[:, 0:pref_ap.shape[1]], 0.0,
                                              op0=ALU.add, op1=ALU.add),
          label=f'xpf_scan_{tag}')
    p.add('V', lambda s: s.tensor_copy(rsc[:, 0:1], rowcnt_col),
          label=f'xpf_cp_{tag}')
    p.add('V', lambda s: s.memset(rsc[:, 1:32], 0.0))
    p.add('V', lambda s: s.transpose(rsc[:, 0:32], rsc[:, 0:32]))
    p.add('V', lambda s: s.tensor_tensor_scan(rsc[:, 0:32], rsc[:, 0:32],
                                              ctx.zeroF[:, 0:32], 0.0,
                                              op0=ALU.add, op1=ALU.add))
    p.add('V', lambda s: s.transpose(rsc[:, 0:32], rsc[:, 0:32]))
    p.add('V', lambda s: s.stream_shuffle(rsc[:, 1:2], rsc[:, 0:1], [31] * 32),
          label=f'xpf_qt_{tag}')
    p.add('V', lambda s: s.memset(rsc[0:32, 2:3], 0.0))
    p.add('V', lambda s: s.stream_shuffle(rsc[32:64, 2:3], rsc[0:32, 1:2],
                                          [0] * 32))
    p.add('V', lambda s: s.tensor_tensor(rsc[32:64, 3:4], rsc[32:64, 2:3],
                                         rsc[32:64, 1:2], op=ALU.add))
    p.add('V', lambda s: s.stream_shuffle(rsc[64:96, 2:3], rsc[32:64, 3:4],
                                          [0] * 32))
    p.add('V', lambda s: s.tensor_tensor(rsc[64:96, 3:4], rsc[64:96, 2:3],
                                         rsc[64:96, 1:2], op=ALU.add))
    p.add('V', lambda s: s.stream_shuffle(rsc[96:128, 2:3], rsc[64:96, 3:4],
                                          [0] * 32))
    p.add('V', lambda s: s.tensor_tensor(rsc[96:128, 3:4], rsc[96:128, 2:3],
                                         rsc[96:128, 1:2], op=ALU.add))
    p.add('V', lambda s: s.stream_shuffle(rsc[0:32, 4:5], rsc[96:128, 3:4],
                                          [0] * 32))
    p.add('V', lambda s: s.stream_shuffle(rsc[32:64, 4:5], rsc[96:128, 3:4],
                                          [0] * 32))
    p.add('V', lambda s: s.stream_shuffle(rsc[64:96, 4:5], rsc[96:128, 3:4],
                                          [0] * 32))
    p.add('V', lambda s: s.stream_shuffle(rsc[96:128, 4:5], rsc[96:128, 3:4],
                                          [0] * 32))
    p.add('V', lambda s: s.tensor_copy(tot_col, rsc[:, 4:5]),
          label=f'xpf_tot_{tag}')
    p.add('V', lambda s: s.tensor_tensor(rsc[:, 5:6], rsc[:, 0:1], rowcnt_col,
                                         op=ALU.subtract))
    p.add('V', lambda s: s.tensor_tensor(rsc[:, 5:6], rsc[:, 5:6], rsc[:, 2:3],
                                         op=ALU.add))
    m = p.add('V', lambda s: s.tensor_scalar(pref_ap, pref_ap, rsc[:, 5:6],
                                             None, op0=ALU.add),
              label=f'xpf_done_{tag}')
    return m


def routing(ctx, marks, tag, score_dep, extra_dep=None):
    p, d = ctx.p, ctx.d
    F = ctx.F
    s2 = ctx.s2
    m0 = p.add('V', lambda s: s.tensor_copy(ctx.relu2[:], s2[:]),
               deps=[score_dep], label=f'r{tag}_init')
    for it in range(ctx.n_iters):
        p.add('V', lambda s: s.tensor_tensor(ctx.e2[:], s2[:], ctx.relu2[:],
                                             op=ALU.subtract),
              label=f'r{tag}_sb{it}')
        mv = p.barrier('V', [], label=f'r{tag}_sbb{it}')
        p.add('S', lambda s: s.activation(ctx.e2[:, 0:F], ctx.e2[:, 0:F],
                                          AF.Exp, accum_out=ctx.racc[:, 0:1]),
              deps=[mv], label=f'r{tag}_e1_{it}')
        m_e2 = p.add('S', lambda s: s.activation(ctx.e2[:, F:2 * F],
                                                 ctx.e2[:, F:2 * F], AF.Exp,
                                                 accum_out=ctx.racc[:, 1:2]),
                     label=f'r{tag}_e2_{it}')
        m_sum = xpart_sum_rep(ctx, ctx.racc[:, 0:2], 2, ctx.racc[:, 2:4],
                              dep=m_e2, tag=f'{tag}i{it}')
        m_ln = p.add('S', lambda s: s.activation(ctx.racc[:, 4:6],
                                                 ctx.racc[:, 2:4], AF.Ln),
                     deps=[m_sum], label=f'r{tag}_ln{it}')
        m_a = p.add('V', lambda s: s.tensor_tensor(ctx.aval[:], ctx.logk[:],
                                                   ctx.racc[:, 4:6],
                                                   op=ALU.subtract),
                    deps=[m_ln, marks['consts']], label=f'r{tag}_a{it}')
        p.add('S', lambda s: s.activation(ctx.relu2[:, 0:F], s2[:, 0:F],
                                          AF.Relu, bias=ctx.aval[:, 0:1]),
              deps=[m_a], label=f'r{tag}_r1_{it}')
        m_r2 = p.add('S', lambda s: s.activation(ctx.relu2[:, F:2 * F],
                                                 s2[:, F:2 * F], AF.Relu,
                                                 bias=ctx.aval[:, 1:2]),
                     label=f'r{tag}_r2_{it}')
        p.barrier('V', [m_r2], label=f'r{tag}_endit{it}')
    m_coor = p.barrier('V', [], label=f'r{tag}_coor_done')
    marks[f'aval_{tag}'] = m_coor

    res = {}
    for r, (kk, list_ap, idx_sb) in enumerate(
            [(ctx.NQ, d['LISTQ'], ctx.idxq), (ctx.NKV, d['LISTKV'], ctx.idxkv)]):
        s_ap = s2[:, r * F:(r + 1) * F]
        A = ctx.selscr[:, 0:F]
        B = ctx.selscr[:, F:2 * F]
        C = ctx.selscr[:, 2 * F:3 * F]
        D = ctx.selscr[:, 3 * F:4 * F]
        cv = ctx.colv
        m_pref_list = p.add('GQ', (lambda s, list_ap=list_ap, kk=kk:
                            s.dma_start(out=list_ap.rearrange('(q i) -> q i',
                                                              i=kk // 128),
                                        in_=ctx.skipc[:, 0:kk // 128])),
                            deps=[marks['skipc'], extra_dep,
                                  marks.get(f'list_free_{tag}_{r}')],
                            label=f'sel{tag}{r}_prefill', slot=f'list{r}')
        p.add('V', (lambda s, r=r: s.tensor_scalar(
            cv[4][:], ctx.aval[:, r:r + 1], -1.0, None, op0=ALU.mult)),
            deps=[m_coor], label=f'sel{tag}{r}_na')
        p.add('V', (lambda s, s_ap=s_ap, D=D: s.tensor_scalar(
            D, s_ap, cv[4][:], 0.0, op0=ALU.is_ge, op1=ALU.add,
            accum_out=ctx.racc[:, 6:7])), label=f'sel{tag}{r}_mask')
        p.add('V', (lambda s, C=C, D=D: s.tensor_scalar(
            C, D, float(1e30), None, op0=ALU.mult)), label=f'sel{tag}{r}_nt1')
        p.add('V', (lambda s, s_ap=s_ap, C=C: s.tensor_tensor(
            C, s_ap, C, op=ALU.subtract)), label=f'sel{tag}{r}_nt2')
        xpart_prefix(ctx, D, B, ctx.racc[:, 6:7], cv[5][:], tag=f'{tag}m{r}')
        p.add('V', (lambda s, A=A, B=B, D=D, kk=kk: s.scalar_tensor_tensor(
            A, B, float(kk), D, op0=ALU.is_le, op1=ALU.mult)),
            label=f'sel{tag}{r}_tied')
        p.add('V', (lambda s, kk=kk: s.tensor_scalar(
            cv[6][:], cv[5][:], -1.0, float(kk), op0=ALU.mult, op1=ALU.add)),
            label=f'sel{tag}{r}_f1')
        p.add('V', lambda s: s.tensor_scalar(cv[6][:], cv[6][:], 0.0, None,
                                             op0=ALU.max),
              label=f'sel{tag}{r}_f2')
        p.add('V', lambda s: s.tensor_scalar(cv[7][:], cv[4][:], -64.0, None,
                                             op0=ALU.add))
        p.add('V', lambda s: s.tensor_copy(ctx.racc[:, 7:8], cv[4][:]))
        for it in range(ctx.bis_iters):
            p.add('V', lambda s: s.tensor_tensor(ctx.rsc[:, 8:9], cv[7][:],
                                                 ctx.racc[:, 7:8], op=ALU.add))
            p.add('V', lambda s: s.tensor_scalar(ctx.rsc[:, 8:9],
                                                 ctx.rsc[:, 8:9], 0.5, None,
                                                 op0=ALU.mult))
            p.add('V', lambda s: s.tensor_copy(ctx.rsc[:, 10:11],
                                               ctx.rsc[:, 8:9]))
            p.add('V', (lambda s, C=C, D=D: s.tensor_scalar(
                D, C, ctx.rsc[:, 10:11], 0.0, op0=ALU.is_ge, op1=ALU.add,
                accum_out=ctx.racc[:, 0:1])), label=f'b{tag}{r}_{it}')
            xpart_sum_rep(ctx, ctx.racc[:, 0:1], 1, ctx.racc[:, 1:2],
                          tag=f'b{tag}{r}{it}')
            p.add('V', lambda s: s.tensor_tensor(ctx.rsc[:, 11:12],
                                                 ctx.racc[:, 1:2], cv[6][:],
                                                 op=ALU.is_ge))
            p.add('V', lambda s: s.tensor_tensor(ctx.rsc[:, 12:13],
                                                 ctx.rsc[:, 10:11], cv[7][:],
                                                 op=ALU.subtract))
            p.add('V', lambda s: s.tensor_tensor(ctx.rsc[:, 12:13],
                                                 ctx.rsc[:, 12:13],
                                                 ctx.rsc[:, 11:12],
                                                 op=ALU.mult))
            p.add('V', lambda s: s.tensor_tensor(cv[7][:], cv[7][:],
                                                 ctx.rsc[:, 12:13],
                                                 op=ALU.add))
            p.add('V', lambda s: s.tensor_tensor(ctx.rsc[:, 12:13],
                                                 ctx.racc[:, 7:8],
                                                 ctx.rsc[:, 10:11],
                                                 op=ALU.subtract))
            p.add('V', lambda s: s.tensor_tensor(ctx.rsc[:, 12:13],
                                                 ctx.rsc[:, 12:13],
                                                 ctx.rsc[:, 11:12],
                                                 op=ALU.mult))
            p.add('V', lambda s: s.tensor_tensor(ctx.racc[:, 7:8],
                                                 ctx.rsc[:, 10:11],
                                                 ctx.rsc[:, 12:13],
                                                 op=ALU.add))
        p.add('V', (lambda s, C=C, D=D: s.tensor_scalar(
            D, C, cv[7][:], 0.0, op0=ALU.is_ge, op1=ALU.add,
            accum_out=ctx.racc[:, 0:1])), label=f'sel{tag}{r}_fillm')
        xpart_prefix(ctx, D, B, ctx.racc[:, 0:1], ctx.racc[:, 1:2],
                     tag=f'{tag}f{r}')
        p.add('V', (lambda s, B=B: s.tensor_scalar(
            B, B, cv[6][:], None, op0=ALU.is_le)), label=f'sel{tag}{r}_fle')
        p.add('V', (lambda s, B=B, D=D: s.tensor_tensor(
            D, D, B, op=ALU.mult)), label=f'sel{tag}{r}_fsel')
        p.add('V', (lambda s, A=A, D=D: s.tensor_tensor(A, A, D, op=ALU.add)),
              label=f'sel{tag}{r}_sel')
        p.add('V', (lambda s, A=A, D=D: s.tensor_scalar(
            D, A, 0.0, 0.0, op0=ALU.add, op1=ALU.add,
            accum_out=ctx.racc[:, 2:3])), label=f'sel{tag}{r}_selcnt')
        xpart_prefix(ctx, A, B, ctx.racc[:, 2:3], ctx.racc[:, 3:4],
                     tag=f'{tag}s{r}')
        p.add('V', (lambda s, B=B: s.tensor_scalar(
            B, B, float(-1.0 - BIG_DEST), None, op0=ALU.add)),
            label=f'sel{tag}{r}_d1')
        p.add('V', (lambda s, A=A, B=B: s.tensor_tensor(B, B, A, op=ALU.mult)),
              label=f'sel{tag}{r}_d2')
        p.add('V', (lambda s, B=B: s.tensor_scalar(
            ctx.dest_f[:], B, BIG_DEST, None, op0=ALU.add)),
            label=f'sel{tag}{r}_d3')
        m_du = p.add('V', lambda s: s.tensor_copy(ctx.dest_u[:], ctx.dest_f[:]),
                     deps=[getattr(ctx, '_dest_war', None)],
                     label=f'sel{tag}{r}_du')
        sc_marks = []
        for f in range(F):
            ctx.p.bounds_vals.add(kk - 1)
            def sca(s, f=f, list_ap=list_ap, kk=kk):
                return s.indirect_dma_start(
                    out=list_ap.unsqueeze(1),
                    out_offset=IndirectOffsetOnAxis(ctx.dest_u[:, f:f + 1], 0),
                    in_=ctx.iota_u[:, f:f + 1], in_offset=None,
                    bounds_check=ctx.p.bc_reg(kk - 1), oob_is_err=False)
            sc_marks.append(p.add('GQ', sca,
                                  deps=[m_du, m_pref_list, marks['iota_u']],
                                  label=f'sel{tag}{r}_sc{f}', slot=f'list{r}'))
        def rd(s, list_ap=list_ap, idx_sb=idx_sb, kk=kk):
            return s.dma_start(out=idx_sb[:, 0:kk // 128],
                               in_=list_ap.rearrange('(q i) -> q i',
                                                     i=kk // 128))
        m_idx = p.add('SQ', rd, deps=sc_marks + [extra_dep],
                      label=f'sel{tag}{r}_rd', slot=f'idx{r}')
        ctx._dest_war = p.barrier('P', sc_marks, label=f'sel{tag}{r}_scw')
        res[('idx', r)] = m_idx
        res[('sel_done', r)] = m_du
    marks[f'route_{tag}'] = res
    return res



def declare_light(ctx, tag):
    GA, GB = f'lightA{tag}', f'lightB{tag}'
    ctx.xn2T = [ctx.sb(f'xn2T{tag}_{i}', [128, 8 * 576], BF16, GA) for i in range(2)]
    ctx.kTl = [ctx.sb(f'kTl{tag}_{i}', [128, 4 * 640], BF16, GA) for i in range(3)]
    ctx.qTl = [ctx.sb(f'qTl{tag}_{i}', [128, 4 * 512], BF16, GA) for i in range(2)]
    ctx.vS = [ctx.sb(f'vS{tag}_{i}', [128, 520], BF16, GA) for i in range(12)]
    ctx.esim = [ctx.sb(f'esim{tag}_{i}', [128, 16 * 128], BF16, GA) for i in range(3)]
    ctx.onorm = ctx.sb(f'onorm{tag}', [128, max(512, ctx.NQ)], BF16, GB,
                       side='right')
    ctx.onormT = ctx.sb(f'onormT{tag}', [128, 512], BF16, GB, side='right')
    ctx.lstage = [ctx.sb(f'lstage{tag}_{i}', [128, 1024], F32, GB, side='right') for i in range(2)]
    ctx.tcast = [ctx.sb(f'tcast{tag}_{i}', [128, 1024], BF16, GB, side='right') for i in range(2)]
    ctx.lrec = ctx.sb(f'lrec{tag}', [128, 16], F32, GB, side='right')


def declare_heavy(ctx, tag):
    G = f'heavy{tag}'
    ctx.hT = [ctx.sb(f'hT{tag}_0', [128, 8 * 512], BF16, G)] * 2
    ctx.kTh = ctx.sb(f'kTh{tag}', [128, 4 * 1024], BF16, G)
    ctx.qTh = ctx.sb(f'qTh{tag}', [128, 4 * 1024], BF16, G)
    ctx.vh = [ctx.sb(f'vh{tag}_{i}', [128, 520], BF16, G) for i in range(8)]
    ctx.esimh = ctx.sb(f'esimh{tag}', [128, 1024], BF16, G)
    ctx.oacc = ctx.sb(f'oacc{tag}', [128, 8 * 520], F32, G)
    ctx.hvout = ctx.sb(f'hvout{tag}', [128, 1024], F32, G)


def emit_ln2_tile(ctx, marks, xb, tcast, tag, dep=None):
    p = ctx.p
    p.add('V', lambda s: s.bn_stats(ctx.stat[:, 0:6], xb[:, 0:512]),
          deps=[dep], label=f'{tag}_bn1')
    p.add('V', lambda s: s.bn_stats(ctx.stat[:, 6:12], xb[:, 512:1024]),
          label=f'{tag}_bn2')
    m_ag = p.add('V', lambda s: s.bn_aggr(ctx.stat[:, 12:14], ctx.stat[:, 0:12]),
                 label=f'{tag}_ag')
    m_sq = p.add('S', lambda s: s.activation(ctx.colv[1][:], ctx.stat[:, 13:14],
                                             AF.Sqrt, bias=ctx.consts[:, 0:1]),
                 deps=[m_ag], label=f'{tag}_sq')
    p.add('V', lambda s: s.reciprocal(ctx.colv[2][:], ctx.colv[1][:]),
          deps=[m_sq], label=f'{tag}_rc')
    m_nb = p.add('V', lambda s: s.scalar_tensor_tensor(
        ctx.colv[3][:], ctx.stat[:, 12:13], -1.0, ctx.colv[2][:],
        op0=ALU.mult, op1=ALU.mult), label=f'{tag}_nb')
    m_t = p.add('S', lambda s: s.activation(tcast[:], xb[:], AF.Identity,
                                            bias=ctx.colv[3][:],
                                            scale=ctx.colv[2][:]),
                deps=[m_nb], label=f'{tag}_t')
    return m_t


def emit_transposes(ctx, src_bf, dst, dst_cols, deps, tag, nch=8):
    p = ctx.p
    tr_last = None
    for j in range(nch):
        def tr(s, j=j):
            return s.transpose(ctx.ps_tr[:, j * 128:(j + 1) * 128],
                               src_bf[:, j * 128:(j + 1) * 128], ctx.ident_bf[:])
        tr_last = p.add('T', tr,
                        deps=(deps + [marks_get(ctx, 'pstr_bank_war')])
                        if j == 0 else [],
                        label=f'{tag}_tr{j}')
    last = None
    for j in range(nch):
        def ev(s, j=j):
            return s.tensor_copy(dst_cols(j), ctx.ps_tr[:, j * 128:(j + 1) * 128])
        last = p.add('V', ev, deps=[tr_last] if j == 0 else [],
                     label=f'{tag}_ev{j}')
    marks_set(ctx, 'pstr_bank_war', last)
    return last


def marks_get(ctx, key):
    return getattr(ctx, '_mk', {}).get(key)


def marks_set(ctx, key, val):
    if not hasattr(ctx, '_mk'):
        ctx._mk = {}
    ctx._mk[key] = val


def phase_light(ctx, marks, xln_ap, x_ap, x2_ap, tag):
    p, d = ctx.p, ctx.d
    NG, F = ctx.NG, ctx.F
    wffn = marks[f'wffn_{tag}']
    gst = [dict() for _ in range(NG + 1)]
    x2_stores = []

    def kTcols(g, j, t0, t1):
        buf = ctx.kTl[g % 3]
        return buf[:, j * 640 + t0:j * 640 + t1]

    def proj_group(g):
        st = gst[g]
        xbuf = ctx.xn2T[g % 2]
        war_xn2 = (gst[g - 2]['xn2_done'] if g >= 2
                   else marks[f'lmemsets_{tag}'])
        tile_marks = []
        for k in range(4):
            xb = ctx.xb[k % 2]
            tc = ctx.tcast[k % 2]
            m_ld = p.add('SQ', (lambda s, g=g, k=k, xb=xb: s.dma_start(
                out=xb[:], in_=xln_ap[g * 512 + k * 128:g * 512 + k * 128 + 128, :])),
                deps=[marks_get(ctx, f'lxb{k % 2}_war'),
                      marks.get(f'xb{k % 2}_war'),
                      marks.get(f'xlnt_{tag}_{g * 4 + k}',
                                marks[f'xln_done_{tag}'])],
                label=f'l{tag}g{g}ld{k}', slot=f'xb{k % 2}')
            marks[f'xb{k % 2}_war'] = None
            m_t = emit_ln2_tile(ctx, marks, xb, tc, f'l{tag}g{g}k{k}',
                                dep=m_ld)
            m_tp = emit_transposes(
                ctx, tc, xbuf,
                (lambda j, g=g, k=k, xbuf=xbuf: xbuf[:, j * 576 + 64 + k * 128:
                                                     j * 576 + 64 + k * 128 + 128]),
                deps=[m_t, war_xn2, wffn], tag=f'l{tag}g{g}k{k}')
            marks_set(ctx, f'lxb{k % 2}_war',
                      p.barrier('V', [m_t, m_tp], label=f'l{tag}g{g}xbw{k}'))
            tile_marks.append(m_tp)
        if g > 0:
            prev = ctx.xn2T[(g - 1) % 2]
            def pad(s, xbuf=xbuf, prev=prev):
                return s.tensor_copy(
                    xbuf[:].rearrange('p (c t) -> p c t', t=576)[:, :, 0:64],
                    prev[:].rearrange('p (c t) -> p c t', t=576)[:, :, 512:576])
            m_pad = p.add('P', pad, deps=[tile_marks[-1], gst[g - 1]['xn2_done']],
                          label=f'l{tag}g{g}pad')
        else:
            m_pad = tile_marks[-1]
        st['xn2'] = p.barrier('V', tile_marks + [m_pad], label=f'l{tag}g{g}xn2')
        if getattr(ctx, 'lg_limit', None) == 'proj_ln':
            st['xn2_done'] = st['xn2']
            st['k_ready'] = st['xn2']
            return

        qbuf = ctx.qTl[g % 2]
        kbuf = ctx.kTl[g % 3]
        war_q = (gst[g - 2].get('q_used') if g >= 2
                 else marks[f'lmemsets_{tag}'])
        war_k = (gst[g - 3].get('k_used') if g >= 3
                 else marks[f'lmemsets_{tag}'])
        qk_marks = []
        for c in range(8):
            psum = ctx.ps_a if c % 2 == 0 else ctx.ps_b
            ps_war = marks_get(ctx, f'l_ps{"ab"[c % 2]}_war')
            mm_last = None
            for kc in range(8):
                def mm(s, c=c, kc=kc, psum=psum, xbuf=xbuf):
                    return s.matmul(
                        psum[:, 0:512],
                        ctx.wreg[:, W_QKV + kc * 1536 + c * 128:
                                 W_QKV + kc * 1536 + c * 128 + 128],
                        xbuf[:, kc * 576 + 64:kc * 576 + 64 + 512],
                        start=(kc == 0), stop=False)
                mm_last = p.add('T', mm,
                                deps=([st['xn2'], wffn, ps_war] if kc == 0 else []),
                                label=f'l{tag}g{g}qk{c}_{kc}')
            def mmc0(s, c=c, psum=psum):
                return s.matmul(psum[:, 0:512], ctx.c0row[0:1, c * 128:c * 128 + 128],
                                ctx.ones_rowb[0:1, 0:512], start=False, stop=True)
            mm_last = p.add('T', mmc0, deps=[marks['onesr']],
                            label=f'l{tag}g{g}qkc0{c}')
            if c < 4:
                def ev(s, c=c, psum=psum, qbuf=qbuf):
                    return s.tensor_copy(qbuf[:, c * 512:(c + 1) * 512],
                                         psum[:, 0:512])
                m_ev = p.add('V', ev, deps=[mm_last, war_q], label=f'l{tag}g{g}qe{c}')
            else:
                def ev(s, c=c, psum=psum, kbuf=kbuf):
                    return s.tensor_copy(
                        kbuf[:, (c - 4) * 640 + 64:(c - 4) * 640 + 64 + 512],
                        psum[:, 0:512])
                m_ev = p.add('V', ev, deps=[mm_last, war_k], label=f'l{tag}g{g}ke{c}')
            marks_set(ctx, f'l_ps{"ab"[c % 2]}_war', m_ev)
            qk_marks.append(m_ev)
        st['q'] = p.barrier('V', qk_marks[:4], label=f'l{tag}g{g}q')
        st['k_mid'] = p.barrier('V', qk_marks[4:], label=f'l{tag}g{g}k')
        if getattr(ctx, 'lg_limit', None) == 'proj_qk':
            st['xn2_done'] = p.barrier('V', qk_marks, label=f'l{tag}g{g}xd')
            st['k_ready'] = st['k_mid']
            return
        if g > 0:
            prev = ctx.kTl[(g - 1) % 3]
            def padl(s, kbuf=kbuf, prev=prev):
                return s.tensor_copy(
                    kbuf[:].rearrange('p (c t) -> p c t', t=640)[:, 0:4, 0:64],
                    prev[:].rearrange('p (c t) -> p c t', t=640)[:, 0:4, 512:576])
            m_kl = p.add('P', padl, deps=[st['k_mid'], gst[g - 1]['k_mid']],
                         label=f'l{tag}g{g}kpl')
            def padr(s, kbuf=kbuf, prev=prev):
                return s.tensor_copy(
                    prev[:].rearrange('p (c t) -> p c t', t=640)[:, 0:4, 576:640],
                    kbuf[:].rearrange('p (c t) -> p c t', t=640)[:, 0:4, 64:128])
            m_kr = p.add('P', padr, label=f'l{tag}g{g}kpr')
            st['k_padl'] = m_kl
            gst[g - 1]['k_ready'] = p.barrier('P', [m_kr],
                                              label=f'l{tag}g{g - 1}kready')
        if g == 0:
            st['k_padl'] = st['k_mid']
        if g == NG - 1:
            st['k_ready'] = p.barrier('P', [st['k_mid'], st['k_padl']],
                                      label=f'l{tag}g{g}kready_last')

        v_marks = []
        for m in range(5 if g == NG - 1 else 4):
            vt = ctx.vS[(4 * g + m) % 12]
            war_v = marks_get(ctx, f'vS{(4 * g + m) % 12}_war')
            ps_war = marks_get(ctx, 'l_psc_war')
            mw = 128 if m < 4 else 64
            mm_last = None
            for kc in range(8):
                def mm(s, m=m, kc=kc, xbuf=xbuf, mw=mw):
                    return s.matmul(
                        ctx.ps_c[0:mw, 0:512],
                        xbuf[:, kc * 576 + m * 128:kc * 576 + m * 128 + mw],
                        ctx.wreg[:, W_QKV + kc * 1536 + 1024:
                                 W_QKV + kc * 1536 + 1536],
                        start=(kc == 0), stop=False)
                mm_last = p.add('T', mm,
                                deps=([st['xn2'], ps_war] if kc == 0 else []),
                                label=f'l{tag}g{g}v{m}_{kc}')
            def mmc0(s, mw=mw):
                return s.matmul(ctx.ps_c[0:mw, 0:512],
                                ctx.ones_rowb[0:1, 0:mw],
                                ctx.c0row[0:1, 1024:1536], start=False, stop=True)
            mm_last = p.add('T', mmc0, label=f'l{tag}g{g}vc0{m}')
            def ev(s, vt=vt, mw=mw):
                return s.tensor_copy(
                    vt[0:mw, :].rearrange('p (h e) -> p h e', e=65)[:, :, 0:64],
                    ctx.ps_c[0:mw, 0:512].rearrange('p (h e) -> p h e', e=64))
            m_ev = p.add('V', ev, deps=[mm_last, war_v], label=f'l{tag}g{g}ve{m}')
            marks_set(ctx, 'l_psc_war', m_ev)
            marks_set(ctx, f'vS{(4 * g + m) % 12}_filled', m_ev)
            v_marks.append(m_ev)
        st['v'] = p.barrier('V', v_marks, label=f'l{tag}g{g}v')
        st['xn2_done'] = p.barrier('V', qk_marks + v_marks, label=f'l{tag}g{g}xd')

    def light_group(g):
        st = gst[g]
        qbuf = ctx.qTl[g % 2]
        kbuf = ctx.kTl[g % 3]
        lim_g = getattr(ctx, 'lg_limit', None)
        nblocks = {'lg_sims': 1, 'lg_o': 1, 'lg_b0': 1,
                   'lg_b2': 2, 'lg_b3': 3}.get(lim_g, 4)
        if lim_g == 'lg_sims4':
            lim_g = 'lg_sims'
        elif lim_g == 'lg_o4':
            lim_g = 'lg_o'
        q_used, k_used, v_used, lx_marks = [], [], [], []
        for b in range(nblocks):
            ei = (4 * g + b) % 3
            esim = ctx.esim[ei]
            t0g = b * 128
            first = (g == 0 and b == 0)
            last = (g == NG - 1 and b == 3 and lim_g != 'lg_nolast')
            prev_burst = marks_get(ctx, 'lpsu_burst') or \
                [marks[f'lmemsets_{tag}']]
            for pair in ((0, 2), (4, 6), (1, 3), (5, 7)):
                mm_last = None
                for h in pair:
                    c = h // 2
                    pb = (h % 2) * 64
                    for kb in range(2):
                        slot = ((h // 2) % 2) * 2 + kb
                        def mm(s, c=c, pb=pb, kb=kb, t0g=t0g, slot=slot,
                               kbuf=kbuf, qbuf=qbuf):
                            return s.matmul(
                                ctx.ps_u[:, slot * 128:slot * 128 + 128],
                                kbuf[:, c * 640 + t0g + kb * 128:
                                     c * 640 + t0g + kb * 128 + 128][pb:pb + 64, :],
                                qbuf[:, c * 512 + t0g:c * 512 + t0g + 128][pb:pb + 64, :],
                                start=True, stop=True,
                                tile_position=(pb, 0))
                        dep0 = ([st['k_ready'], st['q']] + prev_burst
                                if mm_last is None else [])
                        mm_last = p.add('T', mm, deps=dep0,
                                        label=f'l{tag}g{g}b{b}s{h}_{kb}')
                burst_exps = []
                for h in pair:
                    for kb in range(2):
                        slot = ((h // 2) % 2) * 2 + kb
                        et = esim[:, (h * 2 + kb) * 128:(h * 2 + kb) * 128 + 128]
                        ew = marks_get(ctx, f'esim{ei}_{h * 2 + kb}_war')
                        exp_marks = []
                        if kb == 0:
                            if not first:
                                exp_marks.append(p.add('S', (lambda s, et=et, slot=slot:
                                    s.activation(et[0:64, 0:64],
                                                 ctx.ps_u[0:64, slot * 128:slot * 128 + 64],
                                                 AF.Exp)), deps=[mm_last, ew],
                                    label=f'l{tag}g{g}b{b}eA1h{h}'))
                            else:
                                exp_marks.append(p.add('V', (lambda s, et=et:
                                    s.memset(et[0:64, 0:64], 0.0)), deps=[mm_last, ew],
                                    label=f'l{tag}g{g}b{b}eA1zh{h}'))
                            exp_marks.append(p.add('S', (lambda s, et=et, slot=slot:
                                s.activation(et[64:128, :],
                                             ctx.ps_u[64:128, slot * 128:slot * 128 + 128],
                                             AF.Exp)), deps=[mm_last, ew],
                                label=f'l{tag}g{g}b{b}eA2h{h}'))
                        else:
                            exp_marks.append(p.add('S', (lambda s, et=et, slot=slot:
                                s.activation(et[0:64, :],
                                             ctx.ps_u[0:64, slot * 128:slot * 128 + 128],
                                             AF.Exp)), deps=[mm_last, ew],
                                label=f'l{tag}g{g}b{b}eB1h{h}'))
                            if not last:
                                exp_marks.append(p.add('S', (lambda s, et=et, slot=slot:
                                    s.activation(et[64:128, 64:128],
                                                 ctx.ps_u[64:128, slot * 128 + 64:
                                                          slot * 128 + 128],
                                                 AF.Exp)), deps=[mm_last, ew],
                                    label=f'l{tag}g{g}b{b}eB2h{h}'))
                            else:
                                exp_marks.append(p.add('V', (lambda s, et=et:
                                    s.memset(et[64:128, 64:128], 0.0)), deps=[mm_last, ew],
                                    label=f'l{tag}g{g}b{b}eB2zh{h}'))
                        m_exp = p.barrier('S', exp_marks,
                                          label=f'l{tag}g{g}b{b}exph{h}kb{kb}')
                        marks_set(ctx, f'esim_exp_{h}_{kb}', m_exp)
                        burst_exps.append(m_exp)
                prev_burst = burst_exps
            marks_set(ctx, 'lpsu_burst', prev_burst)
            if lim_g == 'lg_sims':
                m_se = p.barrier('S', prev_burst, label=f'lgsims{g}')
                x2_stores.append(m_se)
                continue
            vA = ctx.vS[(4 * g + b) % 12]
            vB = ctx.vS[(4 * g + b + 1) % 12]
            vA_dep = marks_get(ctx, f'vS{(4 * g + b) % 12}_filled')
            vB_dep = (marks_get(ctx, f'vS{(4 * g + b + 1) % 12}_filled')
                      if not last else None)
            for half in range(2):
                o_marks = []
                for hh in range(4):
                    h = half * 4 + hh
                    po = hh * 65
                    for kb in range(2):
                        et = esim[:, (h * 2 + kb) * 128:(h * 2 + kb) * 128 + 128]
                        vt = vA if kb == 0 else vB
                        def mm(s, et=et, vt=vt, h=h, po=po, kb=kb):
                            return s.matmul(
                                ctx.ps_f[:, po:po + 65], et,
                                vt[:, h * 65:h * 65 + 65],
                                start=(kb == 0), stop=(kb == 1))
                        deps_o = [marks_get(ctx, f'esim_exp_{h}_{kb}'),
                                  vA_dep, vB_dep]
                        if kb == 0:
                            deps_o.append(marks_get(ctx, 'lps_f_war'))
                        o_marks.append(p.add('T', mm, deps=deps_o,
                                             label=f'l{tag}g{g}b{b}o{h}_{kb}'))
                def rc(s):
                    return s.reciprocal(
                        ctx.lrec[:, 0:4],
                        ctx.ps_f[:, 0:260].rearrange('p (h e) -> p h e',
                                                     e=65)[:, :, 64:65].squeeze(2))
                m_rc = p.add('V', rc, deps=o_marks, label=f'l{tag}g{g}b{b}rc{half}')
                def nm(s, half=half):
                    return s.tensor_tensor(
                        ctx.onorm[:, half * 256:half * 256 + 256]
                           .rearrange('p (h e) -> p h e', e=64),
                        ctx.ps_f[:, 0:260].rearrange('p (h e) -> p h e',
                                                     e=65)[:, :, 0:64],
                        ctx.lrec[:, 0:4].unsqueeze(2).to_broadcast([128, 4, 64]),
                        op=ALU.mult)
                m_nm = p.add('V', nm, deps=[marks_get(ctx, 'onorm_war')],
                             label=f'l{tag}g{g}b{b}nm{half}')
                marks_set(ctx, 'lps_f_war', m_nm)
            if lim_g == 'lg_o':
                x2_stores.append(marks_get(ctx, 'lps_f_war'))
                continue
            m_ot = emit_transposes(
                ctx, ctx.onorm,
                ctx.onormT, (lambda j: ctx.onormT[:, j * 128:(j + 1) * 128]),
                deps=[marks_get(ctx, 'lps_f_war'), marks_get(ctx, 'onormT_war')],
                tag=f'l{tag}g{g}b{b}ot', nch=4)
            marks_set(ctx, 'onorm_war', m_ot)
            if lim_g == 'lg_nowo':
                marks_set(ctx, 'onormT_war', m_ot)
                for h in range(8):
                    for kb in range(2):
                        marks_set(ctx, f'esim{ei}_{h * 2 + kb}_war', m_ot)
                x2_stores.append(m_ot)
                continue
            wo_marks = []
            for half in range(2):
                psum = ctx.ps_d if half == 0 else ctx.ps_e
                mm_last = None
                for kc in range(4):
                    def mm(s, kc=kc, half=half, psum=psum):
                        return s.matmul(
                            psum[:, 0:512],
                            ctx.onormT[:, kc * 128:(kc + 1) * 128],
                            ctx.wreg[:, W_OUTL + kc * 1024 + half * 512:
                                     W_OUTL + kc * 1024 + half * 512 + 512],
                            start=(kc == 0), stop=(kc == 3))
                    mm_last = p.add('T', mm,
                                    deps=([m_ot, marks_get(ctx, f'lps_{"de"[half]}_war')]
                                          if kc == 0 else []),
                                    label=f'l{tag}g{g}b{b}wo{half}_{kc}')
                wo_marks.append(mm_last)
            if getattr(ctx, 'dbg_onorm', None) is not None:
                m_dbgo = p.add('AQ', (lambda s, g=g, b=b: s.dma_start(
                    out=ctx.dbg_onorm[(g * 4 + b) * 128:(g * 4 + b) * 128 + 128, :],
                    in_=ctx.onorm[:])), deps=[m_ot], label=f'dbgo{g}{b}',
                    slot='dbgo')
                m_ot = p.barrier('V', [m_ot, m_dbgo], label=f'dbgob{g}{b}')
            m_dbge = None
            if getattr(ctx, 'dbg_esim', None) is not None:
                m_dbge = p.add('AQ', (lambda s, g=g, b=b, ei=ei: s.dma_start(
                    out=ctx.dbg_esim[(g * 4 + b) * 128:(g * 4 + b) * 128 + 128, :],
                    in_=ctx.esim[ei][:])), deps=[m_ot],
                    label=f'dbge{g}{b}', slot='dbge')
            marks_set(ctx, 'onormT_war', wo_marks[-1])
            m_ou = p.barrier('T', wo_marks, label=f'l{tag}g{g}b{b}oused')
            if m_dbge is not None:
                m_ou = p.barrier('T', [m_ou, m_dbge], label=f'l{tag}g{g}b{b}ou2')
            for h in range(8):
                for kb in range(2):
                    marks_set(ctx, f'esim{ei}_{h * 2 + kb}_war', m_ou)
            if lim_g == 'lg_noc':
                m_w = p.barrier('V', wo_marks, label=f'lgnoc{g}b{b}')
                marks_set(ctx, 'lps_d_war', m_w)
                marks_set(ctx, 'lps_e_war', m_w)
                marks_set(ctx, 'cxb_war', m_w)
                x2_stores.append(m_w)
                continue
            xb = ctx.xb[2]
            tok0 = g * 512 + b * 128
            m_ldx = p.add('SQ', (lambda s, tok0=tok0, xb=xb: s.dma_start(
                out=xb[:], in_=x_ap[tok0:tok0 + 128, :])),
                deps=[marks_get(ctx, 'cxb_war'), marks.get('xb2_war')],
                label=f'l{tag}g{g}b{b}ldx', slot='xb2')
            marks['xb2_war'] = None
            ls = ctx.lstage[b % 2]
            dep_ls = marks_get(ctx, f'lstage{b % 2}_war')
            m_c1 = p.add('V', (lambda s, xb=xb, ls=ls: s.scalar_tensor_tensor(
                ls[:, 0:512], ctx.ps_d[:, 0:512], 0.5, xb[:, 0:512],
                op0=ALU.mult, op1=ALU.add)),
                deps=[wo_marks[0], m_ldx, dep_ls], label=f'l{tag}g{g}b{b}c1')
            m_c2 = p.add('V', (lambda s, xb=xb, ls=ls: s.scalar_tensor_tensor(
                ls[:, 512:1024], ctx.ps_e[:, 0:512], 0.5, xb[:, 512:1024],
                op0=ALU.mult, op1=ALU.add)),
                deps=[wo_marks[1]], label=f'l{tag}g{g}b{b}c2')
            marks_set(ctx, 'lps_d_war', m_c1)
            marks_set(ctx, 'lps_e_war', m_c2)
            m_c3 = p.add('V', (lambda s, ls=ls: s.tensor_tensor(
                ls[:], ls[:], ctx.nullh_b[:], op=ALU.add)),
                deps=[marks['rt_b']], label=f'l{tag}g{g}b{b}c3')
            marks_set(ctx, 'cxb_war', m_c3)
            m_st = p.add('AQ', (lambda s, tok0=tok0, ls=ls: s.dma_start(
                out=x2_ap[tok0:tok0 + 128, :], in_=ls[:])),
                deps=[m_c3], label=f'l{tag}g{g}b{b}st',
                slot=f'x2st{tag}_{b % 2}')
            marks_set(ctx, f'lstage{b % 2}_war', m_st)
            x2_stores.append(m_st)
        st['q_used'] = p.barrier('T', [], label=f'l{tag}g{g}qu')
        st['k_used'] = st['q_used']
        st['v_used'] = st['q_used']
        for m in range(4):
            marks_set(ctx, f'vS{(4 * g + m) % 12}_war', st['q_used'])

    lim = getattr(ctx, 'lg_limit', None)
    if lim is not None and lim.startswith('proj'):
        for g in range(NG):
            proj_group(g)
        m_all = p.barrier('V', [gst[g]['xn2_done'] for g in range(NG)]
                          + [gst[NG - 1]['k_ready']], label=f'l{tag}_projdone')
        marks[f'x2_done_{tag}'] = m_all
        return m_all
    proj_group(0)
    for g in range(1, NG):
        proj_group(g)
        light_group(g - 1)
    light_group(NG - 1)
    m_all = p.barrier('V', x2_stores, label=f'l{tag}_x2done')
    marks[f'x2_done_{tag}'] = m_all
    return m_all


def prep_light_memsets(ctx, marks, tag, dep=None):
    p = ctx.p
    first = [dep]
    for i, vt in enumerate(ctx.vS):
        m1 = p.add('P', (lambda s, vt=vt: s.memset(vt[:], 0.0)), deps=first)
        first = []
        m2 = p.add('P', (lambda s, vt=vt: s.memset(
            vt[:].rearrange('p (h e) -> p h e', e=65)[:, :, 64:65], 1.0)))
        marks_set(ctx, f'vS{i}_war', m2)
        marks_set(ctx, f'vS{i}_filled', None)
    for bb in range(3):
        p.add('P', (lambda s, bb=bb: s.memset(ctx.esim[bb][:], 0.0)))
    for i, b in enumerate(ctx.kTl):
        p.add('P', (lambda s, b=b: s.memset(b[:], 0.0)))
    for i, b in enumerate(ctx.xn2T):
        p.add('P', (lambda s, b=b: s.memset(b[:], 0.0)))
    m = p.barrier('P', [], label=f'lmemsets{tag}')
    marks[f'lmemsets_{tag}'] = m
    pw = p.barrier('V', [marks.get(f'wffn_{tag}'), marks.get('_psu_war'),
                         marks.get('psum_free')], label=f'lpsw{tag}')
    for k in ['l_psa_war', 'l_psb_war', 'l_psc_war', 'lps_d_war', 'lps_e_war',
              'lps_f_war']:
        marks_set(ctx, k, pw)
    marks_set(ctx, 'lpsu_burst', [pw])
    for bb in range(3):
        for t in range(16):
            marks_set(ctx, f'esim{bb}_{t}_war', m)
    for k in ['hTbuf_war', 'esimh_war', 'kTh_war', 'qTh_war', 'onorm_war',
              'onormT_war', 'honorm_war', 'htc0_war', 'htc1_war',
              'hxb0_war', 'hxb1_war']:
        marks_set(ctx, k, m)
    for i in range(2):
        marks_set(ctx, f'lxb{i}_war', m)
        marks_set(ctx, f'lstage{i}_war', m)
    marks_set(ctx, 'cxb_war', m)
    return m


def prep_heavy_memsets(ctx, marks, tag, dep=None):
    p = ctx.p
    pw = p.barrier('V', [dep], label=f'hpsw{tag}')
    for k in ['h_psa_war', 'h_psb_war', 'h_psc_war', 'h_psd_war',
              'h_psfu_war']:
        marks_set(ctx, k, pw)
    first = [dep]
    for i, vt in enumerate(ctx.vh):
        m1 = p.add('P', (lambda s, vt=vt: s.memset(vt[:], 0.0)), deps=first)
        first = []
        m2 = p.add('P', (lambda s, vt=vt: s.memset(
            vt[:].rearrange('p (h e) -> p h e', e=65)[:, :, 64:65], 1.0)))
        marks_set(ctx, f'vh{i}_war', m2)
        marks_set(ctx, f'vh{i}_filled', None)
    m = p.barrier('P', [], label=f'hmemsets{tag}')
    for k in ['hTbuf_war', 'esimh_war', 'kTh_war', 'qTh_war',
              'htc0_war', 'htc1_war', 'hxb0_war', 'hxb1_war', 'h_psfu_war']:
        marks_set(ctx, k, m)
    return m



def phase_heavy(ctx, marks, xln_ap, x2_ap, tag, route_res):
    p, d = ctx.p, ctx.d
    wffn = marks[f'wffn_{tag}']
    m_x2 = marks[f'x2_done_{tag}']

    def gather_norm_T(idx_sb, idx_col, dep_idx, dstT, dst_tok, tg):
        xb = ctx.xb[idx_col % 2]
        dep_xb = marks_get(ctx, f'hxb{idx_col % 2}_war')
        dep_xb2 = marks.get(f'xb{idx_col % 2}_war')
        ctx.p.bounds_vals.add(ctx.T - 1)
        def gth(s):
            return s.indirect_dma_start(
                out=xb[:], out_offset=None, in_=xln_ap,
                in_offset=IndirectOffsetOnAxis(idx_sb[:, idx_col:idx_col + 1], 0),
                bounds_check=ctx.p.bc_reg(ctx.T - 1), oob_is_err=False)
        m_g = p.add('GQ', gth, deps=[dep_idx, dep_xb, dep_xb2,
                                     marks[f'xln_done_{tag}'],
                                     marks[f'x2_done_{tag}']],
                    label=f'h{tg}_g', slot=f'hxb{idx_col % 2}')
        m_sq = p.add('S', (lambda s, xb=xb: s.activation(
            ctx.junkw[:], xb[:], AF.Square, accum_out=ctx.colv[1][:])),
            deps=[m_g, marks.get('junkw_free')], label=f'h{tg}_ssq')
        marks['junkw_free'] = m_sq
        m_rt = p.add('S', lambda s: s.activation(ctx.colv[2][:], ctx.colv[1][:],
                                                 AF.Sqrt), deps=[m_sq],
                     label=f'h{tg}_rt')
        p.add('V', lambda s: s.tensor_scalar(ctx.colv[2][:], ctx.colv[2][:],
                                             ctx.consts[:, 1:2], None,
                                             op0=ALU.max), deps=[m_rt])
        p.add('V', lambda s: s.reciprocal(ctx.colv[3][:], ctx.colv[2][:]))
        m_sc = p.add('V', lambda s: s.tensor_scalar(ctx.colv[3][:],
                                                    ctx.colv[3][:], 32.0, None,
                                                    op0=ALU.mult))
        tc = ctx.tcast[idx_col % 2]
        dep_tc = marks_get(ctx, f'htc{idx_col % 2}_war')
        m_c = p.add('S', (lambda s, xb=xb, tc=tc: s.activation(
            tc[:], xb[:], AF.Identity, scale=ctx.colv[3][:])),
            deps=[m_sc, dep_tc], label=f'h{tg}_cast')
        marks_set(ctx, f'hxb{idx_col % 2}_war', m_c)
        m_tp = emit_transposes(
            ctx, tc, dstT,
            (lambda j, dst_tok=dst_tok, dstT=dstT:
             dstT[:, j * 512 + dst_tok:j * 512 + dst_tok + 128]),
            deps=[m_c, marks_get(ctx, 'hTbuf_war')], tag=f'h{tg}_tp')
        marks_set(ctx, f'htc{idx_col % 2}_war', m_tp)
        return m_tp

    NQg = ctx.NQ // 512 if ctx.NQ >= 512 else 1
    qtok_per_g = min(ctx.NQ, 512)
    for qg in range(NQg):
        tps = []
        for t in range(qtok_per_g // 128):
            tps.append(gather_norm_T(ctx.idxq, qg * 4 + t, route_res[('idx', 0)],
                                     ctx.hT[0], t * 128, f'{tag}q{qg}_{t}'))
        m_sub = p.barrier('V', tps, label=f'h{tag}q{qg}sub')
        for c in range(4):
            psum = ctx.ps_a if c % 2 == 0 else ctx.ps_b
            ps_war = marks_get(ctx, f'h_ps{"ab"[c % 2]}_war')
            mm_last = None
            for kc in range(8):
                def mm(s, c=c, kc=kc, psum=psum):
                    return s.matmul(
                        psum[:, 0:qtok_per_g],
                        ctx.wreg[:, W_QH + kc * 512 + c * 128:
                                 W_QH + kc * 512 + c * 128 + 128],
                        ctx.hT[0][:, kc * 512:kc * 512 + qtok_per_g],
                        start=(kc == 0), stop=(kc == 7))
                mm_last = p.add('T', mm,
                                deps=([m_sub, wffn, ps_war] if kc == 0 else []),
                                label=f'h{tag}qp{qg}_{c}_{kc}')
            def ev(s, c=c, qg=qg, psum=psum):
                return s.tensor_scalar(
                    ctx.qTh[:, c * ctx.NQ + qg * 512:
                            c * ctx.NQ + qg * 512 + qtok_per_g],
                    psum[:, 0:qtok_per_g], 0.125, None, op0=ALU.mult)
            m_ev = p.add('V', ev, deps=[mm_last,
                                        marks_get(ctx, 'qTh_war')],
                         label=f'h{tag}qpe{qg}_{c}')
            marks_set(ctx, f'h_ps{"ab"[c % 2]}_war', m_ev)
        m_used = p.barrier('T', [], label=f'h{tag}q{qg}u')
        marks_set(ctx, 'htc0_war', m_used)
        marks_set(ctx, 'htc1_war', m_used)
        marks_set(ctx, 'hTbuf_war', m_used)
    m_qready = p.barrier('V', [], label=f'h{tag}_qready')

    NKVh = ctx.NKV // 2
    oev_marks = []
    for half in range(2):
        ksub_marks = []
        nsub = max(NKVh // 512, 1)
        subtok = min(NKVh, 512)
        for sub in range(nsub):
            tps = []
            for t in range(subtok // 128):
                col = half * (NKVh // 128) + sub * 4 + t
                tps.append(gather_norm_T(ctx.idxkv, col, route_res[('idx', 1)],
                                         ctx.hT[1], t * 128,
                                         f'{tag}kv{half}_{sub}_{t}'))
            m_sub = p.barrier('V', tps, label=f'h{tag}kv{half}{sub}sub')
            for c in range(4):
                psum = ctx.ps_a if c % 2 == 0 else ctx.ps_b
                ps_war = marks_get(ctx, f'h_ps{"ab"[c % 2]}_war')
                mm_last = None
                for kc in range(8):
                    def mm(s, c=c, kc=kc, psum=psum):
                        return s.matmul(
                            psum[:, 0:subtok],
                            ctx.wreg[:, W_KVH + kc * 1024 + c * 128:
                                     W_KVH + kc * 1024 + c * 128 + 128],
                            ctx.hT[1][:, kc * 512:kc * 512 + subtok],
                            start=(kc == 0), stop=(kc == 7))
                    mm_last = p.add('T', mm,
                                    deps=([m_sub, wffn, ps_war,
                                           marks_get(ctx, 'kTh_war')]
                                          if kc == 0 else []),
                                    label=f'h{tag}kp{half}{sub}_{c}_{kc}')
                def ev(s, c=c, sub=sub, psum=psum):
                    return s.tensor_copy(
                        ctx.kTh[:, c * 1024 + sub * 512:
                                c * 1024 + sub * 512 + subtok],
                        psum[:, 0:subtok])
                m_ev = p.add('V', ev, deps=[mm_last], label=f'h{tag}kpe{half}{sub}_{c}')
                marks_set(ctx, f'h_ps{"ab"[c % 2]}_war', m_ev)
                ksub_marks.append(m_ev)
            for t in range(subtok // 128):
                kb = sub * 4 + t
                vt = ctx.vh[kb]
                ps_war = marks_get(ctx, 'h_psc_war')
                mm_last = None
                for kc in range(8):
                    def mm(s, t=t, kc=kc):
                        return s.matmul(
                            ctx.ps_c[:, 0:512],
                            ctx.hT[1][:, kc * 512 + t * 128:kc * 512 + t * 128 + 128],
                            ctx.wreg[:, W_KVH + kc * 1024 + 512:
                                     W_KVH + kc * 1024 + 1024],
                            start=(kc == 0), stop=(kc == 7))
                    mm_last = p.add('T', mm,
                                    deps=([m_sub, ps_war,
                                           marks_get(ctx, f'vh{kb}_war')]
                                          if kc == 0 else []),
                                    label=f'h{tag}vp{half}{kb}_{kc}')
                def ev(s, vt=vt):
                    return s.tensor_copy(
                        vt[:].rearrange('p (h e) -> p h e', e=65)[:, :, 0:64],
                        ctx.ps_c[:, 0:512].rearrange('p (h e) -> p h e', e=64))
                m_ev = p.add('V', ev, deps=[mm_last], label=f'h{tag}vpe{half}{kb}')
                marks_set(ctx, 'h_psc_war', m_ev)
                marks_set(ctx, f'vh{kb}_filled', m_ev)
                ksub_marks.append(m_ev)
            m_used = p.barrier('T', [], label=f'h{tag}kv{half}{sub}u')
            marks_set(ctx, 'htc0_war', m_used)
            marks_set(ctx, 'htc1_war', m_used)
            marks_set(ctx, 'hTbuf_war', m_used)
        m_kv = p.barrier('V', ksub_marks, label=f'h{tag}kv{half}done')

        nkb = NKVh // 128
        for h in range(8):
            c, pb = h // 2, (h % 2) * 64
            o_first = True
            if half == 0:
                def nmm(s, c=c, pb=pb, h=h, nn=0):
                    return s.matmul(
                        ctx.ps_c[0:1, 0:min(ctx.NQ, 512)],
                        ctx.nullk_bf[pb:pb + 64, h:h + 1],
                        ctx.qTh[pb:pb + 64, c * ctx.NQ:c * ctx.NQ + min(ctx.NQ, 512)],
                        start=True, stop=True, tile_position=(pb, 0))
                m_n1 = p.add('T', nmm, deps=[m_qready, marks['nullkv'],
                                             marks_get(ctx, 'h_psc_war')],
                             label=f'h{tag}ns{half}h{h}a')
                ms = [m_n1]
                if ctx.NQ > 512:
                    def nmm2(s, c=c, pb=pb, h=h):
                        return s.matmul(
                            ctx.ps_d[0:1, 0:512],
                            ctx.nullk_bf[pb:pb + 64, h:h + 1],
                            ctx.qTh[pb:pb + 64, c * ctx.NQ + 512:c * ctx.NQ + 1024],
                            start=True, stop=True, tile_position=(pb, 0))
                    ms.append(p.add('T', nmm2,
                                    deps=[marks_get(ctx, 'h_psd_war')],
                                    label=f'h{tag}ns{half}h{h}b'))
                m_ne = p.add('S', (lambda s: s.activation(
                    ctx.esimh[0:1, 0:min(ctx.NQ, 512)],
                    ctx.ps_c[0:1, 0:min(ctx.NQ, 512)], AF.Exp)),
                    deps=ms + [marks_get(ctx, 'esimh_war')],
                    label=f'h{tag}ne{half}h{h}a')
                if ctx.NQ > 512:
                    m_ne = p.add('S', (lambda s: s.activation(
                        ctx.esimh[0:1, 512:1024], ctx.ps_d[0:1, 0:512], AF.Exp)),
                        label=f'h{tag}ne{half}h{h}b')
                marks_set(ctx, 'h_psc_war', m_ne)
                marks_set(ctx, 'h_psd_war', m_ne)
                for qc in range(ctx.NQ // 128):
                    po = (qc * 65) if qc < 7 else 0
                    pst = ctx.ps_f if qc < 7 else ctx.ps_u
                    def omm(s, qc=qc, po=po, pst=pst, h=h):
                        return s.matmul(
                            pst[:, po:po + 65],
                            ctx.esimh[0:1, qc * 128:qc * 128 + 128],
                            ctx.nullv_bf[0:1, h * 65:h * 65 + 65],
                            start=True, stop=False)
                    m_no = p.add('T', omm,
                                 deps=[m_ne, marks_get(ctx, 'h_psfu_war')],
                                 label=f'h{tag}no{half}h{h}q{qc}')
                marks_set(ctx, 'esimh_war',
                          p.barrier('T', [], label=f'h{tag}nob{half}h{h}'))
                o_first = False
            for kb in range(nkb):
                sim_ms = []
                for nn in range(ctx.NQ // 512 if ctx.NQ >= 512 else 1):
                    pst = ctx.ps_c if nn == 0 else ctx.ps_d
                    def smm(s, c=c, pb=pb, kb=kb, nn=nn, pst=pst):
                        return s.matmul(
                            pst[:, 0:min(ctx.NQ, 512)],
                            ctx.kTh[pb:pb + 64, c * 1024 + kb * 128:
                                    c * 1024 + kb * 128 + 128],
                            ctx.qTh[pb:pb + 64, c * ctx.NQ + nn * 512:
                                    c * ctx.NQ + nn * 512 + min(ctx.NQ, 512)],
                            start=True, stop=True, tile_position=(pb, 0))
                    sim_ms.append(p.add('T', smm,
                                        deps=[m_kv, m_qready,
                                              marks_get(ctx, f'h_ps{"cd"[nn]}_war')],
                                        label=f'h{tag}s{half}h{h}k{kb}n{nn}'))
                exp_ms = []
                for nn in range(ctx.NQ // 512 if ctx.NQ >= 512 else 1):
                    pst = ctx.ps_c if nn == 0 else ctx.ps_d
                    def ee(s, nn=nn, pst=pst):
                        return s.activation(
                            ctx.esimh[:, nn * 512:nn * 512 + min(ctx.NQ, 512)],
                            pst[:, 0:min(ctx.NQ, 512)], AF.Exp)
                    exp_ms.append(p.add('S', ee,
                                        deps=[sim_ms[nn],
                                              marks_get(ctx, 'esimh_war')],
                                        label=f'h{tag}e{half}h{h}k{kb}n{nn}'))
                    marks_set(ctx, f'h_ps{"cd"[nn]}_war', exp_ms[-1])
                m_exp = p.barrier('S', exp_ms, label=f'h{tag}eb{half}h{h}k{kb}')
                o_ms = []
                for qc in range(ctx.NQ // 128):
                    po = (qc * 65) if qc < 7 else 0
                    pst = ctx.ps_f if qc < 7 else ctx.ps_u
                    laststop = (kb == nkb - 1)
                    def omm(s, qc=qc, po=po, pst=pst, h=h, kb=kb,
                            laststop=laststop, o_first=o_first):
                        return s.matmul(
                            pst[:, po:po + 65],
                            ctx.esimh[:, qc * 128:qc * 128 + 128],
                            ctx.vh[kb][:, h * 65:h * 65 + 65],
                            start=False if not (o_first and kb == 0) else True,
                            stop=laststop)
                    dps = [m_exp, marks_get(ctx, f'vh{kb}_filled')]
                    if o_first and kb == 0:
                        dps.append(marks_get(ctx, 'h_psfu_war'))
                    o_ms.append(p.add('T', omm, deps=dps,
                                      label=f'h{tag}o{half}h{h}k{kb}q{qc}'))
                m_oall = p.barrier('T', [], label=f'h{tag}ou{half}h{h}k{kb}')
                marks_set(ctx, 'esimh_war', m_oall)
            n1 = min(ctx.NQ // 128, 7) * 65
            if half == 0:
                def ev1(s, h=h, n1=n1):
                    return s.tensor_copy(ctx.oacc[:, h * 520:h * 520 + n1],
                                         ctx.ps_f[:, 0:n1])
                def ev2(s, h=h):
                    return s.tensor_copy(ctx.oacc[:, h * 520 + 455:h * 520 + 520],
                                         ctx.ps_u[:, 0:65])
            else:
                def ev1(s, h=h, n1=n1):
                    return s.tensor_tensor(ctx.oacc[:, h * 520:h * 520 + n1],
                                           ctx.oacc[:, h * 520:h * 520 + n1],
                                           ctx.ps_f[:, 0:n1], op=ALU.add)
                def ev2(s, h=h):
                    return s.tensor_tensor(ctx.oacc[:, h * 520 + 455:h * 520 + 520],
                                           ctx.oacc[:, h * 520 + 455:h * 520 + 520],
                                           ctx.ps_u[:, 0:65], op=ALU.add)
            mo1 = p.add('V', ev1, deps=[m_oall], label=f'h{tag}oe{half}h{h}a')
            mo2 = (p.add('V', ev2, label=f'h{tag}oe{half}h{h}b')
                   if ctx.NQ // 128 == 8 else mo1)
            oev_marks.append(mo2)
            m_psfu = p.barrier('V', [mo1, mo2], label=f'h{tag}oef{half}h{h}')
            marks_set(ctx, 'h_psfu_war', m_psfu)
        marks_set(ctx, 'kTh_war', p.barrier('T', [], label=f'h{tag}kthu{half}'))
        for kb in range(nkb):
            marks_set(ctx, f'vh{kb}_war', marks_get(ctx, 'kTh_war'))

    m_oacc = p.barrier('V', oev_marks, label=f'h{tag}_oacc')

    nqc = ctx.NQ // 128
    oT = ctx.hT[0]
    oT_war = p.barrier('T', [], label=f'h{tag}_oTwar')
    tp_ms = []
    for cc in range(4):
        nm_ms = []
        for hh in range(2):
            h = 2 * cc + hh
            oh = ctx.oacc[:, h * 520:h * 520 + 520].rearrange(
                'p (q e) -> p q e', e=65)
            m_rc = p.add('V', (lambda s, oh=oh, hh=hh: s.reciprocal(
                ctx.lrec[:, hh * 8:hh * 8 + nqc],
                oh[:, 0:nqc, 64:65].squeeze(2))),
                deps=[m_oacc], label=f'h{tag}nr{h}')
            def nm(s, oh=oh, hh=hh):
                return s.tensor_tensor(
                    ctx.onorm[:, 0:nqc * 128].rearrange(
                        'p (q w e) -> p q w e', w=2, e=64)[:, :, hh],
                    oh[:, 0:nqc, 0:64],
                    ctx.lrec[:, hh * 8:hh * 8 + nqc].unsqueeze(2)
                       .to_broadcast([128, nqc, 64]),
                    op=ALU.mult)
            nm_ms.append(p.add('V', nm, deps=[marks_get(ctx, 'honorm_war')],
                               label=f'h{tag}nm{h}'))
        m_nm = p.barrier('V', nm_ms, label=f'h{tag}nmb{cc}')
        tr_last = None
        for qc in range(nqc):
            slot = qc % 8
            def tr(s, qc=qc, slot=slot):
                return s.transpose(ctx.ps_tr[:, slot * 128:slot * 128 + 128],
                                   ctx.onorm[:, qc * 128:qc * 128 + 128],
                                   ctx.ident_bf[:])
            tr_last = p.add('T', tr,
                            deps=([m_nm, oT_war,
                                   marks_get(ctx, 'pstr_bank_war')]
                                  if qc == 0 else []),
                            label=f'h{tag}tr{cc}_{qc}')
        ev_last = None
        for qc in range(nqc):
            slot = qc % 8
            def ev(s, cc=cc, qc=qc, slot=slot):
                return s.tensor_copy(
                    oT[:, cc * ctx.NQ + qc * 128:cc * ctx.NQ + qc * 128 + 128],
                    ctx.ps_tr[:, slot * 128:slot * 128 + 128])
            ev_last = p.add('V', ev, deps=[tr_last] if qc == 0 else [],
                            label=f'h{tag}te{cc}_{qc}')
            tp_ms.append(ev_last)
        marks_set(ctx, 'pstr_bank_war', ev_last)
        marks_set(ctx, 'honorm_war', p.barrier('T', [], label=f'h{tag}onw{cc}'))
    m_oT = p.barrier('V', tp_ms, label=f'h{tag}_oT')

    m_sc_prev = None
    for qc in range(nqc):
        wo_ms = []
        for half in range(2):
            psum = ctx.ps_c if half == 0 else ctx.ps_d
            mm_last = None
            for kc in range(4):
                def mm(s, qc=qc, kc=kc, half=half, psum=psum):
                    return s.matmul(
                        psum[:, 0:512],
                        oT[:, kc * ctx.NQ + qc * 128:kc * ctx.NQ + qc * 128 + 128],
                        ctx.wreg[:, W_OUTH + kc * 1024 + half * 512:
                                 W_OUTH + kc * 1024 + half * 512 + 512],
                        start=(kc == 0), stop=(kc == 3))
                mm_last = p.add('T', mm,
                                deps=([m_oT, marks_get(ctx, f'h_ps{"cd"[half]}_war')]
                                      if kc == 0 else []),
                                label=f'h{tag}wo{qc}_{half}_{kc}')
            wo_ms.append(mm_last)
        dd = []
        for half in range(2):
            psum = ctx.ps_c if half == 0 else ctx.ps_d
            def de(s, half=half, psum=psum):
                return s.scalar_tensor_tensor(
                    ctx.hvout[:, half * 512:half * 512 + 512], psum[:, 0:512],
                    0.5, ctx.nullh_b[:, half * 512:half * 512 + 512],
                    op0=ALU.mult, op1=ALU.subtract)
            m_d = p.add('V', de, deps=[wo_ms[half], m_sc_prev],
                        label=f'h{tag}d{qc}_{half}')
            marks_set(ctx, f'h_ps{"cd"[half]}_war', m_d)
            dd.append(m_d)
        ctx.p.bounds_vals.add(ctx.T - 1)
        def sca(s, qc=qc):
            return s.indirect_dma_start(
                out=x2_ap, out_offset=IndirectOffsetOnAxis(
                    ctx.idxq[:, qc:qc + 1], 0),
                in_=ctx.hvout[:], in_offset=None, compute_op=ALU.add,
                bounds_check=ctx.p.bc_reg(ctx.T - 1), oob_is_err=True)
        m_sc_prev = p.add('GQ', sca, deps=dd + [m_x2, route_res[('idx', 0)]],
                          label=f'h{tag}scat{qc}', slot='hscat')
    m_done = p.barrier('V', [m_sc_prev], label=f'h{tag}_done')
    marks[f'heavy_{tag}'] = m_done
    return m_done



def declare_gla(ctx):
    G = 'gla'
    ctx.xlnT = ctx.sb('g_xlnT', [128, 8 * 512], BF16, G)
    ctx.gqT = ctx.sb('g_qT', [128, 8 * 512], BF16, G)
    ctx.gkT_ = ctx.sb('g_kT', [128, 8 * 512], BF16, G)
    ctx.ggk = ctx.sb('g_gk', [128, 8 * 512], F32, G)
    ctx.gbT = ctx.sb('g_bT', [128, 1024], F32, G)
    ctx.gfT = ctx.sb('g_fT', [128, 1024], F32, G)
    ctx.gqp = ctx.sb('g_qp', [128, 1024], BF16, G)
    ctx.gkpT = ctx.sb('g_kpT', [128, 1024], BF16, G)
    ctx.gkp = ctx.sb('g_kp', [128, 1024], BF16, G)
    ctx.gv = ctx.sb('g_v', [128, 4 * 1024], BF16, G)
    ctx.gA = ctx.sb('g_A', [128, 256], BF16, G)
    ctx.gS = ctx.sb('g_S', [128, 1024], F32, G)
    ctx.gSbf = ctx.sb('g_Sbf', [128, 1024], BF16, 'glaR', side='right')
    ctx.gob = ctx.sb('g_ob', [128, 1024], F32, G)
    ctx.gsg = ctx.gfT
    ctx.gt1 = ctx.sb('g_t1', [32, 512], BF16, G)
    ctx.gtmp = [ctx.sb(f'g_tmp{i}', [128, 512], F32, G) for i in range(2)]
    ctx.gcast = ctx.sb('g_cast', [128, 1024], BF16, G)
    ctx.gofb = ctx.gkpT
    ctx.gofT = ctx.gqp
    ctx.gred = ctx.sb('g_red', [128, 16], F32, G)
    ctx.gx3 = [ctx.gob] * 2


def phase_gla(ctx, marks, x2_ap, x3_ap, xln3_ap, dep_start):
    p, d = ctx.p, ctx.d
    NG = ctx.NG
    wg = marks['wgla']
    m_S0 = p.add('V', lambda s: s.memset(ctx.gS[:], 0.0), deps=[dep_start],
                 label='gla_S0')
    gpw = p.barrier('V', [dep_start, marks.get('_psu_war')], label='gpsw')
    for k in ['g_psa_war', 'g_psb_war', 'g_psc_war', 'g_psd_war', 'g_pse_war',
              'g_psu_war', 'g_psf_war']:
        marks_set(ctx, k, gpw)
    m_t1ones = p.add('V', lambda s: s.memset(ctx.gt1[0:32, :], 1.0),
                     label='gla_t1ones')
    m_sbfz = p.add('P', lambda s: s.memset(ctx.gSbf[:], 0.0),
                   deps=[dep_start], label='gla_sbfz')
    marks_set(ctx, 'gSbf_war', m_sbfz)
    marks_set(ctx, 'gS_war', m_S0)
    x3_stores = []

    for g in range(NG):
        tile_marks = []
        for k in range(4):
            xb = ctx.xb[k % 2]
            m_ld = p.add('SQ', (lambda s, g=g, k=k, xb=xb: s.dma_start(
                out=xb[:], in_=x2_ap[g * 512 + k * 128:g * 512 + k * 128 + 128, :])),
                deps=[marks_get(ctx, f'gxb{k % 2}_war'), marks.get('heavy_1'),
                      dep_start, marks.get(f'xb{k % 2}_war')],
                label=f'gg{g}ld{k}', slot=f'xb{k % 2}')
            marks[f'xb{k % 2}_war'] = None
            p.add('V', (lambda s, xb=xb: s.bn_stats(ctx.stat[:, 0:6],
                                                    xb[:, 0:512])), deps=[m_ld])
            p.add('V', (lambda s, xb=xb: s.bn_stats(ctx.stat[:, 6:12],
                                                    xb[:, 512:1024])))
            m_ag = p.add('V', lambda s: s.bn_aggr(ctx.stat[:, 12:14],
                                                  ctx.stat[:, 0:12]))
            m_sq = p.add('S', lambda s: s.activation(ctx.colv[1][:],
                                                     ctx.stat[:, 13:14], AF.Sqrt,
                                                     bias=ctx.consts[:, 0:1]),
                         deps=[m_ag], label=f'gg{g}sq{k}')
            p.add('V', lambda s: s.reciprocal(ctx.colv[2][:], ctx.colv[1][:]),
                  deps=[m_sq])
            m_nb = p.add('V', lambda s: s.scalar_tensor_tensor(
                ctx.colv[3][:], ctx.stat[:, 12:13], -1.0, ctx.colv[2][:],
                op0=ALU.mult, op1=ALU.mult))
            m_c = p.add('S', (lambda s, xb=xb: s.activation(
                ctx.gcast[:], xb[:], AF.Identity, bias=ctx.colv[3][:],
                scale=ctx.colv[2][:])),
                deps=[m_nb, marks_get(ctx, 'gcast_war')], label=f'gg{g}c{k}')
            marks_set(ctx, f'gxb{k % 2}_war',
                      p.barrier('S', [m_c], label=f'gg{g}xbw{k}'))
            m_tp = emit_transposes(
                ctx, ctx.gcast, ctx.xlnT,
                (lambda j, k=k: ctx.xlnT[:, j * 512 + k * 128:
                                         j * 512 + k * 128 + 128]),
                deps=[m_c, marks_get(ctx, 'xlnT_war')], tag=f'gg{g}k{k}')
            marks_set(ctx, 'gcast_war', m_tp)
            tile_marks.append(m_tp)
        m_xT = p.barrier('V', tile_marks, label=f'gg{g}xT')

        pm = {}
        for nm, woff, dstbuf in [('q', W_GQ, ctx.gqT), ('k', W_GK, ctx.gkT_)]:
            evs = []
            for c in range(8):
                psum = ctx.ps_a if c % 2 == 0 else ctx.ps_b
                ps_war = marks_get(ctx, f'g_ps{"ab"[c % 2]}_war')
                mm_last = None
                for kc in range(8):
                    def mm(s, c=c, kc=kc, psum=psum, woff=woff):
                        return s.matmul(
                            psum[:, 0:512],
                            ctx.wreg[:, woff + kc * 1024 + c * 128:
                                     woff + kc * 1024 + c * 128 + 128],
                            ctx.xlnT[:, kc * 512:(kc + 1) * 512],
                            start=(kc == 0), stop=(kc == 7))
                    mm_last = p.add('T', mm,
                                    deps=([m_xT, wg, ps_war] if kc == 0 else []),
                                    label=f'gg{g}{nm}p{c}_{kc}')
                def ev(s, c=c, psum=psum, dstbuf=dstbuf):
                    return s.tensor_copy(dstbuf[:, c * 512:(c + 1) * 512],
                                         psum[:, 0:512])
                m_ev = p.add('V', ev, deps=[mm_last,
                                            marks_get(ctx, f'g{nm}T_war')],
                             label=f'gg{g}{nm}e{c}')
                marks_set(ctx, f'g_ps{"ab"[c % 2]}_war', m_ev)
                evs.append(m_ev)
            pm[nm] = p.barrier('V', evs, label=f'gg{g}{nm}done')
        mm_last = None
        ps_war = marks_get(ctx, 'g_psc_war')
        for kc in range(8):
            def mm(s, kc=kc):
                return s.matmul(ctx.ps_c[0:16, 0:512],
                                ctx.wg1_bf[:, kc * 16:kc * 16 + 16],
                                ctx.xlnT[:, kc * 512:(kc + 1) * 512],
                                start=(kc == 0), stop=(kc == 7))
            mm_last = p.add('T', mm, deps=([m_xT, wg, ps_war] if kc == 0 else []),
                            label=f'gg{g}t1_{kc}')
        m_t1 = p.add('V', lambda s: s.tensor_copy(ctx.gt1[0:16, :],
                                                  ctx.ps_c[0:16, 0:512]),
                     deps=[mm_last, m_t1ones, marks_get(ctx, 'gt1_war')],
                     label=f'gg{g}t1e')
        marks_set(ctx, 'g_psc_war', m_t1)
        gk_evs = []
        for c in range(8):
            def mmz(s, c=c):
                return s.matmul(ctx.ps_c[:, 0:512],
                                ctx.wg2a_bf[0:17, c * 128:c * 128 + 128],
                                ctx.gt1[0:17, :], start=True, stop=True)
            m_z = p.add('T', mmz, deps=[m_t1, marks_get(ctx, 'g_psc_war'), wg],
                        label=f'gg{g}z{c}')
            m_u1 = p.add('S', (lambda s: s.activation(ctx.gtmp[0][:],
                                                      ctx.ps_c[:, 0:512], AF.Abs)),
                         deps=[m_z, marks_get(ctx, 'gtmp0_war')],
                         label=f'gg{g}abs{c}')
            p.add('S', (lambda s: s.activation(
                ctx.gtmp[1][:], ctx.gtmp[0][:], AF.Exp,
                scale=ctx.consts[:, 2:3])),
                deps=[marks_get(ctx, 'gtmp1_war'), marks['consts']],
                label=f'gg{g}exp{c}')
            m_u3 = p.add('S', (lambda s: s.activation(
                ctx.gtmp[0][:], ctx.gtmp[1][:], AF.Ln, bias=1.0)),
                label=f'gg{g}ln{c}')
            marks_set(ctx, 'gtmp1_war', m_u3)
            def fin(s, c=c):
                return s.scalar_tensor_tensor(
                    ctx.ggk[:, c * 512:(c + 1) * 512], ctx.ps_c[:, 0:512], 0.0,
                    ctx.gtmp[0][:], op0=ALU.min, op1=ALU.subtract)
            m_gk = p.add('V', fin, deps=[m_u3, marks_get(ctx, 'ggk_war')],
                         label=f'gg{g}gk{c}')
            marks_set(ctx, 'gtmp0_war', m_gk)
            marks_set(ctx, 'g_psc_war', m_gk)
            gk_evs.append(m_gk)
        pm['gk'] = p.barrier('V', gk_evs, label=f'gg{g}gkdone')
        v_evs = []
        for t in range(4):
            for half in range(2):
                psum = ctx.ps_d if half == 0 else ctx.ps_e
                ps_war = marks_get(ctx, f'g_ps{"de"[half]}_war')
                mm_last = None
                for kc in range(8):
                    def mm(s, t=t, kc=kc, half=half, psum=psum):
                        return s.matmul(
                            psum[:, 0:512],
                            ctx.xlnT[:, kc * 512 + t * 128:kc * 512 + t * 128 + 128],
                            ctx.wreg[:, W_GV + kc * 1024 + half * 512:
                                     W_GV + kc * 1024 + half * 512 + 512],
                            start=(kc == 0), stop=(kc == 7))
                    mm_last = p.add('T', mm,
                                    deps=([m_xT, wg, ps_war] if kc == 0 else []),
                                    label=f'gg{g}v{t}_{half}_{kc}')
                def ev(s, t=t, half=half, psum=psum):
                    return s.tensor_copy(
                        ctx.gv[:, t * 1024 + half * 512:
                               t * 1024 + half * 512 + 512],
                        psum[:, 0:512])
                m_ev = p.add('V', ev, deps=[mm_last, marks_get(ctx, 'gv_war')],
                             label=f'gg{g}ve{t}_{half}')
                marks_set(ctx, f'g_ps{"de"[half]}_war', m_ev)
                v_evs.append(m_ev)
        pm['v'] = p.barrier('V', v_evs, label=f'gg{g}vdone')
        if g == NG - 1:
            marks['gla_qkfree'] = p.barrier('V', [pm['q'], pm['k']],
                                            label='gla_qkfree')
            marks['gla_vfree'] = pm['v']

        glast_T = []
        for cw in range(4):
            ch = g * 4 + cw
            par = ch % 2
            m_scan0 = None
            for r in range(8):
                def sc(s, r=r, cw=cw):
                    return s.tensor_tensor_scan(
                        ctx.gbT[:, r * 128:(r + 1) * 128],
                        ctx.ggk[:, r * 512 + cw * 128:r * 512 + cw * 128 + 128],
                        ctx.zero128[:], 0.0, op0=ALU.add, op1=ALU.add)
                m_s = p.add('V', sc,
                            deps=[pm['gk'], marks_get(ctx, 'gbT_war'),
                                  marks['consts']],
                            label=f'gc{ch}scan{r}')
                if m_scan0 is None:
                    m_scan0 = m_s
            m_b = p.barrier('V', [], label=f'gc{ch}b')
            m_f = p.add('S', lambda s: s.activation(ctx.gfT[:], ctx.gbT[:],
                                                    AF.Exp,
                                                    scale=ctx.consts[:, 4:5]),
                        deps=[m_b, marks_get(ctx, 'gfT_war'),
                              marks_get(ctx, 'gsg_war')], label=f'gc{ch}f')
            m_e = p.add('S', lambda s: s.activation(ctx.gbT[:], ctx.gbT[:],
                                                    AF.Exp,
                                                    scale=ctx.consts[:, 3:4]),
                        deps=[marks_get(ctx, 'geT_war')], label=f'gc{ch}e')
            qv = ctx.gqT[:].rearrange('p (c t) -> p c t', t=512)[:, :, cw * 128:
                                                                cw * 128 + 128]
            kv = ctx.gkT_[:].rearrange('p (c t) -> p c t', t=512)[:, :, cw * 128:
                                                                 cw * 128 + 128]
            m_qp = p.add('P', (lambda s, qv=qv: s.tensor_tensor(
                ctx.gqp[:].rearrange('p (c t) -> p c t', t=128), qv,
                ctx.gbT[:].rearrange('p (c t) -> p c t', t=128), op=ALU.mult)),
                deps=[m_e, pm['q'], marks_get(ctx, 'gqp_war')],
                label=f'gc{ch}qp')
            m_kp = p.add('P', (lambda s, kv=kv: s.tensor_tensor(
                ctx.gkpT[:].rearrange('p (c t) -> p c t', t=128), kv,
                ctx.gfT[:].rearrange('p (c t) -> p c t', t=128), op=ALU.mult)),
                deps=[m_f, pm['k'], marks_get(ctx, 'gkpT_war')],
                label=f'gc{ch}kp')
            p.add('V', (lambda s, par=par: s.tensor_copy(
                ctx.gSbf[0:64, :].rearrange('p (h e) -> p h e', e=128)[:, :, 0:64],
                ctx.gS[0:64, par * 512:par * 512 + 512]
                   .rearrange('p (h e) -> p h e', e=64))),
                deps=[marks_get(ctx, 'gS_war'), marks_get(ctx, 'gSbf_war')],
                label=f'gc{ch}sbfa')
            m_sbf = p.add('V', (lambda s, par=par: s.tensor_copy(
                ctx.gSbf[64:128, :].rearrange('p (h e) -> p h e', e=128)[:, :, 64:128],
                ctx.gS[64:128, par * 512:par * 512 + 512]
                   .rearrange('p (h e) -> p h e', e=64))),
                label=f'gc{ch}sbfb')
            m_kt = emit_transposes(
                ctx, ctx.gkpT, ctx.gkp,
                (lambda j: ctx.gkp[:, j * 128:(j + 1) * 128]),
                deps=[m_kp, marks_get(ctx, 'gkp_war')], tag=f'gc{ch}kt')
            oev = []
            for hp in range(8):
                def mmo2(s, hp=hp):
                    return s.matmul(
                        ctx.ps_u[:, 0:128],
                        ctx.gqp[:, hp * 128:hp * 128 + 128],
                        ctx.gSbf[:, hp * 128:hp * 128 + 128],
                        start=True, stop=True)
                m_o2 = p.add('T', mmo2,
                             deps=[m_sbf, m_qp, marks_get(ctx, 'g_psu_war')],
                             label=f'gc{ch}o2_{hp}')
                def mmt(s, hp=hp, cw=cw):
                    return s.matmul(
                        ctx.ps_u[:, 384:512],
                        ctx.gkp[:, hp * 128:hp * 128 + 128],
                        ctx.gv[:, cw * 1024 + hp * 128:cw * 1024 + hp * 128 + 128],
                        start=True, stop=True)
                t_ms = [p.add('T', mmt, deps=[m_kt, pm['v']],
                              label=f'gc{ch}T{hp}')]
                def mma0(s, hp=hp):
                    return s.matmul(
                        ctx.ps_u[:, 128:256],
                        ctx.gkpT[0:64, hp * 128:hp * 128 + 128],
                        ctx.gqp[0:64, hp * 128:hp * 128 + 128],
                        start=True, stop=True, tile_position=(0, 0))
                m_a0 = p.add('T', mma0, deps=[m_kp],
                             label=f'gc{ch}A{hp}_0')
                def mma1(s, hp=hp):
                    return s.matmul(
                        ctx.ps_f[:, 0:128],
                        ctx.gkpT[64:128, hp * 128:hp * 128 + 128],
                        ctx.gqp[64:128, hp * 128:hp * 128 + 128],
                        start=True, stop=True, tile_position=(64, 0))
                m_a1 = p.add('T', mma1,
                             deps=[marks_get(ctx, 'g_psf_war')],
                             label=f'gc{ch}A{hp}_1')
                def mev0(s):
                    return s.tensor_tensor(ctx.gA[:, 0:128], ctx.ps_u[:, 128:256],
                                           ctx.trimask[:], op=ALU.mult)
                m_A0 = p.add('V', mev0, deps=[m_a0, marks['tri'],
                                              marks_get(ctx, 'gA_war')],
                             label=f'gc{ch}Am{hp}0')
                def mev1(s):
                    return s.tensor_tensor(ctx.gA[:, 128:256], ctx.ps_f[:, 0:128],
                                           ctx.trimask[:], op=ALU.mult)
                m_A1 = p.add('V', mev1, deps=[m_a1], label=f'gc{ch}Am{hp}1')
                m_A = m_A1
                o_ms = [m_o2]
                for hh in range(2):
                    def mmo1(s, hp=hp, hh=hh, cw=cw):
                        return s.matmul(
                            ctx.ps_u[:, 256 + hh * 64:256 + hh * 64 + 64],
                            ctx.gA[:, hh * 128:hh * 128 + 128],
                            ctx.gv[:, cw * 1024 + hp * 128 + hh * 64:
                                   cw * 1024 + hp * 128 + hh * 64 + 64],
                            start=True, stop=True)
                    o_ms.append(p.add('T', mmo1,
                                      deps=[m_A0 if hh == 0 else m_A1, pm['v']],
                                      label=f'gc{ch}o1_{hp}_{hh}'))
                def oev_a(s, hp=hp):
                    return s.tensor_copy(ctx.gob[:, hp * 128:hp * 128 + 128],
                                         ctx.ps_u[:, 256:384])
                p.add('V', oev_a, deps=o_ms + t_ms + [marks_get(ctx, 'gob_war')],
                      label=f'gc{ch}oea{hp}')
                def oev_(s, hp=hp):
                    return s.tensor_tensor(ctx.gob[:, hp * 128:hp * 128 + 128],
                                           ctx.gob[:, hp * 128:hp * 128 + 128],
                                           ctx.ps_u[:, 0:128], op=ALU.add)
                m_oe = p.add('V', oev_, label=f'gc{ch}oe{hp}')
                oev.append(m_oe)
                def su1a(s, hp=hp, par=par):
                    return s.tensor_tensor(
                        ctx.gS[0:64, (1 - par) * 512 + hp * 64:
                               (1 - par) * 512 + hp * 64 + 64],
                        ctx.gS[0:64, par * 512 + hp * 64:par * 512 + hp * 64 + 64],
                        ctx.ps_u[0:64, 384:448], op=ALU.add)
                m_s1a = p.add('V', su1a, deps=t_ms + o_ms, label=f'gc{ch}su1a_{hp}')
                def su1b(s, hp=hp, par=par):
                    return s.tensor_tensor(
                        ctx.gS[64:128, (1 - par) * 512 + hp * 64:
                               (1 - par) * 512 + hp * 64 + 64],
                        ctx.gS[64:128, par * 512 + hp * 64:par * 512 + hp * 64 + 64],
                        ctx.ps_u[64:128, 448:512], op=ALU.add)
                m_s1b = p.add('V', su1b, label=f'gc{ch}su1b_{hp}')
                def su2(s, hp=hp, par=par, cw=cw):
                    return s.tensor_scalar(
                        ctx.gS[:, (1 - par) * 512 + hp * 64:
                               (1 - par) * 512 + hp * 64 + 64],
                        ctx.gS[:, (1 - par) * 512 + hp * 64:
                               (1 - par) * 512 + hp * 64 + 64],
                        ctx.gbT[:, hp * 128 + 127:hp * 128 + 128], None,
                        op0=ALU.mult)
                m_s2 = p.add('V', su2, label=f'gc{ch}su2_{hp}')
                marks_set(ctx, 'g_psu_war', p.barrier('V', [m_A0, m_oe, m_s2],
                                                      label=f'gc{ch}pu{hp}'))
                marks_set(ctx, 'g_psf_war', m_A1)
            m_S = p.barrier('V', [m_qp, m_kp], label=f'gc{ch}S')
            marks_set(ctx, 'gS_war', m_S)
            m_bw = p.barrier('V', [m_S], label=f'gc{ch}bw')
            marks_set(ctx, 'gbT_war', m_bw)
            marks_set(ctx, 'geT_war', None)
            marks_set(ctx, 'geT_war', m_S)
            marks_set(ctx, 'gfT_war', m_S)
            m_qpw = p.barrier('T', [], label=f'gc{ch}qpw')
            marks_set(ctx, 'gqp_war', m_qpw)
            marks_set(ctx, 'gSbf_war', m_qpw)
            marks_set(ctx, 'gkpT_war', marks_get(ctx, 'gqp_war'))
            marks_set(ctx, 'gkp_war', marks_get(ctx, 'gqp_war'))
            marks_set(ctx, 'gA_war', marks_get(ctx, 'gqp_war'))

            m_ob = p.barrier('V', oev, label=f'gc{ch}ob')
            m_sqr = p.add('V', lambda s: s.tensor_tensor(ctx.gsg[:], ctx.gob[:],
                                                         ctx.gob[:],
                                                         op=ALU.mult),
                          deps=[m_ob, m_kp, marks_get(ctx, 'gsg_war')],
                          label=f'gc{ch}sq')
            m_red = p.add('V', lambda s: s.tensor_reduce(
                ctx.gred[:].unsqueeze(2),
                ctx.gsg[:].rearrange('p (h e) -> p h e', e=64),
                axis=AX.X, op=ALU.add), label=f'gc{ch}red')
            m_r1 = p.add('S', lambda s: s.activation(ctx.gred[:], ctx.gred[:],
                                                     AF.Sqrt,
                                                     bias=ctx.consts[:, 0:1],
                                                     scale=ctx.consts[:, 5:6]),
                         deps=[m_red], label=f'gc{ch}r1')
            m_r2 = p.add('V', lambda s: s.reciprocal(ctx.gred[:], ctx.gred[:]),
                         deps=[m_r1], label=f'gc{ch}r2')
            sil_ms = []
            for half in range(2):
                psum = ctx.ps_d if half == 0 else ctx.ps_e
                ps_war = marks_get(ctx, f'g_ps{"de"[half]}_war')
                mm_last = None
                for kc in range(8):
                    def mm(s, kc=kc, half=half, psum=psum, cw=cw):
                        return s.matmul(
                            psum[:, 0:512],
                            ctx.xlnT[:, kc * 512 + cw * 128:kc * 512 + cw * 128 + 128],
                            ctx.wreg[:, W_GG + kc * 1024 + half * 512:
                                     W_GG + kc * 1024 + half * 512 + 512],
                            start=(kc == 0), stop=(kc == 7))
                    mm_last = p.add('T', mm,
                                    deps=([m_xT, ps_war, m_sqr] if kc == 0 else []),
                                    label=f'gc{ch}gp{half}_{kc}')
                m_sil = p.add('S', (lambda s, half=half, psum=psum: s.activation(
                    ctx.gsg[:, half * 512:half * 512 + 512], psum[:, 0:512],
                    AF.Silu)), deps=[mm_last, m_red], label=f'gc{ch}sil{half}')
                sil_ms.append(m_sil)
                marks_set(ctx, f'g_ps{"de"[half]}_war', m_sil)
            m_sg2 = p.add('V', lambda s: s.tensor_tensor(
                ctx.gsg[:].rearrange('p (h e) -> p h e', e=64),
                ctx.gsg[:].rearrange('p (h e) -> p h e', e=64),
                ctx.gred[:].unsqueeze(2).to_broadcast([128, 16, 64]),
                op=ALU.mult), deps=sil_ms + [m_r2], label=f'gc{ch}sg2')
            m_of = p.add('V', lambda s: s.tensor_tensor(ctx.gofb[:], ctx.gob[:],
                                                        ctx.gsg[:],
                                                        op=ALU.mult),
                         deps=[marks_get(ctx, 'gofb_war'),
                               marks_get(ctx, 'gqp_war')], label=f'gc{ch}of')
            marks_set(ctx, 'gsg_war', m_of)
            marks_set(ctx, 'gob_war', m_of)
            m_ofT = emit_transposes(
                ctx, ctx.gofb, ctx.gofT,
                (lambda j: ctx.gofT[:, j * 128:(j + 1) * 128]),
                deps=[m_of, marks_get(ctx, 'gofT_war'),
                      marks_get(ctx, 'gqp_war')], tag=f'gc{ch}ofT')
            wo_ms = []
            for half in range(2):
                psum = ctx.ps_d if half == 0 else ctx.ps_e
                mm_last = None
                for kc in range(8):
                    def mm(s, kc=kc, half=half, psum=psum):
                        return s.matmul(
                            psum[:, 0:512],
                            ctx.gofT[:, kc * 128:(kc + 1) * 128],
                            ctx.wreg[:, W_GO + kc * 1024 + half * 512:
                                     W_GO + kc * 1024 + half * 512 + 512],
                            start=(kc == 0), stop=(kc == 7))
                    mm_last = p.add('T', mm,
                                    deps=([m_ofT,
                                           marks_get(ctx, f'g_ps{"de"[half]}_war')]
                                          if kc == 0 else []),
                                    label=f'gc{ch}wo{half}_{kc}')
                wo_ms.append(mm_last)
            marks_set(ctx, 'gofT_war', wo_ms[-1])
            m_qpw2 = p.barrier('T', [], label=f'gc{ch}qpw2')
            marks_set(ctx, 'gqp_war', m_qpw2)
            marks_set(ctx, 'gkpT_war', m_qpw2)
            marks_set(ctx, 'gofb_war', None)
            xb = ctx.xb[2]
            x3b = ctx.gx3[cw % 2]
            tok0 = ch * 128
            m_ldx = p.add('SQ', (lambda s, tok0=tok0, xb=xb: s.dma_start(
                out=xb[:], in_=x2_ap[tok0:tok0 + 128, :])),
                deps=[marks_get(ctx, 'gxb2_war'), marks.get('xb2_war')],
                label=f'gc{ch}ldx', slot='xb2')
            marks['xb2_war'] = None
            add_ms = []
            for half in range(2):
                psum = ctx.ps_d if half == 0 else ctx.ps_e
                def ad(s, half=half, psum=psum, xb=xb, x3b=x3b):
                    return s.tensor_tensor(x3b[:, half * 512:half * 512 + 512],
                                           xb[:, half * 512:half * 512 + 512],
                                           psum[:, 0:512], op=ALU.add)
                m_ad = p.add('V', ad, deps=[wo_ms[half], m_ldx,
                                            marks_get(ctx, 'gx3_war')],
                             label=f'gc{ch}add{half}')
                marks_set(ctx, f'g_ps{"de"[half]}_war', m_ad)
                add_ms.append(m_ad)
            m_x3 = p.barrier('V', add_ms, label=f'gc{ch}x3')
            marks_set(ctx, 'gxb2_war', m_x3)
            m_st1 = p.add('AQ', (lambda s, tok0=tok0, x3b=x3b: s.dma_start(
                out=x3_ap[tok0:tok0 + 128, :], in_=x3b[:])),
                deps=[m_x3], label=f'gc{ch}st3', slot='x3st')
            p.add('V', (lambda s, x3b=x3b: s.bn_stats(ctx.stat[:, 0:6],
                                                      x3b[:, 0:512])),
                  deps=[m_x3])
            p.add('V', (lambda s, x3b=x3b: s.bn_stats(ctx.stat[:, 6:12],
                                                      x3b[:, 512:1024])))
            m_ag = p.add('V', lambda s: s.bn_aggr(ctx.stat[:, 12:14],
                                                  ctx.stat[:, 0:12]))
            m_sq2 = p.add('S', lambda s: s.activation(ctx.colv[1][:],
                                                      ctx.stat[:, 13:14],
                                                      AF.Sqrt,
                                                      bias=ctx.consts[:, 0:1]),
                          deps=[m_ag], label=f'gc{ch}lsq')
            p.add('V', lambda s: s.reciprocal(ctx.colv[2][:], ctx.colv[1][:]),
                  deps=[m_sq2])
            m_nb = p.add('V', lambda s: s.scalar_tensor_tensor(
                ctx.colv[3][:], ctx.stat[:, 12:13], -1.0, ctx.colv[2][:],
                op0=ALU.mult, op1=ALU.mult))
            m_lnw = p.add('S', (lambda s, x3b=x3b: s.activation(
                ctx.junkw[:], x3b[:], AF.Identity, bias=ctx.colv[3][:],
                scale=ctx.colv[2][:])),
                deps=[m_nb, marks.get('junkw_free')], label=f'gc{ch}lnw')
            m_st2 = p.add('AQ', (lambda s, tok0=tok0: s.dma_start(
                out=xln3_ap[tok0:tok0 + 128, :], in_=ctx.junkw[:])),
                deps=[m_lnw], label=f'gc{ch}st4', slot='xln3st')
            marks['junkw_free'] = m_st2
            m_x3w = p.barrier('V', [m_st1, m_st2], label=f'gc{ch}x3w')
            marks_set(ctx, 'gx3_war', m_x3w)
            marks_set(ctx, 'gob_war', m_x3w)
            x3_stores.extend([m_st1, m_st2])
            glast_T.append(marks_get(ctx, 'gqp_war'))
        m_gu = p.barrier('T', [], label=f'gg{g}used')
        m_gv = p.barrier('V', [], label=f'gg{g}usedv')
        marks_set(ctx, 'xlnT_war', m_gu)
        marks_set(ctx, 'gv_war', m_gu)
        marks_set(ctx, 'gqT_war', m_gv)
        marks_set(ctx, 'gkT_war', m_gv)
        marks_set(ctx, 'ggk_war', m_gv)
        marks_set(ctx, 'gt1_war', m_gu)
    m_all = p.barrier('V', x3_stores, label='gla_done')
    marks['gla_done'] = m_all
    return m_all



def build_all(T=8192, NQ=1024, NKV=2048, n_iters=50, bis_iters=30,
              detect_races=True, upto='all'):
    ctx = Ctx(T, NQ, NKV, n_iters=n_iters, bis_iters=bis_iters,
              detect_races=detect_races)
    declare(ctx)
    marks = {}
    prep_consts(ctx, marks)
    prep_weights_ffn(ctx, marks, '1')
    prep_null_kv(ctx, marks)
    phase_ln_scores(ctx, marks, ctx.d['x'], ctx.d['XLN'], '1')
    r1 = routing(ctx, marks, '1', marks['scores_1'])
    declare_light(ctx, '1')
    prep_light_memsets(ctx, marks, '1')
    if upto == 'light1':
        ctx.dbg_onorm = ctx.dram('d_onormall', [T, 512], BF16, 'ExternalOutput')
        ctx.dbg_esim = ctx.dram('d_esimall', [T, 2048], BF16, 'ExternalOutput')
    phase_light(ctx, marks, ctx.d['XLN'], ctx.d['x'], ctx.d['X2'], '1')
    if upto == 'light1':
        dumps = [('d_qT', ctx.qTl[0], BF16), ('d_kT', ctx.kTl[0], BF16),
                 ('d_v0', ctx.vS[0], BF16), ('d_v1', ctx.vS[1], BF16),
                 ('d_v3', ctx.vS[3], BF16), ('d_v4', ctx.vS[4], BF16),
                 ('d_esim', ctx.esim[0], BF16), ('d_onorm', ctx.onorm, BF16),
                 ('d_onormT', ctx.onormT, BF16), ('d_xn2T', ctx.xn2T[0], BF16)]
        dps = []
        for nm, t, dt_ in dumps:
            ap = ctx.dram(nm, list(t.shape), dt_, 'ExternalOutput')
            dps.append(ctx.p.add('AQ', (lambda s, ap=ap, t=t: s.dma_start(
                out=ap, in_=t[:])), deps=[marks['x2_done_1']],
                label=f'dump{nm}', slot='dbg'))
        ctx.p.barrier('S', [marks['x2_done_1'], r1[('idx', 0)],
                            r1[('idx', 1)]] + dps, label='FINAL')
        ctx.p.emit()
        return ctx, marks
    ctx.free_group('lightA1')
    declare_heavy(ctx, '1')
    prep_heavy_memsets(ctx, marks, '1', dep=marks['x2_done_1'])
    phase_heavy(ctx, marks, ctx.d['XLN'], ctx.d['X2'], '1', r1)
    ctx.free_group('lightB1')
    ctx.free_group('heavy1')
    if upto == 'heavy1':
        ctx.p.barrier('S', [marks['heavy_1']], label='FINAL')
        ctx.p.emit()
        return ctx, marks
    prep_weights_gla(ctx, marks, dep=None)
    declare_gla(ctx)
    phase_gla(ctx, marks, ctx.d['X2'], ctx.d['X3'], ctx.d['XLN3'],
              dep_start=marks['heavy_1'])
    ctx.free_group('glaR')
    ctx.free_group('gla')
    if upto == 'gla':
        ctx.p.barrier('S', [marks['gla_done']], label='FINAL')
        ctx.p.emit()
        return ctx, marks
    marks['psum_free'] = marks['gla_done']
    prep_weights_ffn(ctx, marks, '2', dep=None, reload=True)
    phase_ln_scores(ctx, marks, None, ctx.d['XLN3'], '2',
                    dep=marks['gla_done'])
    marks['xln_done_2'] = marks['gla_done']
    r2 = routing(ctx, marks, '2', marks['scores_2'],
                 extra_dep=marks['heavy_1'])
    declare_light(ctx, '2')
    prep_light_memsets(ctx, marks, '2', dep=marks['gla_done'])
    phase_light(ctx, marks, ctx.d['XLN3'], ctx.d['X3'], ctx.d['out'], '2')
    ctx.free_group('lightA2')
    declare_heavy(ctx, '2')
    prep_heavy_memsets(ctx, marks, '2', dep=marks['x2_done_2'])
    phase_heavy(ctx, marks, ctx.d['XLN3'], ctx.d['out'], '2', r2)
    ctx.p.barrier('S', [marks['heavy_2'], marks['x2_done_2']], label='FINAL')
    ctx.p.emit()
    return ctx, marks



_NC_CACHE = {}


def _get_program():
    key = 'main'
    if key not in _NC_CACHE:
        ctx, marks = build_all()
        _NC_CACHE[key] = ctx
    return _NC_CACHE[key]


def kernel(**inputs):
    x = np.ascontiguousarray(inputs['L0_attn_x'], dtype=np.float32)
    B = x.shape[0]
    ctx = _get_program()

    w_kv = np.ascontiguousarray(
        np.asarray(inputs['L0_ffn_w_kv_h'], np.float32)
        .reshape(1024, 8, 2, 64).transpose(0, 2, 1, 3).reshape(1024, 1024))
    wg2a = np.ascontiguousarray(np.concatenate(
        [np.asarray(inputs['L0_attn_Wg2'], np.float32),
         np.asarray(inputs['L0_attn_bg2'], np.float32)[None, :]], axis=0))
    base = {
        'Wq': np.ascontiguousarray(inputs['L0_attn_Wq'], np.float32),
        'Wk': np.ascontiguousarray(inputs['L0_attn_Wk'], np.float32),
        'Wv': np.ascontiguousarray(inputs['L0_attn_Wv'], np.float32),
        'Wg': np.ascontiguousarray(inputs['L0_attn_Wg'], np.float32),
        'Wo': np.ascontiguousarray(inputs['L0_attn_Wo'], np.float32),
        'Wg1': np.ascontiguousarray(inputs['L0_attn_Wg1'], np.float32),
        'Wg2a': wg2a,
        'g_norm_w': np.ascontiguousarray(
            np.asarray(inputs['L0_attn_g_norm_w'], np.float32)[None, :]),
        'qkv_l': np.ascontiguousarray(inputs['L0_ffn_w_qkv_l'], np.float32),
        'w_out_l': np.ascontiguousarray(inputs['L0_ffn_w_out_l'], np.float32),
        'null_q': np.ascontiguousarray(
            np.asarray(inputs['L0_ffn_null_q'], np.float32)[None, :]),
        'rt_q': np.ascontiguousarray(inputs['L0_ffn_rt_q'], np.float32),
        'rt_kv': np.ascontiguousarray(inputs['L0_ffn_rt_kv'], np.float32),
        'rms_g': np.ascontiguousarray(
            np.asarray(inputs['L0_ffn_rms_g'], np.float32)[None, :]),
        'w_q_h': np.ascontiguousarray(inputs['L0_ffn_w_q_h'], np.float32),
        'w_kv_h': w_kv,
        'null_kv': np.ascontiguousarray(
            np.asarray(inputs['L0_ffn_null_kv'], np.float32).reshape(2, 512)),
        'w_out_h': np.ascontiguousarray(inputs['L0_ffn_w_out_h'], np.float32),
        'ln_g': np.ascontiguousarray(
            np.asarray(inputs['L0_ffn_ln_g'], np.float32)[None, :]),
        'ln_b': np.ascontiguousarray(
            np.asarray(inputs['L0_ffn_ln_b'], np.float32)[None, :]),
    }
    in_maps = []
    for b in range(B):
        m = dict(base)
        m['x'] = np.ascontiguousarray(x[b])
        in_maps.append(m)
    res = run_bass_kernel_spmd(ctx.nc, in_maps, core_ids=list(range(B)))
    out = np.stack([np.asarray(res.results[b]['out']) for b in range(B)], axis=0)
    return out.astype(np.float32)


# version 06
# speedup vs baseline: 1.0499x; 1.0499x over previous
import numpy as np
import concourse.bass as bass
import concourse.mybir as mybir
from concourse.bass import IndirectOffsetOnAxis
from concourse.bass_utils import run_bass_kernel_spmd

F32 = mybir.dt.float32
BF16 = mybir.dt.bfloat16
U32 = mybir.dt.uint32
U8 = mybir.dt.uint8
AF = mybir.ActivationFunctionType
ALU = mybir.AluOpType
AX = mybir.AxisListType


COMPUTE = ('T', 'V', 'S', 'P')
QUEUES = ('SQ', 'AQ', 'GQ')
ALL_ENG = COMPUTE + QUEUES
STREAM = {'T': 'tensor', 'V': 'vector', 'S': 'scalar', 'P': 'gpsimd',
          'SQ': 'sync', 'AQ': 'scalar', 'GQ': 'gpsimd'}
INC = {'T': 1, 'V': 1, 'S': 1, 'P': 1, 'SQ': 16, 'AQ': 16, 'GQ': 16}
AUTO_SERIAL = ('V', 'S', 'P')


class _Op:
    __slots__ = ('eng', 'idx', 'fn', 'deps', 'needs_inc', 'label', 'slot')

    def __init__(self, eng, idx, fn, deps, label, slot=None):
        self.eng, self.idx, self.fn, self.deps = eng, idx, fn, deps
        self.needs_inc = False
        self.label = label
        self.slot = slot


class Prog:
    def __init__(self, nc):
        self.nc = nc
        self.seq = []
        self.n = {e: 0 for e in ALL_ENG}
        self.bounds_vals = set()
        self._bc_regs = {}

    def bc_reg(self, val):
        return self._bc_regs[val]

    def add(self, eng, fn, deps=(), label='', slot=None, barrier=False):
        deps = [dd for dd in deps if dd is not None]
        if (eng in AUTO_SERIAL or (barrier and eng in COMPUTE)) \
                and self.n[eng] > 0:
            deps.append((eng, self.n[eng] - 1))
        if eng in QUEUES:
            assert slot is not None, f'queue op {label} needs a slot'
        op = _Op(eng, self.n[eng], fn, deps, label, slot)
        self.n[eng] += 1
        self.seq.append(op)
        return (eng, op.idx)

    def barrier(self, eng, deps, label='barrier'):
        return self.add(eng, lambda s: s.nop(), deps, label, barrier=True)

    def emit(self):
        self.label_map = {}
        nc = self.nc
        by_eng = {e: [] for e in ALL_ENG}
        for op in self.seq:
            by_eng[op.eng].append(op)
        for op in self.seq:
            if op.eng in QUEUES:
                op.needs_inc = True
            for (de, di) in op.deps:
                by_eng[de][di].needs_inc = True
        slot_of = {}
        slot_counts = {}
        for e in ALL_ENG:
            c = 0
            for op in by_eng[e]:
                if op.needs_inc:
                    if e in QUEUES:
                        key = (e, op.slot)
                        slot_counts[key] = slot_counts.get(key, 0) + 1
                        slot_of[(e, op.idx)] = (key, slot_counts[key] * INC[e])
                    else:
                        slot_of[(e, op.idx)] = ((e, 0), (c + 1) * INC[e])
                    c += 1

        self._cms = []
        sems = {}
        for e in COMPUTE:
            cm = nc.semaphore(f'sem_{e}_0')
            sems[(e, 0)] = cm.__enter__()
            self._cms.append(cm)
        for key in slot_counts:
            cm = nc.semaphore(f'sem_{key[0]}_{key[1]}')
            sems[key] = cm.__enter__()
            self._cms.append(cm)
        blk_cm = nc.Block()
        blk = blk_cm.__enter__()
        self._cms.append(blk_cm)

        streams = {}
        for op in self.seq:
            streams.setdefault(STREAM[op.eng], []).append(op)

        def make_body(ops_list, is_gpsimd=False):
            def body(section):
                if is_gpsimd:
                    for val in sorted(self.bounds_vals):
                        r = section.alloc_register(f'bc{val}')
                        section.reg_mov(r, val)
                        self._bc_regs[val] = r
                for op in ops_list:
                    need = {}
                    for (de, di) in op.deps:
                        key, v = slot_of[(de, di)]
                        need[key] = max(need.get(key, 0), v)
                    for key, v in need.items():
                        section.wait_ge(sems[key], v)
                    inst = op.fn(section)
                    if inst is not None:
                        try:
                            self.label_map[inst.ins.name] = (op.eng, op.idx,
                                                             op.label)
                        except Exception:
                            pass
                    if op.needs_inc:
                        assert inst is not None, f'{op.label}: needs inc, no inst'
                        key, v = slot_of[(op.eng, op.idx)]
                        inst.then_inc(sems[key], INC[op.eng])
            return body

        for stream_name, ops_list in streams.items():
            getattr(blk, stream_name)(
                make_body(ops_list, is_gpsimd=(stream_name == 'gpsimd')))
        return blk

    def close(self, exc=(None, None, None)):
        for cm in reversed(self._cms):
            cm.__exit__(*exc)


ARENA_NB = 82688


class Ctx:
    def __init__(self, T, NQ, NKV, n_iters=50, bis_iters=30, detect_races=True):
        self.T = T
        self.F = T // 128
        self.NG = T // 512
        self.NQ, self.NKV = NQ, NKV
        self.n_iters = n_iters
        self.bis_iters = bis_iters
        self.nc = bass.Bass('TRN2', target_bir_lowering=False,
                            detect_race_conditions=detect_races)
        self.p = Prog(self.nc)
        self._sb_bytes = 0
        self._cms = []
        self._wstg_war = [None] * 2
        self._arena = None
        self._aL, self._aR = 0, ARENA_NB
        self._apeak = 0
        self._groups = {}
        self._gstack = {'left': [], 'right': []}

    def sb(self, name, shape, dtype, group=None, side=None):
        if group is None:
            cm = self.nc.sbuf_tensor(name, list(shape), dtype)
            t = cm.__enter__()
            self._cms.append(cm)
            self._sb_bytes += int(np.prod(shape[1:])) * mybir.dt.size(dtype) * 128
            return t
        if self._arena is None:
            cm = self.nc.sbuf_tensor('arena', [128, ARENA_NB], U8)
            self._arena = cm.__enter__()
            self._cms.append(cm)
            self._sb_bytes += ARENA_NB * 128
        nb = int(np.prod(shape[1:])) * mybir.dt.size(dtype)
        nb_al = (nb + 31) & ~31
        sd = side or 'left'
        if group not in self._groups:
            self._groups[group] = (sd, self._aL if sd == 'left' else self._aR)
            self._gstack[sd].append(group)
        assert self._groups[group][0] == sd, f'{group}: side mismatch'
        assert self._gstack[sd][-1] == group, \
            f'{group}: allocations into non-top group'
        if sd == 'left':
            off = self._aL
            self._aL += nb_al
        else:
            self._aR -= nb_al
            off = self._aR
        assert self._aL <= self._aR, f'arena overflow at {name}'
        self._apeak = max(self._apeak, self._aL + (ARENA_NB - self._aR))
        return self._arena[0:shape[0], off:off + nb].bitcast(dtype)

    def free_group(self, group):
        sd, saved = self._groups.pop(group)
        assert self._gstack[sd][-1] == group, f'free {group}: not stack top'
        self._gstack[sd].pop()
        if sd == 'left':
            self._aL = saved
        else:
            self._aR = saved

    def ps(self, name, shape, dtype=F32):
        cm = self.nc.psum_tensor(name, list(shape), dtype)
        t = cm.__enter__()
        self._cms.append(cm)
        return t

    def dram(self, name, shape, dtype, kind):
        return self.nc.dram_tensor(name, list(shape), dtype, kind=kind).ap()

    def close(self):
        self.p.close()
        for cm in reversed(self._cms):
            cm.__exit__(None, None, None)



W_QKV = 0
W_OUTL = 12288
W_QH = 16384
W_KVH = 20480
W_OUTH = 28672
W_GQ, W_GK, W_GV, W_GG, W_GO = 0, 8192, 16384, 24576, 32768


def declare(ctx):
    T, DM = ctx.T, 1024
    d = {}
    d['x'] = ctx.dram('x', [T, DM], F32, 'ExternalInput')
    d['out'] = ctx.dram('out', [T, DM], F32, 'ExternalOutput')
    for nm, shp in [('Wq', [DM, DM]), ('Wk', [DM, DM]), ('Wv', [DM, DM]),
                    ('Wg', [DM, DM]), ('Wo', [DM, DM]), ('Wg1', [DM, 16]),
                    ('Wg2a', [17, DM]), ('g_norm_w', [1, 64]),
                    ('qkv_l', [DM, 1536]), ('w_out_l', [512, DM]),
                    ('null_q', [1, DM]), ('rt_q', [1, DM]), ('rt_kv', [1, DM]),
                    ('rms_g', [1, DM]), ('w_q_h', [DM, 512]),
                    ('w_kv_h', [DM, DM]), ('null_kv', [2, 512]),
                    ('w_out_h', [512, DM]), ('ln_g', [1, DM]), ('ln_b', [1, DM])]:
        d[nm] = ctx.dram(nm, shp, F32, 'ExternalInput')
    d['XLN'] = ctx.dram('XLN', [T, DM], F32, 'Internal')
    d['WSTASH'] = ctx.dram('WSTASH', [128, 32768], BF16, 'Internal')
    d['X2'] = ctx.dram('X2', [T, DM], F32, 'Internal')
    d['X3'] = ctx.dram('X3', [T, DM], F32, 'Internal')
    d['XLN3'] = ctx.dram('XLN3', [T, DM], F32, 'Internal')
    d['LIGHT'] = ctx.dram('LIGHT', [T, DM], F32, 'Internal')
    d['LISTQ'] = ctx.dram('LISTQ', [max(ctx.NQ, 128)], U32, 'Internal')
    d['LISTKV'] = ctx.dram('LISTKV', [max(ctx.NKV, 128)], U32, 'Internal')
    ctx.d = d

    F = ctx.F
    ctx.wreg = ctx.sb('wreg', [128, 40960], BF16)
    ctx.wstg = [ctx.sb(f'wstg{i}', [128, 512], F32) for i in range(2)]
    ctx.ones_col = ctx.sb('ones_col', [128, 1], F32)
    ctx.ones_colb = ctx.sb('ones_colb', [128, 1], BF16)
    ctx.ones_rowb = ctx.sb('ones_rowb', [1, 512], BF16)
    ctx.ones_rowf = ctx.sb('ones_rowf', [1, 128], F32)
    ctx.half_rowf = ctx.sb('half_rowf', [1, 128], F32)
    ctx.consts = ctx.sb('consts', [128, 8], F32)
    ctx.ident_bf = ctx.sb('ident_bf', [128, 128], BF16)
    ctx.ident_f = ctx.sb('ident_f', [128, 128], F32)
    ctx.trimask = ctx.sb('trimask', [128, 128], BF16)
    ctx.rtq_b = ctx.sb('rtq_b', [128, DM], F32)
    ctx.rtkv_b = ctx.sb('rtkv_b', [128, DM], F32)
    ctx.nullh_b = ctx.sb('nullh_b', [128, DM], F32)
    ctx.smallw = ctx.sb('smallw', [128, 40], F32)
    ctx.c0row = ctx.sb('c0row', [1, 1536], BF16)
    ctx.iota_tok = ctx.sb('iota_tok', [128, F], F32)
    ctx.gnw_col = ctx.sb('gnw_col', [128, 1], F32)
    ctx.wg1_bf = ctx.sb('wg1_bf', [128, 128], BF16)
    ctx.wg2a_bf = ctx.sb('wg2a_bf', [17, 1024], BF16)
    ctx.nullk_bf = ctx.sb('nullk_bf', [128, 8], BF16)
    ctx.nullv_bf = ctx.sb('nullv_bf', [1, 520], BF16)
    ctx.s2 = ctx.sb('s2', [128, 2 * F], F32)
    ctx.relu2 = ctx.sb('relu2', [128, 2 * F], F32)
    ctx.e2 = ctx.sb('e2', [128, 2 * F], F32)
    ctx.racc = ctx.sb('racc', [128, 8], F32)
    ctx.rsc = ctx.sb('rsc', [128, 32], F32)
    ctx.aval = ctx.sb('aval', [128, 2], F32)
    ctx.logk = ctx.sb('logk', [128, 2], F32)
    ctx.selscr = ctx.sb('selscr', [128, 4 * F], F32)
    ctx.dest_f = ctx.sb('dest_f', [128, F], F32)
    ctx.dest_u = ctx.sb('dest_u', [128, F], U32)
    ctx.iota_u = ctx.sb('iota_u', [128, F], U32)
    ctx.idxq = ctx.sb('idxq', [128, 16], U32)
    ctx.idxkv = ctx.sb('idxkv', [128, 16], U32)
    ctx.xb = [ctx.sb(f'xb{i}', [128, DM], F32) for i in range(3)]
    ctx.stat = ctx.sb('stat', [128, 16], F32)
    ctx.colv = [ctx.sb(f'colv{i}', [128, 1], F32) for i in range(8)]
    ctx.zeroF = ctx.sb('zeroF', [128, max(F, 32)], F32)
    ctx.zero128 = ctx.sb('zero128', [128, 128], F32)
    ctx.skipc = ctx.sb('skipc', [128, 16], U32)
    ctx.junk = ctx.sb('junk', [128, 128], F32)
    ctx.junkw = ctx.sb('junkw', [128, DM], F32)

    ctx.ps_a = ctx.ps('ps_a', [128, 512])
    ctx.ps_b = ctx.ps('ps_b', [128, 512])
    ctx.ps_c = ctx.ps('ps_c', [128, 512])
    ctx.ps_d = ctx.ps('ps_d', [128, 512])
    ctx.ps_e = ctx.ps('ps_e', [128, 512])
    ctx.ps_f = ctx.ps('ps_f', [128, 512])
    ctx.ps_tr = ctx.ps('ps_tr', [128, 1024], BF16)
    ctx.ps_u = ctx.ps('ps_u', [128, 512])


def prep_consts(ctx, marks):
    p = ctx.p
    F = ctx.F
    p.add('P', lambda s: s.memset(ctx.ones_col[:], 1.0), label='ones')
    m_onesb = p.add('P', lambda s: s.memset(ctx.ones_colb[:], 1.0))
    m_onesr = p.add('P', lambda s: s.memset(ctx.ones_rowb[:], 1.0))
    m_onesrf = p.add('P', lambda s: s.memset(ctx.ones_rowf[:], 1.0))
    m_half = p.add('P', lambda s: s.memset(ctx.half_rowf[:], 0.5))
    p.add('P', lambda s: s.memset(ctx.consts[:, 0:1], 1e-5))
    p.add('P', lambda s: s.memset(ctx.consts[:, 1:2], 1e-12))
    p.add('P', lambda s: s.memset(ctx.consts[:, 2:3], -1.0))
    p.add('P', lambda s: s.memset(ctx.consts[:, 3:4], 0.0625))
    p.add('P', lambda s: s.memset(ctx.consts[:, 4:5], -0.0625))
    p.add('P', lambda s: s.memset(ctx.consts[:, 5:6], 1.0 / 64.0))
    m_iotaf = p.add('P', lambda s: s.iota(ctx.iota_tok[:], [[1, F]], base=0,
                                          channel_multiplier=F,
                                          allow_small_or_imprecise_dtypes=True))
    m_iotau = p.add('P', lambda s: s.iota(ctx.iota_u[:], [[1, F]], base=0,
                                          channel_multiplier=F))
    m_ci = p.add('P', lambda s: s.iota(ctx.junk[:], [[1, 128]], base=0,
                                       channel_multiplier=0,
                                       allow_small_or_imprecise_dtypes=True))
    m_pi = p.add('P', lambda s: s.iota(ctx.colv[0][:], [[0, 1]], base=0,
                                       channel_multiplier=1,
                                       allow_small_or_imprecise_dtypes=True))
    m_id = p.add('V', lambda s: s.tensor_scalar(ctx.ident_bf[:], ctx.junk[:],
                                                ctx.colv[0][:], None,
                                                op0=ALU.is_equal),
                 deps=[m_ci, m_pi], label='ident')
    m_idf = p.add('V', lambda s: s.tensor_scalar(ctx.ident_f[:], ctx.junk[:],
                                                 ctx.colv[0][:], None,
                                                 op0=ALU.is_equal),
                  deps=[m_ci, m_pi], label='identf')
    marks['identf'] = m_idf
    m_tri = p.add('V', lambda s: s.tensor_scalar(ctx.trimask[:], ctx.junk[:],
                                                 ctx.colv[0][:], None,
                                                 op0=ALU.is_ge),
                  deps=[m_ci, m_pi], label='trimask')
    import math
    lkq = float(np.log(np.float32(min(ctx.NQ * 9.0 / 8.0, float(ctx.T)))
                       ).astype(np.float32))
    lkkv = float(np.log(np.float32(min(ctx.NKV * 9.0 / 8.0, float(ctx.T)))
                        ).astype(np.float32))
    p.add('P', lambda s: s.memset(ctx.zeroF[:], 0.0))
    p.add('P', lambda s: s.memset(ctx.zero128[:], 0.0))
    m_skip = p.add('P', lambda s: s.iota(ctx.skipc[:], [[0, 16]],
                                         base=(1 << 22) + 7,
                                         channel_multiplier=0))
    marks['skipc'] = m_skip
    p.add('P', lambda s: s.memset(ctx.logk[:, 0:1], lkq))
    m_lk = p.add('P', lambda s: s.memset(ctx.logk[:, 1:2], lkkv))
    marks['ident'] = m_id
    marks['tri'] = m_tri
    marks['onesb'] = m_onesb
    marks['onesr'] = m_onesr
    marks['onesrf'] = m_onesrf
    marks['half'] = m_half
    marks['iota_u'] = m_iotau
    marks['iota_f'] = m_iotaf
    marks['consts'] = p.barrier('P', [m_lk], label='consts_done')

    d = ctx.d
    rowbuf = ctx.junkw
    war = None
    psu_war = None
    for nm, dst, scale_row, smark in [
            ('rt_q', ctx.rtq_b, ctx.ones_rowf, m_onesrf),
            ('rt_kv', ctx.rtkv_b, ctx.ones_rowf, m_onesrf),
            ('null_q', ctx.nullh_b, ctx.half_rowf, m_half)]:
        m_ld = p.add('SQ', (lambda s, nm=nm: s.dma_start(out=rowbuf[0:1, :],
                                                         in_=d[nm])),
                     deps=[war] if war else [], label=f'ld_{nm}', slot='junkw')
        last = None
        for off in range(0, 1024, 512):
            def mm(s, off=off, scale_row=scale_row):
                return s.matmul(ctx.ps_u[:, 0:512], scale_row[:, 0:128],
                                rowbuf[0:1, off:off + 512], start=True, stop=True)
            m1 = p.add('T', mm, deps=[m_ld, smark, psu_war],
                       label=f'bc_{nm}_{off}')
            def cp(s, off=off, dst=dst):
                return s.tensor_copy(dst[:, off:off + 512], ctx.ps_u[:, 0:512])
            m2 = p.add('V', cp, deps=[m1], label=f'bccp_{nm}_{off}')
            psu_war = m2
            last = m2
        war = last
    marks['rt_b'] = war
    marks['rowbuf_free'] = war
    marks['junkw_free'] = war
    marks['psu_war'] = psu_war



def load_fold_weight(ctx, dram_ap, n_cols, wreg_off, fold, tag,
                     c0_dst=None, kchunks=8, colscale=None, extra_dep=None):
    p = ctx.p
    PIECE = 512
    npieces = (n_cols + PIECE - 1) // PIECE
    war = ctx._wstg_war
    if extra_dep is not None:
        m_x = p.barrier('V', [war[0], war[1], extra_dep],
                        label=f'w{tag}_xdep')
        war[0] = m_x
        war[1] = m_x
    cast_last = None
    c0_last = None
    si = 0
    for c in range(kchunks):
        for j in range(npieces):
            w = min(PIECE, n_cols - j * PIECE)
            sbuf = ctx.wstg[si % 2]
            def ld(s, c=c, j=j, w=w, sbuf=sbuf):
                return s.dma_start(out=sbuf[:, 0:w],
                                   in_=dram_ap[c * 128:(c + 1) * 128,
                                               j * PIECE:j * PIECE + w])
            m_ld = p.add('SQ', ld, deps=[war[si % 2]], label=f'w{tag}_ld{c}_{j}',
                         slot=f'wstg{si % 2}')
            consumers = []
            if c0_dst is not None:
                psum_t, po, lnb_tile, m_lnb = c0_dst[j]
                def mmc0(s, c=c, w=w, sbuf=sbuf, psum_t=psum_t, po=po,
                         lnb_tile=lnb_tile):
                    return s.matmul(psum_t[0:1, po:po + w], lnb_tile[:, c:c + 1],
                                    sbuf[:, 0:w], start=(c == 0),
                                    stop=(c == kchunks - 1))
                c0_last = p.add('T', mmc0, deps=[m_ld, m_lnb],
                                label=f'w{tag}_c0_{c}_{j}')
                consumers.append(c0_last)
            dst = ctx.wreg[:, wreg_off + c * n_cols + j * PIECE:
                           wreg_off + c * n_cols + j * PIECE + w]
            if fold is not None:
                ft, ft_mark = fold
                cs = colscale(j) if colscale is not None else 1.0
                def cast(s, c=c, w=w, sbuf=sbuf, dst=dst, ft=ft, cs=cs):
                    return s.tensor_scalar(dst, sbuf[:, 0:w], ft[:, c:c + 1],
                                           cs, op0=ALU.mult, op1=ALU.mult)
                m_cast = p.add('V', cast, deps=[m_ld, ft_mark],
                               label=f'w{tag}_cast{c}_{j}')
            else:
                def cast(s, w=w, sbuf=sbuf, dst=dst):
                    return s.tensor_copy(dst, sbuf[:, 0:w])
                m_cast = p.add('V', cast, deps=[m_ld], label=f'w{tag}_cast{c}_{j}')
            consumers.append(m_cast)
            war[si % 2] = (p.barrier('V', consumers) if len(consumers) > 1
                           else m_cast)
            cast_last = m_cast
            si += 1
    return cast_last, c0_last


def load_vec_tile(ctx, dram_row, dst_cols, n, tag, dep=None, marks=None,
                  nrows=128):
    p = ctx.p
    nch = n // nrows
    m_ld = p.add('SQ', (lambda s: s.dma_start(out=ctx.junkw[0:1, 0:n],
                                              in_=dram_row)),
                 deps=[dep, marks.get('junkw_free') if marks else None],
                 label=f'vec_{tag}_ld', slot='junkw')
    tr_ms = []
    for c in range(nch):
        def tr(s, c=c):
            return s.transpose(ctx.ps_u[0:nrows, c:c + 1],
                               ctx.junkw[0:1, c * nrows:(c + 1) * nrows],
                               ctx.ident_f[0:1, 0:1])
        tr_ms.append(p.add('T', tr,
                           deps=[m_ld, marks['identf'] if marks else None,
                                 marks.get('_psu_war') if marks else None],
                           label=f'vec_{tag}_t{c}'))
    def ev(s):
        return s.tensor_copy(dst_cols[0:nrows, :], ctx.ps_u[0:nrows, 0:nch])
    m_ev = p.add('V', ev, deps=tr_ms, label=f'vec_{tag}_ev')
    if marks is not None:
        marks['_psu_war'] = m_ev
        marks['junkw_free'] = m_ev
    return m_ev


def prep_weights_ffn(ctx, marks, tag, dep=None, reload=False):
    p, d = ctx.p, ctx.d
    if dep is not None:
        ctx._wstg_war = [dep] * 2
    if not reload:
        m_lng = load_vec_tile(ctx, d['ln_g'], ctx.smallw[:, 0:8], 1024,
                              f'lng{tag}', dep=dep, marks=marks)
        m_lnb = load_vec_tile(ctx, d['ln_b'], ctx.smallw[:, 8:16], 1024,
                              f'lnb{tag}', marks=marks)
        m_rms = load_vec_tile(ctx, d['rms_g'], ctx.smallw[:, 16:24], 1024,
                              f'rms{tag}', marks=marks)
        lng = (ctx.smallw[:, 0:8], m_lng)
        rmsg = (ctx.smallw[:, 16:24], m_rms)
        c0_dst = [(ctx.ps_a, 0, ctx.smallw[:, 8:16], m_lnb),
                  (ctx.ps_b, 0, ctx.smallw[:, 8:16], m_lnb),
                  (ctx.ps_c, 0, ctx.smallw[:, 8:16], m_lnb)]
        m1, m1c0 = load_fold_weight(ctx, d['qkv_l'], 1536, W_QKV, lng,
                                    f'qkv{tag}', c0_dst=c0_dst,
                                    colscale=lambda j: 0.125 if j == 0 else 1.0)
        c0e = None
        for j in range(3):
            pt, po, _, _ = c0_dst[j]
            sc = 0.125 if j == 0 else 1.0
            def ev(s, pt=pt, po=po, j=j, sc=sc):
                return s.tensor_scalar(ctx.c0row[:, j * 512:(j + 1) * 512],
                                       pt[0:1, po:po + 512], sc, None,
                                       op0=ALU.mult)
            c0e = p.add('V', ev, deps=[m1c0], label=f'c0ev{tag}{j}')
        xq = xo = xv = xkv = None
    else:
        ws = d['WSTASH']
        ms = []
        for nm, a, bnd, xdep in [
                ('qo', 0, 16384, marks['gla_qkfree']),
                ('qh', 16384, 20480, marks['gla_vfree']),
                ('kvo', 20480, 32768, marks['gla_done'])]:
            m_l = p.add('SQ', (lambda s, a=a, bnd=bnd: s.dma_start(
                out=ctx.wreg[:, a:bnd], in_=ws[:, a:bnd])),
                deps=[marks['wstash'], xdep], label=f'wrld_{nm}',
                slot=f'wrld{nm}')
            ms.append(m_l)
        mb = p.barrier('V', ms, label=f'wffn_done_{tag}')
        marks[f'wffn_{tag}'] = mb
        return mb
    m2, _ = load_fold_weight(ctx, d['w_out_l'], 1024, W_OUTL, None,
                             f'wol{tag}', kchunks=4, extra_dep=xo)
    m3, _ = load_fold_weight(ctx, d['w_q_h'], 512, W_QH, rmsg, f'wqh{tag}',
                             extra_dep=xv)
    m4, _ = load_fold_weight(ctx, d['w_kv_h'], 1024, W_KVH, rmsg, f'wkvh{tag}',
                             extra_dep=xkv)
    m5, _ = load_fold_weight(ctx, d['w_out_h'], 1024, W_OUTH, None,
                             f'woh{tag}', kchunks=4)
    mb = p.barrier('V', [m1, c0e, m2, m3, m4, m5], label=f'wffn_done_{tag}')
    marks[f'wffn_{tag}'] = mb
    marks['wstash'] = p.add('AQ', lambda s: s.dma_start(
        out=d['WSTASH'], in_=ctx.wreg[:, 0:32768]), deps=[mb],
        label='wstash', slot='wstash')
    return mb


def prep_weights_gla(ctx, marks, dep):
    p, d = ctx.p, ctx.d
    ms = []
    m_ws = marks.get('wstash')
    for nm, off, xdep in [('Wq', W_GQ, marks.get('x2_done_1')),
                          ('Wk', W_GK, marks.get('x2_done_1')),
                          ('Wv', W_GV, marks.get('heavy_1')),
                          ('Wg', W_GG, marks.get('heavy_1'))]:
        xd = p.barrier('V', [xdep, m_ws], label=f'g{nm}_ws')
        m, _ = load_fold_weight(ctx, d[nm], 1024, off, None, f'g{nm}',
                                extra_dep=xd)
        ms.append(m)
    m_gt = load_vec_tile(ctx, d['g_norm_w'], ctx.gnw_col[:, 0:1], 64, 'gnw',
                         dep=marks.get('heavy_1'), marks=marks, nrows=64)
    p.add('V', lambda s: s.stream_shuffle(ctx.gnw_col[64:96, 0:1],
                                          ctx.gnw_col[0:32, 0:1],
                                          list(range(32))), deps=[m_gt])
    p.add('V', lambda s: s.stream_shuffle(ctx.gnw_col[96:128, 0:1],
                                          ctx.gnw_col[32:64, 0:1],
                                          list(range(32))))
    m_rep = p.add('V', lambda s: s.tensor_copy(
        ctx.smallw[:, 24:32], ctx.gnw_col[:, 0:1].to_broadcast([128, 8])),
        label='gnwrep')
    m, _ = load_fold_weight(ctx, d['Wo'], 1024, W_GO,
                            (ctx.smallw[:, 24:32], m_rep), 'gWo')
    ms.append(m)
    for c in range(8):
        def ld(s, c=c):
            return s.dma_start(out=ctx.wstg[c % 2][:, 0:16],
                               in_=d['Wg1'][c * 128:(c + 1) * 128, :])
        m_ld = p.add('SQ', ld, deps=[ctx._wstg_war[c % 2]], label=f'wg1ld{c}',
                     slot=f'wstg{c % 2}')
        def cast(s, c=c):
            return s.tensor_copy(ctx.wg1_bf[:, 16 * c:16 * c + 16],
                                 ctx.wstg[c % 2][:, 0:16])
        m_c = p.add('V', cast, deps=[m_ld], label=f'wg1c{c}')
        ctx._wstg_war[c % 2] = m_c
        ms.append(m_c)
    m_ld = p.add('SQ', lambda s: s.dma_start(out=ctx.junkw[0:17, :],
                                             in_=d['Wg2a']),
                 deps=[marks.get('junkw_free')], label='wg2ld', slot='junkw')
    m_w2 = p.add('V', lambda s: s.tensor_copy(ctx.wg2a_bf[:], ctx.junkw[0:17, :]),
                 deps=[m_ld], label='wg2c')
    ms.append(m_w2)
    marks['junkw_free'] = m_w2
    mb = p.barrier('V', ms, label='wgla_done')
    marks['wgla'] = mb
    return mb


def prep_null_kv(ctx, marks, dep=None):
    p, d = ctx.p, ctx.d
    m_ld = p.add('SQ', lambda s: s.dma_start(out=ctx.junkw[0:1, 0:512],
                                             in_=d['null_kv'][0:1, :]),
                 deps=[dep, marks.get('junkw_free')], label='nkv_ld', slot='junkw')
    tr_ms = []
    for h in range(8):
        def tr(s, h=h):
            return s.transpose(ctx.ps_u[0:64, h:h + 1],
                               ctx.junkw[0:1, h * 64:h * 64 + 64],
                               ctx.ident_f[0:1, 0:1])
        tr_ms.append(p.add('T', tr, deps=[m_ld, marks['identf'],
                                          marks.get('_psu_war')],
                           label=f'nkv_t{h}'))
    m_ev = p.add('V', lambda s: s.tensor_copy(ctx.nullk_bf[0:64, :],
                                              ctx.ps_u[0:64, 0:8]),
                 deps=tr_ms, label='nkv_ev')
    marks['_psu_war'] = m_ev
    p.add('V', lambda s: s.stream_shuffle(ctx.nullk_bf[64:96, :],
                                          ctx.nullk_bf[0:32, :],
                                          list(range(32))))
    p.add('V', lambda s: s.stream_shuffle(ctx.nullk_bf[96:128, :],
                                          ctx.nullk_bf[32:64, :],
                                          list(range(32))))
    m1c = p.add('V', lambda s: s.nop(), label='nkv_kdone')
    m2 = p.add('SQ', lambda s: s.dma_start(out=ctx.junkw[0:1, 512:1024],
                                           in_=d['null_kv'][1:2, :]),
                deps=[m_ld], label='nkv_v', slot='junkw')
    vview = ctx.nullv_bf[0:1, :].rearrange('o (h e) -> o h e', e=65)
    m_vc = p.add('V', (lambda s: s.tensor_copy(
        vview[:, :, 0:64],
        ctx.junkw[0:1, 512:1024].rearrange('o (h e) -> o h e', e=64))),
        deps=[m2], label='nkv_vc')
    m2c = p.add('V', lambda s: s.memset(vview[:, :, 64:65], 1.0),
                label='nkv_ones')
    marks['junkw_free'] = m2c
    marks['nullkv'] = p.barrier('V', [m1c, m2c], label='nullkv_done')
    return marks['nullkv']



def phase_ln_scores(ctx, marks, src_ap, xln_ap, tag, dep=None):
    p = ctx.p
    F = ctx.F
    nbuf = len(ctx.xb)
    war = [marks.get(f'xb{i}_war') for i in range(nbuf)]
    last_marks = []
    src = src_ap if src_ap is not None else xln_ap
    src_r = src.rearrange('(q f) d -> f q d', f=F)
    xln_r = xln_ap.rearrange('(q f) d -> f q d', f=F) if src_ap is not None else None
    junk_dep = marks.get('junkw_free')
    for f in range(F):
        xb = ctx.xb[f % nbuf]
        m_ld = p.add('SQ', (lambda s, f=f, xb=xb: s.dma_start(out=xb[:],
                                                              in_=src_r[f])),
                     deps=[war[f % nbuf], dep], label=f'{tag}_ld{f}',
                     slot=f'xb{f % nbuf}')
        if src_ap is not None:
            p.add('V', (lambda s, xb=xb: s.bn_stats(ctx.stat[:, 0:6],
                                                    xb[:, 0:512])),
                  deps=[m_ld], label=f'{tag}_bn1_{f}')
            p.add('V', (lambda s, xb=xb: s.bn_stats(ctx.stat[:, 6:12],
                                                    xb[:, 512:1024])),
                  deps=[m_ld], label=f'{tag}_bn2_{f}')
            m_ag = p.add('V', lambda s: s.bn_aggr(ctx.stat[:, 12:14],
                                                  ctx.stat[:, 0:12]),
                         label=f'{tag}_ag{f}')
            m_sq = p.add('S', lambda s: s.activation(ctx.colv[1][:],
                                                     ctx.stat[:, 13:14], AF.Sqrt,
                                                     bias=ctx.consts[:, 0:1]),
                         deps=[m_ag, marks['consts']], label=f'{tag}_sq{f}')
            m_rc = p.add('V', lambda s: s.reciprocal(ctx.colv[2][:],
                                                     ctx.colv[1][:]),
                         deps=[m_sq], label=f'{tag}_rc{f}')
            m_nb = p.add('V', lambda s: s.scalar_tensor_tensor(
                ctx.colv[3][:], ctx.stat[:, 12:13], -1.0, ctx.colv[2][:],
                op0=ALU.mult, op1=ALU.mult), label=f'{tag}_nb{f}')
            m_ln = p.add('S', (lambda s, xb=xb: s.activation(
                xb[:], xb[:], AF.Identity, bias=ctx.colv[3][:],
                scale=ctx.colv[2][:])), deps=[m_rc, m_nb], label=f'{tag}_ln{f}')
            m_st = p.add('AQ', (lambda s, f=f, xb=xb: s.dma_start(
                out=xln_r[f], in_=xb[:])), deps=[m_ln], label=f'{tag}_st{f}',
                         slot=f'xlnst{f % nbuf}')
            marks[f'xlnt_{tag}_{f}'] = m_st
        else:
            m_ln = m_ld
            m_st = None
        m_sc1 = p.add('V', (lambda s, f=f, xb=xb: s.scalar_tensor_tensor(
            ctx.junkw[:], xb[:], 1.0, ctx.rtq_b[:], op0=ALU.mult, op1=ALU.mult,
            accum_out=ctx.s2[:, f:f + 1])),
            deps=[m_ln, marks['rt_b'], junk_dep], label=f'{tag}_scq{f}')
        junk_dep = None
        m_sc2 = p.add('V', (lambda s, f=f, xb=xb: s.scalar_tensor_tensor(
            ctx.junkw[:], xb[:], 1.0, ctx.rtkv_b[:], op0=ALU.mult, op1=ALU.mult,
            accum_out=ctx.s2[:, F + f:F + f + 1])),
            deps=[m_ln], label=f'{tag}_sckv{f}')
        frees = [m_sc2] + ([m_st] if m_st else [])
        war[f % nbuf] = p.barrier('V', frees, label=f'{tag}_war{f}')
        last_marks.append(war[f % nbuf])
    for i in range(nbuf):
        marks[f'xb{i}_war'] = war[i]
    marks['junkw_free'] = last_marks[-1]
    marks[f'scores_{tag}'] = last_marks[-1]
    marks[f'xln_done_{tag}'] = p.barrier('V', war, label=f'{tag}_alldone')
    return marks[f'scores_{tag}']



BIG_DEST = float(1 << 22)


def xpart_sum_rep(ctx, src_cols, n, dst_cols, dep=None, tag=''):
    p = ctx.p
    rsc = ctx.rsc
    p.add('V', lambda s: s.tensor_copy(rsc[:, 0:n], src_cols),
          deps=[dep], label=f'xps_cp_{tag}')
    if n < 32:
        p.add('V', lambda s: s.memset(rsc[:, n:32], 0.0))
    p.add('V', lambda s: s.transpose(rsc[:, 0:32], rsc[:, 0:32]),
          label=f'xps_t_{tag}')
    p.add('V', lambda s: s.tensor_reduce(rsc[:, 0:1], rsc[:, 0:32],
                                         axis=AX.X, op=ALU.add),
          label=f'xps_r_{tag}')
    p.add('V', lambda s: s.stream_shuffle(rsc[0:32, 1:2], rsc[32:64, 0:1],
                                          list(range(32))))
    p.add('V', lambda s: s.stream_shuffle(rsc[0:32, 2:3], rsc[64:96, 0:1],
                                          list(range(32))))
    p.add('V', lambda s: s.stream_shuffle(rsc[0:32, 3:4], rsc[96:128, 0:1],
                                          list(range(32))))
    p.add('V', lambda s: s.tensor_tensor(rsc[0:32, 0:1], rsc[0:32, 0:1],
                                         rsc[0:32, 1:2], op=ALU.add))
    p.add('V', lambda s: s.tensor_tensor(rsc[0:32, 2:3], rsc[0:32, 2:3],
                                         rsc[0:32, 3:4], op=ALU.add))
    p.add('V', lambda s: s.tensor_tensor(rsc[0:32, 0:1], rsc[0:32, 0:1],
                                         rsc[0:32, 2:3], op=ALU.add))
    p.add('V', lambda s: s.transpose(rsc[:, 0:32], rsc[:, 0:32]),
          label=f'xps_t2_{tag}')
    p.add('V', lambda s: s.stream_shuffle(rsc[32:64, 0:n], rsc[0:32, 0:n],
                                          [0] * 32))
    p.add('V', lambda s: s.stream_shuffle(rsc[64:96, 0:n], rsc[0:32, 0:n],
                                          [0] * 32))
    p.add('V', lambda s: s.stream_shuffle(rsc[96:128, 0:n], rsc[0:32, 0:n],
                                          [0] * 32))
    p.add('V', lambda s: s.stream_shuffle(rsc[:, 0:n], rsc[:, 0:n], [0] * 32),
          label=f'xps_b_{tag}')
    m3 = p.add('V', lambda s: s.tensor_copy(dst_cols, rsc[:, 0:n]),
               label=f'xps_o_{tag}')
    return m3


def xpart_prefix(ctx, mask_ap, pref_ap, rowcnt_col, tot_col, tag):
    p = ctx.p
    rsc = ctx.rsc
    p.add('V', lambda s: s.tensor_tensor_scan(pref_ap, mask_ap,
                                              ctx.zeroF[:, 0:pref_ap.shape[1]], 0.0,
                                              op0=ALU.add, op1=ALU.add),
          label=f'xpf_scan_{tag}')
    p.add('V', lambda s: s.tensor_copy(rsc[:, 0:1], rowcnt_col),
          label=f'xpf_cp_{tag}')
    p.add('V', lambda s: s.memset(rsc[:, 1:32], 0.0))
    p.add('V', lambda s: s.transpose(rsc[:, 0:32], rsc[:, 0:32]))
    p.add('V', lambda s: s.tensor_tensor_scan(rsc[:, 0:32], rsc[:, 0:32],
                                              ctx.zeroF[:, 0:32], 0.0,
                                              op0=ALU.add, op1=ALU.add))
    p.add('V', lambda s: s.transpose(rsc[:, 0:32], rsc[:, 0:32]))
    p.add('V', lambda s: s.stream_shuffle(rsc[:, 1:2], rsc[:, 0:1], [31] * 32),
          label=f'xpf_qt_{tag}')
    p.add('V', lambda s: s.memset(rsc[0:32, 2:3], 0.0))
    p.add('V', lambda s: s.stream_shuffle(rsc[32:64, 2:3], rsc[0:32, 1:2],
                                          [0] * 32))
    p.add('V', lambda s: s.tensor_tensor(rsc[32:64, 3:4], rsc[32:64, 2:3],
                                         rsc[32:64, 1:2], op=ALU.add))
    p.add('V', lambda s: s.stream_shuffle(rsc[64:96, 2:3], rsc[32:64, 3:4],
                                          [0] * 32))
    p.add('V', lambda s: s.tensor_tensor(rsc[64:96, 3:4], rsc[64:96, 2:3],
                                         rsc[64:96, 1:2], op=ALU.add))
    p.add('V', lambda s: s.stream_shuffle(rsc[96:128, 2:3], rsc[64:96, 3:4],
                                          [0] * 32))
    p.add('V', lambda s: s.tensor_tensor(rsc[96:128, 3:4], rsc[96:128, 2:3],
                                         rsc[96:128, 1:2], op=ALU.add))
    p.add('V', lambda s: s.stream_shuffle(rsc[0:32, 4:5], rsc[96:128, 3:4],
                                          [0] * 32))
    p.add('V', lambda s: s.stream_shuffle(rsc[32:64, 4:5], rsc[96:128, 3:4],
                                          [0] * 32))
    p.add('V', lambda s: s.stream_shuffle(rsc[64:96, 4:5], rsc[96:128, 3:4],
                                          [0] * 32))
    p.add('V', lambda s: s.stream_shuffle(rsc[96:128, 4:5], rsc[96:128, 3:4],
                                          [0] * 32))
    p.add('V', lambda s: s.tensor_copy(tot_col, rsc[:, 4:5]),
          label=f'xpf_tot_{tag}')
    p.add('V', lambda s: s.tensor_tensor(rsc[:, 5:6], rsc[:, 0:1], rowcnt_col,
                                         op=ALU.subtract))
    p.add('V', lambda s: s.tensor_tensor(rsc[:, 5:6], rsc[:, 5:6], rsc[:, 2:3],
                                         op=ALU.add))
    m = p.add('V', lambda s: s.tensor_scalar(pref_ap, pref_ap, rsc[:, 5:6],
                                             None, op0=ALU.add),
              label=f'xpf_done_{tag}')
    return m


def routing(ctx, marks, tag, score_dep, extra_dep=None):
    p, d = ctx.p, ctx.d
    F = ctx.F
    s2 = ctx.s2
    m0 = p.add('V', lambda s: s.tensor_copy(ctx.relu2[:], s2[:]),
               deps=[score_dep], label=f'r{tag}_init')
    for it in range(ctx.n_iters):
        p.add('V', lambda s: s.tensor_tensor(ctx.e2[:], s2[:], ctx.relu2[:],
                                             op=ALU.subtract),
              label=f'r{tag}_sb{it}')
        mv = p.barrier('V', [], label=f'r{tag}_sbb{it}')
        p.add('S', lambda s: s.activation(ctx.e2[:, 0:F], ctx.e2[:, 0:F],
                                          AF.Exp, accum_out=ctx.racc[:, 0:1]),
              deps=[mv], label=f'r{tag}_e1_{it}')
        m_e2 = p.add('S', lambda s: s.activation(ctx.e2[:, F:2 * F],
                                                 ctx.e2[:, F:2 * F], AF.Exp,
                                                 accum_out=ctx.racc[:, 1:2]),
                     label=f'r{tag}_e2_{it}')
        m_sum = xpart_sum_rep(ctx, ctx.racc[:, 0:2], 2, ctx.racc[:, 2:4],
                              dep=m_e2, tag=f'{tag}i{it}')
        m_ln = p.add('S', lambda s: s.activation(ctx.racc[:, 4:6],
                                                 ctx.racc[:, 2:4], AF.Ln),
                     deps=[m_sum], label=f'r{tag}_ln{it}')
        m_a = p.add('V', lambda s: s.tensor_tensor(ctx.aval[:], ctx.logk[:],
                                                   ctx.racc[:, 4:6],
                                                   op=ALU.subtract),
                    deps=[m_ln, marks['consts']], label=f'r{tag}_a{it}')
        p.add('S', lambda s: s.activation(ctx.relu2[:, 0:F], s2[:, 0:F],
                                          AF.Relu, bias=ctx.aval[:, 0:1]),
              deps=[m_a], label=f'r{tag}_r1_{it}')
        m_r2 = p.add('S', lambda s: s.activation(ctx.relu2[:, F:2 * F],
                                                 s2[:, F:2 * F], AF.Relu,
                                                 bias=ctx.aval[:, 1:2]),
                     label=f'r{tag}_r2_{it}')
        p.barrier('V', [m_r2], label=f'r{tag}_endit{it}')
    m_coor = p.barrier('V', [], label=f'r{tag}_coor_done')
    marks[f'aval_{tag}'] = m_coor

    res = {}
    for r, (kk, list_ap, idx_sb) in enumerate(
            [(ctx.NQ, d['LISTQ'], ctx.idxq), (ctx.NKV, d['LISTKV'], ctx.idxkv)]):
        s_ap = s2[:, r * F:(r + 1) * F]
        A = ctx.selscr[:, 0:F]
        B = ctx.selscr[:, F:2 * F]
        C = ctx.selscr[:, 2 * F:3 * F]
        D = ctx.selscr[:, 3 * F:4 * F]
        cv = ctx.colv
        m_pref_list = p.add('GQ', (lambda s, list_ap=list_ap, kk=kk:
                            s.dma_start(out=list_ap.rearrange('(q i) -> q i',
                                                              i=kk // 128),
                                        in_=ctx.skipc[:, 0:kk // 128])),
                            deps=[marks['skipc'], extra_dep,
                                  marks.get(f'list_free_{tag}_{r}')],
                            label=f'sel{tag}{r}_prefill', slot=f'list{r}')
        p.add('V', (lambda s, r=r: s.tensor_scalar(
            cv[4][:], ctx.aval[:, r:r + 1], -1.0, None, op0=ALU.mult)),
            deps=[m_coor], label=f'sel{tag}{r}_na')
        p.add('V', (lambda s, s_ap=s_ap, D=D: s.tensor_scalar(
            D, s_ap, cv[4][:], 0.0, op0=ALU.is_ge, op1=ALU.add,
            accum_out=ctx.racc[:, 6:7])), label=f'sel{tag}{r}_mask')
        p.add('V', (lambda s, C=C, D=D: s.tensor_scalar(
            C, D, float(1e30), None, op0=ALU.mult)), label=f'sel{tag}{r}_nt1')
        p.add('V', (lambda s, s_ap=s_ap, C=C: s.tensor_tensor(
            C, s_ap, C, op=ALU.subtract)), label=f'sel{tag}{r}_nt2')
        xpart_prefix(ctx, D, B, ctx.racc[:, 6:7], cv[5][:], tag=f'{tag}m{r}')
        p.add('V', (lambda s, A=A, B=B, D=D, kk=kk: s.scalar_tensor_tensor(
            A, B, float(kk), D, op0=ALU.is_le, op1=ALU.mult)),
            label=f'sel{tag}{r}_tied')
        p.add('V', (lambda s, kk=kk: s.tensor_scalar(
            cv[6][:], cv[5][:], -1.0, float(kk), op0=ALU.mult, op1=ALU.add)),
            label=f'sel{tag}{r}_f1')
        p.add('V', lambda s: s.tensor_scalar(cv[6][:], cv[6][:], 0.0, None,
                                             op0=ALU.max),
              label=f'sel{tag}{r}_f2')
        p.add('V', lambda s: s.tensor_scalar(cv[7][:], cv[4][:], -64.0, None,
                                             op0=ALU.add))
        p.add('V', lambda s: s.tensor_copy(ctx.racc[:, 7:8], cv[4][:]))
        for it in range(ctx.bis_iters):
            p.add('V', lambda s: s.tensor_tensor(ctx.rsc[:, 8:9], cv[7][:],
                                                 ctx.racc[:, 7:8], op=ALU.add))
            p.add('V', lambda s: s.tensor_scalar(ctx.rsc[:, 8:9],
                                                 ctx.rsc[:, 8:9], 0.5, None,
                                                 op0=ALU.mult))
            p.add('V', lambda s: s.tensor_copy(ctx.rsc[:, 10:11],
                                               ctx.rsc[:, 8:9]))
            p.add('V', (lambda s, C=C, D=D: s.tensor_scalar(
                D, C, ctx.rsc[:, 10:11], 0.0, op0=ALU.is_ge, op1=ALU.add,
                accum_out=ctx.racc[:, 0:1])), label=f'b{tag}{r}_{it}')
            xpart_sum_rep(ctx, ctx.racc[:, 0:1], 1, ctx.racc[:, 1:2],
                          tag=f'b{tag}{r}{it}')
            p.add('V', lambda s: s.tensor_tensor(ctx.rsc[:, 11:12],
                                                 ctx.racc[:, 1:2], cv[6][:],
                                                 op=ALU.is_ge))
            p.add('V', lambda s: s.tensor_tensor(ctx.rsc[:, 12:13],
                                                 ctx.rsc[:, 10:11], cv[7][:],
                                                 op=ALU.subtract))
            p.add('V', lambda s: s.tensor_tensor(ctx.rsc[:, 12:13],
                                                 ctx.rsc[:, 12:13],
                                                 ctx.rsc[:, 11:12],
                                                 op=ALU.mult))
            p.add('V', lambda s: s.tensor_tensor(cv[7][:], cv[7][:],
                                                 ctx.rsc[:, 12:13],
                                                 op=ALU.add))
            p.add('V', lambda s: s.tensor_tensor(ctx.rsc[:, 12:13],
                                                 ctx.racc[:, 7:8],
                                                 ctx.rsc[:, 10:11],
                                                 op=ALU.subtract))
            p.add('V', lambda s: s.tensor_tensor(ctx.rsc[:, 12:13],
                                                 ctx.rsc[:, 12:13],
                                                 ctx.rsc[:, 11:12],
                                                 op=ALU.mult))
            p.add('V', lambda s: s.tensor_tensor(ctx.racc[:, 7:8],
                                                 ctx.rsc[:, 10:11],
                                                 ctx.rsc[:, 12:13],
                                                 op=ALU.add))
        p.add('V', (lambda s, C=C, D=D: s.tensor_scalar(
            D, C, cv[7][:], 0.0, op0=ALU.is_ge, op1=ALU.add,
            accum_out=ctx.racc[:, 0:1])), label=f'sel{tag}{r}_fillm')
        xpart_prefix(ctx, D, B, ctx.racc[:, 0:1], ctx.racc[:, 1:2],
                     tag=f'{tag}f{r}')
        p.add('V', (lambda s, B=B: s.tensor_scalar(
            B, B, cv[6][:], None, op0=ALU.is_le)), label=f'sel{tag}{r}_fle')
        p.add('V', (lambda s, B=B, D=D: s.tensor_tensor(
            D, D, B, op=ALU.mult)), label=f'sel{tag}{r}_fsel')
        p.add('V', (lambda s, A=A, D=D: s.tensor_tensor(A, A, D, op=ALU.add)),
              label=f'sel{tag}{r}_sel')
        p.add('V', (lambda s, A=A, D=D: s.tensor_scalar(
            D, A, 0.0, 0.0, op0=ALU.add, op1=ALU.add,
            accum_out=ctx.racc[:, 2:3])), label=f'sel{tag}{r}_selcnt')
        xpart_prefix(ctx, A, B, ctx.racc[:, 2:3], ctx.racc[:, 3:4],
                     tag=f'{tag}s{r}')
        p.add('V', (lambda s, B=B: s.tensor_scalar(
            B, B, float(-1.0 - BIG_DEST), None, op0=ALU.add)),
            label=f'sel{tag}{r}_d1')
        p.add('V', (lambda s, A=A, B=B: s.tensor_tensor(B, B, A, op=ALU.mult)),
              label=f'sel{tag}{r}_d2')
        p.add('V', (lambda s, B=B: s.tensor_scalar(
            ctx.dest_f[:], B, BIG_DEST, None, op0=ALU.add)),
            label=f'sel{tag}{r}_d3')
        m_du = p.add('V', lambda s: s.tensor_copy(ctx.dest_u[:], ctx.dest_f[:]),
                     deps=[getattr(ctx, '_dest_war', None)],
                     label=f'sel{tag}{r}_du')
        sc_marks = []
        for f in range(F):
            ctx.p.bounds_vals.add(kk - 1)
            def sca(s, f=f, list_ap=list_ap, kk=kk):
                return s.indirect_dma_start(
                    out=list_ap.unsqueeze(1),
                    out_offset=IndirectOffsetOnAxis(ctx.dest_u[:, f:f + 1], 0),
                    in_=ctx.iota_u[:, f:f + 1], in_offset=None,
                    bounds_check=ctx.p.bc_reg(kk - 1), oob_is_err=False)
            sc_marks.append(p.add('GQ', sca,
                                  deps=[m_du, m_pref_list, marks['iota_u']],
                                  label=f'sel{tag}{r}_sc{f}', slot=f'list{r}'))
        def rd(s, list_ap=list_ap, idx_sb=idx_sb, kk=kk):
            return s.dma_start(out=idx_sb[:, 0:kk // 128],
                               in_=list_ap.rearrange('(q i) -> q i',
                                                     i=kk // 128))
        m_idx = p.add('SQ', rd, deps=sc_marks + [extra_dep],
                      label=f'sel{tag}{r}_rd', slot=f'idx{r}')
        ctx._dest_war = p.barrier('P', sc_marks, label=f'sel{tag}{r}_scw')
        res[('idx', r)] = m_idx
        res[('sel_done', r)] = m_du
    marks[f'route_{tag}'] = res
    return res



def declare_light(ctx, tag):
    GA, GB = f'lightA{tag}', f'lightB{tag}'
    ctx.xn2T = [ctx.sb(f'xn2T{tag}_{i}', [128, 8 * 576], BF16, GA) for i in range(2)]
    ctx.kTl = [ctx.sb(f'kTl{tag}_{i}', [128, 4 * 640], BF16, GA) for i in range(3)]
    ctx.qTl = [ctx.sb(f'qTl{tag}_{i}', [128, 4 * 512], BF16, GA) for i in range(2)]
    ctx.vS = [ctx.sb(f'vS{tag}_{i}', [128, 520], BF16, GA) for i in range(12)]
    ctx.esim = [ctx.sb(f'esim{tag}_{i}', [128, 16 * 128], BF16, GA) for i in range(3)]
    ctx.onorm = ctx.sb(f'onorm{tag}', [128, max(512, ctx.NQ)], BF16, GB,
                       side='right')
    ctx.onormT = ctx.sb(f'onormT{tag}', [128, 512], BF16, GB, side='right')
    ctx.lstage = [ctx.sb(f'lstage{tag}_{i}', [128, 1024], F32, GB, side='right') for i in range(2)]
    ctx.tcast = [ctx.sb(f'tcast{tag}_{i}', [128, 1024], BF16, GB, side='right') for i in range(2)]
    ctx.lrec = ctx.sb(f'lrec{tag}', [128, 16], F32, GB, side='right')


def declare_heavy(ctx, tag):
    G = f'heavy{tag}'
    ctx.hT = [ctx.sb(f'hT{tag}_0', [128, 8 * 512], BF16, G)] * 2
    ctx.kTh = ctx.sb(f'kTh{tag}', [128, 4 * 1024], BF16, G)
    ctx.qTh = ctx.sb(f'qTh{tag}', [128, 4 * 1024], BF16, G)
    ctx.vh = [ctx.sb(f'vh{tag}_{i}', [128, 520], BF16, G) for i in range(8)]
    ctx.esimh = ctx.sb(f'esimh{tag}', [128, 1024], BF16, G)
    ctx.oacc = ctx.sb(f'oacc{tag}', [128, 8 * 520], F32, G)
    ctx.hvout = ctx.sb(f'hvout{tag}', [128, 1024], F32, G)


def emit_ln2_tile(ctx, marks, xb, tcast, tag, dep=None):
    p = ctx.p
    p.add('V', lambda s: s.bn_stats(ctx.stat[:, 0:6], xb[:, 0:512]),
          deps=[dep], label=f'{tag}_bn1')
    p.add('V', lambda s: s.bn_stats(ctx.stat[:, 6:12], xb[:, 512:1024]),
          label=f'{tag}_bn2')
    m_ag = p.add('V', lambda s: s.bn_aggr(ctx.stat[:, 12:14], ctx.stat[:, 0:12]),
                 label=f'{tag}_ag')
    m_sq = p.add('S', lambda s: s.activation(ctx.colv[1][:], ctx.stat[:, 13:14],
                                             AF.Sqrt, bias=ctx.consts[:, 0:1]),
                 deps=[m_ag], label=f'{tag}_sq')
    p.add('V', lambda s: s.reciprocal(ctx.colv[2][:], ctx.colv[1][:]),
          deps=[m_sq], label=f'{tag}_rc')
    m_nb = p.add('V', lambda s: s.scalar_tensor_tensor(
        ctx.colv[3][:], ctx.stat[:, 12:13], -1.0, ctx.colv[2][:],
        op0=ALU.mult, op1=ALU.mult), label=f'{tag}_nb')
    m_t = p.add('S', lambda s: s.activation(tcast[:], xb[:], AF.Identity,
                                            bias=ctx.colv[3][:],
                                            scale=ctx.colv[2][:]),
                deps=[m_nb], label=f'{tag}_t')
    return m_t


def emit_transposes(ctx, src_bf, dst, dst_cols, deps, tag, nch=8):
    p = ctx.p
    tr_last = None
    for j in range(nch):
        def tr(s, j=j):
            return s.transpose(ctx.ps_tr[:, j * 128:(j + 1) * 128],
                               src_bf[:, j * 128:(j + 1) * 128], ctx.ident_bf[:])
        tr_last = p.add('T', tr,
                        deps=(deps + [marks_get(ctx, 'pstr_bank_war')])
                        if j == 0 else [],
                        label=f'{tag}_tr{j}')
    last = None
    for j in range(nch):
        def ev(s, j=j):
            return s.tensor_copy(dst_cols(j), ctx.ps_tr[:, j * 128:(j + 1) * 128])
        last = p.add('V', ev, deps=[tr_last] if j == 0 else [],
                     label=f'{tag}_ev{j}')
    marks_set(ctx, 'pstr_bank_war', last)
    return last


def marks_get(ctx, key):
    return getattr(ctx, '_mk', {}).get(key)


def marks_set(ctx, key, val):
    if not hasattr(ctx, '_mk'):
        ctx._mk = {}
    ctx._mk[key] = val


def phase_light(ctx, marks, xln_ap, x_ap, x2_ap, tag):
    p, d = ctx.p, ctx.d
    NG, F = ctx.NG, ctx.F
    wffn = marks[f'wffn_{tag}']
    gst = [dict() for _ in range(NG + 1)]
    x2_stores = []

    def kTcols(g, j, t0, t1):
        buf = ctx.kTl[g % 3]
        return buf[:, j * 640 + t0:j * 640 + t1]

    def proj_group(g):
        st = gst[g]
        xbuf = ctx.xn2T[g % 2]
        war_xn2 = (gst[g - 2]['xn2_done'] if g >= 2
                   else marks[f'lmemsets_{tag}'])
        tile_marks = []
        for k in range(4):
            xb = ctx.xb[k % 2]
            tc = ctx.tcast[k % 2]
            m_ld = p.add('SQ', (lambda s, g=g, k=k, xb=xb: s.dma_start(
                out=xb[:], in_=xln_ap[g * 512 + k * 128:g * 512 + k * 128 + 128, :])),
                deps=[marks_get(ctx, f'lxb{k % 2}_war'),
                      marks.get(f'xb{k % 2}_war'),
                      marks.get(f'xlnt_{tag}_{g * 4 + k}',
                                marks[f'xln_done_{tag}'])],
                label=f'l{tag}g{g}ld{k}', slot=f'xb{k % 2}')
            marks[f'xb{k % 2}_war'] = None
            m_t = emit_ln2_tile(ctx, marks, xb, tc, f'l{tag}g{g}k{k}',
                                dep=m_ld)
            m_tp = emit_transposes(
                ctx, tc, xbuf,
                (lambda j, g=g, k=k, xbuf=xbuf: xbuf[:, j * 576 + 64 + k * 128:
                                                     j * 576 + 64 + k * 128 + 128]),
                deps=[m_t, war_xn2, wffn], tag=f'l{tag}g{g}k{k}')
            marks_set(ctx, f'lxb{k % 2}_war',
                      p.barrier('V', [m_t, m_tp], label=f'l{tag}g{g}xbw{k}'))
            tile_marks.append(m_tp)
        if g > 0:
            prev = ctx.xn2T[(g - 1) % 2]
            def pad(s, xbuf=xbuf, prev=prev):
                return s.tensor_copy(
                    xbuf[:].rearrange('p (c t) -> p c t', t=576)[:, :, 0:64],
                    prev[:].rearrange('p (c t) -> p c t', t=576)[:, :, 512:576])
            m_pad = p.add('P', pad, deps=[tile_marks[-1], gst[g - 1]['xn2_done']],
                          label=f'l{tag}g{g}pad')
        else:
            m_pad = tile_marks[-1]
        st['xn2'] = p.barrier('V', tile_marks + [m_pad], label=f'l{tag}g{g}xn2')
        if getattr(ctx, 'lg_limit', None) == 'proj_ln':
            st['xn2_done'] = st['xn2']
            st['k_ready'] = st['xn2']
            return

        qbuf = ctx.qTl[g % 2]
        kbuf = ctx.kTl[g % 3]
        war_q = (gst[g - 2].get('q_used') if g >= 2
                 else marks[f'lmemsets_{tag}'])
        war_k = (gst[g - 3].get('k_used') if g >= 3
                 else marks[f'lmemsets_{tag}'])
        qk_marks = []
        for c in range(8):
            psum = ctx.ps_a if c % 2 == 0 else ctx.ps_b
            ps_war = marks_get(ctx, f'l_ps{"ab"[c % 2]}_war')
            mm_last = None
            for kc in range(8):
                def mm(s, c=c, kc=kc, psum=psum, xbuf=xbuf):
                    return s.matmul(
                        psum[:, 0:512],
                        ctx.wreg[:, W_QKV + kc * 1536 + c * 128:
                                 W_QKV + kc * 1536 + c * 128 + 128],
                        xbuf[:, kc * 576 + 64:kc * 576 + 64 + 512],
                        start=(kc == 0), stop=False)
                mm_last = p.add('T', mm,
                                deps=([st['xn2'], wffn, ps_war] if kc == 0 else []),
                                label=f'l{tag}g{g}qk{c}_{kc}')
            def mmc0(s, c=c, psum=psum):
                return s.matmul(psum[:, 0:512], ctx.c0row[0:1, c * 128:c * 128 + 128],
                                ctx.ones_rowb[0:1, 0:512], start=False, stop=True)
            mm_last = p.add('T', mmc0, deps=[marks['onesr']],
                            label=f'l{tag}g{g}qkc0{c}')
            if c < 4:
                def ev(s, c=c, psum=psum, qbuf=qbuf):
                    return s.tensor_copy(qbuf[:, c * 512:(c + 1) * 512],
                                         psum[:, 0:512])
                m_ev = p.add('V', ev, deps=[mm_last, war_q], label=f'l{tag}g{g}qe{c}')
            else:
                def ev(s, c=c, psum=psum, kbuf=kbuf):
                    return s.tensor_copy(
                        kbuf[:, (c - 4) * 640 + 64:(c - 4) * 640 + 64 + 512],
                        psum[:, 0:512])
                m_ev = p.add('V', ev, deps=[mm_last, war_k], label=f'l{tag}g{g}ke{c}')
            marks_set(ctx, f'l_ps{"ab"[c % 2]}_war', m_ev)
            qk_marks.append(m_ev)
        st['q'] = p.barrier('V', qk_marks[:4], label=f'l{tag}g{g}q')
        st['k_mid'] = p.barrier('V', qk_marks[4:], label=f'l{tag}g{g}k')
        if getattr(ctx, 'lg_limit', None) == 'proj_qk':
            st['xn2_done'] = p.barrier('V', qk_marks, label=f'l{tag}g{g}xd')
            st['k_ready'] = st['k_mid']
            return
        if g > 0:
            prev = ctx.kTl[(g - 1) % 3]
            def padl(s, kbuf=kbuf, prev=prev):
                return s.tensor_copy(
                    kbuf[:].rearrange('p (c t) -> p c t', t=640)[:, 0:4, 0:64],
                    prev[:].rearrange('p (c t) -> p c t', t=640)[:, 0:4, 512:576])
            m_kl = p.add('P', padl, deps=[st['k_mid'], gst[g - 1]['k_mid']],
                         label=f'l{tag}g{g}kpl')
            def padr(s, kbuf=kbuf, prev=prev):
                return s.tensor_copy(
                    prev[:].rearrange('p (c t) -> p c t', t=640)[:, 0:4, 576:640],
                    kbuf[:].rearrange('p (c t) -> p c t', t=640)[:, 0:4, 64:128])
            m_kr = p.add('P', padr, label=f'l{tag}g{g}kpr')
            st['k_padl'] = m_kl
            gst[g - 1]['k_ready'] = p.barrier('P', [m_kr],
                                              label=f'l{tag}g{g - 1}kready')
        if g == 0:
            st['k_padl'] = st['k_mid']
        if g == NG - 1:
            st['k_ready'] = p.barrier('P', [st['k_mid'], st['k_padl']],
                                      label=f'l{tag}g{g}kready_last')

        v_marks = []
        for m in range(5 if g == NG - 1 else 4):
            vt = ctx.vS[(4 * g + m) % 12]
            war_v = marks_get(ctx, f'vS{(4 * g + m) % 12}_war')
            ps_war = marks_get(ctx, 'l_psc_war')
            mw = 128 if m < 4 else 64
            mm_last = None
            for kc in range(8):
                def mm(s, m=m, kc=kc, xbuf=xbuf, mw=mw):
                    return s.matmul(
                        ctx.ps_c[0:mw, 0:512],
                        xbuf[:, kc * 576 + m * 128:kc * 576 + m * 128 + mw],
                        ctx.wreg[:, W_QKV + kc * 1536 + 1024:
                                 W_QKV + kc * 1536 + 1536],
                        start=(kc == 0), stop=False)
                mm_last = p.add('T', mm,
                                deps=([st['xn2'], ps_war] if kc == 0 else []),
                                label=f'l{tag}g{g}v{m}_{kc}')
            def mmc0(s, mw=mw):
                return s.matmul(ctx.ps_c[0:mw, 0:512],
                                ctx.ones_rowb[0:1, 0:mw],
                                ctx.c0row[0:1, 1024:1536], start=False, stop=True)
            mm_last = p.add('T', mmc0, label=f'l{tag}g{g}vc0{m}')
            def ev(s, vt=vt, mw=mw):
                return s.tensor_copy(
                    vt[0:mw, :].rearrange('p (h e) -> p h e', e=65)[:, :, 0:64],
                    ctx.ps_c[0:mw, 0:512].rearrange('p (h e) -> p h e', e=64))
            m_ev = p.add('V', ev, deps=[mm_last, war_v], label=f'l{tag}g{g}ve{m}')
            marks_set(ctx, 'l_psc_war', m_ev)
            marks_set(ctx, f'vS{(4 * g + m) % 12}_filled', m_ev)
            v_marks.append(m_ev)
        st['v'] = p.barrier('V', v_marks, label=f'l{tag}g{g}v')
        st['xn2_done'] = p.barrier('V', qk_marks + v_marks, label=f'l{tag}g{g}xd')

    def light_group(g):
        st = gst[g]
        qbuf = ctx.qTl[g % 2]
        kbuf = ctx.kTl[g % 3]
        lim_g = getattr(ctx, 'lg_limit', None)
        nblocks = {'lg_sims': 1, 'lg_o': 1, 'lg_b0': 1,
                   'lg_b2': 2, 'lg_b3': 3}.get(lim_g, 4)
        if lim_g == 'lg_sims4':
            lim_g = 'lg_sims'
        elif lim_g == 'lg_o4':
            lim_g = 'lg_o'
        q_used, k_used, v_used, lx_marks = [], [], [], []
        for b in range(nblocks):
            ei = (4 * g + b) % 3
            esim = ctx.esim[ei]
            t0g = b * 128
            first = (g == 0 and b == 0)
            last = (g == NG - 1 and b == 3 and lim_g != 'lg_nolast')
            prev_burst = marks_get(ctx, 'lpsu_burst') or \
                [marks[f'lmemsets_{tag}']]
            for pair in ((0, 2), (4, 6), (1, 3), (5, 7)):
                mm_last = None
                for h in pair:
                    c = h // 2
                    pb = (h % 2) * 64
                    for kb in range(2):
                        slot = ((h // 2) % 2) * 2 + kb
                        def mm(s, c=c, pb=pb, kb=kb, t0g=t0g, slot=slot,
                               kbuf=kbuf, qbuf=qbuf):
                            return s.matmul(
                                ctx.ps_u[:, slot * 128:slot * 128 + 128],
                                kbuf[:, c * 640 + t0g + kb * 128:
                                     c * 640 + t0g + kb * 128 + 128][pb:pb + 64, :],
                                qbuf[:, c * 512 + t0g:c * 512 + t0g + 128][pb:pb + 64, :],
                                start=True, stop=True,
                                tile_position=(pb, 0))
                        dep0 = ([st['k_ready'], st['q']] + prev_burst
                                if mm_last is None else [])
                        mm_last = p.add('T', mm, deps=dep0,
                                        label=f'l{tag}g{g}b{b}s{h}_{kb}')
                burst_exps = []
                if not first and not last:
                    ha = pair[0]
                    a0 = (ha * 2) // 4
                    b0 = (ha * 2) % 4
                    es_r = esim.rearrange('p (a b c) -> p a b c', b=4, c=128)
                    ps_r = ctx.ps_u.rearrange('p (x y c) -> p x y c', y=2, c=128)
                    ew_all = [marks_get(ctx, f'esim{ei}_{h * 2 + kb}_war')
                              for h in pair for kb in range(2)]
                    mms = []
                    for nm, prow, pcol, drow, db, dcol in [
                            ('mA1', slice(0, 64), slice(0, 64),
                             slice(0, 64), b0, slice(0, 64)),
                            ('mA2', slice(64, 128), slice(0, 128),
                             slice(64, 128), b0, slice(0, 128)),
                            ('mB1', slice(0, 64), slice(0, 128),
                             slice(0, 64), b0 + 1, slice(0, 128)),
                            ('mB2', slice(64, 128), slice(64, 128),
                             slice(64, 128), b0 + 1, slice(64, 128))]:
                        y = 0 if nm.startswith('mA') else 1
                        def ee(s, prow=prow, pcol=pcol, drow=drow, db=db,
                               dcol=dcol, y=y, a0=a0, es_r=es_r, ps_r=ps_r):
                            return s.activation(
                                es_r[drow, a0:a0 + 2, db, dcol],
                                ps_r[prow, 0:2, y, pcol], AF.Exp)
                        mms.append(p.add('S', ee, deps=[mm_last] + ew_all,
                                         label=f'l{tag}g{g}b{b}{nm}p{pair[0]}'))
                    m_exp = p.barrier('S', mms,
                                      label=f'l{tag}g{g}b{b}expp{pair[0]}')
                    for h in pair:
                        for kb in range(2):
                            marks_set(ctx, f'esim_exp_{h}_{kb}', m_exp)
                    burst_exps = [m_exp]
                    prev_burst = burst_exps
                    continue
                for h in pair:
                    for kb in range(2):
                        slot = ((h // 2) % 2) * 2 + kb
                        et = esim[:, (h * 2 + kb) * 128:(h * 2 + kb) * 128 + 128]
                        ew = marks_get(ctx, f'esim{ei}_{h * 2 + kb}_war')
                        exp_marks = []
                        if kb == 0:
                            if not first:
                                exp_marks.append(p.add('S', (lambda s, et=et, slot=slot:
                                    s.activation(et[0:64, 0:64],
                                                 ctx.ps_u[0:64, slot * 128:slot * 128 + 64],
                                                 AF.Exp)), deps=[mm_last, ew],
                                    label=f'l{tag}g{g}b{b}eA1h{h}'))
                            else:
                                exp_marks.append(p.add('V', (lambda s, et=et:
                                    s.memset(et[0:64, 0:64], 0.0)), deps=[mm_last, ew],
                                    label=f'l{tag}g{g}b{b}eA1zh{h}'))
                            exp_marks.append(p.add('S', (lambda s, et=et, slot=slot:
                                s.activation(et[64:128, :],
                                             ctx.ps_u[64:128, slot * 128:slot * 128 + 128],
                                             AF.Exp)), deps=[mm_last, ew],
                                label=f'l{tag}g{g}b{b}eA2h{h}'))
                        else:
                            exp_marks.append(p.add('S', (lambda s, et=et, slot=slot:
                                s.activation(et[0:64, :],
                                             ctx.ps_u[0:64, slot * 128:slot * 128 + 128],
                                             AF.Exp)), deps=[mm_last, ew],
                                label=f'l{tag}g{g}b{b}eB1h{h}'))
                            if not last:
                                exp_marks.append(p.add('S', (lambda s, et=et, slot=slot:
                                    s.activation(et[64:128, 64:128],
                                                 ctx.ps_u[64:128, slot * 128 + 64:
                                                          slot * 128 + 128],
                                                 AF.Exp)), deps=[mm_last, ew],
                                    label=f'l{tag}g{g}b{b}eB2h{h}'))
                            else:
                                exp_marks.append(p.add('V', (lambda s, et=et:
                                    s.memset(et[64:128, 64:128], 0.0)), deps=[mm_last, ew],
                                    label=f'l{tag}g{g}b{b}eB2zh{h}'))
                        m_exp = p.barrier('S', exp_marks,
                                          label=f'l{tag}g{g}b{b}exph{h}kb{kb}')
                        marks_set(ctx, f'esim_exp_{h}_{kb}', m_exp)
                        burst_exps.append(m_exp)
                prev_burst = burst_exps
            marks_set(ctx, 'lpsu_burst', prev_burst)
            if lim_g == 'lg_sims':
                m_se = p.barrier('S', prev_burst, label=f'lgsims{g}')
                x2_stores.append(m_se)
                continue
            vA = ctx.vS[(4 * g + b) % 12]
            vB = ctx.vS[(4 * g + b + 1) % 12]
            vA_dep = marks_get(ctx, f'vS{(4 * g + b) % 12}_filled')
            vB_dep = (marks_get(ctx, f'vS{(4 * g + b + 1) % 12}_filled')
                      if not last else None)
            for half in range(2):
                o_marks = []
                for hh in range(4):
                    h = half * 4 + hh
                    po = hh * 65
                    for kb in range(2):
                        et = esim[:, (h * 2 + kb) * 128:(h * 2 + kb) * 128 + 128]
                        vt = vA if kb == 0 else vB
                        def mm(s, et=et, vt=vt, h=h, po=po, kb=kb):
                            return s.matmul(
                                ctx.ps_f[:, po:po + 65], et,
                                vt[:, h * 65:h * 65 + 65],
                                start=(kb == 0), stop=(kb == 1))
                        deps_o = [marks_get(ctx, f'esim_exp_{h}_{kb}'),
                                  vA_dep, vB_dep]
                        if kb == 0:
                            deps_o.append(marks_get(ctx, 'lps_f_war'))
                        o_marks.append(p.add('T', mm, deps=deps_o,
                                             label=f'l{tag}g{g}b{b}o{h}_{kb}'))
                def rc(s):
                    return s.reciprocal(
                        ctx.lrec[:, 0:4],
                        ctx.ps_f[:, 0:260].rearrange('p (h e) -> p h e',
                                                     e=65)[:, :, 64:65].squeeze(2))
                m_rc = p.add('V', rc, deps=o_marks, label=f'l{tag}g{g}b{b}rc{half}')
                def nm(s, half=half):
                    return s.tensor_tensor(
                        ctx.onorm[:, half * 256:half * 256 + 256]
                           .rearrange('p (h e) -> p h e', e=64),
                        ctx.ps_f[:, 0:260].rearrange('p (h e) -> p h e',
                                                     e=65)[:, :, 0:64],
                        ctx.lrec[:, 0:4].unsqueeze(2).to_broadcast([128, 4, 64]),
                        op=ALU.mult)
                m_nm = p.add('V', nm, deps=[marks_get(ctx, 'onorm_war')],
                             label=f'l{tag}g{g}b{b}nm{half}')
                marks_set(ctx, 'lps_f_war', m_nm)
            if lim_g == 'lg_o':
                x2_stores.append(marks_get(ctx, 'lps_f_war'))
                continue
            m_ot = emit_transposes(
                ctx, ctx.onorm,
                ctx.onormT, (lambda j: ctx.onormT[:, j * 128:(j + 1) * 128]),
                deps=[marks_get(ctx, 'lps_f_war'), marks_get(ctx, 'onormT_war')],
                tag=f'l{tag}g{g}b{b}ot', nch=4)
            marks_set(ctx, 'onorm_war', m_ot)
            if lim_g == 'lg_nowo':
                marks_set(ctx, 'onormT_war', m_ot)
                for h in range(8):
                    for kb in range(2):
                        marks_set(ctx, f'esim{ei}_{h * 2 + kb}_war', m_ot)
                x2_stores.append(m_ot)
                continue
            wo_marks = []
            for half in range(2):
                psum = ctx.ps_d if half == 0 else ctx.ps_e
                mm_last = None
                for kc in range(4):
                    def mm(s, kc=kc, half=half, psum=psum):
                        return s.matmul(
                            psum[:, 0:512],
                            ctx.onormT[:, kc * 128:(kc + 1) * 128],
                            ctx.wreg[:, W_OUTL + kc * 1024 + half * 512:
                                     W_OUTL + kc * 1024 + half * 512 + 512],
                            start=(kc == 0), stop=(kc == 3))
                    mm_last = p.add('T', mm,
                                    deps=([m_ot, marks_get(ctx, f'lps_{"de"[half]}_war')]
                                          if kc == 0 else []),
                                    label=f'l{tag}g{g}b{b}wo{half}_{kc}')
                wo_marks.append(mm_last)
            if getattr(ctx, 'dbg_onorm', None) is not None:
                m_dbgo = p.add('AQ', (lambda s, g=g, b=b: s.dma_start(
                    out=ctx.dbg_onorm[(g * 4 + b) * 128:(g * 4 + b) * 128 + 128, :],
                    in_=ctx.onorm[:])), deps=[m_ot], label=f'dbgo{g}{b}',
                    slot='dbgo')
                m_ot = p.barrier('V', [m_ot, m_dbgo], label=f'dbgob{g}{b}')
            m_dbge = None
            if getattr(ctx, 'dbg_esim', None) is not None:
                m_dbge = p.add('AQ', (lambda s, g=g, b=b, ei=ei: s.dma_start(
                    out=ctx.dbg_esim[(g * 4 + b) * 128:(g * 4 + b) * 128 + 128, :],
                    in_=ctx.esim[ei][:])), deps=[m_ot],
                    label=f'dbge{g}{b}', slot='dbge')
            marks_set(ctx, 'onormT_war', wo_marks[-1])
            m_ou = p.barrier('T', wo_marks, label=f'l{tag}g{g}b{b}oused')
            if m_dbge is not None:
                m_ou = p.barrier('T', [m_ou, m_dbge], label=f'l{tag}g{g}b{b}ou2')
            for h in range(8):
                for kb in range(2):
                    marks_set(ctx, f'esim{ei}_{h * 2 + kb}_war', m_ou)
            if lim_g == 'lg_noc':
                m_w = p.barrier('V', wo_marks, label=f'lgnoc{g}b{b}')
                marks_set(ctx, 'lps_d_war', m_w)
                marks_set(ctx, 'lps_e_war', m_w)
                marks_set(ctx, 'cxb_war', m_w)
                x2_stores.append(m_w)
                continue
            xb = ctx.xb[2]
            tok0 = g * 512 + b * 128
            m_ldx = p.add('SQ', (lambda s, tok0=tok0, xb=xb: s.dma_start(
                out=xb[:], in_=x_ap[tok0:tok0 + 128, :])),
                deps=[marks_get(ctx, 'cxb_war'), marks.get('xb2_war')],
                label=f'l{tag}g{g}b{b}ldx', slot='xb2')
            marks['xb2_war'] = None
            ls = ctx.lstage[b % 2]
            dep_ls = marks_get(ctx, f'lstage{b % 2}_war')
            m_c1 = p.add('V', (lambda s, xb=xb, ls=ls: s.scalar_tensor_tensor(
                ls[:, 0:512], ctx.ps_d[:, 0:512], 0.5, xb[:, 0:512],
                op0=ALU.mult, op1=ALU.add)),
                deps=[wo_marks[0], m_ldx, dep_ls], label=f'l{tag}g{g}b{b}c1')
            m_c2 = p.add('V', (lambda s, xb=xb, ls=ls: s.scalar_tensor_tensor(
                ls[:, 512:1024], ctx.ps_e[:, 0:512], 0.5, xb[:, 512:1024],
                op0=ALU.mult, op1=ALU.add)),
                deps=[wo_marks[1]], label=f'l{tag}g{g}b{b}c2')
            marks_set(ctx, 'lps_d_war', m_c1)
            marks_set(ctx, 'lps_e_war', m_c2)
            m_c3 = p.add('V', (lambda s, ls=ls: s.tensor_tensor(
                ls[:], ls[:], ctx.nullh_b[:], op=ALU.add)),
                deps=[marks['rt_b']], label=f'l{tag}g{g}b{b}c3')
            marks_set(ctx, 'cxb_war', m_c3)
            m_st = p.add('AQ', (lambda s, tok0=tok0, ls=ls: s.dma_start(
                out=x2_ap[tok0:tok0 + 128, :], in_=ls[:])),
                deps=[m_c3], label=f'l{tag}g{g}b{b}st',
                slot=f'x2st{tag}_{b % 2}')
            marks_set(ctx, f'lstage{b % 2}_war', m_st)
            x2_stores.append(m_st)
        st['q_used'] = p.barrier('T', [], label=f'l{tag}g{g}qu')
        st['k_used'] = st['q_used']
        st['v_used'] = st['q_used']
        for m in range(4):
            marks_set(ctx, f'vS{(4 * g + m) % 12}_war', st['q_used'])

    lim = getattr(ctx, 'lg_limit', None)
    if lim is not None and lim.startswith('proj'):
        for g in range(NG):
            proj_group(g)
        m_all = p.barrier('V', [gst[g]['xn2_done'] for g in range(NG)]
                          + [gst[NG - 1]['k_ready']], label=f'l{tag}_projdone')
        marks[f'x2_done_{tag}'] = m_all
        return m_all
    proj_group(0)
    for g in range(1, NG):
        proj_group(g)
        light_group(g - 1)
    light_group(NG - 1)
    m_all = p.barrier('V', x2_stores, label=f'l{tag}_x2done')
    marks[f'x2_done_{tag}'] = m_all
    return m_all


def prep_light_memsets(ctx, marks, tag, dep=None):
    p = ctx.p
    first = [dep]
    for i, vt in enumerate(ctx.vS):
        m1 = p.add('P', (lambda s, vt=vt: s.memset(vt[:], 0.0)), deps=first)
        first = []
        m2 = p.add('P', (lambda s, vt=vt: s.memset(
            vt[:].rearrange('p (h e) -> p h e', e=65)[:, :, 64:65], 1.0)))
        marks_set(ctx, f'vS{i}_war', m2)
        marks_set(ctx, f'vS{i}_filled', None)
    for bb in range(3):
        p.add('P', (lambda s, bb=bb: s.memset(ctx.esim[bb][:], 0.0)))
    for i, b in enumerate(ctx.kTl):
        p.add('P', (lambda s, b=b: s.memset(b[:], 0.0)))
    for i, b in enumerate(ctx.xn2T):
        p.add('P', (lambda s, b=b: s.memset(b[:], 0.0)))
    m = p.barrier('P', [], label=f'lmemsets{tag}')
    marks[f'lmemsets_{tag}'] = m
    pw = p.barrier('V', [marks.get(f'wffn_{tag}'), marks.get('_psu_war'),
                         marks.get('psum_free')], label=f'lpsw{tag}')
    for k in ['l_psa_war', 'l_psb_war', 'l_psc_war', 'lps_d_war', 'lps_e_war',
              'lps_f_war']:
        marks_set(ctx, k, pw)
    marks_set(ctx, 'lpsu_burst', [pw])
    for bb in range(3):
        for t in range(16):
            marks_set(ctx, f'esim{bb}_{t}_war', m)
    for k in ['hTbuf_war', 'esimh_war', 'kTh_war', 'qTh_war', 'onorm_war',
              'onormT_war', 'honorm_war', 'htc0_war', 'htc1_war',
              'hxb0_war', 'hxb1_war']:
        marks_set(ctx, k, m)
    for i in range(2):
        marks_set(ctx, f'lxb{i}_war', m)
        marks_set(ctx, f'lstage{i}_war', m)
    marks_set(ctx, 'cxb_war', m)
    return m


def prep_heavy_memsets(ctx, marks, tag, dep=None):
    p = ctx.p
    pw = p.barrier('V', [dep], label=f'hpsw{tag}')
    for k in ['h_psa_war', 'h_psb_war', 'h_psc_war', 'h_psd_war',
              'h_psfu_war']:
        marks_set(ctx, k, pw)
    first = [dep]
    for i, vt in enumerate(ctx.vh):
        m1 = p.add('P', (lambda s, vt=vt: s.memset(vt[:], 0.0)), deps=first)
        first = []
        m2 = p.add('P', (lambda s, vt=vt: s.memset(
            vt[:].rearrange('p (h e) -> p h e', e=65)[:, :, 64:65], 1.0)))
        marks_set(ctx, f'vh{i}_war', m2)
        marks_set(ctx, f'vh{i}_filled', None)
    m = p.barrier('P', [], label=f'hmemsets{tag}')
    for k in ['hTbuf_war', 'esimh_war', 'kTh_war', 'qTh_war',
              'htc0_war', 'htc1_war', 'hxb0_war', 'hxb1_war', 'h_psfu_war']:
        marks_set(ctx, k, m)
    return m



def phase_heavy(ctx, marks, xln_ap, x2_ap, tag, route_res):
    p, d = ctx.p, ctx.d
    wffn = marks[f'wffn_{tag}']
    m_x2 = marks[f'x2_done_{tag}']

    def gather_norm_T(idx_sb, idx_col, dep_idx, dstT, dst_tok, tg):
        xb = ctx.xb[idx_col % 2]
        dep_xb = marks_get(ctx, f'hxb{idx_col % 2}_war')
        dep_xb2 = marks.get(f'xb{idx_col % 2}_war')
        ctx.p.bounds_vals.add(ctx.T - 1)
        def gth(s):
            return s.indirect_dma_start(
                out=xb[:], out_offset=None, in_=xln_ap,
                in_offset=IndirectOffsetOnAxis(idx_sb[:, idx_col:idx_col + 1], 0),
                bounds_check=ctx.p.bc_reg(ctx.T - 1), oob_is_err=False)
        m_g = p.add('GQ', gth, deps=[dep_idx, dep_xb, dep_xb2,
                                     marks[f'xln_done_{tag}'],
                                     marks[f'x2_done_{tag}']],
                    label=f'h{tg}_g', slot=f'hxb{idx_col % 2}')
        m_sq = p.add('S', (lambda s, xb=xb: s.activation(
            ctx.junkw[:], xb[:], AF.Square, accum_out=ctx.colv[1][:])),
            deps=[m_g, marks.get('junkw_free')], label=f'h{tg}_ssq')
        marks['junkw_free'] = m_sq
        m_rt = p.add('S', lambda s: s.activation(ctx.colv[2][:], ctx.colv[1][:],
                                                 AF.Sqrt), deps=[m_sq],
                     label=f'h{tg}_rt')
        p.add('V', lambda s: s.tensor_scalar(ctx.colv[2][:], ctx.colv[2][:],
                                             ctx.consts[:, 1:2], None,
                                             op0=ALU.max), deps=[m_rt])
        p.add('V', lambda s: s.reciprocal(ctx.colv[3][:], ctx.colv[2][:]))
        m_sc = p.add('V', lambda s: s.tensor_scalar(ctx.colv[3][:],
                                                    ctx.colv[3][:], 32.0, None,
                                                    op0=ALU.mult))
        tc = ctx.tcast[idx_col % 2]
        dep_tc = marks_get(ctx, f'htc{idx_col % 2}_war')
        m_c = p.add('S', (lambda s, xb=xb, tc=tc: s.activation(
            tc[:], xb[:], AF.Identity, scale=ctx.colv[3][:])),
            deps=[m_sc, dep_tc], label=f'h{tg}_cast')
        marks_set(ctx, f'hxb{idx_col % 2}_war', m_c)
        m_tp = emit_transposes(
            ctx, tc, dstT,
            (lambda j, dst_tok=dst_tok, dstT=dstT:
             dstT[:, j * 512 + dst_tok:j * 512 + dst_tok + 128]),
            deps=[m_c, marks_get(ctx, 'hTbuf_war')], tag=f'h{tg}_tp')
        marks_set(ctx, f'htc{idx_col % 2}_war', m_tp)
        return m_tp

    NQg = ctx.NQ // 512 if ctx.NQ >= 512 else 1
    qtok_per_g = min(ctx.NQ, 512)
    for qg in range(NQg):
        tps = []
        for t in range(qtok_per_g // 128):
            tps.append(gather_norm_T(ctx.idxq, qg * 4 + t, route_res[('idx', 0)],
                                     ctx.hT[0], t * 128, f'{tag}q{qg}_{t}'))
        m_sub = p.barrier('V', tps, label=f'h{tag}q{qg}sub')
        for c in range(4):
            psum = ctx.ps_a if c % 2 == 0 else ctx.ps_b
            ps_war = marks_get(ctx, f'h_ps{"ab"[c % 2]}_war')
            mm_last = None
            for kc in range(8):
                def mm(s, c=c, kc=kc, psum=psum):
                    return s.matmul(
                        psum[:, 0:qtok_per_g],
                        ctx.wreg[:, W_QH + kc * 512 + c * 128:
                                 W_QH + kc * 512 + c * 128 + 128],
                        ctx.hT[0][:, kc * 512:kc * 512 + qtok_per_g],
                        start=(kc == 0), stop=(kc == 7))
                mm_last = p.add('T', mm,
                                deps=([m_sub, wffn, ps_war] if kc == 0 else []),
                                label=f'h{tag}qp{qg}_{c}_{kc}')
            def ev(s, c=c, qg=qg, psum=psum):
                return s.tensor_scalar(
                    ctx.qTh[:, c * ctx.NQ + qg * 512:
                            c * ctx.NQ + qg * 512 + qtok_per_g],
                    psum[:, 0:qtok_per_g], 0.125, None, op0=ALU.mult)
            m_ev = p.add('V', ev, deps=[mm_last,
                                        marks_get(ctx, 'qTh_war')],
                         label=f'h{tag}qpe{qg}_{c}')
            marks_set(ctx, f'h_ps{"ab"[c % 2]}_war', m_ev)
        m_used = p.barrier('T', [], label=f'h{tag}q{qg}u')
        marks_set(ctx, 'htc0_war', m_used)
        marks_set(ctx, 'htc1_war', m_used)
        marks_set(ctx, 'hTbuf_war', m_used)
    m_qready = p.barrier('V', [], label=f'h{tag}_qready')

    NKVh = ctx.NKV // 2
    oev_marks = []
    for half in range(2):
        ksub_marks = []
        nsub = max(NKVh // 512, 1)
        subtok = min(NKVh, 512)
        for sub in range(nsub):
            tps = []
            for t in range(subtok // 128):
                col = half * (NKVh // 128) + sub * 4 + t
                tps.append(gather_norm_T(ctx.idxkv, col, route_res[('idx', 1)],
                                         ctx.hT[1], t * 128,
                                         f'{tag}kv{half}_{sub}_{t}'))
            m_sub = p.barrier('V', tps, label=f'h{tag}kv{half}{sub}sub')
            for c in range(4):
                psum = ctx.ps_a if c % 2 == 0 else ctx.ps_b
                ps_war = marks_get(ctx, f'h_ps{"ab"[c % 2]}_war')
                mm_last = None
                for kc in range(8):
                    def mm(s, c=c, kc=kc, psum=psum):
                        return s.matmul(
                            psum[:, 0:subtok],
                            ctx.wreg[:, W_KVH + kc * 1024 + c * 128:
                                     W_KVH + kc * 1024 + c * 128 + 128],
                            ctx.hT[1][:, kc * 512:kc * 512 + subtok],
                            start=(kc == 0), stop=(kc == 7))
                    mm_last = p.add('T', mm,
                                    deps=([m_sub, wffn, ps_war,
                                           marks_get(ctx, 'kTh_war')]
                                          if kc == 0 else []),
                                    label=f'h{tag}kp{half}{sub}_{c}_{kc}')
                def ev(s, c=c, sub=sub, psum=psum):
                    return s.tensor_copy(
                        ctx.kTh[:, c * 1024 + sub * 512:
                                c * 1024 + sub * 512 + subtok],
                        psum[:, 0:subtok])
                m_ev = p.add('V', ev, deps=[mm_last], label=f'h{tag}kpe{half}{sub}_{c}')
                marks_set(ctx, f'h_ps{"ab"[c % 2]}_war', m_ev)
                ksub_marks.append(m_ev)
            for t in range(subtok // 128):
                kb = sub * 4 + t
                vt = ctx.vh[kb]
                ps_war = marks_get(ctx, 'h_psc_war')
                mm_last = None
                for kc in range(8):
                    def mm(s, t=t, kc=kc):
                        return s.matmul(
                            ctx.ps_c[:, 0:512],
                            ctx.hT[1][:, kc * 512 + t * 128:kc * 512 + t * 128 + 128],
                            ctx.wreg[:, W_KVH + kc * 1024 + 512:
                                     W_KVH + kc * 1024 + 1024],
                            start=(kc == 0), stop=(kc == 7))
                    mm_last = p.add('T', mm,
                                    deps=([m_sub, ps_war,
                                           marks_get(ctx, f'vh{kb}_war')]
                                          if kc == 0 else []),
                                    label=f'h{tag}vp{half}{kb}_{kc}')
                def ev(s, vt=vt):
                    return s.tensor_copy(
                        vt[:].rearrange('p (h e) -> p h e', e=65)[:, :, 0:64],
                        ctx.ps_c[:, 0:512].rearrange('p (h e) -> p h e', e=64))
                m_ev = p.add('V', ev, deps=[mm_last], label=f'h{tag}vpe{half}{kb}')
                marks_set(ctx, 'h_psc_war', m_ev)
                marks_set(ctx, f'vh{kb}_filled', m_ev)
                ksub_marks.append(m_ev)
            m_used = p.barrier('T', [], label=f'h{tag}kv{half}{sub}u')
            marks_set(ctx, 'htc0_war', m_used)
            marks_set(ctx, 'htc1_war', m_used)
            marks_set(ctx, 'hTbuf_war', m_used)
        m_kv = p.barrier('V', ksub_marks, label=f'h{tag}kv{half}done')

        nkb = NKVh // 128
        for h in range(8):
            c, pb = h // 2, (h % 2) * 64
            o_first = True
            if half == 0:
                def nmm(s, c=c, pb=pb, h=h, nn=0):
                    return s.matmul(
                        ctx.ps_c[0:1, 0:min(ctx.NQ, 512)],
                        ctx.nullk_bf[pb:pb + 64, h:h + 1],
                        ctx.qTh[pb:pb + 64, c * ctx.NQ:c * ctx.NQ + min(ctx.NQ, 512)],
                        start=True, stop=True, tile_position=(pb, 0))
                m_n1 = p.add('T', nmm, deps=[m_qready, marks['nullkv'],
                                             marks_get(ctx, 'h_psc_war')],
                             label=f'h{tag}ns{half}h{h}a')
                ms = [m_n1]
                if ctx.NQ > 512:
                    def nmm2(s, c=c, pb=pb, h=h):
                        return s.matmul(
                            ctx.ps_d[0:1, 0:512],
                            ctx.nullk_bf[pb:pb + 64, h:h + 1],
                            ctx.qTh[pb:pb + 64, c * ctx.NQ + 512:c * ctx.NQ + 1024],
                            start=True, stop=True, tile_position=(pb, 0))
                    ms.append(p.add('T', nmm2,
                                    deps=[marks_get(ctx, 'h_psd_war')],
                                    label=f'h{tag}ns{half}h{h}b'))
                m_ne = p.add('S', (lambda s: s.activation(
                    ctx.esimh[0:1, 0:min(ctx.NQ, 512)],
                    ctx.ps_c[0:1, 0:min(ctx.NQ, 512)], AF.Exp)),
                    deps=ms + [marks_get(ctx, 'esimh_war')],
                    label=f'h{tag}ne{half}h{h}a')
                if ctx.NQ > 512:
                    m_ne = p.add('S', (lambda s: s.activation(
                        ctx.esimh[0:1, 512:1024], ctx.ps_d[0:1, 0:512], AF.Exp)),
                        label=f'h{tag}ne{half}h{h}b')
                marks_set(ctx, 'h_psc_war', m_ne)
                marks_set(ctx, 'h_psd_war', m_ne)
                for qc in range(ctx.NQ // 128):
                    po = (qc * 65) if qc < 7 else 0
                    pst = ctx.ps_f if qc < 7 else ctx.ps_u
                    def omm(s, qc=qc, po=po, pst=pst, h=h):
                        return s.matmul(
                            pst[:, po:po + 65],
                            ctx.esimh[0:1, qc * 128:qc * 128 + 128],
                            ctx.nullv_bf[0:1, h * 65:h * 65 + 65],
                            start=True, stop=False)
                    m_no = p.add('T', omm,
                                 deps=[m_ne, marks_get(ctx, 'h_psfu_war')],
                                 label=f'h{tag}no{half}h{h}q{qc}')
                marks_set(ctx, 'esimh_war',
                          p.barrier('T', [], label=f'h{tag}nob{half}h{h}'))
                o_first = False
            for kb in range(nkb):
                sim_ms = []
                for nn in range(ctx.NQ // 512 if ctx.NQ >= 512 else 1):
                    pst = ctx.ps_c if nn == 0 else ctx.ps_d
                    def smm(s, c=c, pb=pb, kb=kb, nn=nn, pst=pst):
                        return s.matmul(
                            pst[:, 0:min(ctx.NQ, 512)],
                            ctx.kTh[pb:pb + 64, c * 1024 + kb * 128:
                                    c * 1024 + kb * 128 + 128],
                            ctx.qTh[pb:pb + 64, c * ctx.NQ + nn * 512:
                                    c * ctx.NQ + nn * 512 + min(ctx.NQ, 512)],
                            start=True, stop=True, tile_position=(pb, 0))
                    sim_ms.append(p.add('T', smm,
                                        deps=[m_kv, m_qready,
                                              marks_get(ctx, f'h_ps{"cd"[nn]}_war')],
                                        label=f'h{tag}s{half}h{h}k{kb}n{nn}'))
                exp_ms = []
                for nn in range(ctx.NQ // 512 if ctx.NQ >= 512 else 1):
                    pst = ctx.ps_c if nn == 0 else ctx.ps_d
                    def ee(s, nn=nn, pst=pst):
                        return s.activation(
                            ctx.esimh[:, nn * 512:nn * 512 + min(ctx.NQ, 512)],
                            pst[:, 0:min(ctx.NQ, 512)], AF.Exp)
                    exp_ms.append(p.add('S', ee,
                                        deps=[sim_ms[nn],
                                              marks_get(ctx, 'esimh_war')],
                                        label=f'h{tag}e{half}h{h}k{kb}n{nn}'))
                    marks_set(ctx, f'h_ps{"cd"[nn]}_war', exp_ms[-1])
                m_exp = p.barrier('S', exp_ms, label=f'h{tag}eb{half}h{h}k{kb}')
                o_ms = []
                for qc in range(ctx.NQ // 128):
                    po = (qc * 65) if qc < 7 else 0
                    pst = ctx.ps_f if qc < 7 else ctx.ps_u
                    laststop = (kb == nkb - 1)
                    def omm(s, qc=qc, po=po, pst=pst, h=h, kb=kb,
                            laststop=laststop, o_first=o_first):
                        return s.matmul(
                            pst[:, po:po + 65],
                            ctx.esimh[:, qc * 128:qc * 128 + 128],
                            ctx.vh[kb][:, h * 65:h * 65 + 65],
                            start=False if not (o_first and kb == 0) else True,
                            stop=laststop)
                    dps = [m_exp, marks_get(ctx, f'vh{kb}_filled')]
                    if o_first and kb == 0:
                        dps.append(marks_get(ctx, 'h_psfu_war'))
                    o_ms.append(p.add('T', omm, deps=dps,
                                      label=f'h{tag}o{half}h{h}k{kb}q{qc}'))
                m_oall = p.barrier('T', [], label=f'h{tag}ou{half}h{h}k{kb}')
                marks_set(ctx, 'esimh_war', m_oall)
            n1 = min(ctx.NQ // 128, 7) * 65
            if half == 0:
                def ev1(s, h=h, n1=n1):
                    return s.tensor_copy(ctx.oacc[:, h * 520:h * 520 + n1],
                                         ctx.ps_f[:, 0:n1])
                def ev2(s, h=h):
                    return s.tensor_copy(ctx.oacc[:, h * 520 + 455:h * 520 + 520],
                                         ctx.ps_u[:, 0:65])
            else:
                def ev1(s, h=h, n1=n1):
                    return s.tensor_tensor(ctx.oacc[:, h * 520:h * 520 + n1],
                                           ctx.oacc[:, h * 520:h * 520 + n1],
                                           ctx.ps_f[:, 0:n1], op=ALU.add)
                def ev2(s, h=h):
                    return s.tensor_tensor(ctx.oacc[:, h * 520 + 455:h * 520 + 520],
                                           ctx.oacc[:, h * 520 + 455:h * 520 + 520],
                                           ctx.ps_u[:, 0:65], op=ALU.add)
            mo1 = p.add('V', ev1, deps=[m_oall], label=f'h{tag}oe{half}h{h}a')
            mo2 = (p.add('V', ev2, label=f'h{tag}oe{half}h{h}b')
                   if ctx.NQ // 128 == 8 else mo1)
            oev_marks.append(mo2)
            m_psfu = p.barrier('V', [mo1, mo2], label=f'h{tag}oef{half}h{h}')
            marks_set(ctx, 'h_psfu_war', m_psfu)
        marks_set(ctx, 'kTh_war', p.barrier('T', [], label=f'h{tag}kthu{half}'))
        for kb in range(nkb):
            marks_set(ctx, f'vh{kb}_war', marks_get(ctx, 'kTh_war'))

    m_oacc = p.barrier('V', oev_marks, label=f'h{tag}_oacc')

    nqc = ctx.NQ // 128
    oT = ctx.hT[0]
    oT_war = p.barrier('T', [], label=f'h{tag}_oTwar')
    tp_ms = []
    for cc in range(4):
        nm_ms = []
        for hh in range(2):
            h = 2 * cc + hh
            oh = ctx.oacc[:, h * 520:h * 520 + 520].rearrange(
                'p (q e) -> p q e', e=65)
            m_rc = p.add('V', (lambda s, oh=oh, hh=hh: s.reciprocal(
                ctx.lrec[:, hh * 8:hh * 8 + nqc],
                oh[:, 0:nqc, 64:65].squeeze(2))),
                deps=[m_oacc], label=f'h{tag}nr{h}')
            def nm(s, oh=oh, hh=hh):
                return s.tensor_tensor(
                    ctx.onorm[:, 0:nqc * 128].rearrange(
                        'p (q w e) -> p q w e', w=2, e=64)[:, :, hh],
                    oh[:, 0:nqc, 0:64],
                    ctx.lrec[:, hh * 8:hh * 8 + nqc].unsqueeze(2)
                       .to_broadcast([128, nqc, 64]),
                    op=ALU.mult)
            nm_ms.append(p.add('V', nm, deps=[marks_get(ctx, 'honorm_war')],
                               label=f'h{tag}nm{h}'))
        m_nm = p.barrier('V', nm_ms, label=f'h{tag}nmb{cc}')
        tr_last = None
        for qc in range(nqc):
            slot = qc % 8
            def tr(s, qc=qc, slot=slot):
                return s.transpose(ctx.ps_tr[:, slot * 128:slot * 128 + 128],
                                   ctx.onorm[:, qc * 128:qc * 128 + 128],
                                   ctx.ident_bf[:])
            tr_last = p.add('T', tr,
                            deps=([m_nm, oT_war,
                                   marks_get(ctx, 'pstr_bank_war')]
                                  if qc == 0 else []),
                            label=f'h{tag}tr{cc}_{qc}')
        ev_last = None
        for qc in range(nqc):
            slot = qc % 8
            def ev(s, cc=cc, qc=qc, slot=slot):
                return s.tensor_copy(
                    oT[:, cc * ctx.NQ + qc * 128:cc * ctx.NQ + qc * 128 + 128],
                    ctx.ps_tr[:, slot * 128:slot * 128 + 128])
            ev_last = p.add('V', ev, deps=[tr_last] if qc == 0 else [],
                            label=f'h{tag}te{cc}_{qc}')
            tp_ms.append(ev_last)
        marks_set(ctx, 'pstr_bank_war', ev_last)
        marks_set(ctx, 'honorm_war', p.barrier('T', [], label=f'h{tag}onw{cc}'))
    m_oT = p.barrier('V', tp_ms, label=f'h{tag}_oT')

    m_sc_prev = None
    for qc in range(nqc):
        wo_ms = []
        for half in range(2):
            psum = ctx.ps_c if half == 0 else ctx.ps_d
            mm_last = None
            for kc in range(4):
                def mm(s, qc=qc, kc=kc, half=half, psum=psum):
                    return s.matmul(
                        psum[:, 0:512],
                        oT[:, kc * ctx.NQ + qc * 128:kc * ctx.NQ + qc * 128 + 128],
                        ctx.wreg[:, W_OUTH + kc * 1024 + half * 512:
                                 W_OUTH + kc * 1024 + half * 512 + 512],
                        start=(kc == 0), stop=(kc == 3))
                mm_last = p.add('T', mm,
                                deps=([m_oT, marks_get(ctx, f'h_ps{"cd"[half]}_war')]
                                      if kc == 0 else []),
                                label=f'h{tag}wo{qc}_{half}_{kc}')
            wo_ms.append(mm_last)
        dd = []
        for half in range(2):
            psum = ctx.ps_c if half == 0 else ctx.ps_d
            def de(s, half=half, psum=psum):
                return s.scalar_tensor_tensor(
                    ctx.hvout[:, half * 512:half * 512 + 512], psum[:, 0:512],
                    0.5, ctx.nullh_b[:, half * 512:half * 512 + 512],
                    op0=ALU.mult, op1=ALU.subtract)
            m_d = p.add('V', de, deps=[wo_ms[half], m_sc_prev],
                        label=f'h{tag}d{qc}_{half}')
            marks_set(ctx, f'h_ps{"cd"[half]}_war', m_d)
            dd.append(m_d)
        ctx.p.bounds_vals.add(ctx.T - 1)
        def sca(s, qc=qc):
            return s.indirect_dma_start(
                out=x2_ap, out_offset=IndirectOffsetOnAxis(
                    ctx.idxq[:, qc:qc + 1], 0),
                in_=ctx.hvout[:], in_offset=None, compute_op=ALU.add,
                bounds_check=ctx.p.bc_reg(ctx.T - 1), oob_is_err=True)
        m_sc_prev = p.add('GQ', sca, deps=dd + [m_x2, route_res[('idx', 0)]],
                          label=f'h{tag}scat{qc}', slot='hscat')
    m_done = p.barrier('V', [m_sc_prev], label=f'h{tag}_done')
    marks[f'heavy_{tag}'] = m_done
    return m_done



def declare_gla(ctx):
    G = 'gla'
    ctx.xlnT = ctx.sb('g_xlnT', [128, 8 * 512], BF16, G)
    ctx.gqT = ctx.sb('g_qT', [128, 8 * 512], BF16, G)
    ctx.gkT_ = ctx.sb('g_kT', [128, 8 * 512], BF16, G)
    ctx.ggk = ctx.sb('g_gk', [128, 8 * 512], F32, G)
    ctx.gbT = ctx.sb('g_bT', [128, 1024], F32, G)
    ctx.gfT = ctx.sb('g_fT', [128, 1024], F32, G)
    ctx.gqp = ctx.sb('g_qp', [128, 1024], BF16, G)
    ctx.gkpT = ctx.sb('g_kpT', [128, 1024], BF16, G)
    ctx.gkp = ctx.sb('g_kp', [128, 1024], BF16, G)
    ctx.gv = ctx.sb('g_v', [128, 4 * 1024], BF16, G)
    ctx.gA = ctx.sb('g_A', [128, 256], BF16, G)
    ctx.gS = ctx.sb('g_S', [128, 1024], F32, G)
    ctx.gSbf = ctx.sb('g_Sbf', [128, 1024], BF16, 'glaR', side='right')
    ctx.gob = ctx.sb('g_ob', [128, 1024], F32, G)
    ctx.gsg = ctx.gfT
    ctx.gt1 = ctx.sb('g_t1', [32, 512], BF16, G)
    ctx.gtmp = [ctx.sb(f'g_tmp{i}', [128, 512], F32, G) for i in range(2)]
    ctx.gcast = ctx.sb('g_cast', [128, 1024], BF16, G)
    ctx.gofb = ctx.gkpT
    ctx.gofT = ctx.gqp
    ctx.gred = ctx.sb('g_red', [128, 16], F32, G)
    ctx.gx3 = [ctx.gob] * 2


def phase_gla(ctx, marks, x2_ap, x3_ap, xln3_ap, dep_start):
    p, d = ctx.p, ctx.d
    NG = ctx.NG
    wg = marks['wgla']
    m_S0 = p.add('V', lambda s: s.memset(ctx.gS[:], 0.0), deps=[dep_start],
                 label='gla_S0')
    gpw = p.barrier('V', [dep_start, marks.get('_psu_war')], label='gpsw')
    for k in ['g_psa_war', 'g_psb_war', 'g_psc_war', 'g_psd_war', 'g_pse_war',
              'g_psu_war', 'g_psf_war']:
        marks_set(ctx, k, gpw)
    m_t1ones = p.add('V', lambda s: s.memset(ctx.gt1[0:32, :], 1.0),
                     label='gla_t1ones')
    m_sbfz = p.add('P', lambda s: s.memset(ctx.gSbf[:], 0.0),
                   deps=[dep_start], label='gla_sbfz')
    marks_set(ctx, 'gSbf_war', m_sbfz)
    marks_set(ctx, 'gS_war', m_S0)
    x3_stores = []

    for g in range(NG):
        tile_marks = []
        for k in range(4):
            xb = ctx.xb[k % 2]
            m_ld = p.add('SQ', (lambda s, g=g, k=k, xb=xb: s.dma_start(
                out=xb[:], in_=x2_ap[g * 512 + k * 128:g * 512 + k * 128 + 128, :])),
                deps=[marks_get(ctx, f'gxb{k % 2}_war'), marks.get('heavy_1'),
                      dep_start, marks.get(f'xb{k % 2}_war')],
                label=f'gg{g}ld{k}', slot=f'xb{k % 2}')
            marks[f'xb{k % 2}_war'] = None
            p.add('V', (lambda s, xb=xb: s.bn_stats(ctx.stat[:, 0:6],
                                                    xb[:, 0:512])), deps=[m_ld])
            p.add('V', (lambda s, xb=xb: s.bn_stats(ctx.stat[:, 6:12],
                                                    xb[:, 512:1024])))
            m_ag = p.add('V', lambda s: s.bn_aggr(ctx.stat[:, 12:14],
                                                  ctx.stat[:, 0:12]))
            m_sq = p.add('S', lambda s: s.activation(ctx.colv[1][:],
                                                     ctx.stat[:, 13:14], AF.Sqrt,
                                                     bias=ctx.consts[:, 0:1]),
                         deps=[m_ag], label=f'gg{g}sq{k}')
            p.add('V', lambda s: s.reciprocal(ctx.colv[2][:], ctx.colv[1][:]),
                  deps=[m_sq])
            m_nb = p.add('V', lambda s: s.scalar_tensor_tensor(
                ctx.colv[3][:], ctx.stat[:, 12:13], -1.0, ctx.colv[2][:],
                op0=ALU.mult, op1=ALU.mult))
            m_c = p.add('S', (lambda s, xb=xb: s.activation(
                ctx.gcast[:], xb[:], AF.Identity, bias=ctx.colv[3][:],
                scale=ctx.colv[2][:])),
                deps=[m_nb, marks_get(ctx, 'gcast_war')], label=f'gg{g}c{k}')
            marks_set(ctx, f'gxb{k % 2}_war',
                      p.barrier('S', [m_c], label=f'gg{g}xbw{k}'))
            m_tp = emit_transposes(
                ctx, ctx.gcast, ctx.xlnT,
                (lambda j, k=k: ctx.xlnT[:, j * 512 + k * 128:
                                         j * 512 + k * 128 + 128]),
                deps=[m_c, marks_get(ctx, 'xlnT_war')], tag=f'gg{g}k{k}')
            marks_set(ctx, 'gcast_war', m_tp)
            tile_marks.append(m_tp)
        m_xT = p.barrier('V', tile_marks, label=f'gg{g}xT')

        pm = {}
        for nm, woff, dstbuf in [('q', W_GQ, ctx.gqT), ('k', W_GK, ctx.gkT_)]:
            evs = []
            for c in range(8):
                psum = ctx.ps_a if c % 2 == 0 else ctx.ps_b
                ps_war = marks_get(ctx, f'g_ps{"ab"[c % 2]}_war')
                mm_last = None
                for kc in range(8):
                    def mm(s, c=c, kc=kc, psum=psum, woff=woff):
                        return s.matmul(
                            psum[:, 0:512],
                            ctx.wreg[:, woff + kc * 1024 + c * 128:
                                     woff + kc * 1024 + c * 128 + 128],
                            ctx.xlnT[:, kc * 512:(kc + 1) * 512],
                            start=(kc == 0), stop=(kc == 7))
                    mm_last = p.add('T', mm,
                                    deps=([m_xT, wg, ps_war] if kc == 0 else []),
                                    label=f'gg{g}{nm}p{c}_{kc}')
                def ev(s, c=c, psum=psum, dstbuf=dstbuf):
                    return s.tensor_copy(dstbuf[:, c * 512:(c + 1) * 512],
                                         psum[:, 0:512])
                m_ev = p.add('V', ev, deps=[mm_last,
                                            marks_get(ctx, f'g{nm}T_war')],
                             label=f'gg{g}{nm}e{c}')
                marks_set(ctx, f'g_ps{"ab"[c % 2]}_war', m_ev)
                evs.append(m_ev)
            pm[nm] = p.barrier('V', evs, label=f'gg{g}{nm}done')
        mm_last = None
        ps_war = marks_get(ctx, 'g_psc_war')
        for kc in range(8):
            def mm(s, kc=kc):
                return s.matmul(ctx.ps_c[0:16, 0:512],
                                ctx.wg1_bf[:, kc * 16:kc * 16 + 16],
                                ctx.xlnT[:, kc * 512:(kc + 1) * 512],
                                start=(kc == 0), stop=(kc == 7))
            mm_last = p.add('T', mm, deps=([m_xT, wg, ps_war] if kc == 0 else []),
                            label=f'gg{g}t1_{kc}')
        m_t1 = p.add('V', lambda s: s.tensor_copy(ctx.gt1[0:16, :],
                                                  ctx.ps_c[0:16, 0:512]),
                     deps=[mm_last, m_t1ones, marks_get(ctx, 'gt1_war')],
                     label=f'gg{g}t1e')
        marks_set(ctx, 'g_psc_war', m_t1)
        gk_evs = []
        for c in range(8):
            def mmz(s, c=c):
                return s.matmul(ctx.ps_c[:, 0:512],
                                ctx.wg2a_bf[0:17, c * 128:c * 128 + 128],
                                ctx.gt1[0:17, :], start=True, stop=True)
            m_z = p.add('T', mmz, deps=[m_t1, marks_get(ctx, 'g_psc_war'), wg],
                        label=f'gg{g}z{c}')
            m_u1 = p.add('S', (lambda s: s.activation(ctx.gtmp[0][:],
                                                      ctx.ps_c[:, 0:512], AF.Abs)),
                         deps=[m_z, marks_get(ctx, 'gtmp0_war')],
                         label=f'gg{g}abs{c}')
            p.add('S', (lambda s: s.activation(
                ctx.gtmp[1][:], ctx.gtmp[0][:], AF.Exp,
                scale=ctx.consts[:, 2:3])),
                deps=[marks_get(ctx, 'gtmp1_war'), marks['consts']],
                label=f'gg{g}exp{c}')
            m_u3 = p.add('S', (lambda s: s.activation(
                ctx.gtmp[0][:], ctx.gtmp[1][:], AF.Ln, bias=1.0)),
                label=f'gg{g}ln{c}')
            marks_set(ctx, 'gtmp1_war', m_u3)
            def fin(s, c=c):
                return s.scalar_tensor_tensor(
                    ctx.ggk[:, c * 512:(c + 1) * 512], ctx.ps_c[:, 0:512], 0.0,
                    ctx.gtmp[0][:], op0=ALU.min, op1=ALU.subtract)
            m_gk = p.add('V', fin, deps=[m_u3, marks_get(ctx, 'ggk_war')],
                         label=f'gg{g}gk{c}')
            marks_set(ctx, 'gtmp0_war', m_gk)
            marks_set(ctx, 'g_psc_war', m_gk)
            gk_evs.append(m_gk)
        pm['gk'] = p.barrier('V', gk_evs, label=f'gg{g}gkdone')
        v_evs = []
        for t in range(4):
            for half in range(2):
                psum = ctx.ps_d if half == 0 else ctx.ps_e
                ps_war = marks_get(ctx, f'g_ps{"de"[half]}_war')
                mm_last = None
                for kc in range(8):
                    def mm(s, t=t, kc=kc, half=half, psum=psum):
                        return s.matmul(
                            psum[:, 0:512],
                            ctx.xlnT[:, kc * 512 + t * 128:kc * 512 + t * 128 + 128],
                            ctx.wreg[:, W_GV + kc * 1024 + half * 512:
                                     W_GV + kc * 1024 + half * 512 + 512],
                            start=(kc == 0), stop=(kc == 7))
                    mm_last = p.add('T', mm,
                                    deps=([m_xT, wg, ps_war] if kc == 0 else []),
                                    label=f'gg{g}v{t}_{half}_{kc}')
                def ev(s, t=t, half=half, psum=psum):
                    return s.tensor_copy(
                        ctx.gv[:, t * 1024 + half * 512:
                               t * 1024 + half * 512 + 512],
                        psum[:, 0:512])
                m_ev = p.add('V', ev, deps=[mm_last, marks_get(ctx, 'gv_war')],
                             label=f'gg{g}ve{t}_{half}')
                marks_set(ctx, f'g_ps{"de"[half]}_war', m_ev)
                v_evs.append(m_ev)
        pm['v'] = p.barrier('V', v_evs, label=f'gg{g}vdone')
        if g == NG - 1:
            marks['gla_qkfree'] = p.barrier('V', [pm['q'], pm['k']],
                                            label='gla_qkfree')
            marks['gla_vfree'] = pm['v']

        glast_T = []
        for cw in range(4):
            ch = g * 4 + cw
            par = ch % 2
            m_scan0 = None
            for r in range(8):
                def sc(s, r=r, cw=cw):
                    return s.tensor_tensor_scan(
                        ctx.gbT[:, r * 128:(r + 1) * 128],
                        ctx.ggk[:, r * 512 + cw * 128:r * 512 + cw * 128 + 128],
                        ctx.zero128[:], 0.0, op0=ALU.add, op1=ALU.add)
                m_s = p.add('V', sc,
                            deps=[pm['gk'], marks_get(ctx, 'gbT_war'),
                                  marks['consts']],
                            label=f'gc{ch}scan{r}')
                if m_scan0 is None:
                    m_scan0 = m_s
            m_b = p.barrier('V', [], label=f'gc{ch}b')
            m_f = p.add('S', lambda s: s.activation(ctx.gfT[:], ctx.gbT[:],
                                                    AF.Exp,
                                                    scale=ctx.consts[:, 4:5]),
                        deps=[m_b, marks_get(ctx, 'gfT_war'),
                              marks_get(ctx, 'gsg_war')], label=f'gc{ch}f')
            m_e = p.add('S', lambda s: s.activation(ctx.gbT[:], ctx.gbT[:],
                                                    AF.Exp,
                                                    scale=ctx.consts[:, 3:4]),
                        deps=[marks_get(ctx, 'geT_war')], label=f'gc{ch}e')
            qv = ctx.gqT[:].rearrange('p (c t) -> p c t', t=512)[:, :, cw * 128:
                                                                cw * 128 + 128]
            kv = ctx.gkT_[:].rearrange('p (c t) -> p c t', t=512)[:, :, cw * 128:
                                                                 cw * 128 + 128]
            m_qp = p.add('V', (lambda s, qv=qv: s.tensor_tensor(
                ctx.gqp[:].rearrange('p (c t) -> p c t', t=128), qv,
                ctx.gbT[:].rearrange('p (c t) -> p c t', t=128), op=ALU.mult)),
                deps=[m_e, pm['q'], marks_get(ctx, 'gqp_war')],
                label=f'gc{ch}qp')
            m_kp = p.add('V', (lambda s, kv=kv: s.tensor_tensor(
                ctx.gkpT[:].rearrange('p (c t) -> p c t', t=128), kv,
                ctx.gfT[:].rearrange('p (c t) -> p c t', t=128), op=ALU.mult)),
                deps=[m_f, pm['k'], marks_get(ctx, 'gkpT_war')],
                label=f'gc{ch}kp')
            p.add('V', (lambda s, par=par: s.tensor_copy(
                ctx.gSbf[0:64, :].rearrange('p (h e) -> p h e', e=128)[:, :, 0:64],
                ctx.gS[0:64, par * 512:par * 512 + 512]
                   .rearrange('p (h e) -> p h e', e=64))),
                deps=[marks_get(ctx, 'gS_war'), marks_get(ctx, 'gSbf_war')],
                label=f'gc{ch}sbfa')
            m_sbf = p.add('V', (lambda s, par=par: s.tensor_copy(
                ctx.gSbf[64:128, :].rearrange('p (h e) -> p h e', e=128)[:, :, 64:128],
                ctx.gS[64:128, par * 512:par * 512 + 512]
                   .rearrange('p (h e) -> p h e', e=64))),
                label=f'gc{ch}sbfb')
            m_kt = emit_transposes(
                ctx, ctx.gkpT, ctx.gkp,
                (lambda j: ctx.gkp[:, j * 128:(j + 1) * 128]),
                deps=[m_kp, marks_get(ctx, 'gkp_war')], tag=f'gc{ch}kt')
            oev = []
            for hp in range(8):
                def mmo2(s, hp=hp):
                    return s.matmul(
                        ctx.ps_u[:, 0:128],
                        ctx.gqp[:, hp * 128:hp * 128 + 128],
                        ctx.gSbf[:, hp * 128:hp * 128 + 128],
                        start=True, stop=True)
                m_o2 = p.add('T', mmo2,
                             deps=[m_sbf, m_qp, marks_get(ctx, 'g_psu_war')],
                             label=f'gc{ch}o2_{hp}')
                def mmt(s, hp=hp, cw=cw):
                    return s.matmul(
                        ctx.ps_u[:, 384:512],
                        ctx.gkp[:, hp * 128:hp * 128 + 128],
                        ctx.gv[:, cw * 1024 + hp * 128:cw * 1024 + hp * 128 + 128],
                        start=True, stop=True)
                t_ms = [p.add('T', mmt, deps=[m_kt, pm['v']],
                              label=f'gc{ch}T{hp}')]
                def mma0(s, hp=hp):
                    return s.matmul(
                        ctx.ps_u[:, 128:256],
                        ctx.gkpT[0:64, hp * 128:hp * 128 + 128],
                        ctx.gqp[0:64, hp * 128:hp * 128 + 128],
                        start=True, stop=True, tile_position=(0, 0))
                m_a0 = p.add('T', mma0, deps=[m_kp],
                             label=f'gc{ch}A{hp}_0')
                def mma1(s, hp=hp):
                    return s.matmul(
                        ctx.ps_f[:, 0:128],
                        ctx.gkpT[64:128, hp * 128:hp * 128 + 128],
                        ctx.gqp[64:128, hp * 128:hp * 128 + 128],
                        start=True, stop=True, tile_position=(64, 0))
                m_a1 = p.add('T', mma1,
                             deps=[marks_get(ctx, 'g_psf_war')],
                             label=f'gc{ch}A{hp}_1')
                def mev0(s):
                    return s.tensor_tensor(ctx.gA[:, 0:128], ctx.ps_u[:, 128:256],
                                           ctx.trimask[:], op=ALU.mult)
                m_A0 = p.add('V', mev0, deps=[m_a0, marks['tri'],
                                              marks_get(ctx, 'gA_war')],
                             label=f'gc{ch}Am{hp}0')
                def mev1(s):
                    return s.tensor_tensor(ctx.gA[:, 128:256], ctx.ps_f[:, 0:128],
                                           ctx.trimask[:], op=ALU.mult)
                m_A1 = p.add('V', mev1, deps=[m_a1], label=f'gc{ch}Am{hp}1')
                m_A = m_A1
                o_ms = [m_o2]
                for hh in range(2):
                    def mmo1(s, hp=hp, hh=hh, cw=cw):
                        return s.matmul(
                            ctx.ps_u[:, 256 + hh * 64:256 + hh * 64 + 64],
                            ctx.gA[:, hh * 128:hh * 128 + 128],
                            ctx.gv[:, cw * 1024 + hp * 128 + hh * 64:
                                   cw * 1024 + hp * 128 + hh * 64 + 64],
                            start=True, stop=True)
                    o_ms.append(p.add('T', mmo1,
                                      deps=[m_A0 if hh == 0 else m_A1, pm['v']],
                                      label=f'gc{ch}o1_{hp}_{hh}'))
                def oev_a(s, hp=hp):
                    return s.tensor_copy(ctx.gob[:, hp * 128:hp * 128 + 128],
                                         ctx.ps_u[:, 256:384])
                p.add('V', oev_a, deps=o_ms + t_ms + [marks_get(ctx, 'gob_war')],
                      label=f'gc{ch}oea{hp}')
                def oev_(s, hp=hp):
                    return s.tensor_tensor(ctx.gob[:, hp * 128:hp * 128 + 128],
                                           ctx.gob[:, hp * 128:hp * 128 + 128],
                                           ctx.ps_u[:, 0:128], op=ALU.add)
                m_oe = p.add('V', oev_, label=f'gc{ch}oe{hp}')
                oev.append(m_oe)
                def su1a(s, hp=hp, par=par):
                    return s.tensor_tensor(
                        ctx.gS[0:64, (1 - par) * 512 + hp * 64:
                               (1 - par) * 512 + hp * 64 + 64],
                        ctx.gS[0:64, par * 512 + hp * 64:par * 512 + hp * 64 + 64],
                        ctx.ps_u[0:64, 384:448], op=ALU.add)
                m_s1a = p.add('V', su1a, deps=t_ms + o_ms, label=f'gc{ch}su1a_{hp}')
                def su1b(s, hp=hp, par=par):
                    return s.tensor_tensor(
                        ctx.gS[64:128, (1 - par) * 512 + hp * 64:
                               (1 - par) * 512 + hp * 64 + 64],
                        ctx.gS[64:128, par * 512 + hp * 64:par * 512 + hp * 64 + 64],
                        ctx.ps_u[64:128, 448:512], op=ALU.add)
                m_s1b = p.add('V', su1b, label=f'gc{ch}su1b_{hp}')
                def su2(s, hp=hp, par=par, cw=cw):
                    return s.tensor_scalar(
                        ctx.gS[:, (1 - par) * 512 + hp * 64:
                               (1 - par) * 512 + hp * 64 + 64],
                        ctx.gS[:, (1 - par) * 512 + hp * 64:
                               (1 - par) * 512 + hp * 64 + 64],
                        ctx.gbT[:, hp * 128 + 127:hp * 128 + 128], None,
                        op0=ALU.mult)
                m_s2 = p.add('V', su2, label=f'gc{ch}su2_{hp}')
                marks_set(ctx, 'g_psu_war', p.barrier('V', [m_A0, m_oe, m_s2],
                                                      label=f'gc{ch}pu{hp}'))
                marks_set(ctx, 'g_psf_war', m_A1)
            m_S = p.barrier('V', [], label=f'gc{ch}S')
            marks_set(ctx, 'gS_war', m_S)
            m_bw = p.barrier('V', [m_S], label=f'gc{ch}bw')
            marks_set(ctx, 'gbT_war', m_bw)
            marks_set(ctx, 'geT_war', None)
            marks_set(ctx, 'geT_war', m_S)
            marks_set(ctx, 'gfT_war', m_S)
            m_qpw = p.barrier('T', [], label=f'gc{ch}qpw')
            marks_set(ctx, 'gqp_war', m_qpw)
            marks_set(ctx, 'gSbf_war', m_qpw)
            marks_set(ctx, 'gkpT_war', marks_get(ctx, 'gqp_war'))
            marks_set(ctx, 'gkp_war', marks_get(ctx, 'gqp_war'))
            marks_set(ctx, 'gA_war', marks_get(ctx, 'gqp_war'))

            m_ob = p.barrier('V', oev, label=f'gc{ch}ob')
            m_sqr = p.add('V', lambda s: s.tensor_tensor(ctx.gsg[:], ctx.gob[:],
                                                         ctx.gob[:],
                                                         op=ALU.mult),
                          deps=[m_ob, marks_get(ctx, 'gsg_war')],
                          label=f'gc{ch}sq')
            m_red = p.add('V', lambda s: s.tensor_reduce(
                ctx.gred[:].unsqueeze(2),
                ctx.gsg[:].rearrange('p (h e) -> p h e', e=64),
                axis=AX.X, op=ALU.add), label=f'gc{ch}red')
            m_r1 = p.add('S', lambda s: s.activation(ctx.gred[:], ctx.gred[:],
                                                     AF.Sqrt,
                                                     bias=ctx.consts[:, 0:1],
                                                     scale=ctx.consts[:, 5:6]),
                         deps=[m_red], label=f'gc{ch}r1')
            m_r2 = p.add('V', lambda s: s.reciprocal(ctx.gred[:], ctx.gred[:]),
                         deps=[m_r1], label=f'gc{ch}r2')
            sil_ms = []
            for half in range(2):
                psum = ctx.ps_d if half == 0 else ctx.ps_e
                ps_war = marks_get(ctx, f'g_ps{"de"[half]}_war')
                mm_last = None
                for kc in range(8):
                    def mm(s, kc=kc, half=half, psum=psum, cw=cw):
                        return s.matmul(
                            psum[:, 0:512],
                            ctx.xlnT[:, kc * 512 + cw * 128:kc * 512 + cw * 128 + 128],
                            ctx.wreg[:, W_GG + kc * 1024 + half * 512:
                                     W_GG + kc * 1024 + half * 512 + 512],
                            start=(kc == 0), stop=(kc == 7))
                    mm_last = p.add('T', mm,
                                    deps=([m_xT, ps_war, m_sqr] if kc == 0 else []),
                                    label=f'gc{ch}gp{half}_{kc}')
                m_sil = p.add('S', (lambda s, half=half, psum=psum: s.activation(
                    ctx.gsg[:, half * 512:half * 512 + 512], psum[:, 0:512],
                    AF.Silu)), deps=[mm_last, m_red], label=f'gc{ch}sil{half}')
                sil_ms.append(m_sil)
                marks_set(ctx, f'g_ps{"de"[half]}_war', m_sil)
            m_sg2 = p.add('V', lambda s: s.tensor_tensor(
                ctx.gsg[:].rearrange('p (h e) -> p h e', e=64),
                ctx.gsg[:].rearrange('p (h e) -> p h e', e=64),
                ctx.gred[:].unsqueeze(2).to_broadcast([128, 16, 64]),
                op=ALU.mult), deps=sil_ms + [m_r2], label=f'gc{ch}sg2')
            m_of = p.add('V', lambda s: s.tensor_tensor(ctx.gofb[:], ctx.gob[:],
                                                        ctx.gsg[:],
                                                        op=ALU.mult),
                         deps=[marks_get(ctx, 'gofb_war'),
                               marks_get(ctx, 'gqp_war')], label=f'gc{ch}of')
            marks_set(ctx, 'gsg_war', m_of)
            marks_set(ctx, 'gob_war', m_of)
            m_ofT = emit_transposes(
                ctx, ctx.gofb, ctx.gofT,
                (lambda j: ctx.gofT[:, j * 128:(j + 1) * 128]),
                deps=[m_of, marks_get(ctx, 'gofT_war'),
                      marks_get(ctx, 'gqp_war')], tag=f'gc{ch}ofT')
            wo_ms = []
            for half in range(2):
                psum = ctx.ps_d if half == 0 else ctx.ps_e
                mm_last = None
                for kc in range(8):
                    def mm(s, kc=kc, half=half, psum=psum):
                        return s.matmul(
                            psum[:, 0:512],
                            ctx.gofT[:, kc * 128:(kc + 1) * 128],
                            ctx.wreg[:, W_GO + kc * 1024 + half * 512:
                                     W_GO + kc * 1024 + half * 512 + 512],
                            start=(kc == 0), stop=(kc == 7))
                    mm_last = p.add('T', mm,
                                    deps=([m_ofT,
                                           marks_get(ctx, f'g_ps{"de"[half]}_war')]
                                          if kc == 0 else []),
                                    label=f'gc{ch}wo{half}_{kc}')
                wo_ms.append(mm_last)
            marks_set(ctx, 'gofT_war', wo_ms[-1])
            m_qpw2 = p.barrier('T', [], label=f'gc{ch}qpw2')
            marks_set(ctx, 'gqp_war', m_qpw2)
            marks_set(ctx, 'gkpT_war', m_qpw2)
            marks_set(ctx, 'gofb_war', None)
            xb = ctx.xb[2]
            x3b = ctx.gx3[cw % 2]
            tok0 = ch * 128
            m_ldx = p.add('SQ', (lambda s, tok0=tok0, xb=xb: s.dma_start(
                out=xb[:], in_=x2_ap[tok0:tok0 + 128, :])),
                deps=[marks_get(ctx, 'gxb2_war'), marks.get('xb2_war')],
                label=f'gc{ch}ldx', slot='xb2')
            marks['xb2_war'] = None
            add_ms = []
            for half in range(2):
                psum = ctx.ps_d if half == 0 else ctx.ps_e
                def ad(s, half=half, psum=psum, xb=xb, x3b=x3b):
                    return s.tensor_tensor(x3b[:, half * 512:half * 512 + 512],
                                           xb[:, half * 512:half * 512 + 512],
                                           psum[:, 0:512], op=ALU.add)
                m_ad = p.add('V', ad, deps=[wo_ms[half], m_ldx,
                                            marks_get(ctx, 'gx3_war')],
                             label=f'gc{ch}add{half}')
                marks_set(ctx, f'g_ps{"de"[half]}_war', m_ad)
                add_ms.append(m_ad)
            m_x3 = p.barrier('V', add_ms, label=f'gc{ch}x3')
            marks_set(ctx, 'gxb2_war', m_x3)
            m_st1 = p.add('AQ', (lambda s, tok0=tok0, x3b=x3b: s.dma_start(
                out=x3_ap[tok0:tok0 + 128, :], in_=x3b[:])),
                deps=[m_x3], label=f'gc{ch}st3', slot='x3st')
            p.add('V', (lambda s, x3b=x3b: s.bn_stats(ctx.stat[:, 0:6],
                                                      x3b[:, 0:512])),
                  deps=[m_x3])
            p.add('V', (lambda s, x3b=x3b: s.bn_stats(ctx.stat[:, 6:12],
                                                      x3b[:, 512:1024])))
            m_ag = p.add('V', lambda s: s.bn_aggr(ctx.stat[:, 12:14],
                                                  ctx.stat[:, 0:12]))
            m_sq2 = p.add('S', lambda s: s.activation(ctx.colv[1][:],
                                                      ctx.stat[:, 13:14],
                                                      AF.Sqrt,
                                                      bias=ctx.consts[:, 0:1]),
                          deps=[m_ag], label=f'gc{ch}lsq')
            p.add('V', lambda s: s.reciprocal(ctx.colv[2][:], ctx.colv[1][:]),
                  deps=[m_sq2])
            m_nb = p.add('V', lambda s: s.scalar_tensor_tensor(
                ctx.colv[3][:], ctx.stat[:, 12:13], -1.0, ctx.colv[2][:],
                op0=ALU.mult, op1=ALU.mult))
            m_lnw = p.add('S', (lambda s, x3b=x3b: s.activation(
                ctx.junkw[:], x3b[:], AF.Identity, bias=ctx.colv[3][:],
                scale=ctx.colv[2][:])),
                deps=[m_nb, marks.get('junkw_free')], label=f'gc{ch}lnw')
            m_st2 = p.add('AQ', (lambda s, tok0=tok0: s.dma_start(
                out=xln3_ap[tok0:tok0 + 128, :], in_=ctx.junkw[:])),
                deps=[m_lnw], label=f'gc{ch}st4', slot='xln3st')
            marks['junkw_free'] = m_st2
            m_x3w = p.barrier('V', [m_st1, m_st2], label=f'gc{ch}x3w')
            marks_set(ctx, 'gx3_war', m_x3w)
            marks_set(ctx, 'gob_war', m_x3w)
            x3_stores.extend([m_st1, m_st2])
            glast_T.append(marks_get(ctx, 'gqp_war'))
        m_gu = p.barrier('T', [], label=f'gg{g}used')
        m_gv = p.barrier('V', [], label=f'gg{g}usedv')
        marks_set(ctx, 'xlnT_war', m_gu)
        marks_set(ctx, 'gv_war', m_gu)
        marks_set(ctx, 'gqT_war', m_gv)
        marks_set(ctx, 'gkT_war', m_gv)
        marks_set(ctx, 'ggk_war', m_gv)
        marks_set(ctx, 'gt1_war', m_gu)
    m_all = p.barrier('V', x3_stores, label='gla_done')
    marks['gla_done'] = m_all
    return m_all



def build_all(T=8192, NQ=1024, NKV=2048, n_iters=50, bis_iters=30,
              detect_races=True, upto='all'):
    ctx = Ctx(T, NQ, NKV, n_iters=n_iters, bis_iters=bis_iters,
              detect_races=detect_races)
    declare(ctx)
    marks = {}
    prep_consts(ctx, marks)
    prep_weights_ffn(ctx, marks, '1')
    prep_null_kv(ctx, marks)
    phase_ln_scores(ctx, marks, ctx.d['x'], ctx.d['XLN'], '1')
    r1 = routing(ctx, marks, '1', marks['scores_1'])
    declare_light(ctx, '1')
    prep_light_memsets(ctx, marks, '1')
    if upto == 'light1':
        ctx.dbg_onorm = ctx.dram('d_onormall', [T, 512], BF16, 'ExternalOutput')
        ctx.dbg_esim = ctx.dram('d_esimall', [T, 2048], BF16, 'ExternalOutput')
    phase_light(ctx, marks, ctx.d['XLN'], ctx.d['x'], ctx.d['X2'], '1')
    if upto == 'light1':
        dumps = [('d_qT', ctx.qTl[0], BF16), ('d_kT', ctx.kTl[0], BF16),
                 ('d_v0', ctx.vS[0], BF16), ('d_v1', ctx.vS[1], BF16),
                 ('d_v3', ctx.vS[3], BF16), ('d_v4', ctx.vS[4], BF16),
                 ('d_esim', ctx.esim[0], BF16), ('d_onorm', ctx.onorm, BF16),
                 ('d_onormT', ctx.onormT, BF16), ('d_xn2T', ctx.xn2T[0], BF16)]
        dps = []
        for nm, t, dt_ in dumps:
            ap = ctx.dram(nm, list(t.shape), dt_, 'ExternalOutput')
            dps.append(ctx.p.add('AQ', (lambda s, ap=ap, t=t: s.dma_start(
                out=ap, in_=t[:])), deps=[marks['x2_done_1']],
                label=f'dump{nm}', slot='dbg'))
        ctx.p.barrier('S', [marks['x2_done_1'], r1[('idx', 0)],
                            r1[('idx', 1)]] + dps, label='FINAL')
        ctx.p.emit()
        return ctx, marks
    ctx.free_group('lightA1')
    declare_heavy(ctx, '1')
    prep_heavy_memsets(ctx, marks, '1', dep=marks['x2_done_1'])
    phase_heavy(ctx, marks, ctx.d['XLN'], ctx.d['X2'], '1', r1)
    ctx.free_group('lightB1')
    ctx.free_group('heavy1')
    if upto == 'heavy1':
        ctx.p.barrier('S', [marks['heavy_1']], label='FINAL')
        ctx.p.emit()
        return ctx, marks
    prep_weights_gla(ctx, marks, dep=None)
    declare_gla(ctx)
    phase_gla(ctx, marks, ctx.d['X2'], ctx.d['X3'], ctx.d['XLN3'],
              dep_start=marks['heavy_1'])
    ctx.free_group('glaR')
    ctx.free_group('gla')
    if upto == 'gla':
        ctx.p.barrier('S', [marks['gla_done']], label='FINAL')
        ctx.p.emit()
        return ctx, marks
    marks['psum_free'] = marks['gla_done']
    prep_weights_ffn(ctx, marks, '2', dep=None, reload=True)
    phase_ln_scores(ctx, marks, None, ctx.d['XLN3'], '2',
                    dep=marks['gla_done'])
    marks['xln_done_2'] = marks['gla_done']
    r2 = routing(ctx, marks, '2', marks['scores_2'],
                 extra_dep=marks['heavy_1'])
    declare_light(ctx, '2')
    prep_light_memsets(ctx, marks, '2', dep=marks['gla_done'])
    phase_light(ctx, marks, ctx.d['XLN3'], ctx.d['X3'], ctx.d['out'], '2')
    ctx.free_group('lightA2')
    declare_heavy(ctx, '2')
    prep_heavy_memsets(ctx, marks, '2', dep=marks['x2_done_2'])
    phase_heavy(ctx, marks, ctx.d['XLN3'], ctx.d['out'], '2', r2)
    ctx.p.barrier('S', [marks['heavy_2'], marks['x2_done_2']], label='FINAL')
    ctx.p.emit()
    return ctx, marks



_NC_CACHE = {}


def _get_program():
    key = 'main'
    if key not in _NC_CACHE:
        ctx, marks = build_all()
        _NC_CACHE[key] = ctx
    return _NC_CACHE[key]


def kernel(**inputs):
    x = np.ascontiguousarray(inputs['L0_attn_x'], dtype=np.float32)
    B = x.shape[0]
    ctx = _get_program()

    w_kv = np.ascontiguousarray(
        np.asarray(inputs['L0_ffn_w_kv_h'], np.float32)
        .reshape(1024, 8, 2, 64).transpose(0, 2, 1, 3).reshape(1024, 1024))
    wg2a = np.ascontiguousarray(np.concatenate(
        [np.asarray(inputs['L0_attn_Wg2'], np.float32),
         np.asarray(inputs['L0_attn_bg2'], np.float32)[None, :]], axis=0))
    base = {
        'Wq': np.ascontiguousarray(inputs['L0_attn_Wq'], np.float32),
        'Wk': np.ascontiguousarray(inputs['L0_attn_Wk'], np.float32),
        'Wv': np.ascontiguousarray(inputs['L0_attn_Wv'], np.float32),
        'Wg': np.ascontiguousarray(inputs['L0_attn_Wg'], np.float32),
        'Wo': np.ascontiguousarray(inputs['L0_attn_Wo'], np.float32),
        'Wg1': np.ascontiguousarray(inputs['L0_attn_Wg1'], np.float32),
        'Wg2a': wg2a,
        'g_norm_w': np.ascontiguousarray(
            np.asarray(inputs['L0_attn_g_norm_w'], np.float32)[None, :]),
        'qkv_l': np.ascontiguousarray(inputs['L0_ffn_w_qkv_l'], np.float32),
        'w_out_l': np.ascontiguousarray(inputs['L0_ffn_w_out_l'], np.float32),
        'null_q': np.ascontiguousarray(
            np.asarray(inputs['L0_ffn_null_q'], np.float32)[None, :]),
        'rt_q': np.ascontiguousarray(inputs['L0_ffn_rt_q'], np.float32),
        'rt_kv': np.ascontiguousarray(inputs['L0_ffn_rt_kv'], np.float32),
        'rms_g': np.ascontiguousarray(
            np.asarray(inputs['L0_ffn_rms_g'], np.float32)[None, :]),
        'w_q_h': np.ascontiguousarray(inputs['L0_ffn_w_q_h'], np.float32),
        'w_kv_h': w_kv,
        'null_kv': np.ascontiguousarray(
            np.asarray(inputs['L0_ffn_null_kv'], np.float32).reshape(2, 512)),
        'w_out_h': np.ascontiguousarray(inputs['L0_ffn_w_out_h'], np.float32),
        'ln_g': np.ascontiguousarray(
            np.asarray(inputs['L0_ffn_ln_g'], np.float32)[None, :]),
        'ln_b': np.ascontiguousarray(
            np.asarray(inputs['L0_ffn_ln_b'], np.float32)[None, :]),
    }
    in_maps = []
    for b in range(B):
        m = dict(base)
        m['x'] = np.ascontiguousarray(x[b])
        in_maps.append(m)
    res = run_bass_kernel_spmd(ctx.nc, in_maps, core_ids=list(range(B)))
    out = np.stack([np.asarray(res.results[b]['out']) for b in range(B)], axis=0)
    return out.astype(np.float32)


# version 07 (final)
# speedup vs baseline: 1.0552x; 1.0050x over previous
import numpy as np
import concourse.bass as bass
import concourse.mybir as mybir
from concourse.bass import IndirectOffsetOnAxis
from concourse.bass_utils import run_bass_kernel_spmd

F32 = mybir.dt.float32
BF16 = mybir.dt.bfloat16
U32 = mybir.dt.uint32
U8 = mybir.dt.uint8
AF = mybir.ActivationFunctionType
ALU = mybir.AluOpType
AX = mybir.AxisListType


COMPUTE = ('T', 'V', 'S', 'P')
QUEUES = ('SQ', 'AQ', 'GQ')
ALL_ENG = COMPUTE + QUEUES
STREAM = {'T': 'tensor', 'V': 'vector', 'S': 'scalar', 'P': 'gpsimd',
          'SQ': 'sync', 'AQ': 'scalar', 'GQ': 'gpsimd'}
INC = {'T': 1, 'V': 1, 'S': 1, 'P': 1, 'SQ': 16, 'AQ': 16, 'GQ': 16}
AUTO_SERIAL = ('V', 'S', 'P')


class _Op:
    __slots__ = ('eng', 'idx', 'fn', 'deps', 'needs_inc', 'label', 'slot')

    def __init__(self, eng, idx, fn, deps, label, slot=None):
        self.eng, self.idx, self.fn, self.deps = eng, idx, fn, deps
        self.needs_inc = False
        self.label = label
        self.slot = slot


class Prog:
    def __init__(self, nc):
        self.nc = nc
        self.seq = []
        self.n = {e: 0 for e in ALL_ENG}
        self.bounds_vals = set()
        self._bc_regs = {}

    def bc_reg(self, val):
        return self._bc_regs[val]

    def add(self, eng, fn, deps=(), label='', slot=None, barrier=False):
        deps = [dd for dd in deps if dd is not None]
        if (eng in AUTO_SERIAL or (barrier and eng in COMPUTE)) \
                and self.n[eng] > 0:
            deps.append((eng, self.n[eng] - 1))
        if eng in QUEUES:
            assert slot is not None, f'queue op {label} needs a slot'
        op = _Op(eng, self.n[eng], fn, deps, label, slot)
        self.n[eng] += 1
        self.seq.append(op)
        return (eng, op.idx)

    def barrier(self, eng, deps, label='barrier'):
        return self.add(eng, lambda s: s.nop(), deps, label, barrier=True)

    def emit(self):
        self.label_map = {}
        nc = self.nc
        by_eng = {e: [] for e in ALL_ENG}
        for op in self.seq:
            by_eng[op.eng].append(op)
        for op in self.seq:
            if op.eng in QUEUES:
                op.needs_inc = True
            for (de, di) in op.deps:
                by_eng[de][di].needs_inc = True
        slot_of = {}
        slot_counts = {}
        for e in ALL_ENG:
            c = 0
            for op in by_eng[e]:
                if op.needs_inc:
                    if e in QUEUES:
                        key = (e, op.slot)
                        slot_counts[key] = slot_counts.get(key, 0) + 1
                        slot_of[(e, op.idx)] = (key, slot_counts[key] * INC[e])
                    else:
                        slot_of[(e, op.idx)] = ((e, 0), (c + 1) * INC[e])
                    c += 1

        self._cms = []
        sems = {}
        for e in COMPUTE:
            cm = nc.semaphore(f'sem_{e}_0')
            sems[(e, 0)] = cm.__enter__()
            self._cms.append(cm)
        for key in slot_counts:
            cm = nc.semaphore(f'sem_{key[0]}_{key[1]}')
            sems[key] = cm.__enter__()
            self._cms.append(cm)
        blk_cm = nc.Block()
        blk = blk_cm.__enter__()
        self._cms.append(blk_cm)

        streams = {}
        for op in self.seq:
            streams.setdefault(STREAM[op.eng], []).append(op)

        def make_body(ops_list, is_gpsimd=False):
            def body(section):
                if is_gpsimd:
                    for val in sorted(self.bounds_vals):
                        r = section.alloc_register(f'bc{val}')
                        section.reg_mov(r, val)
                        self._bc_regs[val] = r
                for op in ops_list:
                    need = {}
                    for (de, di) in op.deps:
                        key, v = slot_of[(de, di)]
                        need[key] = max(need.get(key, 0), v)
                    for key, v in need.items():
                        section.wait_ge(sems[key], v)
                    inst = op.fn(section)
                    if inst is not None:
                        try:
                            self.label_map[inst.ins.name] = (op.eng, op.idx,
                                                             op.label)
                        except Exception:
                            pass
                    if op.needs_inc:
                        assert inst is not None, f'{op.label}: needs inc, no inst'
                        key, v = slot_of[(op.eng, op.idx)]
                        inst.then_inc(sems[key], INC[op.eng])
            return body

        for stream_name, ops_list in streams.items():
            getattr(blk, stream_name)(
                make_body(ops_list, is_gpsimd=(stream_name == 'gpsimd')))
        return blk

    def close(self, exc=(None, None, None)):
        for cm in reversed(self._cms):
            cm.__exit__(*exc)


ARENA_NB = 82688


class Ctx:
    def __init__(self, T, NQ, NKV, n_iters=50, bis_iters=30, detect_races=True):
        self.T = T
        self.F = T // 128
        self.NG = T // 512
        self.NQ, self.NKV = NQ, NKV
        self.n_iters = n_iters
        self.bis_iters = bis_iters
        self.nc = bass.Bass('TRN2', target_bir_lowering=False,
                            detect_race_conditions=detect_races)
        self.p = Prog(self.nc)
        self._sb_bytes = 0
        self._cms = []
        self._wstg_war = [None] * 2
        self._arena = None
        self._aL, self._aR = 0, ARENA_NB
        self._apeak = 0
        self._groups = {}
        self._gstack = {'left': [], 'right': []}

    def sb(self, name, shape, dtype, group=None, side=None):
        if group is None:
            cm = self.nc.sbuf_tensor(name, list(shape), dtype)
            t = cm.__enter__()
            self._cms.append(cm)
            self._sb_bytes += int(np.prod(shape[1:])) * mybir.dt.size(dtype) * 128
            return t
        if self._arena is None:
            cm = self.nc.sbuf_tensor('arena', [128, ARENA_NB], U8)
            self._arena = cm.__enter__()
            self._cms.append(cm)
            self._sb_bytes += ARENA_NB * 128
        nb = int(np.prod(shape[1:])) * mybir.dt.size(dtype)
        nb_al = (nb + 31) & ~31
        sd = side or 'left'
        if group not in self._groups:
            self._groups[group] = (sd, self._aL if sd == 'left' else self._aR)
            self._gstack[sd].append(group)
        assert self._groups[group][0] == sd, f'{group}: side mismatch'
        assert self._gstack[sd][-1] == group, \
            f'{group}: allocations into non-top group'
        if sd == 'left':
            off = self._aL
            self._aL += nb_al
        else:
            self._aR -= nb_al
            off = self._aR
        assert self._aL <= self._aR, f'arena overflow at {name}'
        self._apeak = max(self._apeak, self._aL + (ARENA_NB - self._aR))
        return self._arena[0:shape[0], off:off + nb].bitcast(dtype)

    def free_group(self, group):
        sd, saved = self._groups.pop(group)
        assert self._gstack[sd][-1] == group, f'free {group}: not stack top'
        self._gstack[sd].pop()
        if sd == 'left':
            self._aL = saved
        else:
            self._aR = saved

    def ps(self, name, shape, dtype=F32):
        cm = self.nc.psum_tensor(name, list(shape), dtype)
        t = cm.__enter__()
        self._cms.append(cm)
        return t

    def dram(self, name, shape, dtype, kind):
        return self.nc.dram_tensor(name, list(shape), dtype, kind=kind).ap()

    def close(self):
        self.p.close()
        for cm in reversed(self._cms):
            cm.__exit__(None, None, None)



W_QKV = 0
W_OUTL = 12288
W_QH = 16384
W_KVH = 20480
W_OUTH = 28672
W_GQ, W_GK, W_GV, W_GG, W_GO = 0, 8192, 16384, 24576, 32768


def declare(ctx):
    T, DM = ctx.T, 1024
    d = {}
    d['x'] = ctx.dram('x', [T, DM], F32, 'ExternalInput')
    d['out'] = ctx.dram('out', [T, DM], F32, 'ExternalOutput')
    for nm, shp in [('Wq', [DM, DM]), ('Wk', [DM, DM]), ('Wv', [DM, DM]),
                    ('Wg', [DM, DM]), ('Wo', [DM, DM]), ('Wg1', [DM, 16]),
                    ('Wg2a', [17, DM]), ('g_norm_w', [1, 64]),
                    ('qkv_l', [DM, 1536]), ('w_out_l', [512, DM]),
                    ('null_q', [1, DM]), ('rt_q', [1, DM]), ('rt_kv', [1, DM]),
                    ('rms_g', [1, DM]), ('w_q_h', [DM, 512]),
                    ('w_kv_h', [DM, DM]), ('null_kv', [2, 512]),
                    ('w_out_h', [512, DM]), ('ln_g', [1, DM]), ('ln_b', [1, DM])]:
        d[nm] = ctx.dram(nm, shp, F32, 'ExternalInput')
    d['XLN'] = ctx.dram('XLN', [T, DM], F32, 'Internal')
    d['WSTASH'] = ctx.dram('WSTASH', [128, 32768], BF16, 'Internal')
    d['X2'] = ctx.dram('X2', [T, DM], F32, 'Internal')
    d['X3'] = ctx.dram('X3', [T, DM], F32, 'Internal')
    d['XLN3'] = ctx.dram('XLN3', [T, DM], F32, 'Internal')
    d['LIGHT'] = ctx.dram('LIGHT', [T, DM], F32, 'Internal')
    d['LISTQ'] = ctx.dram('LISTQ', [max(ctx.NQ, 128)], U32, 'Internal')
    d['LISTKV'] = ctx.dram('LISTKV', [max(ctx.NKV, 128)], U32, 'Internal')
    ctx.d = d

    F = ctx.F
    ctx.wreg = ctx.sb('wreg', [128, 40960], BF16)
    ctx.wstg = [ctx.sb(f'wstg{i}', [128, 512], F32) for i in range(2)]
    ctx.ones_col = ctx.sb('ones_col', [128, 1], F32)
    ctx.ones_colb = ctx.sb('ones_colb', [128, 1], BF16)
    ctx.ones_rowb = ctx.sb('ones_rowb', [1, 512], BF16)
    ctx.ones_rowf = ctx.sb('ones_rowf', [1, 128], F32)
    ctx.half_rowf = ctx.sb('half_rowf', [1, 128], F32)
    ctx.consts = ctx.sb('consts', [128, 8], F32)
    ctx.ident_bf = ctx.sb('ident_bf', [128, 128], BF16)
    ctx.ident_f = ctx.sb('ident_f', [128, 128], F32)
    ctx.trimask = ctx.sb('trimask', [128, 128], BF16)
    ctx.rtq_b = ctx.sb('rtq_b', [128, DM], F32)
    ctx.rtkv_b = ctx.sb('rtkv_b', [128, DM], F32)
    ctx.nullh_b = ctx.sb('nullh_b', [128, DM], F32)
    ctx.smallw = ctx.sb('smallw', [128, 40], F32)
    ctx.c0row = ctx.sb('c0row', [1, 1536], BF16)
    ctx.iota_tok = ctx.sb('iota_tok', [128, F], F32)
    ctx.gnw_col = ctx.sb('gnw_col', [128, 1], F32)
    ctx.wg1_bf = ctx.sb('wg1_bf', [128, 128], BF16)
    ctx.wg2a_bf = ctx.sb('wg2a_bf', [17, 1024], BF16)
    ctx.nullk_bf = ctx.sb('nullk_bf', [128, 8], BF16)
    ctx.nullv_bf = ctx.sb('nullv_bf', [1, 520], BF16)
    ctx.s2 = ctx.sb('s2', [128, 2 * F], F32)
    ctx.relu2 = ctx.sb('relu2', [128, 2 * F], F32)
    ctx.e2 = ctx.sb('e2', [128, 2 * F], F32)
    ctx.racc = ctx.sb('racc', [128, 8], F32)
    ctx.rsc = ctx.sb('rsc', [128, 32], F32)
    ctx.aval = ctx.sb('aval', [128, 2], F32)
    ctx.logk = ctx.sb('logk', [128, 2], F32)
    ctx.selscr = ctx.sb('selscr', [128, 4 * F], F32)
    ctx.dest_f = ctx.sb('dest_f', [128, F], F32)
    ctx.dest_u = ctx.sb('dest_u', [128, F], U32)
    ctx.iota_u = ctx.sb('iota_u', [128, F], U32)
    ctx.idxq = ctx.sb('idxq', [128, 16], U32)
    ctx.idxkv = ctx.sb('idxkv', [128, 16], U32)
    ctx.xb = [ctx.sb(f'xb{i}', [128, DM], F32) for i in range(3)]
    ctx.stat = ctx.sb('stat', [128, 16], F32)
    ctx.colv = [ctx.sb(f'colv{i}', [128, 1], F32) for i in range(8)]
    ctx.zeroF = ctx.sb('zeroF', [128, max(F, 32)], F32)
    ctx.zero128 = ctx.sb('zero128', [128, 128], F32)
    ctx.skipc = ctx.sb('skipc', [128, 16], U32)
    ctx.junk = ctx.sb('junk', [128, 128], F32)
    ctx.junkw = ctx.sb('junkw', [128, DM], F32)

    ctx.ps_a = ctx.ps('ps_a', [128, 512])
    ctx.ps_b = ctx.ps('ps_b', [128, 512])
    ctx.ps_c = ctx.ps('ps_c', [128, 512])
    ctx.ps_d = ctx.ps('ps_d', [128, 512])
    ctx.ps_e = ctx.ps('ps_e', [128, 512])
    ctx.ps_f = ctx.ps('ps_f', [128, 512])
    ctx.ps_tr = ctx.ps('ps_tr', [128, 1024], BF16)
    ctx.ps_u = ctx.ps('ps_u', [128, 512])


def prep_consts(ctx, marks):
    p = ctx.p
    F = ctx.F
    p.add('P', lambda s: s.memset(ctx.ones_col[:], 1.0), label='ones')
    m_onesb = p.add('P', lambda s: s.memset(ctx.ones_colb[:], 1.0))
    m_onesr = p.add('P', lambda s: s.memset(ctx.ones_rowb[:], 1.0))
    m_onesrf = p.add('P', lambda s: s.memset(ctx.ones_rowf[:], 1.0))
    m_half = p.add('P', lambda s: s.memset(ctx.half_rowf[:], 0.5))
    p.add('P', lambda s: s.memset(ctx.consts[:, 0:1], 1e-5))
    p.add('P', lambda s: s.memset(ctx.consts[:, 1:2], 1e-12))
    p.add('P', lambda s: s.memset(ctx.consts[:, 2:3], -1.0))
    p.add('P', lambda s: s.memset(ctx.consts[:, 3:4], 0.0625))
    p.add('P', lambda s: s.memset(ctx.consts[:, 4:5], -0.0625))
    p.add('P', lambda s: s.memset(ctx.consts[:, 5:6], 1.0 / 64.0))
    m_iotaf = p.add('P', lambda s: s.iota(ctx.iota_tok[:], [[1, F]], base=0,
                                          channel_multiplier=F,
                                          allow_small_or_imprecise_dtypes=True))
    m_iotau = p.add('P', lambda s: s.iota(ctx.iota_u[:], [[1, F]], base=0,
                                          channel_multiplier=F))
    m_ci = p.add('P', lambda s: s.iota(ctx.junk[:], [[1, 128]], base=0,
                                       channel_multiplier=0,
                                       allow_small_or_imprecise_dtypes=True))
    m_pi = p.add('P', lambda s: s.iota(ctx.colv[0][:], [[0, 1]], base=0,
                                       channel_multiplier=1,
                                       allow_small_or_imprecise_dtypes=True))
    m_id = p.add('V', lambda s: s.tensor_scalar(ctx.ident_bf[:], ctx.junk[:],
                                                ctx.colv[0][:], None,
                                                op0=ALU.is_equal),
                 deps=[m_ci, m_pi], label='ident')
    m_idf = p.add('V', lambda s: s.tensor_scalar(ctx.ident_f[:], ctx.junk[:],
                                                 ctx.colv[0][:], None,
                                                 op0=ALU.is_equal),
                  deps=[m_ci, m_pi], label='identf')
    marks['identf'] = m_idf
    m_tri = p.add('V', lambda s: s.tensor_scalar(ctx.trimask[:], ctx.junk[:],
                                                 ctx.colv[0][:], None,
                                                 op0=ALU.is_ge),
                  deps=[m_ci, m_pi], label='trimask')
    import math
    lkq = float(np.log(np.float32(min(ctx.NQ * 9.0 / 8.0, float(ctx.T)))
                       ).astype(np.float32))
    lkkv = float(np.log(np.float32(min(ctx.NKV * 9.0 / 8.0, float(ctx.T)))
                        ).astype(np.float32))
    p.add('P', lambda s: s.memset(ctx.zeroF[:], 0.0))
    p.add('P', lambda s: s.memset(ctx.zero128[:], 0.0))
    m_skip = p.add('P', lambda s: s.iota(ctx.skipc[:], [[0, 16]],
                                         base=(1 << 22) + 7,
                                         channel_multiplier=0))
    marks['skipc'] = m_skip
    p.add('P', lambda s: s.memset(ctx.logk[:, 0:1], lkq))
    m_lk = p.add('P', lambda s: s.memset(ctx.logk[:, 1:2], lkkv))
    marks['ident'] = m_id
    marks['tri'] = m_tri
    marks['onesb'] = m_onesb
    marks['onesr'] = m_onesr
    marks['onesrf'] = m_onesrf
    marks['half'] = m_half
    marks['iota_u'] = m_iotau
    marks['iota_f'] = m_iotaf
    marks['consts'] = p.barrier('P', [m_lk], label='consts_done')

    d = ctx.d
    rowbuf = ctx.junkw
    war = None
    psu_war = None
    for nm, dst, scale_row, smark in [
            ('rt_q', ctx.rtq_b, ctx.ones_rowf, m_onesrf),
            ('rt_kv', ctx.rtkv_b, ctx.ones_rowf, m_onesrf),
            ('null_q', ctx.nullh_b, ctx.half_rowf, m_half)]:
        m_ld = p.add('SQ', (lambda s, nm=nm: s.dma_start(out=rowbuf[0:1, :],
                                                         in_=d[nm])),
                     deps=[war] if war else [], label=f'ld_{nm}', slot='junkw')
        last = None
        for off in range(0, 1024, 512):
            def mm(s, off=off, scale_row=scale_row):
                return s.matmul(ctx.ps_u[:, 0:512], scale_row[:, 0:128],
                                rowbuf[0:1, off:off + 512], start=True, stop=True)
            m1 = p.add('T', mm, deps=[m_ld, smark, psu_war],
                       label=f'bc_{nm}_{off}')
            def cp(s, off=off, dst=dst):
                return s.tensor_copy(dst[:, off:off + 512], ctx.ps_u[:, 0:512])
            m2 = p.add('V', cp, deps=[m1], label=f'bccp_{nm}_{off}')
            psu_war = m2
            last = m2
        war = last
    marks['rt_b'] = war
    marks['rowbuf_free'] = war
    marks['junkw_free'] = war
    marks['psu_war'] = psu_war



def load_fold_weight(ctx, dram_ap, n_cols, wreg_off, fold, tag,
                     c0_dst=None, kchunks=8, colscale=None, extra_dep=None):
    p = ctx.p
    PIECE = 512
    npieces = (n_cols + PIECE - 1) // PIECE
    war = ctx._wstg_war
    if extra_dep is not None:
        m_x = p.barrier('V', [war[0], war[1], extra_dep],
                        label=f'w{tag}_xdep')
        war[0] = m_x
        war[1] = m_x
    cast_last = None
    c0_last = None
    si = 0
    for c in range(kchunks):
        for j in range(npieces):
            w = min(PIECE, n_cols - j * PIECE)
            sbuf = ctx.wstg[si % 2]
            def ld(s, c=c, j=j, w=w, sbuf=sbuf):
                return s.dma_start(out=sbuf[:, 0:w],
                                   in_=dram_ap[c * 128:(c + 1) * 128,
                                               j * PIECE:j * PIECE + w])
            m_ld = p.add('SQ', ld, deps=[war[si % 2]], label=f'w{tag}_ld{c}_{j}',
                         slot=f'wstg{si % 2}')
            consumers = []
            if c0_dst is not None:
                psum_t, po, lnb_tile, m_lnb = c0_dst[j]
                def mmc0(s, c=c, w=w, sbuf=sbuf, psum_t=psum_t, po=po,
                         lnb_tile=lnb_tile):
                    return s.matmul(psum_t[0:1, po:po + w], lnb_tile[:, c:c + 1],
                                    sbuf[:, 0:w], start=(c == 0),
                                    stop=(c == kchunks - 1))
                c0_last = p.add('T', mmc0, deps=[m_ld, m_lnb],
                                label=f'w{tag}_c0_{c}_{j}')
                consumers.append(c0_last)
            dst = ctx.wreg[:, wreg_off + c * n_cols + j * PIECE:
                           wreg_off + c * n_cols + j * PIECE + w]
            if fold is not None:
                ft, ft_mark = fold
                cs = colscale(j) if colscale is not None else 1.0
                def cast(s, c=c, w=w, sbuf=sbuf, dst=dst, ft=ft, cs=cs):
                    return s.tensor_scalar(dst, sbuf[:, 0:w], ft[:, c:c + 1],
                                           cs, op0=ALU.mult, op1=ALU.mult)
                m_cast = p.add('V', cast, deps=[m_ld, ft_mark],
                               label=f'w{tag}_cast{c}_{j}')
            else:
                def cast(s, w=w, sbuf=sbuf, dst=dst):
                    return s.tensor_copy(dst, sbuf[:, 0:w])
                m_cast = p.add('V', cast, deps=[m_ld], label=f'w{tag}_cast{c}_{j}')
            consumers.append(m_cast)
            war[si % 2] = (p.barrier('V', consumers) if len(consumers) > 1
                           else m_cast)
            cast_last = m_cast
            si += 1
    return cast_last, c0_last


def load_vec_tile(ctx, dram_row, dst_cols, n, tag, dep=None, marks=None,
                  nrows=128):
    p = ctx.p
    nch = n // nrows
    m_ld = p.add('SQ', (lambda s: s.dma_start(out=ctx.junkw[0:1, 0:n],
                                              in_=dram_row)),
                 deps=[dep, marks.get('junkw_free') if marks else None],
                 label=f'vec_{tag}_ld', slot='junkw')
    tr_ms = []
    for c in range(nch):
        def tr(s, c=c):
            return s.transpose(ctx.ps_u[0:nrows, c:c + 1],
                               ctx.junkw[0:1, c * nrows:(c + 1) * nrows],
                               ctx.ident_f[0:1, 0:1])
        tr_ms.append(p.add('T', tr,
                           deps=[m_ld, marks['identf'] if marks else None,
                                 marks.get('_psu_war') if marks else None],
                           label=f'vec_{tag}_t{c}'))
    def ev(s):
        return s.tensor_copy(dst_cols[0:nrows, :], ctx.ps_u[0:nrows, 0:nch])
    m_ev = p.add('V', ev, deps=tr_ms, label=f'vec_{tag}_ev')
    if marks is not None:
        marks['_psu_war'] = m_ev
        marks['junkw_free'] = m_ev
    return m_ev


def prep_weights_ffn(ctx, marks, tag, dep=None, reload=False):
    p, d = ctx.p, ctx.d
    if dep is not None:
        ctx._wstg_war = [dep] * 2
    if not reload:
        m_lng = load_vec_tile(ctx, d['ln_g'], ctx.smallw[:, 0:8], 1024,
                              f'lng{tag}', dep=dep, marks=marks)
        m_lnb = load_vec_tile(ctx, d['ln_b'], ctx.smallw[:, 8:16], 1024,
                              f'lnb{tag}', marks=marks)
        m_rms = load_vec_tile(ctx, d['rms_g'], ctx.smallw[:, 16:24], 1024,
                              f'rms{tag}', marks=marks)
        lng = (ctx.smallw[:, 0:8], m_lng)
        rmsg = (ctx.smallw[:, 16:24], m_rms)
        c0_dst = [(ctx.ps_a, 0, ctx.smallw[:, 8:16], m_lnb),
                  (ctx.ps_b, 0, ctx.smallw[:, 8:16], m_lnb),
                  (ctx.ps_c, 0, ctx.smallw[:, 8:16], m_lnb)]
        m1, m1c0 = load_fold_weight(ctx, d['qkv_l'], 1536, W_QKV, lng,
                                    f'qkv{tag}', c0_dst=c0_dst,
                                    colscale=lambda j: 0.125 if j == 0 else 1.0)
        c0e = None
        for j in range(3):
            pt, po, _, _ = c0_dst[j]
            sc = 0.125 if j == 0 else 1.0
            def ev(s, pt=pt, po=po, j=j, sc=sc):
                return s.tensor_scalar(ctx.c0row[:, j * 512:(j + 1) * 512],
                                       pt[0:1, po:po + 512], sc, None,
                                       op0=ALU.mult)
            c0e = p.add('V', ev, deps=[m1c0], label=f'c0ev{tag}{j}')
        xq = xo = xv = xkv = None
    else:
        ws = d['WSTASH']
        ms = []
        for nm, a, bnd, xdep in [
                ('qo', 0, 16384, marks['gla_qkfree']),
                ('qh', 16384, 20480, marks['gla_vfree']),
                ('kvo', 20480, 32768, marks['gla_done'])]:
            m_l = p.add('SQ', (lambda s, a=a, bnd=bnd: s.dma_start(
                out=ctx.wreg[:, a:bnd], in_=ws[:, a:bnd])),
                deps=[marks['wstash'], xdep], label=f'wrld_{nm}',
                slot=f'wrld{nm}')
            ms.append(m_l)
        mb = p.barrier('V', ms, label=f'wffn_done_{tag}')
        marks[f'wffn_{tag}'] = mb
        return mb
    m2, _ = load_fold_weight(ctx, d['w_out_l'], 1024, W_OUTL, None,
                             f'wol{tag}', kchunks=4, extra_dep=xo)
    m3, _ = load_fold_weight(ctx, d['w_q_h'], 512, W_QH, rmsg, f'wqh{tag}',
                             extra_dep=xv)
    m4, _ = load_fold_weight(ctx, d['w_kv_h'], 1024, W_KVH, rmsg, f'wkvh{tag}',
                             extra_dep=xkv)
    m5, _ = load_fold_weight(ctx, d['w_out_h'], 1024, W_OUTH, None,
                             f'woh{tag}', kchunks=4)
    mb = p.barrier('V', [m1, c0e, m2, m3, m4, m5], label=f'wffn_done_{tag}')
    marks[f'wffn_{tag}'] = mb
    marks['wstash'] = p.add('AQ', lambda s: s.dma_start(
        out=d['WSTASH'], in_=ctx.wreg[:, 0:32768]), deps=[mb],
        label='wstash', slot='wstash')
    return mb


def prep_weights_gla(ctx, marks, dep):
    p, d = ctx.p, ctx.d
    ms = []
    m_ws = marks.get('wstash')
    for nm, off, xdep in [('Wq', W_GQ, marks.get('x2_done_1')),
                          ('Wk', W_GK, marks.get('x2_done_1')),
                          ('Wv', W_GV, marks.get('heavy_1')),
                          ('Wg', W_GG, marks.get('heavy_1'))]:
        xd = p.barrier('V', [xdep, m_ws], label=f'g{nm}_ws')
        m, _ = load_fold_weight(ctx, d[nm], 1024, off, None, f'g{nm}',
                                extra_dep=xd)
        ms.append(m)
    m_gt = load_vec_tile(ctx, d['g_norm_w'], ctx.gnw_col[:, 0:1], 64, 'gnw',
                         dep=marks.get('heavy_1'), marks=marks, nrows=64)
    p.add('V', lambda s: s.stream_shuffle(ctx.gnw_col[64:96, 0:1],
                                          ctx.gnw_col[0:32, 0:1],
                                          list(range(32))), deps=[m_gt])
    p.add('V', lambda s: s.stream_shuffle(ctx.gnw_col[96:128, 0:1],
                                          ctx.gnw_col[32:64, 0:1],
                                          list(range(32))))
    m_rep = p.add('V', lambda s: s.tensor_copy(
        ctx.smallw[:, 24:32], ctx.gnw_col[:, 0:1].to_broadcast([128, 8])),
        label='gnwrep')
    m, _ = load_fold_weight(ctx, d['Wo'], 1024, W_GO,
                            (ctx.smallw[:, 24:32], m_rep), 'gWo')
    ms.append(m)
    for c in range(8):
        def ld(s, c=c):
            return s.dma_start(out=ctx.wstg[c % 2][:, 0:16],
                               in_=d['Wg1'][c * 128:(c + 1) * 128, :])
        m_ld = p.add('SQ', ld, deps=[ctx._wstg_war[c % 2]], label=f'wg1ld{c}',
                     slot=f'wstg{c % 2}')
        def cast(s, c=c):
            return s.tensor_copy(ctx.wg1_bf[:, 16 * c:16 * c + 16],
                                 ctx.wstg[c % 2][:, 0:16])
        m_c = p.add('V', cast, deps=[m_ld], label=f'wg1c{c}')
        ctx._wstg_war[c % 2] = m_c
        ms.append(m_c)
    m_ld = p.add('SQ', lambda s: s.dma_start(out=ctx.junkw[0:17, :],
                                             in_=d['Wg2a']),
                 deps=[marks.get('junkw_free')], label='wg2ld', slot='junkw')
    m_w2 = p.add('V', lambda s: s.tensor_copy(ctx.wg2a_bf[:], ctx.junkw[0:17, :]),
                 deps=[m_ld], label='wg2c')
    ms.append(m_w2)
    marks['junkw_free'] = m_w2
    mb = p.barrier('V', ms, label='wgla_done')
    marks['wgla'] = mb
    return mb


def prep_null_kv(ctx, marks, dep=None):
    p, d = ctx.p, ctx.d
    m_ld = p.add('SQ', lambda s: s.dma_start(out=ctx.junkw[0:1, 0:512],
                                             in_=d['null_kv'][0:1, :]),
                 deps=[dep, marks.get('junkw_free')], label='nkv_ld', slot='junkw')
    tr_ms = []
    for h in range(8):
        def tr(s, h=h):
            return s.transpose(ctx.ps_u[0:64, h:h + 1],
                               ctx.junkw[0:1, h * 64:h * 64 + 64],
                               ctx.ident_f[0:1, 0:1])
        tr_ms.append(p.add('T', tr, deps=[m_ld, marks['identf'],
                                          marks.get('_psu_war')],
                           label=f'nkv_t{h}'))
    m_ev = p.add('V', lambda s: s.tensor_copy(ctx.nullk_bf[0:64, :],
                                              ctx.ps_u[0:64, 0:8]),
                 deps=tr_ms, label='nkv_ev')
    marks['_psu_war'] = m_ev
    p.add('V', lambda s: s.stream_shuffle(ctx.nullk_bf[64:96, :],
                                          ctx.nullk_bf[0:32, :],
                                          list(range(32))))
    p.add('V', lambda s: s.stream_shuffle(ctx.nullk_bf[96:128, :],
                                          ctx.nullk_bf[32:64, :],
                                          list(range(32))))
    m1c = p.add('V', lambda s: s.nop(), label='nkv_kdone')
    m2 = p.add('SQ', lambda s: s.dma_start(out=ctx.junkw[0:1, 512:1024],
                                           in_=d['null_kv'][1:2, :]),
                deps=[m_ld], label='nkv_v', slot='junkw')
    vview = ctx.nullv_bf[0:1, :].rearrange('o (h e) -> o h e', e=65)
    m_vc = p.add('V', (lambda s: s.tensor_copy(
        vview[:, :, 0:64],
        ctx.junkw[0:1, 512:1024].rearrange('o (h e) -> o h e', e=64))),
        deps=[m2], label='nkv_vc')
    m2c = p.add('V', lambda s: s.memset(vview[:, :, 64:65], 1.0),
                label='nkv_ones')
    marks['junkw_free'] = m2c
    marks['nullkv'] = p.barrier('V', [m1c, m2c], label='nullkv_done')
    return marks['nullkv']



def phase_ln_scores(ctx, marks, src_ap, xln_ap, tag, dep=None):
    p = ctx.p
    F = ctx.F
    nbuf = len(ctx.xb)
    war = [marks.get(f'xb{i}_war') for i in range(nbuf)]
    last_marks = []
    src = src_ap if src_ap is not None else xln_ap
    src_r = src.rearrange('(q f) d -> f q d', f=F)
    xln_r = xln_ap.rearrange('(q f) d -> f q d', f=F) if src_ap is not None else None
    junk_dep = marks.get('junkw_free')
    for f in range(F):
        xb = ctx.xb[f % nbuf]
        m_ld = p.add('SQ', (lambda s, f=f, xb=xb: s.dma_start(out=xb[:],
                                                              in_=src_r[f])),
                     deps=[war[f % nbuf], dep], label=f'{tag}_ld{f}',
                     slot=f'xb{f % nbuf}')
        if src_ap is not None:
            p.add('V', (lambda s, xb=xb: s.bn_stats(ctx.stat[:, 0:6],
                                                    xb[:, 0:512])),
                  deps=[m_ld], label=f'{tag}_bn1_{f}')
            p.add('V', (lambda s, xb=xb: s.bn_stats(ctx.stat[:, 6:12],
                                                    xb[:, 512:1024])),
                  deps=[m_ld], label=f'{tag}_bn2_{f}')
            m_ag = p.add('V', lambda s: s.bn_aggr(ctx.stat[:, 12:14],
                                                  ctx.stat[:, 0:12]),
                         label=f'{tag}_ag{f}')
            m_sq = p.add('S', lambda s: s.activation(ctx.colv[1][:],
                                                     ctx.stat[:, 13:14], AF.Sqrt,
                                                     bias=ctx.consts[:, 0:1]),
                         deps=[m_ag, marks['consts']], label=f'{tag}_sq{f}')
            m_rc = p.add('V', lambda s: s.reciprocal(ctx.colv[2][:],
                                                     ctx.colv[1][:]),
                         deps=[m_sq], label=f'{tag}_rc{f}')
            m_nb = p.add('V', lambda s: s.scalar_tensor_tensor(
                ctx.colv[3][:], ctx.stat[:, 12:13], -1.0, ctx.colv[2][:],
                op0=ALU.mult, op1=ALU.mult), label=f'{tag}_nb{f}')
            m_ln = p.add('S', (lambda s, xb=xb: s.activation(
                xb[:], xb[:], AF.Identity, bias=ctx.colv[3][:],
                scale=ctx.colv[2][:])), deps=[m_rc, m_nb], label=f'{tag}_ln{f}')
            m_st = p.add('AQ', (lambda s, f=f, xb=xb: s.dma_start(
                out=xln_r[f], in_=xb[:])), deps=[m_ln], label=f'{tag}_st{f}',
                         slot=f'xlnst{f % nbuf}')
            marks[f'xlnt_{tag}_{f}'] = m_st
        else:
            m_ln = m_ld
            m_st = None
        m_sc1 = p.add('V', (lambda s, f=f, xb=xb: s.scalar_tensor_tensor(
            ctx.junkw[:], xb[:], 1.0, ctx.rtq_b[:], op0=ALU.mult, op1=ALU.mult,
            accum_out=ctx.s2[:, f:f + 1])),
            deps=[m_ln, marks['rt_b'], junk_dep], label=f'{tag}_scq{f}')
        junk_dep = None
        m_sc2 = p.add('V', (lambda s, f=f, xb=xb: s.scalar_tensor_tensor(
            ctx.junkw[:], xb[:], 1.0, ctx.rtkv_b[:], op0=ALU.mult, op1=ALU.mult,
            accum_out=ctx.s2[:, F + f:F + f + 1])),
            deps=[m_ln], label=f'{tag}_sckv{f}')
        frees = [m_sc2] + ([m_st] if m_st else [])
        war[f % nbuf] = p.barrier('V', frees, label=f'{tag}_war{f}')
        last_marks.append(war[f % nbuf])
    for i in range(nbuf):
        marks[f'xb{i}_war'] = war[i]
    marks['junkw_free'] = last_marks[-1]
    marks[f'scores_{tag}'] = last_marks[-1]
    marks[f'xln_done_{tag}'] = p.barrier('V', war, label=f'{tag}_alldone')
    return marks[f'scores_{tag}']



BIG_DEST = float(1 << 22)


def xpart_sum_rep(ctx, src_cols, n, dst_cols, dep=None, tag=''):
    p = ctx.p
    rsc = ctx.rsc
    p.add('V', lambda s: s.tensor_copy(rsc[:, 0:n], src_cols),
          deps=[dep], label=f'xps_cp_{tag}')
    if n < 32:
        p.add('V', lambda s: s.memset(rsc[:, n:32], 0.0))
    p.add('V', lambda s: s.transpose(rsc[:, 0:32], rsc[:, 0:32]),
          label=f'xps_t_{tag}')
    p.add('V', lambda s: s.tensor_reduce(rsc[:, 0:1], rsc[:, 0:32],
                                         axis=AX.X, op=ALU.add),
          label=f'xps_r_{tag}')
    p.add('V', lambda s: s.stream_shuffle(rsc[0:32, 1:2], rsc[32:64, 0:1],
                                          list(range(32))))
    p.add('V', lambda s: s.stream_shuffle(rsc[0:32, 2:3], rsc[64:96, 0:1],
                                          list(range(32))))
    p.add('V', lambda s: s.stream_shuffle(rsc[0:32, 3:4], rsc[96:128, 0:1],
                                          list(range(32))))
    p.add('V', lambda s: s.tensor_tensor(rsc[0:32, 0:1], rsc[0:32, 0:1],
                                         rsc[0:32, 1:2], op=ALU.add))
    p.add('V', lambda s: s.tensor_tensor(rsc[0:32, 2:3], rsc[0:32, 2:3],
                                         rsc[0:32, 3:4], op=ALU.add))
    p.add('V', lambda s: s.tensor_tensor(rsc[0:32, 0:1], rsc[0:32, 0:1],
                                         rsc[0:32, 2:3], op=ALU.add))
    p.add('V', lambda s: s.transpose(rsc[:, 0:32], rsc[:, 0:32]),
          label=f'xps_t2_{tag}')
    p.add('V', lambda s: s.stream_shuffle(rsc[32:64, 0:n], rsc[0:32, 0:n],
                                          [0] * 32))
    p.add('V', lambda s: s.stream_shuffle(rsc[64:96, 0:n], rsc[0:32, 0:n],
                                          [0] * 32))
    p.add('V', lambda s: s.stream_shuffle(rsc[96:128, 0:n], rsc[0:32, 0:n],
                                          [0] * 32))
    p.add('V', lambda s: s.stream_shuffle(rsc[:, 0:n], rsc[:, 0:n], [0] * 32),
          label=f'xps_b_{tag}')
    m3 = p.add('V', lambda s: s.tensor_copy(dst_cols, rsc[:, 0:n]),
               label=f'xps_o_{tag}')
    return m3


def xpart_prefix(ctx, mask_ap, pref_ap, rowcnt_col, tot_col, tag):
    p = ctx.p
    rsc = ctx.rsc
    p.add('V', lambda s: s.tensor_tensor_scan(pref_ap, mask_ap,
                                              ctx.zeroF[:, 0:pref_ap.shape[1]], 0.0,
                                              op0=ALU.add, op1=ALU.add),
          label=f'xpf_scan_{tag}')
    p.add('V', lambda s: s.tensor_copy(rsc[:, 0:1], rowcnt_col),
          label=f'xpf_cp_{tag}')
    p.add('V', lambda s: s.memset(rsc[:, 1:32], 0.0))
    p.add('V', lambda s: s.transpose(rsc[:, 0:32], rsc[:, 0:32]))
    p.add('V', lambda s: s.tensor_tensor_scan(rsc[:, 0:32], rsc[:, 0:32],
                                              ctx.zeroF[:, 0:32], 0.0,
                                              op0=ALU.add, op1=ALU.add))
    p.add('V', lambda s: s.transpose(rsc[:, 0:32], rsc[:, 0:32]))
    p.add('V', lambda s: s.stream_shuffle(rsc[:, 1:2], rsc[:, 0:1], [31] * 32),
          label=f'xpf_qt_{tag}')
    p.add('V', lambda s: s.memset(rsc[0:32, 2:3], 0.0))
    p.add('V', lambda s: s.stream_shuffle(rsc[32:64, 2:3], rsc[0:32, 1:2],
                                          [0] * 32))
    p.add('V', lambda s: s.tensor_tensor(rsc[32:64, 3:4], rsc[32:64, 2:3],
                                         rsc[32:64, 1:2], op=ALU.add))
    p.add('V', lambda s: s.stream_shuffle(rsc[64:96, 2:3], rsc[32:64, 3:4],
                                          [0] * 32))
    p.add('V', lambda s: s.tensor_tensor(rsc[64:96, 3:4], rsc[64:96, 2:3],
                                         rsc[64:96, 1:2], op=ALU.add))
    p.add('V', lambda s: s.stream_shuffle(rsc[96:128, 2:3], rsc[64:96, 3:4],
                                          [0] * 32))
    p.add('V', lambda s: s.tensor_tensor(rsc[96:128, 3:4], rsc[96:128, 2:3],
                                         rsc[96:128, 1:2], op=ALU.add))
    p.add('V', lambda s: s.stream_shuffle(rsc[0:32, 4:5], rsc[96:128, 3:4],
                                          [0] * 32))
    p.add('V', lambda s: s.stream_shuffle(rsc[32:64, 4:5], rsc[96:128, 3:4],
                                          [0] * 32))
    p.add('V', lambda s: s.stream_shuffle(rsc[64:96, 4:5], rsc[96:128, 3:4],
                                          [0] * 32))
    p.add('V', lambda s: s.stream_shuffle(rsc[96:128, 4:5], rsc[96:128, 3:4],
                                          [0] * 32))
    p.add('V', lambda s: s.tensor_copy(tot_col, rsc[:, 4:5]),
          label=f'xpf_tot_{tag}')
    p.add('V', lambda s: s.tensor_tensor(rsc[:, 5:6], rsc[:, 0:1], rowcnt_col,
                                         op=ALU.subtract))
    p.add('V', lambda s: s.tensor_tensor(rsc[:, 5:6], rsc[:, 5:6], rsc[:, 2:3],
                                         op=ALU.add))
    m = p.add('V', lambda s: s.tensor_scalar(pref_ap, pref_ap, rsc[:, 5:6],
                                             None, op0=ALU.add),
              label=f'xpf_done_{tag}')
    return m


def routing(ctx, marks, tag, score_dep, extra_dep=None):
    p, d = ctx.p, ctx.d
    F = ctx.F
    s2 = ctx.s2
    m0 = p.add('V', lambda s: s.tensor_copy(ctx.relu2[:], s2[:]),
               deps=[score_dep], label=f'r{tag}_init')
    for it in range(ctx.n_iters):
        p.add('V', lambda s: s.tensor_tensor(ctx.e2[:], s2[:], ctx.relu2[:],
                                             op=ALU.subtract),
              label=f'r{tag}_sb{it}')
        mv = p.barrier('V', [], label=f'r{tag}_sbb{it}')
        p.add('S', lambda s: s.activation(ctx.e2[:, 0:F], ctx.e2[:, 0:F],
                                          AF.Exp, accum_out=ctx.racc[:, 0:1]),
              deps=[mv], label=f'r{tag}_e1_{it}')
        m_e2 = p.add('S', lambda s: s.activation(ctx.e2[:, F:2 * F],
                                                 ctx.e2[:, F:2 * F], AF.Exp,
                                                 accum_out=ctx.racc[:, 1:2]),
                     label=f'r{tag}_e2_{it}')
        m_sum = xpart_sum_rep(ctx, ctx.racc[:, 0:2], 2, ctx.racc[:, 2:4],
                              dep=m_e2, tag=f'{tag}i{it}')
        m_ln = p.add('S', lambda s: s.activation(ctx.racc[:, 4:6],
                                                 ctx.racc[:, 2:4], AF.Ln),
                     deps=[m_sum], label=f'r{tag}_ln{it}')
        m_a = p.add('V', lambda s: s.tensor_tensor(ctx.aval[:], ctx.logk[:],
                                                   ctx.racc[:, 4:6],
                                                   op=ALU.subtract),
                    deps=[m_ln, marks['consts']], label=f'r{tag}_a{it}')
        p.add('S', lambda s: s.activation(ctx.relu2[:, 0:F], s2[:, 0:F],
                                          AF.Relu, bias=ctx.aval[:, 0:1]),
              deps=[m_a], label=f'r{tag}_r1_{it}')
        m_r2 = p.add('S', lambda s: s.activation(ctx.relu2[:, F:2 * F],
                                                 s2[:, F:2 * F], AF.Relu,
                                                 bias=ctx.aval[:, 1:2]),
                     label=f'r{tag}_r2_{it}')
        p.barrier('V', [m_r2], label=f'r{tag}_endit{it}')
    m_coor = p.barrier('V', [], label=f'r{tag}_coor_done')
    marks[f'aval_{tag}'] = m_coor

    res = {}
    for r, (kk, list_ap, idx_sb) in enumerate(
            [(ctx.NQ, d['LISTQ'], ctx.idxq), (ctx.NKV, d['LISTKV'], ctx.idxkv)]):
        s_ap = s2[:, r * F:(r + 1) * F]
        A = ctx.selscr[:, 0:F]
        B = ctx.selscr[:, F:2 * F]
        C = ctx.selscr[:, 2 * F:3 * F]
        D = ctx.selscr[:, 3 * F:4 * F]
        cv = ctx.colv
        m_pref_list = p.add('GQ', (lambda s, list_ap=list_ap, kk=kk:
                            s.dma_start(out=list_ap.rearrange('(q i) -> q i',
                                                              i=kk // 128),
                                        in_=ctx.skipc[:, 0:kk // 128])),
                            deps=[marks['skipc'], extra_dep,
                                  marks.get(f'list_free_{tag}_{r}')],
                            label=f'sel{tag}{r}_prefill', slot=f'list{r}')
        p.add('V', (lambda s, r=r: s.tensor_scalar(
            cv[4][:], ctx.aval[:, r:r + 1], -1.0, None, op0=ALU.mult)),
            deps=[m_coor], label=f'sel{tag}{r}_na')
        p.add('V', (lambda s, s_ap=s_ap, D=D: s.tensor_scalar(
            D, s_ap, cv[4][:], 0.0, op0=ALU.is_ge, op1=ALU.add,
            accum_out=ctx.racc[:, 6:7])), label=f'sel{tag}{r}_mask')
        p.add('V', (lambda s, C=C, D=D: s.tensor_scalar(
            C, D, float(1e30), None, op0=ALU.mult)), label=f'sel{tag}{r}_nt1')
        p.add('V', (lambda s, s_ap=s_ap, C=C: s.tensor_tensor(
            C, s_ap, C, op=ALU.subtract)), label=f'sel{tag}{r}_nt2')
        xpart_prefix(ctx, D, B, ctx.racc[:, 6:7], cv[5][:], tag=f'{tag}m{r}')
        p.add('V', (lambda s, A=A, B=B, D=D, kk=kk: s.scalar_tensor_tensor(
            A, B, float(kk), D, op0=ALU.is_le, op1=ALU.mult)),
            label=f'sel{tag}{r}_tied')
        p.add('V', (lambda s, kk=kk: s.tensor_scalar(
            cv[6][:], cv[5][:], -1.0, float(kk), op0=ALU.mult, op1=ALU.add)),
            label=f'sel{tag}{r}_f1')
        p.add('V', lambda s: s.tensor_scalar(cv[6][:], cv[6][:], 0.0, None,
                                             op0=ALU.max),
              label=f'sel{tag}{r}_f2')
        p.add('V', lambda s: s.tensor_scalar(cv[7][:], cv[4][:], -64.0, None,
                                             op0=ALU.add))
        p.add('V', lambda s: s.tensor_copy(ctx.racc[:, 7:8], cv[4][:]))
        for it in range(ctx.bis_iters):
            p.add('V', lambda s: s.tensor_tensor(ctx.rsc[:, 8:9], cv[7][:],
                                                 ctx.racc[:, 7:8], op=ALU.add))
            p.add('V', lambda s: s.tensor_scalar(ctx.rsc[:, 8:9],
                                                 ctx.rsc[:, 8:9], 0.5, None,
                                                 op0=ALU.mult))
            p.add('V', lambda s: s.tensor_copy(ctx.rsc[:, 10:11],
                                               ctx.rsc[:, 8:9]))
            p.add('V', (lambda s, C=C, D=D: s.tensor_scalar(
                D, C, ctx.rsc[:, 10:11], 0.0, op0=ALU.is_ge, op1=ALU.add,
                accum_out=ctx.racc[:, 0:1])), label=f'b{tag}{r}_{it}')
            xpart_sum_rep(ctx, ctx.racc[:, 0:1], 1, ctx.racc[:, 1:2],
                          tag=f'b{tag}{r}{it}')
            p.add('V', lambda s: s.tensor_tensor(ctx.rsc[:, 11:12],
                                                 ctx.racc[:, 1:2], cv[6][:],
                                                 op=ALU.is_ge))
            p.add('V', lambda s: s.tensor_tensor(ctx.rsc[:, 12:13],
                                                 ctx.rsc[:, 10:11], cv[7][:],
                                                 op=ALU.subtract))
            p.add('V', lambda s: s.tensor_tensor(ctx.rsc[:, 12:13],
                                                 ctx.rsc[:, 12:13],
                                                 ctx.rsc[:, 11:12],
                                                 op=ALU.mult))
            p.add('V', lambda s: s.tensor_tensor(cv[7][:], cv[7][:],
                                                 ctx.rsc[:, 12:13],
                                                 op=ALU.add))
            p.add('V', lambda s: s.tensor_tensor(ctx.rsc[:, 12:13],
                                                 ctx.racc[:, 7:8],
                                                 ctx.rsc[:, 10:11],
                                                 op=ALU.subtract))
            p.add('V', lambda s: s.tensor_tensor(ctx.rsc[:, 12:13],
                                                 ctx.rsc[:, 12:13],
                                                 ctx.rsc[:, 11:12],
                                                 op=ALU.mult))
            p.add('V', lambda s: s.tensor_tensor(ctx.racc[:, 7:8],
                                                 ctx.rsc[:, 10:11],
                                                 ctx.rsc[:, 12:13],
                                                 op=ALU.add))
        p.add('V', (lambda s, C=C, D=D: s.tensor_scalar(
            D, C, cv[7][:], 0.0, op0=ALU.is_ge, op1=ALU.add,
            accum_out=ctx.racc[:, 0:1])), label=f'sel{tag}{r}_fillm')
        xpart_prefix(ctx, D, B, ctx.racc[:, 0:1], ctx.racc[:, 1:2],
                     tag=f'{tag}f{r}')
        p.add('V', (lambda s, B=B: s.tensor_scalar(
            B, B, cv[6][:], None, op0=ALU.is_le)), label=f'sel{tag}{r}_fle')
        p.add('V', (lambda s, B=B, D=D: s.tensor_tensor(
            D, D, B, op=ALU.mult)), label=f'sel{tag}{r}_fsel')
        p.add('V', (lambda s, A=A, D=D: s.tensor_tensor(A, A, D, op=ALU.add)),
              label=f'sel{tag}{r}_sel')
        p.add('V', (lambda s, A=A, D=D: s.tensor_scalar(
            D, A, 0.0, 0.0, op0=ALU.add, op1=ALU.add,
            accum_out=ctx.racc[:, 2:3])), label=f'sel{tag}{r}_selcnt')
        xpart_prefix(ctx, A, B, ctx.racc[:, 2:3], ctx.racc[:, 3:4],
                     tag=f'{tag}s{r}')
        p.add('V', (lambda s, B=B: s.tensor_scalar(
            B, B, float(-1.0 - BIG_DEST), None, op0=ALU.add)),
            label=f'sel{tag}{r}_d1')
        p.add('V', (lambda s, A=A, B=B: s.tensor_tensor(B, B, A, op=ALU.mult)),
              label=f'sel{tag}{r}_d2')
        p.add('V', (lambda s, B=B: s.tensor_scalar(
            ctx.dest_f[:], B, BIG_DEST, None, op0=ALU.add)),
            label=f'sel{tag}{r}_d3')
        m_du = p.add('V', lambda s: s.tensor_copy(ctx.dest_u[:], ctx.dest_f[:]),
                     deps=[getattr(ctx, '_dest_war', None)],
                     label=f'sel{tag}{r}_du')
        sc_marks = []
        for f in range(F):
            ctx.p.bounds_vals.add(kk - 1)
            def sca(s, f=f, list_ap=list_ap, kk=kk):
                return s.indirect_dma_start(
                    out=list_ap.unsqueeze(1),
                    out_offset=IndirectOffsetOnAxis(ctx.dest_u[:, f:f + 1], 0),
                    in_=ctx.iota_u[:, f:f + 1], in_offset=None,
                    bounds_check=ctx.p.bc_reg(kk - 1), oob_is_err=False)
            sc_marks.append(p.add('GQ', sca,
                                  deps=[m_du, m_pref_list, marks['iota_u']],
                                  label=f'sel{tag}{r}_sc{f}', slot=f'list{r}'))
        def rd(s, list_ap=list_ap, idx_sb=idx_sb, kk=kk):
            return s.dma_start(out=idx_sb[:, 0:kk // 128],
                               in_=list_ap.rearrange('(q i) -> q i',
                                                     i=kk // 128))
        m_idx = p.add('SQ', rd, deps=sc_marks + [extra_dep],
                      label=f'sel{tag}{r}_rd', slot=f'idx{r}')
        ctx._dest_war = p.barrier('P', sc_marks, label=f'sel{tag}{r}_scw')
        res[('idx', r)] = m_idx
        res[('sel_done', r)] = m_du
    marks[f'route_{tag}'] = res
    return res



def declare_light(ctx, tag):
    GA, GB = f'lightA{tag}', f'lightB{tag}'
    ctx.xn2T = [ctx.sb(f'xn2T{tag}_{i}', [128, 8 * 576], BF16, GA) for i in range(2)]
    ctx.kTl = [ctx.sb(f'kTl{tag}_{i}', [128, 4 * 640], BF16, GA) for i in range(3)]
    ctx.qTl = [ctx.sb(f'qTl{tag}_{i}', [128, 4 * 512], BF16, GA) for i in range(2)]
    ctx.vS = [ctx.sb(f'vS{tag}_{i}', [128, 520], BF16, GA) for i in range(12)]
    ctx.esim = [ctx.sb(f'esim{tag}_{i}', [128, 16 * 128], BF16, GA) for i in range(3)]
    ctx.onorm = ctx.sb(f'onorm{tag}', [128, max(512, ctx.NQ)], BF16, GB,
                       side='right')
    ctx.onormT = ctx.sb(f'onormT{tag}', [128, 512], BF16, GB, side='right')
    ctx.lstage = [ctx.sb(f'lstage{tag}_{i}', [128, 1024], F32, GB, side='right') for i in range(2)]
    ctx.tcast = [ctx.sb(f'tcast{tag}_{i}', [128, 1024], BF16, GB, side='right') for i in range(2)]
    ctx.lrec = ctx.sb(f'lrec{tag}', [128, 16], F32, GB, side='right')


def declare_heavy(ctx, tag):
    G = f'heavy{tag}'
    ctx.hT = [ctx.sb(f'hT{tag}_0', [128, 8 * 512], BF16, G)] * 2
    ctx.kTh = ctx.sb(f'kTh{tag}', [128, 4 * 1024], BF16, G)
    ctx.qTh = ctx.sb(f'qTh{tag}', [128, 4 * 1024], BF16, G)
    ctx.vh = [ctx.sb(f'vh{tag}_{i}', [128, 520], BF16, G) for i in range(8)]
    ctx.esimh = ctx.sb(f'esimh{tag}', [128, 1024], BF16, G)
    ctx.oacc = ctx.sb(f'oacc{tag}', [128, 8 * 520], F32, G)
    ctx.hvout = ctx.sb(f'hvout{tag}', [128, 1024], F32, G)


def emit_ln2_tile(ctx, marks, xb, tcast, tag, dep=None):
    p = ctx.p
    p.add('V', lambda s: s.bn_stats(ctx.stat[:, 0:6], xb[:, 0:512]),
          deps=[dep], label=f'{tag}_bn1')
    p.add('V', lambda s: s.bn_stats(ctx.stat[:, 6:12], xb[:, 512:1024]),
          label=f'{tag}_bn2')
    m_ag = p.add('V', lambda s: s.bn_aggr(ctx.stat[:, 12:14], ctx.stat[:, 0:12]),
                 label=f'{tag}_ag')
    m_sq = p.add('S', lambda s: s.activation(ctx.colv[1][:], ctx.stat[:, 13:14],
                                             AF.Sqrt, bias=ctx.consts[:, 0:1]),
                 deps=[m_ag], label=f'{tag}_sq')
    p.add('V', lambda s: s.reciprocal(ctx.colv[2][:], ctx.colv[1][:]),
          deps=[m_sq], label=f'{tag}_rc')
    m_nb = p.add('V', lambda s: s.scalar_tensor_tensor(
        ctx.colv[3][:], ctx.stat[:, 12:13], -1.0, ctx.colv[2][:],
        op0=ALU.mult, op1=ALU.mult), label=f'{tag}_nb')
    m_t = p.add('S', lambda s: s.activation(tcast[:], xb[:], AF.Identity,
                                            bias=ctx.colv[3][:],
                                            scale=ctx.colv[2][:]),
                deps=[m_nb], label=f'{tag}_t')
    return m_t


def emit_transposes(ctx, src_bf, dst, dst_cols, deps, tag, nch=8):
    p = ctx.p
    tr_last = None
    for j in range(nch):
        def tr(s, j=j):
            return s.transpose(ctx.ps_tr[:, j * 128:(j + 1) * 128],
                               src_bf[:, j * 128:(j + 1) * 128], ctx.ident_bf[:])
        tr_last = p.add('T', tr,
                        deps=(deps + [marks_get(ctx, 'pstr_bank_war')])
                        if j == 0 else [],
                        label=f'{tag}_tr{j}')
    last = None
    for j in range(nch):
        def ev(s, j=j):
            return s.tensor_copy(dst_cols(j), ctx.ps_tr[:, j * 128:(j + 1) * 128])
        last = p.add('V', ev, deps=[tr_last] if j == 0 else [],
                     label=f'{tag}_ev{j}')
    marks_set(ctx, 'pstr_bank_war', last)
    return last


def marks_get(ctx, key):
    return getattr(ctx, '_mk', {}).get(key)


def marks_set(ctx, key, val):
    if not hasattr(ctx, '_mk'):
        ctx._mk = {}
    ctx._mk[key] = val


def phase_light(ctx, marks, xln_ap, x_ap, x2_ap, tag):
    p, d = ctx.p, ctx.d
    NG, F = ctx.NG, ctx.F
    wffn = marks[f'wffn_{tag}']
    gst = [dict() for _ in range(NG + 1)]
    x2_stores = []

    def kTcols(g, j, t0, t1):
        buf = ctx.kTl[g % 3]
        return buf[:, j * 640 + t0:j * 640 + t1]

    def proj_group(g):
        st = gst[g]
        xbuf = ctx.xn2T[g % 2]
        war_xn2 = (gst[g - 2]['xn2_done'] if g >= 2
                   else marks[f'lmemsets_{tag}'])
        tile_marks = []
        for k in range(4):
            xb = ctx.xb[k % 2]
            tc = ctx.tcast[k % 2]
            m_ld = p.add('SQ', (lambda s, g=g, k=k, xb=xb: s.dma_start(
                out=xb[:], in_=xln_ap[g * 512 + k * 128:g * 512 + k * 128 + 128, :])),
                deps=[marks_get(ctx, f'lxb{k % 2}_war'),
                      marks.get(f'xb{k % 2}_war'),
                      marks.get(f'xlnt_{tag}_{g * 4 + k}',
                                marks[f'xln_done_{tag}'])],
                label=f'l{tag}g{g}ld{k}', slot=f'xb{k % 2}')
            marks[f'xb{k % 2}_war'] = None
            m_t = emit_ln2_tile(ctx, marks, xb, tc, f'l{tag}g{g}k{k}',
                                dep=m_ld)
            m_tp = emit_transposes(
                ctx, tc, xbuf,
                (lambda j, g=g, k=k, xbuf=xbuf: xbuf[:, j * 576 + 64 + k * 128:
                                                     j * 576 + 64 + k * 128 + 128]),
                deps=[m_t, war_xn2, wffn], tag=f'l{tag}g{g}k{k}')
            marks_set(ctx, f'lxb{k % 2}_war',
                      p.barrier('V', [m_t, m_tp], label=f'l{tag}g{g}xbw{k}'))
            tile_marks.append(m_tp)
        if g > 0:
            prev = ctx.xn2T[(g - 1) % 2]
            def pad(s, xbuf=xbuf, prev=prev):
                return s.tensor_copy(
                    xbuf[:].rearrange('p (c t) -> p c t', t=576)[:, :, 0:64],
                    prev[:].rearrange('p (c t) -> p c t', t=576)[:, :, 512:576])
            m_pad = p.add('P', pad, deps=[tile_marks[-1], gst[g - 1]['xn2_done']],
                          label=f'l{tag}g{g}pad')
        else:
            m_pad = tile_marks[-1]
        st['xn2'] = p.barrier('V', tile_marks + [m_pad], label=f'l{tag}g{g}xn2')
        if getattr(ctx, 'lg_limit', None) == 'proj_ln':
            st['xn2_done'] = st['xn2']
            st['k_ready'] = st['xn2']
            return

        qbuf = ctx.qTl[g % 2]
        kbuf = ctx.kTl[g % 3]
        war_q = (gst[g - 2].get('q_used') if g >= 2
                 else marks[f'lmemsets_{tag}'])
        war_k = (gst[g - 3].get('k_used') if g >= 3
                 else marks[f'lmemsets_{tag}'])
        qk_marks = []
        for c in range(8):
            psum = ctx.ps_a if c % 2 == 0 else ctx.ps_b
            ps_war = marks_get(ctx, f'l_ps{"ab"[c % 2]}_war')
            mm_last = None
            for kc in range(8):
                def mm(s, c=c, kc=kc, psum=psum, xbuf=xbuf):
                    return s.matmul(
                        psum[:, 0:512],
                        ctx.wreg[:, W_QKV + kc * 1536 + c * 128:
                                 W_QKV + kc * 1536 + c * 128 + 128],
                        xbuf[:, kc * 576 + 64:kc * 576 + 64 + 512],
                        start=(kc == 0), stop=False)
                mm_last = p.add('T', mm,
                                deps=([st['xn2'], wffn, ps_war] if kc == 0 else []),
                                label=f'l{tag}g{g}qk{c}_{kc}')
            def mmc0(s, c=c, psum=psum):
                return s.matmul(psum[:, 0:512], ctx.c0row[0:1, c * 128:c * 128 + 128],
                                ctx.ones_rowb[0:1, 0:512], start=False, stop=True)
            mm_last = p.add('T', mmc0, deps=[marks['onesr']],
                            label=f'l{tag}g{g}qkc0{c}')
            if c < 4:
                def ev(s, c=c, psum=psum, qbuf=qbuf):
                    return s.tensor_copy(qbuf[:, c * 512:(c + 1) * 512],
                                         psum[:, 0:512])
                m_ev = p.add('V', ev, deps=[mm_last, war_q], label=f'l{tag}g{g}qe{c}')
            else:
                def ev(s, c=c, psum=psum, kbuf=kbuf):
                    return s.tensor_copy(
                        kbuf[:, (c - 4) * 640 + 64:(c - 4) * 640 + 64 + 512],
                        psum[:, 0:512])
                m_ev = p.add('V', ev, deps=[mm_last, war_k], label=f'l{tag}g{g}ke{c}')
            marks_set(ctx, f'l_ps{"ab"[c % 2]}_war', m_ev)
            qk_marks.append(m_ev)
        st['q'] = p.barrier('V', qk_marks[:4], label=f'l{tag}g{g}q')
        st['k_mid'] = p.barrier('V', qk_marks[4:], label=f'l{tag}g{g}k')
        if getattr(ctx, 'lg_limit', None) == 'proj_qk':
            st['xn2_done'] = p.barrier('V', qk_marks, label=f'l{tag}g{g}xd')
            st['k_ready'] = st['k_mid']
            return
        if g > 0:
            prev = ctx.kTl[(g - 1) % 3]
            def padl(s, kbuf=kbuf, prev=prev):
                return s.tensor_copy(
                    kbuf[:].rearrange('p (c t) -> p c t', t=640)[:, 0:4, 0:64],
                    prev[:].rearrange('p (c t) -> p c t', t=640)[:, 0:4, 512:576])
            m_kl = p.add('P', padl, deps=[st['k_mid'], gst[g - 1]['k_mid']],
                         label=f'l{tag}g{g}kpl')
            def padr(s, kbuf=kbuf, prev=prev):
                return s.tensor_copy(
                    prev[:].rearrange('p (c t) -> p c t', t=640)[:, 0:4, 576:640],
                    kbuf[:].rearrange('p (c t) -> p c t', t=640)[:, 0:4, 64:128])
            m_kr = p.add('P', padr, label=f'l{tag}g{g}kpr')
            st['k_padl'] = m_kl
            gst[g - 1]['k_ready'] = p.barrier('P', [m_kr],
                                              label=f'l{tag}g{g - 1}kready')
        if g == 0:
            st['k_padl'] = st['k_mid']
        if g == NG - 1:
            st['k_ready'] = p.barrier('P', [st['k_mid'], st['k_padl']],
                                      label=f'l{tag}g{g}kready_last')

        v_marks = []
        for m in range(5 if g == NG - 1 else 4):
            vt = ctx.vS[(4 * g + m) % 12]
            war_v = marks_get(ctx, f'vS{(4 * g + m) % 12}_war')
            ps_war = marks_get(ctx, 'l_psc_war')
            mw = 128 if m < 4 else 64
            mm_last = None
            for kc in range(8):
                def mm(s, m=m, kc=kc, xbuf=xbuf, mw=mw):
                    return s.matmul(
                        ctx.ps_c[0:mw, 0:512],
                        xbuf[:, kc * 576 + m * 128:kc * 576 + m * 128 + mw],
                        ctx.wreg[:, W_QKV + kc * 1536 + 1024:
                                 W_QKV + kc * 1536 + 1536],
                        start=(kc == 0), stop=False)
                mm_last = p.add('T', mm,
                                deps=([st['xn2'], ps_war] if kc == 0 else []),
                                label=f'l{tag}g{g}v{m}_{kc}')
            def mmc0(s, mw=mw):
                return s.matmul(ctx.ps_c[0:mw, 0:512],
                                ctx.ones_rowb[0:1, 0:mw],
                                ctx.c0row[0:1, 1024:1536], start=False, stop=True)
            mm_last = p.add('T', mmc0, label=f'l{tag}g{g}vc0{m}')
            def ev(s, vt=vt, mw=mw):
                return s.tensor_copy(
                    vt[0:mw, :].rearrange('p (h e) -> p h e', e=65)[:, :, 0:64],
                    ctx.ps_c[0:mw, 0:512].rearrange('p (h e) -> p h e', e=64))
            m_ev = p.add('V', ev, deps=[mm_last, war_v], label=f'l{tag}g{g}ve{m}')
            marks_set(ctx, 'l_psc_war', m_ev)
            marks_set(ctx, f'vS{(4 * g + m) % 12}_filled', m_ev)
            v_marks.append(m_ev)
        st['v'] = p.barrier('V', v_marks, label=f'l{tag}g{g}v')
        st['xn2_done'] = p.barrier('V', qk_marks + v_marks, label=f'l{tag}g{g}xd')

    def light_group(g):
        st = gst[g]
        qbuf = ctx.qTl[g % 2]
        kbuf = ctx.kTl[g % 3]
        lim_g = getattr(ctx, 'lg_limit', None)
        nblocks = {'lg_sims': 1, 'lg_o': 1, 'lg_b0': 1,
                   'lg_b2': 2, 'lg_b3': 3}.get(lim_g, 4)
        if lim_g == 'lg_sims4':
            lim_g = 'lg_sims'
        elif lim_g == 'lg_o4':
            lim_g = 'lg_o'
        q_used, k_used, v_used, lx_marks = [], [], [], []
        for b in range(nblocks):
            ei = (4 * g + b) % 3
            esim = ctx.esim[ei]
            t0g = b * 128
            first = (g == 0 and b == 0)
            last = (g == NG - 1 and b == 3 and lim_g != 'lg_nolast')
            prev_burst = marks_get(ctx, 'lpsu_burst') or \
                [marks[f'lmemsets_{tag}']]
            for pair in ((0, 2), (4, 6), (1, 3), (5, 7)):
                mm_last = None
                for h in pair:
                    c = h // 2
                    pb = (h % 2) * 64
                    for kb in range(2):
                        slot = ((h // 2) % 2) * 2 + kb
                        def mm(s, c=c, pb=pb, kb=kb, t0g=t0g, slot=slot,
                               kbuf=kbuf, qbuf=qbuf):
                            return s.matmul(
                                ctx.ps_u[:, slot * 128:slot * 128 + 128],
                                kbuf[:, c * 640 + t0g + kb * 128:
                                     c * 640 + t0g + kb * 128 + 128][pb:pb + 64, :],
                                qbuf[:, c * 512 + t0g:c * 512 + t0g + 128][pb:pb + 64, :],
                                start=True, stop=True,
                                tile_position=(pb, 0))
                        dep0 = ([st['k_ready'], st['q']] + prev_burst
                                if mm_last is None else [])
                        mm_last = p.add('T', mm, deps=dep0,
                                        label=f'l{tag}g{g}b{b}s{h}_{kb}')
                burst_exps = []
                if not first and not last:
                    ha = pair[0]
                    a0 = (ha * 2) // 4
                    b0 = (ha * 2) % 4
                    es_r = esim.rearrange('p (a b c) -> p a b c', b=4, c=128)
                    ps_r = ctx.ps_u.rearrange('p (x y c) -> p x y c', y=2, c=128)
                    ew_all = [marks_get(ctx, f'esim{ei}_{h * 2 + kb}_war')
                              for h in pair for kb in range(2)]
                    mms = []
                    for nm, prow, pcol, drow, db, dcol in [
                            ('mA1', slice(0, 64), slice(0, 64),
                             slice(0, 64), b0, slice(0, 64)),
                            ('mA2', slice(64, 128), slice(0, 128),
                             slice(64, 128), b0, slice(0, 128)),
                            ('mB1', slice(0, 64), slice(0, 128),
                             slice(0, 64), b0 + 1, slice(0, 128)),
                            ('mB2', slice(64, 128), slice(64, 128),
                             slice(64, 128), b0 + 1, slice(64, 128))]:
                        y = 0 if nm.startswith('mA') else 1
                        def ee(s, prow=prow, pcol=pcol, drow=drow, db=db,
                               dcol=dcol, y=y, a0=a0, es_r=es_r, ps_r=ps_r):
                            return s.activation(
                                es_r[drow, a0:a0 + 2, db, dcol],
                                ps_r[prow, 0:2, y, pcol], AF.Exp)
                        mms.append(p.add('S', ee, deps=[mm_last] + ew_all,
                                         label=f'l{tag}g{g}b{b}{nm}p{pair[0]}'))
                    m_exp = p.barrier('S', mms,
                                      label=f'l{tag}g{g}b{b}expp{pair[0]}')
                    for h in pair:
                        for kb in range(2):
                            marks_set(ctx, f'esim_exp_{h}_{kb}', m_exp)
                    burst_exps = [m_exp]
                    prev_burst = burst_exps
                    continue
                for h in pair:
                    for kb in range(2):
                        slot = ((h // 2) % 2) * 2 + kb
                        et = esim[:, (h * 2 + kb) * 128:(h * 2 + kb) * 128 + 128]
                        ew = marks_get(ctx, f'esim{ei}_{h * 2 + kb}_war')
                        exp_marks = []
                        if kb == 0:
                            if not first:
                                exp_marks.append(p.add('S', (lambda s, et=et, slot=slot:
                                    s.activation(et[0:64, 0:64],
                                                 ctx.ps_u[0:64, slot * 128:slot * 128 + 64],
                                                 AF.Exp)), deps=[mm_last, ew],
                                    label=f'l{tag}g{g}b{b}eA1h{h}'))
                            else:
                                exp_marks.append(p.add('V', (lambda s, et=et:
                                    s.memset(et[0:64, 0:64], 0.0)), deps=[mm_last, ew],
                                    label=f'l{tag}g{g}b{b}eA1zh{h}'))
                            exp_marks.append(p.add('S', (lambda s, et=et, slot=slot:
                                s.activation(et[64:128, :],
                                             ctx.ps_u[64:128, slot * 128:slot * 128 + 128],
                                             AF.Exp)), deps=[mm_last, ew],
                                label=f'l{tag}g{g}b{b}eA2h{h}'))
                        else:
                            exp_marks.append(p.add('S', (lambda s, et=et, slot=slot:
                                s.activation(et[0:64, :],
                                             ctx.ps_u[0:64, slot * 128:slot * 128 + 128],
                                             AF.Exp)), deps=[mm_last, ew],
                                label=f'l{tag}g{g}b{b}eB1h{h}'))
                            if not last:
                                exp_marks.append(p.add('S', (lambda s, et=et, slot=slot:
                                    s.activation(et[64:128, 64:128],
                                                 ctx.ps_u[64:128, slot * 128 + 64:
                                                          slot * 128 + 128],
                                                 AF.Exp)), deps=[mm_last, ew],
                                    label=f'l{tag}g{g}b{b}eB2h{h}'))
                            else:
                                exp_marks.append(p.add('V', (lambda s, et=et:
                                    s.memset(et[64:128, 64:128], 0.0)), deps=[mm_last, ew],
                                    label=f'l{tag}g{g}b{b}eB2zh{h}'))
                        m_exp = p.barrier('S', exp_marks,
                                          label=f'l{tag}g{g}b{b}exph{h}kb{kb}')
                        marks_set(ctx, f'esim_exp_{h}_{kb}', m_exp)
                        burst_exps.append(m_exp)
                prev_burst = burst_exps
            marks_set(ctx, 'lpsu_burst', prev_burst)
            if lim_g == 'lg_sims':
                m_se = p.barrier('S', prev_burst, label=f'lgsims{g}')
                x2_stores.append(m_se)
                continue
            vA = ctx.vS[(4 * g + b) % 12]
            vB = ctx.vS[(4 * g + b + 1) % 12]
            vA_dep = marks_get(ctx, f'vS{(4 * g + b) % 12}_filled')
            vB_dep = (marks_get(ctx, f'vS{(4 * g + b + 1) % 12}_filled')
                      if not last else None)
            for half in range(2):
                o_marks = []
                for hh in range(4):
                    h = half * 4 + hh
                    po = hh * 65
                    for kb in range(2):
                        et = esim[:, (h * 2 + kb) * 128:(h * 2 + kb) * 128 + 128]
                        vt = vA if kb == 0 else vB
                        def mm(s, et=et, vt=vt, h=h, po=po, kb=kb):
                            return s.matmul(
                                ctx.ps_f[:, po:po + 65], et,
                                vt[:, h * 65:h * 65 + 65],
                                start=(kb == 0), stop=(kb == 1))
                        deps_o = [marks_get(ctx, f'esim_exp_{h}_{kb}'),
                                  vA_dep, vB_dep]
                        if kb == 0:
                            deps_o.append(marks_get(ctx, 'lps_f_war'))
                        o_marks.append(p.add('T', mm, deps=deps_o,
                                             label=f'l{tag}g{g}b{b}o{h}_{kb}'))
                def rc(s):
                    return s.reciprocal(
                        ctx.lrec[:, 0:4],
                        ctx.ps_f[:, 0:260].rearrange('p (h e) -> p h e',
                                                     e=65)[:, :, 64:65].squeeze(2))
                m_rc = p.add('V', rc, deps=o_marks, label=f'l{tag}g{g}b{b}rc{half}')
                def nm(s, half=half):
                    return s.tensor_tensor(
                        ctx.onorm[:, half * 256:half * 256 + 256]
                           .rearrange('p (h e) -> p h e', e=64),
                        ctx.ps_f[:, 0:260].rearrange('p (h e) -> p h e',
                                                     e=65)[:, :, 0:64],
                        ctx.lrec[:, 0:4].unsqueeze(2).to_broadcast([128, 4, 64]),
                        op=ALU.mult)
                m_nm = p.add('V', nm, deps=[marks_get(ctx, 'onorm_war')],
                             label=f'l{tag}g{g}b{b}nm{half}')
                marks_set(ctx, 'lps_f_war', m_nm)
            if lim_g == 'lg_o':
                x2_stores.append(marks_get(ctx, 'lps_f_war'))
                continue
            m_ot = emit_transposes(
                ctx, ctx.onorm,
                ctx.onormT, (lambda j: ctx.onormT[:, j * 128:(j + 1) * 128]),
                deps=[marks_get(ctx, 'lps_f_war'), marks_get(ctx, 'onormT_war')],
                tag=f'l{tag}g{g}b{b}ot', nch=4)
            marks_set(ctx, 'onorm_war', m_ot)
            if lim_g == 'lg_nowo':
                marks_set(ctx, 'onormT_war', m_ot)
                for h in range(8):
                    for kb in range(2):
                        marks_set(ctx, f'esim{ei}_{h * 2 + kb}_war', m_ot)
                x2_stores.append(m_ot)
                continue
            wo_marks = []
            for half in range(2):
                psum = ctx.ps_d if half == 0 else ctx.ps_e
                mm_last = None
                for kc in range(4):
                    def mm(s, kc=kc, half=half, psum=psum):
                        return s.matmul(
                            psum[:, 0:512],
                            ctx.onormT[:, kc * 128:(kc + 1) * 128],
                            ctx.wreg[:, W_OUTL + kc * 1024 + half * 512:
                                     W_OUTL + kc * 1024 + half * 512 + 512],
                            start=(kc == 0), stop=(kc == 3))
                    mm_last = p.add('T', mm,
                                    deps=([m_ot, marks_get(ctx, f'lps_{"de"[half]}_war')]
                                          if kc == 0 else []),
                                    label=f'l{tag}g{g}b{b}wo{half}_{kc}')
                wo_marks.append(mm_last)
            if getattr(ctx, 'dbg_onorm', None) is not None:
                m_dbgo = p.add('AQ', (lambda s, g=g, b=b: s.dma_start(
                    out=ctx.dbg_onorm[(g * 4 + b) * 128:(g * 4 + b) * 128 + 128, :],
                    in_=ctx.onorm[:])), deps=[m_ot], label=f'dbgo{g}{b}',
                    slot='dbgo')
                m_ot = p.barrier('V', [m_ot, m_dbgo], label=f'dbgob{g}{b}')
            m_dbge = None
            if getattr(ctx, 'dbg_esim', None) is not None:
                m_dbge = p.add('AQ', (lambda s, g=g, b=b, ei=ei: s.dma_start(
                    out=ctx.dbg_esim[(g * 4 + b) * 128:(g * 4 + b) * 128 + 128, :],
                    in_=ctx.esim[ei][:])), deps=[m_ot],
                    label=f'dbge{g}{b}', slot='dbge')
            marks_set(ctx, 'onormT_war', wo_marks[-1])
            m_ou = p.barrier('T', wo_marks, label=f'l{tag}g{g}b{b}oused')
            if m_dbge is not None:
                m_ou = p.barrier('T', [m_ou, m_dbge], label=f'l{tag}g{g}b{b}ou2')
            for h in range(8):
                for kb in range(2):
                    marks_set(ctx, f'esim{ei}_{h * 2 + kb}_war', m_ou)
            if lim_g == 'lg_noc':
                m_w = p.barrier('V', wo_marks, label=f'lgnoc{g}b{b}')
                marks_set(ctx, 'lps_d_war', m_w)
                marks_set(ctx, 'lps_e_war', m_w)
                marks_set(ctx, 'cxb_war', m_w)
                x2_stores.append(m_w)
                continue
            xb = ctx.xb[2]
            tok0 = g * 512 + b * 128
            m_ldx = p.add('SQ', (lambda s, tok0=tok0, xb=xb: s.dma_start(
                out=xb[:], in_=x_ap[tok0:tok0 + 128, :])),
                deps=[marks_get(ctx, 'cxb_war'), marks.get('xb2_war')],
                label=f'l{tag}g{g}b{b}ldx', slot='xb2')
            marks['xb2_war'] = None
            ls = ctx.lstage[b % 2]
            dep_ls = marks_get(ctx, f'lstage{b % 2}_war')
            m_c1 = p.add('V', (lambda s, xb=xb, ls=ls: s.scalar_tensor_tensor(
                ls[:, 0:512], ctx.ps_d[:, 0:512], 0.5, xb[:, 0:512],
                op0=ALU.mult, op1=ALU.add)),
                deps=[wo_marks[0], m_ldx, dep_ls], label=f'l{tag}g{g}b{b}c1')
            m_c2 = p.add('V', (lambda s, xb=xb, ls=ls: s.scalar_tensor_tensor(
                ls[:, 512:1024], ctx.ps_e[:, 0:512], 0.5, xb[:, 512:1024],
                op0=ALU.mult, op1=ALU.add)),
                deps=[wo_marks[1]], label=f'l{tag}g{g}b{b}c2')
            marks_set(ctx, 'lps_d_war', m_c1)
            marks_set(ctx, 'lps_e_war', m_c2)
            m_c3 = p.add('V', (lambda s, ls=ls: s.tensor_tensor(
                ls[:], ls[:], ctx.nullh_b[:], op=ALU.add)),
                deps=[marks['rt_b']], label=f'l{tag}g{g}b{b}c3')
            marks_set(ctx, 'cxb_war', m_c3)
            m_st = p.add('AQ', (lambda s, tok0=tok0, ls=ls: s.dma_start(
                out=x2_ap[tok0:tok0 + 128, :], in_=ls[:])),
                deps=[m_c3], label=f'l{tag}g{g}b{b}st',
                slot=f'x2st{tag}_{b % 2}')
            marks_set(ctx, f'lstage{b % 2}_war', m_st)
            x2_stores.append(m_st)
        st['q_used'] = p.barrier('T', [], label=f'l{tag}g{g}qu')
        st['k_used'] = st['q_used']
        st['v_used'] = st['q_used']
        for m in range(4):
            marks_set(ctx, f'vS{(4 * g + m) % 12}_war', st['q_used'])

    lim = getattr(ctx, 'lg_limit', None)
    if lim is not None and lim.startswith('proj'):
        for g in range(NG):
            proj_group(g)
        m_all = p.barrier('V', [gst[g]['xn2_done'] for g in range(NG)]
                          + [gst[NG - 1]['k_ready']], label=f'l{tag}_projdone')
        marks[f'x2_done_{tag}'] = m_all
        return m_all
    proj_group(0)
    for g in range(1, NG):
        proj_group(g)
        light_group(g - 1)
    light_group(NG - 1)
    m_all = p.barrier('V', x2_stores, label=f'l{tag}_x2done')
    marks[f'x2_done_{tag}'] = m_all
    marks[f'lproj_done_{tag}'] = gst[NG - 1]['xn2_done']
    return m_all


def prep_light_memsets(ctx, marks, tag, dep=None):
    p = ctx.p
    first = [dep]
    for i, vt in enumerate(ctx.vS):
        m1 = p.add('P', (lambda s, vt=vt: s.memset(vt[:], 0.0)), deps=first)
        first = []
        m2 = p.add('P', (lambda s, vt=vt: s.memset(
            vt[:].rearrange('p (h e) -> p h e', e=65)[:, :, 64:65], 1.0)))
        marks_set(ctx, f'vS{i}_war', m2)
        marks_set(ctx, f'vS{i}_filled', None)
    for bb in range(3):
        p.add('P', (lambda s, bb=bb: s.memset(ctx.esim[bb][:], 0.0)))
    for i, b in enumerate(ctx.kTl):
        p.add('P', (lambda s, b=b: s.memset(b[:], 0.0)))
    for i, b in enumerate(ctx.xn2T):
        p.add('P', (lambda s, b=b: s.memset(b[:], 0.0)))
    m = p.barrier('P', [], label=f'lmemsets{tag}')
    marks[f'lmemsets_{tag}'] = m
    pw = p.barrier('V', [marks.get(f'wffn_{tag}'), marks.get('_psu_war'),
                         marks.get('psum_free')], label=f'lpsw{tag}')
    for k in ['l_psa_war', 'l_psb_war', 'l_psc_war', 'lps_d_war', 'lps_e_war',
              'lps_f_war']:
        marks_set(ctx, k, pw)
    marks_set(ctx, 'lpsu_burst', [pw])
    for bb in range(3):
        for t in range(16):
            marks_set(ctx, f'esim{bb}_{t}_war', m)
    for k in ['hTbuf_war', 'esimh_war', 'kTh_war', 'qTh_war', 'onorm_war',
              'onormT_war', 'honorm_war', 'htc0_war', 'htc1_war',
              'hxb0_war', 'hxb1_war']:
        marks_set(ctx, k, m)
    for i in range(2):
        marks_set(ctx, f'lxb{i}_war', m)
        marks_set(ctx, f'lstage{i}_war', m)
    marks_set(ctx, 'cxb_war', m)
    return m


def prep_heavy_memsets(ctx, marks, tag, dep=None, early_dep=None):
    p = ctx.p
    pw = p.barrier('V', [dep], label=f'hpsw{tag}')
    for k in ['h_psa_war', 'h_psb_war', 'h_psc_war', 'h_psd_war',
              'h_psfu_war']:
        marks_set(ctx, k, pw)
    first = [dep]
    for i, vt in enumerate(ctx.vh):
        m1 = p.add('P', (lambda s, vt=vt: s.memset(vt[:], 0.0)), deps=first)
        first = []
        m2 = p.add('P', (lambda s, vt=vt: s.memset(
            vt[:].rearrange('p (h e) -> p h e', e=65)[:, :, 64:65], 1.0)))
        marks_set(ctx, f'vh{i}_war', m2)
        marks_set(ctx, f'vh{i}_filled', None)
    m = p.barrier('P', [], label=f'hmemsets{tag}')
    for k in ['esimh_war', 'kTh_war', 'qTh_war', 'h_psfu_war']:
        marks_set(ctx, k, m)
    me = p.barrier('V', [early_dep if early_dep is not None else dep],
                   label=f'hearly{tag}')
    for k in ['hTbuf_war', 'htc0_war', 'htc1_war', 'hxb0_war', 'hxb1_war']:
        marks_set(ctx, k, me)
    return m



def phase_heavy(ctx, marks, xln_ap, x2_ap, tag, route_res):
    p, d = ctx.p, ctx.d
    wffn = marks[f'wffn_{tag}']
    m_x2 = marks[f'x2_done_{tag}']

    def gather_norm_T(idx_sb, idx_col, dep_idx, dstT, dst_tok, tg):
        xb = ctx.xb[idx_col % 2]
        dep_xb = marks_get(ctx, f'hxb{idx_col % 2}_war')
        dep_xb2 = marks.get(f'xb{idx_col % 2}_war')
        ctx.p.bounds_vals.add(ctx.T - 1)
        def gth(s):
            return s.indirect_dma_start(
                out=xb[:], out_offset=None, in_=xln_ap,
                in_offset=IndirectOffsetOnAxis(idx_sb[:, idx_col:idx_col + 1], 0),
                bounds_check=ctx.p.bc_reg(ctx.T - 1), oob_is_err=False)
        m_g = p.add('GQ', gth, deps=[dep_idx, dep_xb, dep_xb2,
                                     marks[f'xln_done_{tag}'],
                                     marks[f'x2_done_{tag}']],
                    label=f'h{tg}_g', slot=f'hxb{idx_col % 2}')
        m_sq = p.add('S', (lambda s, xb=xb: s.activation(
            ctx.junkw[:], xb[:], AF.Square, accum_out=ctx.colv[1][:])),
            deps=[m_g, marks.get('junkw_free')], label=f'h{tg}_ssq')
        marks['junkw_free'] = m_sq
        m_rt = p.add('S', lambda s: s.activation(ctx.colv[2][:], ctx.colv[1][:],
                                                 AF.Sqrt), deps=[m_sq],
                     label=f'h{tg}_rt')
        p.add('V', lambda s: s.tensor_scalar(ctx.colv[2][:], ctx.colv[2][:],
                                             ctx.consts[:, 1:2], None,
                                             op0=ALU.max), deps=[m_rt])
        p.add('V', lambda s: s.reciprocal(ctx.colv[3][:], ctx.colv[2][:]))
        m_sc = p.add('V', lambda s: s.tensor_scalar(ctx.colv[3][:],
                                                    ctx.colv[3][:], 32.0, None,
                                                    op0=ALU.mult))
        tc = ctx.tcast[idx_col % 2]
        dep_tc = marks_get(ctx, f'htc{idx_col % 2}_war')
        m_c = p.add('S', (lambda s, xb=xb, tc=tc: s.activation(
            tc[:], xb[:], AF.Identity, scale=ctx.colv[3][:])),
            deps=[m_sc, dep_tc], label=f'h{tg}_cast')
        marks_set(ctx, f'hxb{idx_col % 2}_war', m_c)
        m_tp = emit_transposes(
            ctx, tc, dstT,
            (lambda j, dst_tok=dst_tok, dstT=dstT:
             dstT[:, j * 512 + dst_tok:j * 512 + dst_tok + 128]),
            deps=[m_c, marks_get(ctx, 'hTbuf_war')], tag=f'h{tg}_tp')
        marks_set(ctx, f'htc{idx_col % 2}_war', m_tp)
        return m_tp

    NQg = ctx.NQ // 512 if ctx.NQ >= 512 else 1
    qtok_per_g = min(ctx.NQ, 512)
    for qg in range(NQg):
        tps = []
        for t in range(qtok_per_g // 128):
            tps.append(gather_norm_T(ctx.idxq, qg * 4 + t, route_res[('idx', 0)],
                                     ctx.hT[0], t * 128, f'{tag}q{qg}_{t}'))
        m_sub = p.barrier('V', tps, label=f'h{tag}q{qg}sub')
        for c in range(4):
            psum = ctx.ps_a if c % 2 == 0 else ctx.ps_b
            ps_war = marks_get(ctx, f'h_ps{"ab"[c % 2]}_war')
            mm_last = None
            for kc in range(8):
                def mm(s, c=c, kc=kc, psum=psum):
                    return s.matmul(
                        psum[:, 0:qtok_per_g],
                        ctx.wreg[:, W_QH + kc * 512 + c * 128:
                                 W_QH + kc * 512 + c * 128 + 128],
                        ctx.hT[0][:, kc * 512:kc * 512 + qtok_per_g],
                        start=(kc == 0), stop=(kc == 7))
                mm_last = p.add('T', mm,
                                deps=([m_sub, wffn, ps_war] if kc == 0 else []),
                                label=f'h{tag}qp{qg}_{c}_{kc}')
            def ev(s, c=c, qg=qg, psum=psum):
                return s.tensor_scalar(
                    ctx.qTh[:, c * ctx.NQ + qg * 512:
                            c * ctx.NQ + qg * 512 + qtok_per_g],
                    psum[:, 0:qtok_per_g], 0.125, None, op0=ALU.mult)
            m_ev = p.add('V', ev, deps=[mm_last,
                                        marks_get(ctx, 'qTh_war')],
                         label=f'h{tag}qpe{qg}_{c}')
            marks_set(ctx, f'h_ps{"ab"[c % 2]}_war', m_ev)
        m_used = p.barrier('T', [], label=f'h{tag}q{qg}u')
        marks_set(ctx, 'htc0_war', m_used)
        marks_set(ctx, 'htc1_war', m_used)
        marks_set(ctx, 'hTbuf_war', m_used)
    m_qready = p.barrier('V', [], label=f'h{tag}_qready')

    NKVh = ctx.NKV // 2
    oev_marks = []
    for half in range(2):
        ksub_marks = []
        nsub = max(NKVh // 512, 1)
        subtok = min(NKVh, 512)
        for sub in range(nsub):
            tps = []
            for t in range(subtok // 128):
                col = half * (NKVh // 128) + sub * 4 + t
                tps.append(gather_norm_T(ctx.idxkv, col, route_res[('idx', 1)],
                                         ctx.hT[1], t * 128,
                                         f'{tag}kv{half}_{sub}_{t}'))
            m_sub = p.barrier('V', tps, label=f'h{tag}kv{half}{sub}sub')
            for c in range(4):
                psum = ctx.ps_a if c % 2 == 0 else ctx.ps_b
                ps_war = marks_get(ctx, f'h_ps{"ab"[c % 2]}_war')
                mm_last = None
                for kc in range(8):
                    def mm(s, c=c, kc=kc, psum=psum):
                        return s.matmul(
                            psum[:, 0:subtok],
                            ctx.wreg[:, W_KVH + kc * 1024 + c * 128:
                                     W_KVH + kc * 1024 + c * 128 + 128],
                            ctx.hT[1][:, kc * 512:kc * 512 + subtok],
                            start=(kc == 0), stop=(kc == 7))
                    mm_last = p.add('T', mm,
                                    deps=([m_sub, wffn, ps_war,
                                           marks_get(ctx, 'kTh_war')]
                                          if kc == 0 else []),
                                    label=f'h{tag}kp{half}{sub}_{c}_{kc}')
                def ev(s, c=c, sub=sub, psum=psum):
                    return s.tensor_copy(
                        ctx.kTh[:, c * 1024 + sub * 512:
                                c * 1024 + sub * 512 + subtok],
                        psum[:, 0:subtok])
                m_ev = p.add('V', ev, deps=[mm_last], label=f'h{tag}kpe{half}{sub}_{c}')
                marks_set(ctx, f'h_ps{"ab"[c % 2]}_war', m_ev)
                ksub_marks.append(m_ev)
            for t in range(subtok // 128):
                kb = sub * 4 + t
                vt = ctx.vh[kb]
                ps_war = marks_get(ctx, 'h_psc_war')
                mm_last = None
                for kc in range(8):
                    def mm(s, t=t, kc=kc):
                        return s.matmul(
                            ctx.ps_c[:, 0:512],
                            ctx.hT[1][:, kc * 512 + t * 128:kc * 512 + t * 128 + 128],
                            ctx.wreg[:, W_KVH + kc * 1024 + 512:
                                     W_KVH + kc * 1024 + 1024],
                            start=(kc == 0), stop=(kc == 7))
                    mm_last = p.add('T', mm,
                                    deps=([m_sub, ps_war,
                                           marks_get(ctx, f'vh{kb}_war')]
                                          if kc == 0 else []),
                                    label=f'h{tag}vp{half}{kb}_{kc}')
                def ev(s, vt=vt):
                    return s.tensor_copy(
                        vt[:].rearrange('p (h e) -> p h e', e=65)[:, :, 0:64],
                        ctx.ps_c[:, 0:512].rearrange('p (h e) -> p h e', e=64))
                m_ev = p.add('V', ev, deps=[mm_last], label=f'h{tag}vpe{half}{kb}')
                marks_set(ctx, 'h_psc_war', m_ev)
                marks_set(ctx, f'vh{kb}_filled', m_ev)
                ksub_marks.append(m_ev)
            m_used = p.barrier('T', [], label=f'h{tag}kv{half}{sub}u')
            marks_set(ctx, 'htc0_war', m_used)
            marks_set(ctx, 'htc1_war', m_used)
            marks_set(ctx, 'hTbuf_war', m_used)
        m_kv = p.barrier('V', ksub_marks, label=f'h{tag}kv{half}done')

        nkb = NKVh // 128
        for h in range(8):
            c, pb = h // 2, (h % 2) * 64
            o_first = True
            if half == 0:
                def nmm(s, c=c, pb=pb, h=h, nn=0):
                    return s.matmul(
                        ctx.ps_c[0:1, 0:min(ctx.NQ, 512)],
                        ctx.nullk_bf[pb:pb + 64, h:h + 1],
                        ctx.qTh[pb:pb + 64, c * ctx.NQ:c * ctx.NQ + min(ctx.NQ, 512)],
                        start=True, stop=True, tile_position=(pb, 0))
                m_n1 = p.add('T', nmm, deps=[m_qready, marks['nullkv'],
                                             marks_get(ctx, 'h_psc_war')],
                             label=f'h{tag}ns{half}h{h}a')
                ms = [m_n1]
                if ctx.NQ > 512:
                    def nmm2(s, c=c, pb=pb, h=h):
                        return s.matmul(
                            ctx.ps_d[0:1, 0:512],
                            ctx.nullk_bf[pb:pb + 64, h:h + 1],
                            ctx.qTh[pb:pb + 64, c * ctx.NQ + 512:c * ctx.NQ + 1024],
                            start=True, stop=True, tile_position=(pb, 0))
                    ms.append(p.add('T', nmm2,
                                    deps=[marks_get(ctx, 'h_psd_war')],
                                    label=f'h{tag}ns{half}h{h}b'))
                m_ne = p.add('S', (lambda s: s.activation(
                    ctx.esimh[0:1, 0:min(ctx.NQ, 512)],
                    ctx.ps_c[0:1, 0:min(ctx.NQ, 512)], AF.Exp)),
                    deps=ms + [marks_get(ctx, 'esimh_war')],
                    label=f'h{tag}ne{half}h{h}a')
                if ctx.NQ > 512:
                    m_ne = p.add('S', (lambda s: s.activation(
                        ctx.esimh[0:1, 512:1024], ctx.ps_d[0:1, 0:512], AF.Exp)),
                        label=f'h{tag}ne{half}h{h}b')
                marks_set(ctx, 'h_psc_war', m_ne)
                marks_set(ctx, 'h_psd_war', m_ne)
                for qc in range(ctx.NQ // 128):
                    po = (qc * 65) if qc < 7 else 0
                    pst = ctx.ps_f if qc < 7 else ctx.ps_u
                    def omm(s, qc=qc, po=po, pst=pst, h=h):
                        return s.matmul(
                            pst[:, po:po + 65],
                            ctx.esimh[0:1, qc * 128:qc * 128 + 128],
                            ctx.nullv_bf[0:1, h * 65:h * 65 + 65],
                            start=True, stop=False)
                    m_no = p.add('T', omm,
                                 deps=[m_ne, marks_get(ctx, 'h_psfu_war')],
                                 label=f'h{tag}no{half}h{h}q{qc}')
                marks_set(ctx, 'esimh_war',
                          p.barrier('T', [], label=f'h{tag}nob{half}h{h}'))
                o_first = False
            for kb in range(nkb):
                sim_ms = []
                for nn in range(ctx.NQ // 512 if ctx.NQ >= 512 else 1):
                    pst = ctx.ps_c if nn == 0 else ctx.ps_d
                    def smm(s, c=c, pb=pb, kb=kb, nn=nn, pst=pst):
                        return s.matmul(
                            pst[:, 0:min(ctx.NQ, 512)],
                            ctx.kTh[pb:pb + 64, c * 1024 + kb * 128:
                                    c * 1024 + kb * 128 + 128],
                            ctx.qTh[pb:pb + 64, c * ctx.NQ + nn * 512:
                                    c * ctx.NQ + nn * 512 + min(ctx.NQ, 512)],
                            start=True, stop=True, tile_position=(pb, 0))
                    sim_ms.append(p.add('T', smm,
                                        deps=[m_kv, m_qready,
                                              marks_get(ctx, f'h_ps{"cd"[nn]}_war')],
                                        label=f'h{tag}s{half}h{h}k{kb}n{nn}'))
                exp_ms = []
                for nn in range(ctx.NQ // 512 if ctx.NQ >= 512 else 1):
                    pst = ctx.ps_c if nn == 0 else ctx.ps_d
                    def ee(s, nn=nn, pst=pst):
                        return s.activation(
                            ctx.esimh[:, nn * 512:nn * 512 + min(ctx.NQ, 512)],
                            pst[:, 0:min(ctx.NQ, 512)], AF.Exp)
                    exp_ms.append(p.add('S', ee,
                                        deps=[sim_ms[nn],
                                              marks_get(ctx, 'esimh_war')],
                                        label=f'h{tag}e{half}h{h}k{kb}n{nn}'))
                    marks_set(ctx, f'h_ps{"cd"[nn]}_war', exp_ms[-1])
                m_exp = p.barrier('S', exp_ms, label=f'h{tag}eb{half}h{h}k{kb}')
                o_ms = []
                for qc in range(ctx.NQ // 128):
                    po = (qc * 65) if qc < 7 else 0
                    pst = ctx.ps_f if qc < 7 else ctx.ps_u
                    laststop = (kb == nkb - 1)
                    def omm(s, qc=qc, po=po, pst=pst, h=h, kb=kb,
                            laststop=laststop, o_first=o_first):
                        return s.matmul(
                            pst[:, po:po + 65],
                            ctx.esimh[:, qc * 128:qc * 128 + 128],
                            ctx.vh[kb][:, h * 65:h * 65 + 65],
                            start=False if not (o_first and kb == 0) else True,
                            stop=laststop)
                    dps = [m_exp, marks_get(ctx, f'vh{kb}_filled')]
                    if o_first and kb == 0:
                        dps.append(marks_get(ctx, 'h_psfu_war'))
                    o_ms.append(p.add('T', omm, deps=dps,
                                      label=f'h{tag}o{half}h{h}k{kb}q{qc}'))
                m_oall = p.barrier('T', [], label=f'h{tag}ou{half}h{h}k{kb}')
                marks_set(ctx, 'esimh_war', m_oall)
            n1 = min(ctx.NQ // 128, 7) * 65
            if half == 0:
                def ev1(s, h=h, n1=n1):
                    return s.tensor_copy(ctx.oacc[:, h * 520:h * 520 + n1],
                                         ctx.ps_f[:, 0:n1])
                def ev2(s, h=h):
                    return s.tensor_copy(ctx.oacc[:, h * 520 + 455:h * 520 + 520],
                                         ctx.ps_u[:, 0:65])
            else:
                def ev1(s, h=h, n1=n1):
                    return s.tensor_tensor(ctx.oacc[:, h * 520:h * 520 + n1],
                                           ctx.oacc[:, h * 520:h * 520 + n1],
                                           ctx.ps_f[:, 0:n1], op=ALU.add)
                def ev2(s, h=h):
                    return s.tensor_tensor(ctx.oacc[:, h * 520 + 455:h * 520 + 520],
                                           ctx.oacc[:, h * 520 + 455:h * 520 + 520],
                                           ctx.ps_u[:, 0:65], op=ALU.add)
            mo1 = p.add('V', ev1, deps=[m_oall], label=f'h{tag}oe{half}h{h}a')
            mo2 = (p.add('V', ev2, label=f'h{tag}oe{half}h{h}b')
                   if ctx.NQ // 128 == 8 else mo1)
            oev_marks.append(mo2)
            m_psfu = p.barrier('V', [mo1, mo2], label=f'h{tag}oef{half}h{h}')
            marks_set(ctx, 'h_psfu_war', m_psfu)
        marks_set(ctx, 'kTh_war', p.barrier('T', [], label=f'h{tag}kthu{half}'))
        for kb in range(nkb):
            marks_set(ctx, f'vh{kb}_war', marks_get(ctx, 'kTh_war'))

    m_oacc = p.barrier('V', oev_marks, label=f'h{tag}_oacc')

    nqc = ctx.NQ // 128
    oT = ctx.hT[0]
    oT_war = p.barrier('T', [], label=f'h{tag}_oTwar')
    tp_ms = []
    for cc in range(4):
        nm_ms = []
        for hh in range(2):
            h = 2 * cc + hh
            oh = ctx.oacc[:, h * 520:h * 520 + 520].rearrange(
                'p (q e) -> p q e', e=65)
            m_rc = p.add('V', (lambda s, oh=oh, hh=hh: s.reciprocal(
                ctx.lrec[:, hh * 8:hh * 8 + nqc],
                oh[:, 0:nqc, 64:65].squeeze(2))),
                deps=[m_oacc], label=f'h{tag}nr{h}')
            def nm(s, oh=oh, hh=hh):
                return s.tensor_tensor(
                    ctx.onorm[:, 0:nqc * 128].rearrange(
                        'p (q w e) -> p q w e', w=2, e=64)[:, :, hh],
                    oh[:, 0:nqc, 0:64],
                    ctx.lrec[:, hh * 8:hh * 8 + nqc].unsqueeze(2)
                       .to_broadcast([128, nqc, 64]),
                    op=ALU.mult)
            nm_ms.append(p.add('V', nm, deps=[marks_get(ctx, 'honorm_war')],
                               label=f'h{tag}nm{h}'))
        m_nm = p.barrier('V', nm_ms, label=f'h{tag}nmb{cc}')
        tr_last = None
        for qc in range(nqc):
            slot = qc % 8
            def tr(s, qc=qc, slot=slot):
                return s.transpose(ctx.ps_tr[:, slot * 128:slot * 128 + 128],
                                   ctx.onorm[:, qc * 128:qc * 128 + 128],
                                   ctx.ident_bf[:])
            tr_last = p.add('T', tr,
                            deps=([m_nm, oT_war,
                                   marks_get(ctx, 'pstr_bank_war')]
                                  if qc == 0 else []),
                            label=f'h{tag}tr{cc}_{qc}')
        ev_last = None
        for qc in range(nqc):
            slot = qc % 8
            def ev(s, cc=cc, qc=qc, slot=slot):
                return s.tensor_copy(
                    oT[:, cc * ctx.NQ + qc * 128:cc * ctx.NQ + qc * 128 + 128],
                    ctx.ps_tr[:, slot * 128:slot * 128 + 128])
            ev_last = p.add('V', ev, deps=[tr_last] if qc == 0 else [],
                            label=f'h{tag}te{cc}_{qc}')
            tp_ms.append(ev_last)
        marks_set(ctx, 'pstr_bank_war', ev_last)
        marks_set(ctx, 'honorm_war', p.barrier('T', [], label=f'h{tag}onw{cc}'))
    m_oT = p.barrier('V', tp_ms, label=f'h{tag}_oT')

    m_sc_prev = None
    for qc in range(nqc):
        wo_ms = []
        for half in range(2):
            psum = ctx.ps_c if half == 0 else ctx.ps_d
            mm_last = None
            for kc in range(4):
                def mm(s, qc=qc, kc=kc, half=half, psum=psum):
                    return s.matmul(
                        psum[:, 0:512],
                        oT[:, kc * ctx.NQ + qc * 128:kc * ctx.NQ + qc * 128 + 128],
                        ctx.wreg[:, W_OUTH + kc * 1024 + half * 512:
                                 W_OUTH + kc * 1024 + half * 512 + 512],
                        start=(kc == 0), stop=(kc == 3))
                mm_last = p.add('T', mm,
                                deps=([m_oT, marks_get(ctx, f'h_ps{"cd"[half]}_war')]
                                      if kc == 0 else []),
                                label=f'h{tag}wo{qc}_{half}_{kc}')
            wo_ms.append(mm_last)
        dd = []
        for half in range(2):
            psum = ctx.ps_c if half == 0 else ctx.ps_d
            def de(s, half=half, psum=psum):
                return s.scalar_tensor_tensor(
                    ctx.hvout[:, half * 512:half * 512 + 512], psum[:, 0:512],
                    0.5, ctx.nullh_b[:, half * 512:half * 512 + 512],
                    op0=ALU.mult, op1=ALU.subtract)
            m_d = p.add('V', de, deps=[wo_ms[half], m_sc_prev],
                        label=f'h{tag}d{qc}_{half}')
            marks_set(ctx, f'h_ps{"cd"[half]}_war', m_d)
            dd.append(m_d)
        ctx.p.bounds_vals.add(ctx.T - 1)
        def sca(s, qc=qc):
            return s.indirect_dma_start(
                out=x2_ap, out_offset=IndirectOffsetOnAxis(
                    ctx.idxq[:, qc:qc + 1], 0),
                in_=ctx.hvout[:], in_offset=None, compute_op=ALU.add,
                bounds_check=ctx.p.bc_reg(ctx.T - 1), oob_is_err=True)
        m_sc_prev = p.add('GQ', sca, deps=dd + [m_x2, route_res[('idx', 0)]],
                          label=f'h{tag}scat{qc}', slot='hscat')
    m_done = p.barrier('V', [m_sc_prev], label=f'h{tag}_done')
    marks[f'heavy_{tag}'] = m_done
    return m_done



def declare_gla(ctx):
    G = 'gla'
    ctx.xlnT = ctx.sb('g_xlnT', [128, 8 * 512], BF16, G)
    ctx.gqT = ctx.sb('g_qT', [128, 8 * 512], BF16, G)
    ctx.gkT_ = ctx.sb('g_kT', [128, 8 * 512], BF16, G)
    ctx.ggk = ctx.sb('g_gk', [128, 8 * 512], F32, G)
    ctx.gbT = ctx.sb('g_bT', [128, 1024], F32, G)
    ctx.gfT = ctx.sb('g_fT', [128, 1024], F32, G)
    ctx.gqp = ctx.sb('g_qp', [128, 1024], BF16, G)
    ctx.gkpT = ctx.sb('g_kpT', [128, 1024], BF16, G)
    ctx.gkp = ctx.sb('g_kp', [128, 1024], BF16, G)
    ctx.gv = ctx.sb('g_v', [128, 4 * 1024], BF16, G)
    ctx.gA = ctx.sb('g_A', [128, 256], BF16, G)
    ctx.gS = ctx.sb('g_S', [128, 1024], F32, G)
    ctx.gSbf = ctx.sb('g_Sbf', [128, 1024], BF16, 'glaR', side='right')
    ctx.gob = ctx.sb('g_ob', [128, 1024], F32, G)
    ctx.gsg = ctx.gfT
    ctx.gt1 = ctx.sb('g_t1', [32, 512], BF16, G)
    ctx.gtmp = [ctx.sb(f'g_tmp{i}', [128, 512], F32, G) for i in range(2)]
    ctx.gcast = ctx.sb('g_cast', [128, 1024], BF16, G)
    ctx.gofb = ctx.gkpT
    ctx.gofT = ctx.gqp
    ctx.gred = ctx.sb('g_red', [128, 16], F32, G)
    ctx.gx3 = [ctx.gob] * 2


def phase_gla(ctx, marks, x2_ap, x3_ap, xln3_ap, dep_start):
    p, d = ctx.p, ctx.d
    NG = ctx.NG
    wg = marks['wgla']
    m_S0 = p.add('V', lambda s: s.memset(ctx.gS[:], 0.0), deps=[dep_start],
                 label='gla_S0')
    gpw = p.barrier('V', [dep_start, marks.get('_psu_war')], label='gpsw')
    for k in ['g_psa_war', 'g_psb_war', 'g_psc_war', 'g_psd_war', 'g_pse_war',
              'g_psu_war', 'g_psf_war']:
        marks_set(ctx, k, gpw)
    m_t1ones = p.add('V', lambda s: s.memset(ctx.gt1[0:32, :], 1.0),
                     label='gla_t1ones')
    m_sbfz = p.add('P', lambda s: s.memset(ctx.gSbf[:], 0.0),
                   deps=[dep_start], label='gla_sbfz')
    marks_set(ctx, 'gSbf_war', m_sbfz)
    marks_set(ctx, 'gS_war', m_S0)
    x3_stores = []

    for g in range(NG):
        tile_marks = []
        for k in range(4):
            xb = ctx.xb[k % 2]
            m_ld = p.add('SQ', (lambda s, g=g, k=k, xb=xb: s.dma_start(
                out=xb[:], in_=x2_ap[g * 512 + k * 128:g * 512 + k * 128 + 128, :])),
                deps=[marks_get(ctx, f'gxb{k % 2}_war'), marks.get('heavy_1'),
                      dep_start, marks.get(f'xb{k % 2}_war')],
                label=f'gg{g}ld{k}', slot=f'xb{k % 2}')
            marks[f'xb{k % 2}_war'] = None
            p.add('V', (lambda s, xb=xb: s.bn_stats(ctx.stat[:, 0:6],
                                                    xb[:, 0:512])), deps=[m_ld])
            p.add('V', (lambda s, xb=xb: s.bn_stats(ctx.stat[:, 6:12],
                                                    xb[:, 512:1024])))
            m_ag = p.add('V', lambda s: s.bn_aggr(ctx.stat[:, 12:14],
                                                  ctx.stat[:, 0:12]))
            m_sq = p.add('S', lambda s: s.activation(ctx.colv[1][:],
                                                     ctx.stat[:, 13:14], AF.Sqrt,
                                                     bias=ctx.consts[:, 0:1]),
                         deps=[m_ag], label=f'gg{g}sq{k}')
            p.add('V', lambda s: s.reciprocal(ctx.colv[2][:], ctx.colv[1][:]),
                  deps=[m_sq])
            m_nb = p.add('V', lambda s: s.scalar_tensor_tensor(
                ctx.colv[3][:], ctx.stat[:, 12:13], -1.0, ctx.colv[2][:],
                op0=ALU.mult, op1=ALU.mult))
            m_c = p.add('S', (lambda s, xb=xb: s.activation(
                ctx.gcast[:], xb[:], AF.Identity, bias=ctx.colv[3][:],
                scale=ctx.colv[2][:])),
                deps=[m_nb, marks_get(ctx, 'gcast_war')], label=f'gg{g}c{k}')
            marks_set(ctx, f'gxb{k % 2}_war',
                      p.barrier('S', [m_c], label=f'gg{g}xbw{k}'))
            m_tp = emit_transposes(
                ctx, ctx.gcast, ctx.xlnT,
                (lambda j, k=k: ctx.xlnT[:, j * 512 + k * 128:
                                         j * 512 + k * 128 + 128]),
                deps=[m_c, marks_get(ctx, 'xlnT_war')], tag=f'gg{g}k{k}')
            marks_set(ctx, 'gcast_war', m_tp)
            tile_marks.append(m_tp)
        m_xT = p.barrier('V', tile_marks, label=f'gg{g}xT')

        pm = {}
        for nm, woff, dstbuf in [('q', W_GQ, ctx.gqT), ('k', W_GK, ctx.gkT_)]:
            evs = []
            for c in range(8):
                psum = ctx.ps_a if c % 2 == 0 else ctx.ps_b
                ps_war = marks_get(ctx, f'g_ps{"ab"[c % 2]}_war')
                mm_last = None
                for kc in range(8):
                    def mm(s, c=c, kc=kc, psum=psum, woff=woff):
                        return s.matmul(
                            psum[:, 0:512],
                            ctx.wreg[:, woff + kc * 1024 + c * 128:
                                     woff + kc * 1024 + c * 128 + 128],
                            ctx.xlnT[:, kc * 512:(kc + 1) * 512],
                            start=(kc == 0), stop=(kc == 7))
                    mm_last = p.add('T', mm,
                                    deps=([m_xT, wg, ps_war] if kc == 0 else []),
                                    label=f'gg{g}{nm}p{c}_{kc}')
                def ev(s, c=c, psum=psum, dstbuf=dstbuf):
                    return s.tensor_copy(dstbuf[:, c * 512:(c + 1) * 512],
                                         psum[:, 0:512])
                m_ev = p.add('V', ev, deps=[mm_last,
                                            marks_get(ctx, f'g{nm}T_war')],
                             label=f'gg{g}{nm}e{c}')
                marks_set(ctx, f'g_ps{"ab"[c % 2]}_war', m_ev)
                evs.append(m_ev)
            pm[nm] = p.barrier('V', evs, label=f'gg{g}{nm}done')
        mm_last = None
        ps_war = marks_get(ctx, 'g_psc_war')
        for kc in range(8):
            def mm(s, kc=kc):
                return s.matmul(ctx.ps_c[0:16, 0:512],
                                ctx.wg1_bf[:, kc * 16:kc * 16 + 16],
                                ctx.xlnT[:, kc * 512:(kc + 1) * 512],
                                start=(kc == 0), stop=(kc == 7))
            mm_last = p.add('T', mm, deps=([m_xT, wg, ps_war] if kc == 0 else []),
                            label=f'gg{g}t1_{kc}')
        m_t1 = p.add('V', lambda s: s.tensor_copy(ctx.gt1[0:16, :],
                                                  ctx.ps_c[0:16, 0:512]),
                     deps=[mm_last, m_t1ones, marks_get(ctx, 'gt1_war')],
                     label=f'gg{g}t1e')
        marks_set(ctx, 'g_psc_war', m_t1)
        gk_evs = []
        for c in range(8):
            def mmz(s, c=c):
                return s.matmul(ctx.ps_c[:, 0:512],
                                ctx.wg2a_bf[0:17, c * 128:c * 128 + 128],
                                ctx.gt1[0:17, :], start=True, stop=True)
            m_z = p.add('T', mmz, deps=[m_t1, marks_get(ctx, 'g_psc_war'), wg],
                        label=f'gg{g}z{c}')
            m_u1 = p.add('S', (lambda s: s.activation(ctx.gtmp[0][:],
                                                      ctx.ps_c[:, 0:512], AF.Abs)),
                         deps=[m_z, marks_get(ctx, 'gtmp0_war')],
                         label=f'gg{g}abs{c}')
            p.add('S', (lambda s: s.activation(
                ctx.gtmp[1][:], ctx.gtmp[0][:], AF.Exp,
                scale=ctx.consts[:, 2:3])),
                deps=[marks_get(ctx, 'gtmp1_war'), marks['consts']],
                label=f'gg{g}exp{c}')
            m_u3 = p.add('S', (lambda s: s.activation(
                ctx.gtmp[0][:], ctx.gtmp[1][:], AF.Ln, bias=1.0)),
                label=f'gg{g}ln{c}')
            marks_set(ctx, 'gtmp1_war', m_u3)
            def fin(s, c=c):
                return s.scalar_tensor_tensor(
                    ctx.ggk[:, c * 512:(c + 1) * 512], ctx.ps_c[:, 0:512], 0.0,
                    ctx.gtmp[0][:], op0=ALU.min, op1=ALU.subtract)
            m_gk = p.add('V', fin, deps=[m_u3, marks_get(ctx, 'ggk_war')],
                         label=f'gg{g}gk{c}')
            marks_set(ctx, 'gtmp0_war', m_gk)
            marks_set(ctx, 'g_psc_war', m_gk)
            gk_evs.append(m_gk)
        pm['gk'] = p.barrier('V', gk_evs, label=f'gg{g}gkdone')
        v_evs = []
        for t in range(4):
            for half in range(2):
                psum = ctx.ps_d if half == 0 else ctx.ps_e
                ps_war = marks_get(ctx, f'g_ps{"de"[half]}_war')
                mm_last = None
                for kc in range(8):
                    def mm(s, t=t, kc=kc, half=half, psum=psum):
                        return s.matmul(
                            psum[:, 0:512],
                            ctx.xlnT[:, kc * 512 + t * 128:kc * 512 + t * 128 + 128],
                            ctx.wreg[:, W_GV + kc * 1024 + half * 512:
                                     W_GV + kc * 1024 + half * 512 + 512],
                            start=(kc == 0), stop=(kc == 7))
                    mm_last = p.add('T', mm,
                                    deps=([m_xT, wg, ps_war] if kc == 0 else []),
                                    label=f'gg{g}v{t}_{half}_{kc}')
                def ev(s, t=t, half=half, psum=psum):
                    return s.tensor_copy(
                        ctx.gv[:, t * 1024 + half * 512:
                               t * 1024 + half * 512 + 512],
                        psum[:, 0:512])
                m_ev = p.add('V', ev, deps=[mm_last, marks_get(ctx, 'gv_war')],
                             label=f'gg{g}ve{t}_{half}')
                marks_set(ctx, f'g_ps{"de"[half]}_war', m_ev)
                v_evs.append(m_ev)
        pm['v'] = p.barrier('V', v_evs, label=f'gg{g}vdone')
        if g == NG - 1:
            marks['gla_qkfree'] = p.barrier('V', [pm['q'], pm['k']],
                                            label='gla_qkfree')
            marks['gla_vfree'] = pm['v']

        glast_T = []
        for cw in range(4):
            ch = g * 4 + cw
            par = ch % 2
            m_scan0 = None
            for r in range(8):
                def sc(s, r=r, cw=cw):
                    return s.tensor_tensor_scan(
                        ctx.gbT[:, r * 128:(r + 1) * 128],
                        ctx.ggk[:, r * 512 + cw * 128:r * 512 + cw * 128 + 128],
                        ctx.zero128[:], 0.0, op0=ALU.add, op1=ALU.add)
                m_s = p.add('V', sc,
                            deps=[pm['gk'], marks_get(ctx, 'gbT_war'),
                                  marks['consts']],
                            label=f'gc{ch}scan{r}')
                if m_scan0 is None:
                    m_scan0 = m_s
            m_b = p.barrier('V', [], label=f'gc{ch}b')
            m_f = p.add('S', lambda s: s.activation(ctx.gfT[:], ctx.gbT[:],
                                                    AF.Exp,
                                                    scale=ctx.consts[:, 4:5]),
                        deps=[m_b, marks_get(ctx, 'gfT_war'),
                              marks_get(ctx, 'gsg_war')], label=f'gc{ch}f')
            m_e = p.add('S', lambda s: s.activation(ctx.gbT[:], ctx.gbT[:],
                                                    AF.Exp,
                                                    scale=ctx.consts[:, 3:4]),
                        deps=[marks_get(ctx, 'geT_war')], label=f'gc{ch}e')
            qv = ctx.gqT[:].rearrange('p (c t) -> p c t', t=512)[:, :, cw * 128:
                                                                cw * 128 + 128]
            kv = ctx.gkT_[:].rearrange('p (c t) -> p c t', t=512)[:, :, cw * 128:
                                                                 cw * 128 + 128]
            m_qp = p.add('V', (lambda s, qv=qv: s.tensor_tensor(
                ctx.gqp[:].rearrange('p (c t) -> p c t', t=128), qv,
                ctx.gbT[:].rearrange('p (c t) -> p c t', t=128), op=ALU.mult)),
                deps=[m_e, pm['q'], marks_get(ctx, 'gqp_war')],
                label=f'gc{ch}qp')
            m_kp = p.add('V', (lambda s, kv=kv: s.tensor_tensor(
                ctx.gkpT[:].rearrange('p (c t) -> p c t', t=128), kv,
                ctx.gfT[:].rearrange('p (c t) -> p c t', t=128), op=ALU.mult)),
                deps=[m_f, pm['k'], marks_get(ctx, 'gkpT_war')],
                label=f'gc{ch}kp')
            p.add('V', (lambda s, par=par: s.tensor_copy(
                ctx.gSbf[0:64, :].rearrange('p (h e) -> p h e', e=128)[:, :, 0:64],
                ctx.gS[0:64, par * 512:par * 512 + 512]
                   .rearrange('p (h e) -> p h e', e=64))),
                deps=[marks_get(ctx, 'gS_war'), marks_get(ctx, 'gSbf_war')],
                label=f'gc{ch}sbfa')
            m_sbf = p.add('V', (lambda s, par=par: s.tensor_copy(
                ctx.gSbf[64:128, :].rearrange('p (h e) -> p h e', e=128)[:, :, 64:128],
                ctx.gS[64:128, par * 512:par * 512 + 512]
                   .rearrange('p (h e) -> p h e', e=64))),
                label=f'gc{ch}sbfb')
            m_kt = emit_transposes(
                ctx, ctx.gkpT, ctx.gkp,
                (lambda j: ctx.gkp[:, j * 128:(j + 1) * 128]),
                deps=[m_kp, marks_get(ctx, 'gkp_war')], tag=f'gc{ch}kt')
            oev = []
            for hp in range(8):
                def mmo2(s, hp=hp):
                    return s.matmul(
                        ctx.ps_u[:, 0:128],
                        ctx.gqp[:, hp * 128:hp * 128 + 128],
                        ctx.gSbf[:, hp * 128:hp * 128 + 128],
                        start=True, stop=True)
                m_o2 = p.add('T', mmo2,
                             deps=[m_sbf, m_qp, marks_get(ctx, 'g_psu_war')],
                             label=f'gc{ch}o2_{hp}')
                def mmt(s, hp=hp, cw=cw):
                    return s.matmul(
                        ctx.ps_u[:, 384:512],
                        ctx.gkp[:, hp * 128:hp * 128 + 128],
                        ctx.gv[:, cw * 1024 + hp * 128:cw * 1024 + hp * 128 + 128],
                        start=True, stop=True)
                t_ms = [p.add('T', mmt, deps=[m_kt, pm['v']],
                              label=f'gc{ch}T{hp}')]
                def mma0(s, hp=hp):
                    return s.matmul(
                        ctx.ps_u[:, 128:256],
                        ctx.gkpT[0:64, hp * 128:hp * 128 + 128],
                        ctx.gqp[0:64, hp * 128:hp * 128 + 128],
                        start=True, stop=True, tile_position=(0, 0))
                m_a0 = p.add('T', mma0, deps=[m_kp],
                             label=f'gc{ch}A{hp}_0')
                def mma1(s, hp=hp):
                    return s.matmul(
                        ctx.ps_f[:, 0:128],
                        ctx.gkpT[64:128, hp * 128:hp * 128 + 128],
                        ctx.gqp[64:128, hp * 128:hp * 128 + 128],
                        start=True, stop=True, tile_position=(64, 0))
                m_a1 = p.add('T', mma1,
                             deps=[marks_get(ctx, 'g_psf_war')],
                             label=f'gc{ch}A{hp}_1')
                def mev0(s):
                    return s.tensor_tensor(ctx.gA[:, 0:128], ctx.ps_u[:, 128:256],
                                           ctx.trimask[:], op=ALU.mult)
                m_A0 = p.add('V', mev0, deps=[m_a0, marks['tri'],
                                              marks_get(ctx, 'gA_war')],
                             label=f'gc{ch}Am{hp}0')
                def mev1(s):
                    return s.tensor_tensor(ctx.gA[:, 128:256], ctx.ps_f[:, 0:128],
                                           ctx.trimask[:], op=ALU.mult)
                m_A1 = p.add('V', mev1, deps=[m_a1], label=f'gc{ch}Am{hp}1')
                m_A = m_A1
                o_ms = [m_o2]
                for hh in range(2):
                    def mmo1(s, hp=hp, hh=hh, cw=cw):
                        return s.matmul(
                            ctx.ps_u[:, 256 + hh * 64:256 + hh * 64 + 64],
                            ctx.gA[:, hh * 128:hh * 128 + 128],
                            ctx.gv[:, cw * 1024 + hp * 128 + hh * 64:
                                   cw * 1024 + hp * 128 + hh * 64 + 64],
                            start=True, stop=True)
                    o_ms.append(p.add('T', mmo1,
                                      deps=[m_A0 if hh == 0 else m_A1, pm['v']],
                                      label=f'gc{ch}o1_{hp}_{hh}'))
                def oev_a(s, hp=hp):
                    return s.tensor_copy(ctx.gob[:, hp * 128:hp * 128 + 128],
                                         ctx.ps_u[:, 256:384])
                p.add('V', oev_a, deps=o_ms + t_ms + [marks_get(ctx, 'gob_war')],
                      label=f'gc{ch}oea{hp}')
                def oev_(s, hp=hp):
                    return s.tensor_tensor(ctx.gob[:, hp * 128:hp * 128 + 128],
                                           ctx.gob[:, hp * 128:hp * 128 + 128],
                                           ctx.ps_u[:, 0:128], op=ALU.add)
                m_oe = p.add('V', oev_, label=f'gc{ch}oe{hp}')
                oev.append(m_oe)
                def su1a(s, hp=hp, par=par):
                    return s.tensor_tensor(
                        ctx.gS[0:64, (1 - par) * 512 + hp * 64:
                               (1 - par) * 512 + hp * 64 + 64],
                        ctx.gS[0:64, par * 512 + hp * 64:par * 512 + hp * 64 + 64],
                        ctx.ps_u[0:64, 384:448], op=ALU.add)
                m_s1a = p.add('V', su1a, deps=t_ms + o_ms, label=f'gc{ch}su1a_{hp}')
                def su1b(s, hp=hp, par=par):
                    return s.tensor_tensor(
                        ctx.gS[64:128, (1 - par) * 512 + hp * 64:
                               (1 - par) * 512 + hp * 64 + 64],
                        ctx.gS[64:128, par * 512 + hp * 64:par * 512 + hp * 64 + 64],
                        ctx.ps_u[64:128, 448:512], op=ALU.add)
                m_s1b = p.add('V', su1b, label=f'gc{ch}su1b_{hp}')
                def su2(s, hp=hp, par=par, cw=cw):
                    return s.tensor_scalar(
                        ctx.gS[:, (1 - par) * 512 + hp * 64:
                               (1 - par) * 512 + hp * 64 + 64],
                        ctx.gS[:, (1 - par) * 512 + hp * 64:
                               (1 - par) * 512 + hp * 64 + 64],
                        ctx.gbT[:, hp * 128 + 127:hp * 128 + 128], None,
                        op0=ALU.mult)
                m_s2 = p.add('V', su2, label=f'gc{ch}su2_{hp}')
                marks_set(ctx, 'g_psu_war', p.barrier('V', [m_A0, m_oe, m_s2],
                                                      label=f'gc{ch}pu{hp}'))
                marks_set(ctx, 'g_psf_war', m_A1)
            m_S = p.barrier('V', [], label=f'gc{ch}S')
            marks_set(ctx, 'gS_war', m_S)
            m_bw = p.barrier('V', [m_S], label=f'gc{ch}bw')
            marks_set(ctx, 'gbT_war', m_bw)
            marks_set(ctx, 'geT_war', None)
            marks_set(ctx, 'geT_war', m_S)
            marks_set(ctx, 'gfT_war', m_S)
            m_qpw = p.barrier('T', [], label=f'gc{ch}qpw')
            marks_set(ctx, 'gqp_war', m_qpw)
            marks_set(ctx, 'gSbf_war', m_qpw)
            marks_set(ctx, 'gkpT_war', marks_get(ctx, 'gqp_war'))
            marks_set(ctx, 'gkp_war', marks_get(ctx, 'gqp_war'))
            marks_set(ctx, 'gA_war', marks_get(ctx, 'gqp_war'))

            m_ob = p.barrier('V', oev, label=f'gc{ch}ob')
            m_sqr = p.add('V', lambda s: s.tensor_tensor(ctx.gsg[:], ctx.gob[:],
                                                         ctx.gob[:],
                                                         op=ALU.mult),
                          deps=[m_ob, marks_get(ctx, 'gsg_war')],
                          label=f'gc{ch}sq')
            m_red = p.add('V', lambda s: s.tensor_reduce(
                ctx.gred[:].unsqueeze(2),
                ctx.gsg[:].rearrange('p (h e) -> p h e', e=64),
                axis=AX.X, op=ALU.add), label=f'gc{ch}red')
            m_r1 = p.add('S', lambda s: s.activation(ctx.gred[:], ctx.gred[:],
                                                     AF.Sqrt,
                                                     bias=ctx.consts[:, 0:1],
                                                     scale=ctx.consts[:, 5:6]),
                         deps=[m_red], label=f'gc{ch}r1')
            m_r2 = p.add('V', lambda s: s.reciprocal(ctx.gred[:], ctx.gred[:]),
                         deps=[m_r1], label=f'gc{ch}r2')
            sil_ms = []
            for half in range(2):
                psum = ctx.ps_d if half == 0 else ctx.ps_e
                ps_war = marks_get(ctx, f'g_ps{"de"[half]}_war')
                mm_last = None
                for kc in range(8):
                    def mm(s, kc=kc, half=half, psum=psum, cw=cw):
                        return s.matmul(
                            psum[:, 0:512],
                            ctx.xlnT[:, kc * 512 + cw * 128:kc * 512 + cw * 128 + 128],
                            ctx.wreg[:, W_GG + kc * 1024 + half * 512:
                                     W_GG + kc * 1024 + half * 512 + 512],
                            start=(kc == 0), stop=(kc == 7))
                    mm_last = p.add('T', mm,
                                    deps=([m_xT, ps_war, m_sqr] if kc == 0 else []),
                                    label=f'gc{ch}gp{half}_{kc}')
                m_sil = p.add('S', (lambda s, half=half, psum=psum: s.activation(
                    ctx.gsg[:, half * 512:half * 512 + 512], psum[:, 0:512],
                    AF.Silu)), deps=[mm_last, m_red], label=f'gc{ch}sil{half}')
                sil_ms.append(m_sil)
                marks_set(ctx, f'g_ps{"de"[half]}_war', m_sil)
            m_sg2 = p.add('V', lambda s: s.tensor_tensor(
                ctx.gsg[:].rearrange('p (h e) -> p h e', e=64),
                ctx.gsg[:].rearrange('p (h e) -> p h e', e=64),
                ctx.gred[:].unsqueeze(2).to_broadcast([128, 16, 64]),
                op=ALU.mult), deps=sil_ms + [m_r2], label=f'gc{ch}sg2')
            m_of = p.add('V', lambda s: s.tensor_tensor(ctx.gofb[:], ctx.gob[:],
                                                        ctx.gsg[:],
                                                        op=ALU.mult),
                         deps=[marks_get(ctx, 'gofb_war'),
                               marks_get(ctx, 'gqp_war')], label=f'gc{ch}of')
            marks_set(ctx, 'gsg_war', m_of)
            marks_set(ctx, 'gob_war', m_of)
            m_ofT = emit_transposes(
                ctx, ctx.gofb, ctx.gofT,
                (lambda j: ctx.gofT[:, j * 128:(j + 1) * 128]),
                deps=[m_of, marks_get(ctx, 'gofT_war'),
                      marks_get(ctx, 'gqp_war')], tag=f'gc{ch}ofT')
            wo_ms = []
            for half in range(2):
                psum = ctx.ps_d if half == 0 else ctx.ps_e
                mm_last = None
                for kc in range(8):
                    def mm(s, kc=kc, half=half, psum=psum):
                        return s.matmul(
                            psum[:, 0:512],
                            ctx.gofT[:, kc * 128:(kc + 1) * 128],
                            ctx.wreg[:, W_GO + kc * 1024 + half * 512:
                                     W_GO + kc * 1024 + half * 512 + 512],
                            start=(kc == 0), stop=(kc == 7))
                    mm_last = p.add('T', mm,
                                    deps=([m_ofT,
                                           marks_get(ctx, f'g_ps{"de"[half]}_war')]
                                          if kc == 0 else []),
                                    label=f'gc{ch}wo{half}_{kc}')
                wo_ms.append(mm_last)
            marks_set(ctx, 'gofT_war', wo_ms[-1])
            m_qpw2 = p.barrier('T', [], label=f'gc{ch}qpw2')
            marks_set(ctx, 'gqp_war', m_qpw2)
            marks_set(ctx, 'gkpT_war', m_qpw2)
            marks_set(ctx, 'gofb_war', None)
            xb = ctx.xb[2]
            x3b = ctx.gx3[cw % 2]
            tok0 = ch * 128
            m_ldx = p.add('SQ', (lambda s, tok0=tok0, xb=xb: s.dma_start(
                out=xb[:], in_=x2_ap[tok0:tok0 + 128, :])),
                deps=[marks_get(ctx, 'gxb2_war'), marks.get('xb2_war')],
                label=f'gc{ch}ldx', slot='xb2')
            marks['xb2_war'] = None
            add_ms = []
            for half in range(2):
                psum = ctx.ps_d if half == 0 else ctx.ps_e
                def ad(s, half=half, psum=psum, xb=xb, x3b=x3b):
                    return s.tensor_tensor(x3b[:, half * 512:half * 512 + 512],
                                           xb[:, half * 512:half * 512 + 512],
                                           psum[:, 0:512], op=ALU.add)
                m_ad = p.add('V', ad, deps=[wo_ms[half], m_ldx,
                                            marks_get(ctx, 'gx3_war')],
                             label=f'gc{ch}add{half}')
                marks_set(ctx, f'g_ps{"de"[half]}_war', m_ad)
                add_ms.append(m_ad)
            m_x3 = p.barrier('V', add_ms, label=f'gc{ch}x3')
            marks_set(ctx, 'gxb2_war', m_x3)
            m_st1 = p.add('AQ', (lambda s, tok0=tok0, x3b=x3b: s.dma_start(
                out=x3_ap[tok0:tok0 + 128, :], in_=x3b[:])),
                deps=[m_x3], label=f'gc{ch}st3', slot='x3st')
            p.add('V', (lambda s, x3b=x3b: s.bn_stats(ctx.stat[:, 0:6],
                                                      x3b[:, 0:512])),
                  deps=[m_x3])
            p.add('V', (lambda s, x3b=x3b: s.bn_stats(ctx.stat[:, 6:12],
                                                      x3b[:, 512:1024])))
            m_ag = p.add('V', lambda s: s.bn_aggr(ctx.stat[:, 12:14],
                                                  ctx.stat[:, 0:12]))
            m_sq2 = p.add('S', lambda s: s.activation(ctx.colv[1][:],
                                                      ctx.stat[:, 13:14],
                                                      AF.Sqrt,
                                                      bias=ctx.consts[:, 0:1]),
                          deps=[m_ag], label=f'gc{ch}lsq')
            p.add('V', lambda s: s.reciprocal(ctx.colv[2][:], ctx.colv[1][:]),
                  deps=[m_sq2])
            m_nb = p.add('V', lambda s: s.scalar_tensor_tensor(
                ctx.colv[3][:], ctx.stat[:, 12:13], -1.0, ctx.colv[2][:],
                op0=ALU.mult, op1=ALU.mult))
            m_lnw = p.add('S', (lambda s, x3b=x3b: s.activation(
                ctx.junkw[:], x3b[:], AF.Identity, bias=ctx.colv[3][:],
                scale=ctx.colv[2][:])),
                deps=[m_nb, marks.get('junkw_free')], label=f'gc{ch}lnw')
            m_st2 = p.add('AQ', (lambda s, tok0=tok0: s.dma_start(
                out=xln3_ap[tok0:tok0 + 128, :], in_=ctx.junkw[:])),
                deps=[m_lnw], label=f'gc{ch}st4', slot='xln3st')
            marks['junkw_free'] = m_st2
            m_x3w = p.barrier('V', [m_st1, m_st2], label=f'gc{ch}x3w')
            marks_set(ctx, 'gx3_war', m_x3w)
            marks_set(ctx, 'gob_war', m_x3w)
            x3_stores.extend([m_st1, m_st2])
            glast_T.append(marks_get(ctx, 'gqp_war'))
        m_gu = p.barrier('T', [], label=f'gg{g}used')
        m_gv = p.barrier('V', [], label=f'gg{g}usedv')
        marks_set(ctx, 'xlnT_war', m_gu)
        marks_set(ctx, 'gv_war', m_gu)
        marks_set(ctx, 'gqT_war', m_gv)
        marks_set(ctx, 'gkT_war', m_gv)
        marks_set(ctx, 'ggk_war', m_gv)
        marks_set(ctx, 'gt1_war', m_gu)
    m_all = p.barrier('V', x3_stores, label='gla_done')
    marks['gla_done'] = m_all
    return m_all



def build_all(T=8192, NQ=1024, NKV=2048, n_iters=50, bis_iters=30,
              detect_races=True, upto='all'):
    ctx = Ctx(T, NQ, NKV, n_iters=n_iters, bis_iters=bis_iters,
              detect_races=detect_races)
    declare(ctx)
    marks = {}
    prep_consts(ctx, marks)
    prep_weights_ffn(ctx, marks, '1')
    prep_null_kv(ctx, marks)
    phase_ln_scores(ctx, marks, ctx.d['x'], ctx.d['XLN'], '1')
    r1 = routing(ctx, marks, '1', marks['scores_1'])
    declare_light(ctx, '1')
    prep_light_memsets(ctx, marks, '1')
    if upto == 'light1':
        ctx.dbg_onorm = ctx.dram('d_onormall', [T, 512], BF16, 'ExternalOutput')
        ctx.dbg_esim = ctx.dram('d_esimall', [T, 2048], BF16, 'ExternalOutput')
    phase_light(ctx, marks, ctx.d['XLN'], ctx.d['x'], ctx.d['X2'], '1')
    if upto == 'light1':
        dumps = [('d_qT', ctx.qTl[0], BF16), ('d_kT', ctx.kTl[0], BF16),
                 ('d_v0', ctx.vS[0], BF16), ('d_v1', ctx.vS[1], BF16),
                 ('d_v3', ctx.vS[3], BF16), ('d_v4', ctx.vS[4], BF16),
                 ('d_esim', ctx.esim[0], BF16), ('d_onorm', ctx.onorm, BF16),
                 ('d_onormT', ctx.onormT, BF16), ('d_xn2T', ctx.xn2T[0], BF16)]
        dps = []
        for nm, t, dt_ in dumps:
            ap = ctx.dram(nm, list(t.shape), dt_, 'ExternalOutput')
            dps.append(ctx.p.add('AQ', (lambda s, ap=ap, t=t: s.dma_start(
                out=ap, in_=t[:])), deps=[marks['x2_done_1']],
                label=f'dump{nm}', slot='dbg'))
        ctx.p.barrier('S', [marks['x2_done_1'], r1[('idx', 0)],
                            r1[('idx', 1)]] + dps, label='FINAL')
        ctx.p.emit()
        return ctx, marks
    ctx.free_group('lightA1')
    declare_heavy(ctx, '1')
    prep_heavy_memsets(ctx, marks, '1', dep=marks['x2_done_1'],
                       early_dep=marks['lproj_done_1'])
    phase_heavy(ctx, marks, ctx.d['XLN'], ctx.d['X2'], '1', r1)
    ctx.free_group('lightB1')
    ctx.free_group('heavy1')
    if upto == 'heavy1':
        ctx.p.barrier('S', [marks['heavy_1']], label='FINAL')
        ctx.p.emit()
        return ctx, marks
    prep_weights_gla(ctx, marks, dep=None)
    declare_gla(ctx)
    phase_gla(ctx, marks, ctx.d['X2'], ctx.d['X3'], ctx.d['XLN3'],
              dep_start=marks['heavy_1'])
    ctx.free_group('glaR')
    ctx.free_group('gla')
    if upto == 'gla':
        ctx.p.barrier('S', [marks['gla_done']], label='FINAL')
        ctx.p.emit()
        return ctx, marks
    marks['psum_free'] = marks['gla_done']
    prep_weights_ffn(ctx, marks, '2', dep=None, reload=True)
    phase_ln_scores(ctx, marks, None, ctx.d['XLN3'], '2',
                    dep=marks['gla_done'])
    marks['xln_done_2'] = marks['gla_done']
    r2 = routing(ctx, marks, '2', marks['scores_2'],
                 extra_dep=marks['heavy_1'])
    declare_light(ctx, '2')
    prep_light_memsets(ctx, marks, '2', dep=marks['gla_done'])
    phase_light(ctx, marks, ctx.d['XLN3'], ctx.d['X3'], ctx.d['out'], '2')
    ctx.free_group('lightA2')
    declare_heavy(ctx, '2')
    prep_heavy_memsets(ctx, marks, '2', dep=marks['x2_done_2'],
                       early_dep=marks['lproj_done_2'])
    phase_heavy(ctx, marks, ctx.d['XLN3'], ctx.d['out'], '2', r2)
    ctx.p.barrier('S', [marks['heavy_2'], marks['x2_done_2']], label='FINAL')
    ctx.p.emit()
    return ctx, marks



_NC_CACHE = {}


def _get_program():
    key = 'main'
    if key not in _NC_CACHE:
        ctx, marks = build_all()
        _NC_CACHE[key] = ctx
    return _NC_CACHE[key]


def kernel(**inputs):
    x = np.ascontiguousarray(inputs['L0_attn_x'], dtype=np.float32)
    B = x.shape[0]
    ctx = _get_program()

    w_kv = np.ascontiguousarray(
        np.asarray(inputs['L0_ffn_w_kv_h'], np.float32)
        .reshape(1024, 8, 2, 64).transpose(0, 2, 1, 3).reshape(1024, 1024))
    wg2a = np.ascontiguousarray(np.concatenate(
        [np.asarray(inputs['L0_attn_Wg2'], np.float32),
         np.asarray(inputs['L0_attn_bg2'], np.float32)[None, :]], axis=0))
    base = {
        'Wq': np.ascontiguousarray(inputs['L0_attn_Wq'], np.float32),
        'Wk': np.ascontiguousarray(inputs['L0_attn_Wk'], np.float32),
        'Wv': np.ascontiguousarray(inputs['L0_attn_Wv'], np.float32),
        'Wg': np.ascontiguousarray(inputs['L0_attn_Wg'], np.float32),
        'Wo': np.ascontiguousarray(inputs['L0_attn_Wo'], np.float32),
        'Wg1': np.ascontiguousarray(inputs['L0_attn_Wg1'], np.float32),
        'Wg2a': wg2a,
        'g_norm_w': np.ascontiguousarray(
            np.asarray(inputs['L0_attn_g_norm_w'], np.float32)[None, :]),
        'qkv_l': np.ascontiguousarray(inputs['L0_ffn_w_qkv_l'], np.float32),
        'w_out_l': np.ascontiguousarray(inputs['L0_ffn_w_out_l'], np.float32),
        'null_q': np.ascontiguousarray(
            np.asarray(inputs['L0_ffn_null_q'], np.float32)[None, :]),
        'rt_q': np.ascontiguousarray(inputs['L0_ffn_rt_q'], np.float32),
        'rt_kv': np.ascontiguousarray(inputs['L0_ffn_rt_kv'], np.float32),
        'rms_g': np.ascontiguousarray(
            np.asarray(inputs['L0_ffn_rms_g'], np.float32)[None, :]),
        'w_q_h': np.ascontiguousarray(inputs['L0_ffn_w_q_h'], np.float32),
        'w_kv_h': w_kv,
        'null_kv': np.ascontiguousarray(
            np.asarray(inputs['L0_ffn_null_kv'], np.float32).reshape(2, 512)),
        'w_out_h': np.ascontiguousarray(inputs['L0_ffn_w_out_h'], np.float32),
        'ln_g': np.ascontiguousarray(
            np.asarray(inputs['L0_ffn_ln_g'], np.float32)[None, :]),
        'ln_b': np.ascontiguousarray(
            np.asarray(inputs['L0_ffn_ln_b'], np.float32)[None, :]),
    }
    in_maps = []
    for b in range(B):
        m = dict(base)
        m['x'] = np.ascontiguousarray(x[b])
        in_maps.append(m)
    res = run_bass_kernel_spmd(ctx.nc, in_maps, core_ids=list(range(B)))
    out = np.stack([np.asarray(res.results[b]['out']) for b in range(B)], axis=0)
    return out.astype(np.float32)
